# version 15
# baseline (speedup 1.0000x reference)
import math
from contextlib import ExitStack

import numpy as np
import ml_dtypes
import concourse.bass as bass
import concourse.mybir as mybir
from concourse.bass_utils import run_bass_kernel_spmd

F32 = mybir.dt.float32
BF16 = mybir.dt.bfloat16
AF = mybir.ActivationFunctionType
ALU = mybir.AluOpType
AX = mybir.AxisListType.X

T = 2048
C = 1024
NH = 16
HD = 64
NKC = C // 128
NTT = T // 128
I = 2048
NE = 8
RMS_EPS = 1e-5
FRMS_EPS = 1.1920929e-07
LN_EPS = 1e-5
ALPHA = 1.702
LIMIT = 7.0
MSHIFT = 60.0
DBG = {}

ENGINES = ("pe", "act", "dve", "pool", "sp")
BLOCK_ATTR = {"pe": "tensor", "act": "scalar", "dve": "vector", "pool": "gpsimd", "sp": "sync"}


class Buf:
    __slots__ = ("w", "r", "mw", "wl")
    def __init__(self, mw=False):
        self.w = None
        self.r = []
        self.mw = mw
        self.wl = []


class Prog:
    DMA_POOL = 20

    def __init__(self):
        self.eng_ops = {e: [] for e in ENGINES}
        self.dma_count = {"sp": 0, "pool": 0}

    def op(self, eng, fn, reads=(), writes=(), deps=(), dma=False):
        dep_list = [d for d in deps if d is not None]
        for b in reads:
            if b.mw:
                dep_list.extend(b.wl)
            elif b.w is not None:
                dep_list.append(b.w)
        for b in writes:
            dep_list.extend(b.r)
            if b.mw:
                pass
            elif b.w is not None:
                dep_list.append(b.w)
        rec = {"eng": eng, "fn": fn, "deps": tuple(dep_list), "dma": dma,
               "needed": False, "val": None, "dma_idx": None}
        if dma:
            rec["dma_idx"] = self.dma_count[eng]
            self.dma_count[eng] += 1
        self.eng_ops[eng].append(rec)
        for b in reads:
            b.r.append(rec)
        for b in writes:
            if b.mw:
                b.wl.append(rec)
            else:
                b.w = rec
                b.r = []
        return rec

    def barrier(self):
        lasts = [self.eng_ops[e][-1] for e in ENGINES if self.eng_ops[e]]
        for q in ("sp", "pool"):
            slot_last = {}
            for r in self.eng_ops[q]:
                if r["dma"]:
                    slot_last[r["dma_idx"] % self.DMA_POOL] = r
            lasts.extend(slot_last.values())
        nops = []
        for e in ENGINES:
            nops.append(self.op(e, lambda eng: eng.nop(), deps=lasts))
        return nops

    def emit(self, nc, block, sems, dma_sems):
        for e in ENGINES:
            for r in self.eng_ops[e]:
                for d in r["deps"]:
                    d["needed"] = True
        for e in ENGINES:
            cum = 0
            for r in self.eng_ops[e]:
                if r["dma"]:
                    k = r["dma_idx"]
                    r["val"] = (k // self.DMA_POOL + 1) * 16
                elif r["needed"]:
                    cum += 1
                    r["val"] = cum
        counts = {e: sum(1 for r in self.eng_ops[e]) for e in ENGINES}

        def sem_of(r):
            if r["dma"]:
                return dma_sems[r["eng"]][r["dma_idx"] % self.DMA_POOL]
            return sems[r["eng"]]

        def body(eng_name):
            def go(eng):
                lastwait = {}
                sem = sems[eng_name]
                for r in self.eng_ops[eng_name]:
                    if r["dma"] and r["dma_idx"] >= self.DMA_POOL:
                        dsem = dma_sems[eng_name][r["dma_idx"] % self.DMA_POOL]
                        v = (r["dma_idx"] // self.DMA_POOL) * 16
                        key = id(dsem)
                        if lastwait.get(key, -1) < v:
                            eng.wait_ge(dsem, v)
                            lastwait[key] = v
                    for d in r["deps"]:
                        dsem = sem_of(d)
                        v = d["val"]
                        if v is None:
                            raise RuntimeError("dep without value")
                        key = id(dsem)
                        if lastwait.get(key, -1) < v:
                            eng.wait_ge(dsem, v)
                            lastwait[key] = v
                    instr = r["fn"](eng)
                    if r["dma"]:
                        instr.then_inc(sem_of(r), 16)
                    elif r["needed"]:
                        instr.then_inc(sem, 1)
            return go

        for e in ENGINES:
            getattr(block, BLOCK_ATTR[e])(body(e))
        return counts


def splits(lo, hi, align=512):
    out = []
    a = lo
    while a < hi:
        b = min(hi, (a // align + 1) * align)
        out.append((a, b))
        a = b
    return out


def host_consts():
    bf = ml_dtypes.bfloat16
    c = {}
    c["ident_bf"] = np.eye(128, dtype=bf)
    c["ident_f32"] = np.eye(128, dtype=np.float32)
    r = np.arange(128)
    c["slmat"] = (r[:, None] > r[None, :]).astype(bf)
    c["utmask"] = (r[:, None] <= r[None, :]).astype(bf)
    kc = np.zeros((64, T), dtype=np.float32)
    ind = (np.arange(T)[None, :] // 128 == np.arange(16)[:, None]).astype(np.float32)
    kc[0:16] = ind
    kc[32:48] = ind
    c["kconst"] = kc.astype(bf)
    ec = np.zeros((128, 256), dtype=np.float32)
    for J in range(16):
        ec[:, 16 * J + J] = 1.0
    c["ecols"] = ec.astype(bf)
    r16 = np.arange(16)
    c["l16"] = (r16[:, None] > r16[None, :]).astype(bf)
    c["neg1"] = np.full((1, 16), -1.0, dtype=bf)
    c["ones64"] = np.ones((64, 1), dtype=bf)
    op_ = np.zeros((128, 65), dtype=np.float32)
    op_[0:64, 0] = 1.0
    op_[64:128, 64] = 1.0
    c["onespair"] = op_.astype(bf)
    c["ones1x64"] = np.ones((1, 64), dtype=bf)
    c["onesrow"] = np.ones((1, 128), dtype=bf)
    oh = np.zeros((8, 8 * 128), dtype=np.float32)
    for e in range(8):
        oh[e, 128 * e:128 * (e + 1)] = 1.0 / ALPHA
    c["oneh8"] = oh.astype(bf)
    fc = np.zeros((128, 4), dtype=np.float32)
    fc[:, 0] = LN_EPS
    fc[:, 1] = FRMS_EPS
    fc[:, 2] = -math.log(8.0)
    fc[:, 3] = 1.0
    c["fconst"] = fc
    t = np.arange(T, dtype=np.float64)
    c["cosb64"] = np.tile(np.cos(t)[None, :], (64, 1)).astype(bf)
    sb64 = np.zeros((64, T), dtype=np.float64)
    sb64[0:32] = -np.sin(t)[None, :]
    sb64[32:64] = np.sin(t)[None, :]
    c["sinb64"] = sb64.astype(bf)
    return c


def build_nc(taps=(), heads=NH, experts=NE, seqs=(0, 1), attn_lvl=6):
    nc = bass.Bass("TRN2", target_bir_lowering=False)
    tap_set = set(taps)

    x_d = nc.dram_tensor("x", [2, T, C], F32, kind="ExternalInput").ap()
    wT = {}
    for nm in ("wqT", "wkT", "wvT", "wpT", "wprojT"):
        wT[nm] = nc.dram_tensor(nm, [C, C], F32, kind="ExternalInput").ap()
    qrw_d = nc.dram_tensor("qrw", [1, 64], F32, kind="ExternalInput").ap()
    krw_d = nc.dram_tensor("krw", [1, 64], F32, kind="ExternalInput").ap()
    rwl_d = nc.dram_tensor("rw_l", [128, 64], F32, kind="ExternalInput").ap()
    rb_d = nc.dram_tensor("rb", [1, 8], F32, kind="ExternalInput").ap()
    gup_d = nc.dram_tensor("gup", [NE, C, 2 * I], F32, kind="ExternalInput").ap()
    gub_d = nc.dram_tensor("gub_l", [NE, 128, 32], F32, kind="ExternalInput").ap()
    dwn_d = nc.dram_tensor("dwn", [NE, I, C], F32, kind="ExternalInput").ap()
    dbias_d = nc.dram_tensor("dbias", [NE, C], F32, kind="ExternalInput").ap()
    cst_d = {}
    cshapes = {"ident_bf": ([128, 128], BF16), "ident_f32": ([128, 128], F32),
               "slmat": ([128, 128], BF16), "utmask": ([128, 128], BF16),
               "kconst": ([64, T], BF16), "ecols": ([128, 256], BF16),
               "l16": ([16, 16], BF16), "neg1": ([1, 16], BF16),
               "ones64": ([64, 1], BF16), "ones1x64": ([1, 64], BF16),
               "onespair": ([128, 65], BF16),
               "onesrow": ([1, 128], BF16), "oneh8": ([8, 8 * 128], BF16),
               "cosb64": ([64, T], BF16), "sinb64": ([64, T], BF16),
               "fconst": ([128, 4], F32)}
    for nm, (shp, dt) in cshapes.items():
        cst_d[nm] = nc.dram_tensor(nm, shp, dt, kind="ExternalInput").ap()
    out_d = nc.dram_tensor("out", [2, T, C], F32, kind="ExternalOutput").ap()
    xnscr = nc.dram_tensor("xnscr", [128, NKC * 2 * T], BF16, kind="Internal").ap()
    wscr = {nm: nc.dram_tensor("wscr_" + nm, [C, C], BF16, kind="Internal").ap()
            for nm in ("wqT", "wkT", "wvT", "wpT", "wprojT")}
    gupb = nc.dram_tensor("gupb", [NE, C, 2 * I], BF16, kind="Internal").ap()
    dwnb = nc.dram_tensor("dwnb", [NE, I, C], BF16, kind="Internal").ap()
    accd = nc.dram_tensor("accd", [NKC, 128, 2 * T], F32, kind="Internal").ap()
    yscr = nc.dram_tensor("yscr", [2, NH, 64, T], BF16, kind="Internal").ap()

    tap_out = {}
    p = Prog()

    def tap(name, ap_fn, shape, dtype, deps):
        if name not in tap_set:
            return
        td = nc.dram_tensor("tap_" + name, shape, F32, kind="ExternalOutput").ap()
        tap_out[name] = td
        p.op("pool", lambda e, td=td, ap_fn=ap_fn: e.dma_start(out=td, in_=ap_fn()),
             deps=deps, dma=True)

    with ExitStack() as es:
        sems = {}
        for e in ENGINES:
            sems[e] = es.enter_context(nc.semaphore("sem_" + e))
        dma_sems = {q: [es.enter_context(nc.semaphore(f"dsem_{q}_{i}"))
                        for i in range(Prog.DMA_POOL)] for q in ("sp", "pool")}

        pA = es.enter_context(nc.psum_tensor("pA", [128, 1024], F32)).ap()
        pB = es.enter_context(nc.psum_tensor("pB", [128, 1024], F32)).ap()
        pY = es.enter_context(nc.psum_tensor("pY", [128, 1024], F32)).ap()
        pS1 = es.enter_context(nc.psum_tensor("pS1", [128, 512], F32)).ap()
        pS2 = es.enter_context(nc.psum_tensor("pS2", [128, 512], F32)).ap()
        bufA0, bufA1, bufB0, bufB1 = Buf(), Buf(), Buf(), Buf()
        bufY, bufS1, bufS2 = Buf(), Buf(), Buf()

        sb = lambda name, shp, dt: es.enter_context(nc.sbuf_tensor(name, shp, dt)).ap()
        cs = {nm: sb("sb_" + nm, shp, dt) for nm, (shp, dt) in cshapes.items() if nm != "kconst"}
        qrw_b = sb("qrw_b", [65, 64], BF16)
        krw_b = sb("krw_b", [65, 64], BF16)
        onesp64 = sb("onesp64", [65, 64], BF16)
        rw_sb = sb("rw_sb", [128, 128], BF16)
        rw_f = sb("rw_f", [128, 64], F32)
        rb_b = sb("rb_b", [1, 8], BF16)
        db_sb = sb("db_sb", [8, C], BF16)
        gthi = sb("gthi", [8, 2 * T], BF16)
        gtlo = sb("gtlo", [8, 2 * T], BF16)
        b_cs = {nm: Buf() for nm in cshapes}
        b_qrw, b_krw, b_rw, b_rb, b_db, b_rwf = Buf(), Buf(), Buf(), Buf(), Buf(), Buf()
        b_gthi, b_gtlo = Buf(mw=True), Buf(mw=True)
        b_xnscr, b_accd, b_yscr, b_out = Buf(mw=True), Buf(mw=True), Buf(mw=True), Buf(mw=True)

        b_wscr = {nm: Buf(mw=True) for nm in wscr}
        b_gupb = Buf(mw=True)
        b_dwnb = Buf(mw=True)
        for nm in wscr:
            p.op("pool", lambda e, nm=nm: e.dma_start(out=wscr[nm], in_=wT[nm]),
                 writes=[b_wscr[nm]], dma=True)
        for ex in range(NE):
            p.op("pool", lambda e, ex=ex: e.dma_start(out=gupb[ex], in_=gup_d[ex]),
                 writes=[b_gupb], dma=True)
            p.op("pool", lambda e, ex=ex: e.dma_start(out=dwnb[ex], in_=dwn_d[ex]),
                 writes=[b_dwnb], dma=True)
        for nm in cs:
            pass
        for nm in [n for n in cshapes if n != "kconst"]:
            p.op("sp", lambda e, nm=nm: e.dma_start(out=cs[nm][:, :], in_=cst_d[nm]),
                 writes=[b_cs[nm]], dma=True)
        p.op("pool", lambda e: e.dma_start(out=qrw_b[0:1, :], in_=qrw_d), writes=[b_qrw], dma=True)
        p.op("pool", lambda e: e.dma_start(out=qrw_b[64:65, :], in_=qrw_d), writes=[b_qrw], dma=True)
        p.op("pool", lambda e: e.dma_start(out=krw_b[0:1, :], in_=krw_d), writes=[b_krw], dma=True)
        p.op("pool", lambda e: e.dma_start(out=krw_b[64:65, :], in_=krw_d), writes=[b_krw], dma=True)
        p.op("sp", lambda e: e.dma_start(out=onesp64[0:1, :], in_=cst_d["ones1x64"]),
             writes=[b_cs["ones1x64"]], dma=True)
        p.op("sp", lambda e: e.dma_start(out=onesp64[64:65, :], in_=cst_d["ones1x64"]),
             writes=[b_cs["ones1x64"]], dma=True)
        p.op("pool", lambda e: e.dma_start(out=rw_f[:, :], in_=rwl_d), writes=[b_rwf], dma=True)
        p.op("dve", lambda e: e.tensor_copy(rw_sb[:, 0:64], rw_f[:, :]), reads=[b_rwf], writes=[b_rw])
        p.op("dve", lambda e: e.tensor_tensor(out=rw_sb[:, 64:128], in0=rw_f[:, :],
                                              in1=rw_sb[:, 0:64], op=ALU.subtract),
             reads=[b_rwf], writes=[b_rw])
        p.op("pool", lambda e: e.dma_start(out=rb_b[:, :], in_=rb_d), writes=[b_rb], dma=True)
        p.op("pool", lambda e: e.dma_start(out=db_sb[:, :], in_=dbias_d), writes=[b_db], dma=True)

        main_es = ExitStack()
        msb = lambda name, shp, dt: main_es.enter_context(nc.sbuf_tensor(name, shp, dt)).ap()
        lsb = msb

        xstage = [lsb(f"xstage{i}", [128, C], F32) for i in range(2)]
        b_xstage = [Buf(), Buf()]
        xntm = [lsb(f"xntm{i}", [128, C], BF16) for i in range(2)]
        b_xntm = [Buf(), Buf()]
        xnfst = [lsb(f"xnfst{i}", [128, C], BF16) for i in range(2)]
        b_xnfst = [Buf(), Buf()]
        dump = lsb("dump", [128, C], BF16)
        b_dump = Buf()
        nmean = [lsb(f"nmean{i}", [128, 1], F32) for i in range(2)]
        b_nmean = [Buf(), Buf()]
        vsum = [lsb(f"vsum{i}", [128, 1], F32) for i in range(2)]
        b_vsum = [Buf(), Buf()]
        rstd_ln = [lsb(f"rstdln{i}", [128, 1], F32) for i in range(2)]
        b_rstd_ln = [Buf(), Buf()]
        sm8 = lsb("sm8", [128, 8], F32)
        m8a = lsb("m8a", [128, 1], F32)
        m8b = lsb("m8b", [128, 8], F32)
        g8 = lsb("g8", [128, 8], F32)
        b_sm8, b_m8a, b_m8b, b_g8 = Buf(), Buf(), Buf(), Buf()

        for s in range(2):
            for tt in range(NTT):
                par = tt % 2
                gc0 = s * T + tt * 128
                xs, bx = xstage[par], b_xstage[par]
                p.op("sp", lambda e, s=s, tt=tt, xs=xs: e.dma_start(
                    out=xs[:, :], in_=x_d[s, tt * 128:(tt + 1) * 128, :]),
                    writes=[bx], dma=True)
                nm, bnm = nmean[par], b_nmean[par]
                p.op("dve", lambda e, xs=xs, nm=nm: e.reduce_sum(nm[:, :], xs[:, :], axis=AX),
                     reads=[bx], writes=[bnm])
                p.op("dve", lambda e, nm=nm: e.tensor_scalar(
                    out=nm[:, :], in0=nm[:, :], scalar1=-1.0 / C, scalar2=None, op0=ALU.mult),
                    writes=[bnm])
                vs, bvs = vsum[par], b_vsum[par]
                p.op("act", lambda e, xs=xs, nm=nm, vs=vs: e.activation(
                    out=dump[:, :], in_=xs[:, :], func=AF.Square, bias=nm[:, :], accum_out=vs[:, :]),
                    reads=[bx, bnm], writes=[b_dump, bvs])
                rl, brl = rstd_ln[par], b_rstd_ln[par]
                p.op("act", lambda e, vs=vs, rl=rl: e.activation(
                    out=rl[:, :], in_=vs[:, :], func=AF.Ln, scale=1.0 / C,
                    bias=cs["fconst"][:, 0:1]),
                    reads=[bvs, b_cs["fconst"]], writes=[brl])
                p.op("act", lambda e, rl=rl: e.activation(
                    out=rl[:, :], in_=rl[:, :], func=AF.Exp, scale=-0.5),
                    writes=[brl])
                xt, bxt = xntm[par], b_xntm[par]
                p.op("dve", lambda e, xs=xs, nm=nm, rl=rl, xt=xt: e.tensor_scalar(
                    out=xt[:, :], in0=xs[:, :], scalar1=nm[:, :], op0=ALU.add,
                    scalar2=rl[:, :], op1=ALU.mult),
                    reads=[bx, bnm, brl], writes=[bxt])
                if "xn_tm" in tap_set:
                    if "xn_tm" not in tap_out:
                        tap_out["xn_tm"] = nc.dram_tensor("tap_xn_tm", [2 * NTT, 128, C], F32,
                                                          kind="ExternalOutput").ap()
                    p.op("pool", lambda e, s=s, tt=tt, xt=xt: e.dma_start(
                        out=tap_out["xn_tm"][s * NTT + tt], in_=xt[:, :]),
                        reads=[bxt], dma=True)
                pT = pS2.bitcast(BF16)
                p.op("pe", lambda e, xt=xt, pT=pT: [
                    e.transpose(out=pT[:, kc * 128:(kc + 1) * 128],
                                in_=xt[:, kc * 128:(kc + 1) * 128],
                                identity=cs["ident_bf"][:, :])
                    for kc in range(NKC)][-1],
                    reads=[bxt, b_cs["ident_bf"]], writes=[bufS2])
                xf, bxf = xnfst[par], b_xnfst[par]
                p.op("dve", lambda e, pT=pT, xf=xf: e.tensor_copy(xf[:, :], pT[:, :]),
                     reads=[bufS2], writes=[bxf])
                p.op("sp", lambda e, xf=xf, gc0=gc0: e.dma_start(
                    out=xnscr.rearrange("p (kc t) -> p kc t", kc=NKC)[:, :, gc0:gc0 + 128],
                    in_=xf.rearrange("p (kc t) -> p kc t", kc=NKC)[:, :, :]),
                    reads=[bxf], writes=[b_xnscr], dma=True)
                p.op("pe", lambda e, xf=xf: [
                    e.matmul(out=pS1[:, 0:8], lhsT=xf[:, kc * 128:(kc + 1) * 128],
                             rhs=rw_sb[:, hl * 64 + kc * 8:hl * 64 + (kc + 1) * 8],
                             start=(kc == 0 and hl == 0), stop=False)
                    for hl in range(2) for kc in range(NKC)][-1],
                    reads=[bxf, b_rw], writes=[bufS1])
                p.op("pe", lambda e: e.matmul(out=pS1[:, 0:8], lhsT=cs["onesrow"][:, :],
                                              rhs=rb_b[:, :], start=False, stop=True),
                     reads=[b_cs["onesrow"], b_rb], writes=[bufS1])
                p.op("dve", lambda e: e.reduce_max(m8a[:, :], pS1[:, 0:8], axis=AX),
                     reads=[bufS1], writes=[b_m8a])
                p.op("dve", lambda e: e.tensor_scalar(
                    out=m8a[:, :], in0=m8a[:, :], scalar1=-1.0, scalar2=None, op0=ALU.mult),
                    writes=[b_m8a])
                p.op("dve", lambda e: e.tensor_copy(sm8[:, :], pS1[:, 0:8]),
                     reads=[bufS1], writes=[b_sm8])
                p.op("act", lambda e: e.activation(out=sm8[:, :], in_=sm8[:, :],
                                                   func=AF.Exp, bias=m8a[:, :]),
                     reads=[b_m8a], writes=[b_sm8])
                p.op("dve", lambda e: e.tensor_scalar(
                    out=dump[:, 0:8], in0=sm8[:, :], scalar1=1.0, scalar2=None,
                    op0=ALU.mult, op1=ALU.add, accum_out=m8a[:, :]),
                    reads=[b_sm8], writes=[b_dump, b_m8a])
                p.op("dve", lambda e: e.reciprocal(m8a[:, :], m8a[:, :]), writes=[b_m8a])
                p.op("dve", lambda e: e.tensor_scalar(
                    out=sm8[:, :], in0=sm8[:, :], scalar1=m8a[:, :], scalar2=None, op0=ALU.mult),
                    reads=[b_m8a], writes=[b_sm8])
                p.op("dve", lambda e: e.reduce_max(m8a[:, :], sm8[:, :], axis=AX),
                     reads=[b_sm8], writes=[b_m8a])
                p.op("dve", lambda e: e.tensor_scalar(
                    out=m8b[:, :], in0=sm8[:, :], scalar1=m8a[:, :], scalar2=None, op0=ALU.is_ge),
                    reads=[b_sm8, b_m8a], writes=[b_m8b])
                p.op("dve", lambda e: e.scalar_tensor_tensor(
                    out=m8b[:, :], in0=m8b[:, :], scalar=-4.0, in1=sm8[:, :],
                    op0=ALU.mult, op1=ALU.add),
                    reads=[b_sm8], writes=[b_m8b])
                p.op("dve", lambda e: e.reduce_max(m8a[:, :], m8b[:, :], axis=AX),
                     reads=[b_m8b], writes=[b_m8a])
                p.op("dve", lambda e: e.scalar_tensor_tensor(
                    out=g8[:, :], in0=sm8[:, :], scalar=m8a[:, :], in1=sm8[:, :],
                    op0=ALU.is_ge, op1=ALU.mult),
                    reads=[b_sm8, b_m8a], writes=[b_g8])
                p.op("pe", lambda e: e.transpose(out=pS2[0:8, 0:128], in_=g8[:, :],
                                                 identity=cs["ident_f32"][:, :]),
                     reads=[b_g8, b_cs["ident_f32"]], writes=[bufS2])
                p.op("dve", lambda e, gc0=gc0: e.tensor_copy(
                    gthi[:, gc0:gc0 + 128], pS2[0:8, 0:128]),
                    reads=[bufS2], writes=[b_gthi])
                p.op("dve", lambda e, gc0=gc0: e.tensor_tensor(
                    out=gtlo[:, gc0:gc0 + 128], in0=pS2[0:8, 0:128],
                    in1=gthi[:, gc0:gc0 + 128], op=ALU.subtract),
                    reads=[bufS2, b_gthi], writes=[b_gtlo])
        tap("xn_fm", lambda: xnscr, [128, NKC * 2 * T], BF16, deps=list(b_xnscr.wl))
        tap("g8", lambda: gthi, [8, 2 * T], BF16, deps=list(b_gthi.wl))
        p.barrier()

        xng = [msb(f"xng{i}", [128, NKC * 512], BF16) for i in range(2)]
        b_xng = [Buf(), Buf()]
        wstream = msb("wstream", [128, 8192], BF16)
        b_wh = [Buf(), Buf()]
        b_wproj = Buf()
        biasT = msb("biasT", [128, 17408], BF16)
        b_biasT = [Buf() for _ in range(16)]
        offJ = [2048 * J - 64 * J * (J - 1) for J in range(16)]
        ls1_ = [msb(f"ls1_{i}", [128, 1024], F32) for i in range(2)]
        b_ls1_ = [Buf(), Buf()]
        ls1, b_ls1 = ls1_[0], b_ls1_[0]
        bufY0, bufY1 = Buf(), Buf()
        q_ext = [msb(f"q_ext{i}", [128, T], BF16) for i in range(2)]
        k_ext = [msb(f"k_ext{i}", [128, T], BF16) for i in range(2)]
        protc = [msb(f"protc{i}", [128, T], BF16) for i in range(2)]
        p64x2 = [msb(f"p64x2{i}", [128, T], BF16) for i in range(2)]
        v_ext = [msb(f"v_ext{i}", [128, 16 * 68], BF16) for i in range(2)]
        b_qe = [Buf(), Buf()]
        b_ke = [Buf(), Buf()]
        b_pr = [Buf(), Buf()]
        b_p6 = [Buf(), Buf()]
        b_ve = [Buf(), Buf()]
        Tf = [msb(f"Tf{i}", [128, 512], F32) for i in range(3)]
        b_Tf = [Buf(), Buf(), Buf()]
        sqb = [msb(f"sqb{i}", [128, 512], BF16) for i in range(3)]
        b_sq = [Buf(), Buf(), Buf()]
        v_fm = msb("v_fm", [128, T], BF16)
        b_vfm = Buf()
        ET = [msb(f"ET{i}", [128, 1024], BF16) for i in range(2)]
        b_ET = [Buf(), Buf()]
        csth = msb("csth", [16, 512], BF16)
        cstl = msb("cstl", [16, 512], BF16)
        b_csth, b_cstl = Buf(), Buf()
        rows65 = [msb(f"rows65_{i}", [65, 512], F32) for i in range(2)]
        b_rows65 = [Buf(), Buf()]
        rstd65 = [msb(f"rstd65_{i}", [65, 512], BF16) for i in range(3)]
        b_rstd65 = [Buf(), Buf(), Buf()]
        rowkq = [msb(f"rowkq{i}", [1, 512], F32) for i in range(2)]
        b_rowkq = [Buf(), Buf()]
        kmx = msb("kmx", [1, 1], F32)
        kmg = msb("kmg", [1, 1], F32)
        b_kmx, b_kmg = Buf(), Buf()
        m_row = msb("m_row", [1, T], BF16)
        b_mrow = Buf()
        rzrow = msb("rzrow", [1, 1024], F32)
        rzhi = msb("rzhi", [1, 1024], BF16)
        rzlo = msb("rzlo", [1, 1024], BF16)
        b_rz, b_rzhi, b_rzlo = Buf(), Buf(), Buf()
        rzb_sb = msb("rzb_sb", [64, 1024], F32)
        b_rzb = Buf()
        ystage = [rows65[i].bitcast(BF16)[0:64, 0:1024] for i in range(2)]
        b_yst = b_rows65
        ao_st = [p64x2[0].bitcast(F32)[:, 0:512]]
        b_ao = [b_p6[0]]

        for i in range(2):
            p.op("dve", lambda e, i=i: e.memset(q_ext[i][:, :], 0.0), writes=[b_qe[i]])
            p.op("sp", lambda e, i=i: e.dma_start(out=k_ext[i][0:64, :], in_=cst_d["kconst"]),
                 writes=[b_ke[i]], dma=True)
            p.op("dve", lambda e, i=i: e.memset(v_ext[i][:, :], 0.0), writes=[b_ve[i]])
            p.op("dve", lambda e, i=i: e.memset(
                v_ext[i].rearrange("p (b c) -> p b c", c=68)[:, :, 64:65], 1.0),
                writes=[b_ve[i]])

        PSMALL = [(pS1, bufS1), (pS2, bufS2)]

        for s in seqs:
            sc0 = s * T
            for hp in range(heads // 2):
                wpar = hp % 2
                wof = wpar * 4096
                bw = b_wh[wpar]
                for ti, nm in enumerate(("wkT", "wqT", "wpT", "wvT")):
                    p.op("sp", lambda e, nm=nm, ti=ti, hp=hp, wof=wof: e.dma_start(
                        out=wstream[:, wof + ti * 1024: wof + (ti + 1) * 1024].rearrange("p (kc m) -> p kc m", kc=NKC)[:, :, :],
                        in_=wscr[nm].rearrange("(kc p) m -> p kc m", p=128)[:, :, hp * 128:(hp + 1) * 128]),
                        writes=[bw], reads=[b_wscr[nm]],
                        deps=([b_wproj.w] if b_wproj.w is not None else []) + list(b_wproj.r), dma=True)

                for g in range(4):
                    c0 = g * 512
                    xpar = g % 2
                    xn_g, bxn = xng[xpar], b_xng[xpar]
                    p.op("sp", lambda e, sc0=sc0, c0=c0, xn_g=xn_g: e.dma_start(
                        out=xn_g.rearrange("p (kc c) -> p kc c", kc=NKC)[:, :, :],
                        in_=xnscr.rearrange("p (kc t) -> p kc t", kc=NKC)[:, :, sc0 + c0:sc0 + c0 + 512]),
                        reads=[b_xnscr], writes=[bxn], dma=True)
                    pbank = {"k": (pA[:, 0:512], bufA0), "q": (pA[:, 512:1024], bufA1),
                             "p": (pB[:, 0:512], bufB0), "v": (pB[:, 512:1024], bufB1)}
                    sbank = {"k": (pS1, bufS1), "q": (pS2, bufS2), "p": (pS1, bufS1)}
                    tfm = {"k": 0, "q": 1, "p": 2}
                    if attn_lvl < 0.2:
                        continue
                    for ti, tn in enumerate(("k", "q", "p", "v")):
                        pp_, bpp = pbank[tn]
                        wslice = lambda kc, ti=ti, wof=wof: wstream[:, wof + ti * 1024 + kc * 128: wof + ti * 1024 + (kc + 1) * 128]
                        p.op("pe", lambda e, pp_=pp_, wslice=wslice, xn_g=xn_g: [
                            e.matmul(out=pp_, lhsT=wslice(kc), rhs=xn_g[:, kc * 512:(kc + 1) * 512],
                                     start=(kc == 0), stop=(kc == NKC - 1))
                            for kc in range(NKC)][-1],
                            reads=[bw, bxn], writes=[bpp])
                        if tn == "v":
                            p.op("dve", lambda e, pp_=pp_, c0=c0: e.tensor_copy(
                                v_fm[:, c0:c0 + 512], pp_),
                                reads=[bpp], writes=[b_vfm])
                        else:
                            tf, btf = Tf[tfm[tn]], b_Tf[tfm[tn]]
                            p.op("dve", lambda e, pp_=pp_, tf=tf: e.tensor_copy(
                                tf[:, :], pp_),
                                reads=[bpp], writes=[btf])
                            sq_h, bsq_h = sqb[tfm[tn]], b_sq[tfm[tn]]
                            p.op("act", lambda e, tf=tf, sq_h=sq_h: e.activation(
                                out=sq_h[:, :], in_=tf[:, :], func=AF.Square),
                                reads=[btf], writes=[bsq_h])
                    if attn_lvl < 0.25:
                        continue
                    for tn in ("k", "q", "p"):
                        ps_, bps_ = sbank[tn]
                        sq_h, bsq_h = sqb[tfm[tn]], b_sq[tfm[tn]]
                        p.op("pe", lambda e, ps_=ps_, sq_h=sq_h: e.matmul(
                            out=ps_[0:65, :], lhsT=cs["onespair"][:, :], rhs=sq_h[:, :],
                            start=True, stop=True),
                            reads=[bsq_h, b_cs["onespair"]], writes=[bps_])
                        r65, br65 = rows65[tfm[tn] % 2], b_rows65[tfm[tn] % 2]
                        rs65, brs65 = rstd65[tfm[tn]], b_rstd65[tfm[tn]]
                        epsc = 0 if tn in ("k", "q") else 1
                        p.op("dve", lambda e, ps_=ps_, r65=r65: e.tensor_copy(
                            r65[:, :], ps_[0:65, :]),
                            reads=[bps_], writes=[br65])
                        p.op("act", lambda e, epsc=epsc, r65=r65: e.activation(
                            out=r65[:, :], in_=r65[:, :],
                            func=AF.Ln, scale=1.0 / 64, bias=cs["fconst"][0:65, epsc:epsc + 1]),
                            reads=[b_cs["fconst"]], writes=[br65])
                        if tn == "q":
                            p.op("act", lambda e, r65=r65, rs65=rs65: e.activation(
                                out=rs65[:, :], in_=r65[:, :],
                                func=AF.Exp, scale=-0.5, bias=cs["fconst"][0:65, 2:3]),
                                reads=[br65, b_cs["fconst"]], writes=[brs65])
                        else:
                            p.op("act", lambda e, r65=r65, rs65=rs65: e.activation(
                                out=rs65[:, :], in_=r65[:, :], func=AF.Exp, scale=-0.5),
                                reads=[br65], writes=[brs65])
                    for tn in ("k", "q", "p"):
                        ps_, bps_ = sbank[tn]
                        tf, btf = Tf[tfm[tn]], b_Tf[tfm[tn]]
                        rs65, brs65 = rstd65[tfm[tn]], b_rstd65[tfm[tn]]
                        for hh in range(2):
                            p.op("pe", lambda e, hh=hh, ps_=ps_, rs65=rs65,
                                 lrow={"k": krw_b, "q": qrw_b, "p": onesp64}[tn]: e.matmul(
                                out=ps_[64 * hh:64 * hh + 64, :],
                                lhsT=lrow[64 * hh:64 * hh + 1, :],
                                rhs=rs65[64 * hh:64 * hh + 1, :],
                                start=True, stop=True),
                                reads=[brs65,
                                       {"k": b_krw, "q": b_qrw, "p": b_cs["ones1x64"]}[tn]],
                                writes=[bps_])
                            if tn in ("k", "q"):
                                dst, bdst = (k_ext[hh], b_ke[hh]) if tn == "k" else (q_ext[hh], b_qe[hh])
                                p.op("dve", lambda e, tf=tf, hh=hh, c0=c0, dst=dst, ps_=ps_: e.tensor_tensor(
                                    out=dst[64:128, c0:c0 + 512], in0=tf[64 * hh:64 * hh + 64, :],
                                    in1=ps_[64 * hh:64 * hh + 64, :], op=ALU.mult),
                                    reads=[btf, bps_], writes=[bdst])
                            else:
                                p.op("dve", lambda e, tf=tf, hh=hh, c0=c0, ps_=ps_: e.tensor_tensor(
                                    out=ls1[0:64, 0:512], in0=tf[64 * hh:64 * hh + 64, :],
                                    in1=ps_[64 * hh:64 * hh + 64, :], op=ALU.mult),
                                    reads=[btf, bps_], writes=[b_ls1])
                                pr, bpr = protc[hh], b_pr[hh]
                                p.op("dve", lambda e, c0=c0, pr=pr: e.tensor_tensor(
                                    out=pr[0:64, c0:c0 + 512], in0=ls1[0:64, 0:512],
                                    in1=cs["cosb64"][:, c0:c0 + 512], op=ALU.mult),
                                    reads=[b_ls1, b_cs["cosb64"]], writes=[bpr])
                                p.op("pool", lambda e, c0=c0, pr=pr: e.tensor_tensor(
                                    out=pr[64:96, c0:c0 + 512], in0=ls1[32:64, 0:512],
                                    in1=cs["sinb64"][32:64, c0:c0 + 512], op=ALU.mult),
                                    reads=[b_ls1, b_cs["sinb64"]], writes=[bpr])
                                p.op("pool", lambda e, c0=c0, pr=pr: e.tensor_tensor(
                                    out=pr[96:128, c0:c0 + 512], in0=ls1[0:32, 0:512],
                                    in1=cs["sinb64"][0:32, c0:c0 + 512], op=ALU.mult),
                                    reads=[b_ls1, b_cs["sinb64"]], writes=[bpr])
                                p6, bp6 = p64x2[hh], b_p6[hh]
                                p.op("dve", lambda e, tf=tf, hh=hh, c0=c0, p6=p6: e.tensor_scalar(
                                    out=p6[0:64, c0:c0 + 512], in0=tf[64 * hh:64 * hh + 64, :],
                                    scalar1=1.0 / 64, scalar2=None, op0=ALU.mult),
                                    reads=[btf], writes=[bp6])
                                p.op("pool", lambda e, tf=tf, hh=hh, c0=c0, p6=p6: e.tensor_scalar(
                                    out=p6[64:128, c0:c0 + 512], in0=tf[64 * hh:64 * hh + 64, :],
                                    scalar1=1.0 / 64, scalar2=None, op0=ALU.mult),
                                    reads=[btf], writes=[bp6])
                for jt in range(16 if attn_lvl >= 1 else 0):
                    pT = pS2.bitcast(BF16)
                    p.op("pe", lambda e, jt=jt, pT=pT: e.transpose(
                        out=pT[:, 0:128], in_=v_fm[:, jt * 128:(jt + 1) * 128],
                        identity=cs["ident_bf"][:, :]),
                        reads=[b_vfm, b_cs["ident_bf"]], writes=[bufS2])
                    for hh in range(2):
                        p.op("dve", lambda e, jt=jt, hh=hh, pT=pT: e.tensor_copy(
                            v_ext[hh][:, jt * 68:jt * 68 + 64], pT[:, 64 * hh:64 * hh + 64]),
                            reads=[bufS2], writes=[b_ve[hh]])

                for hh in range(2):
                    h = hp * 2 + hh
                    slope = 2.0 ** (-(h + 1) / 2.0)
                    qe, ke, pr, p6, ve = q_ext[hh], k_ext[hh], protc[hh], p64x2[hh], v_ext[hh]
                    bqe, bke, bpr, bp6, bve = b_qe[hh], b_ke[hh], b_pr[hh], b_p6[hh], b_ve[hh]

                    for g in range(4):
                        c0 = g * 512
                        sq_h, bsq_h = sqb[g % 2], b_sq[g % 2]
                        rk, brk = rowkq[g % 2], b_rowkq[g % 2]
                        bY_h = bufY0 if ((c0 % 1024) == 0) else bufY1
                        p.op("act", lambda e, ke=ke, c0=c0, sq_h=sq_h: e.activation(
                            out=sq_h[0:64, :], in_=ke[64:128, c0:c0 + 512], func=AF.Square),
                            reads=[bke], writes=[bsq_h])
                        p.op("pe", lambda e, c0=c0, sq_h=sq_h: e.matmul(
                            out=pY[0:1, (c0 % 1024):(c0 % 1024) + 512],
                            lhsT=cs["ones64"][:, :], rhs=sq_h[0:64, :],
                            start=True, stop=True),
                            reads=[bsq_h, b_cs["ones64"]], writes=[bY_h])
                        p.op("dve", lambda e, c0=c0, rk=rk: e.tensor_copy(
                            rk[:, :], pY[0:1, (c0 % 1024):(c0 % 1024) + 512]),
                            reads=[bY_h], writes=[brk])
                        if g == 0:
                            p.op("dve", lambda e, rk=rk: e.reduce_max(kmx[:, :], rk[:, :], axis=AX),
                                 reads=[brk], writes=[b_kmx])
                        else:
                            p.op("dve", lambda e, rk=rk: e.reduce_max(kmg[:, :], rk[:, :], axis=AX),
                                 reads=[brk], writes=[b_kmg])
                            p.op("dve", lambda e: e.tensor_tensor(
                                out=kmx[:, :], in0=kmg[:, :], in1=kmx[:, :], op=ALU.max),
                                reads=[b_kmg], writes=[b_kmx])
                    p.op("act", lambda e: e.activation(out=kmx[:, :], in_=kmx[:, :], func=AF.Ln),
                         writes=[b_kmx])
                    p.op("dve", lambda e: e.tensor_scalar(
                        out=kmx[:, :], in0=kmx[:, :], scalar1=0.5, scalar2=None, op0=ALU.mult),
                        writes=[b_kmx])
                    for g in range(4):
                        c0 = g * 512
                        sq_h, bsq_h = sqb[g % 2], b_sq[g % 2]
                        rk, brk = rowkq[g % 2], b_rowkq[g % 2]
                        bY_h = bufY0 if ((c0 % 1024) == 0) else bufY1
                        p.op("act", lambda e, qe=qe, c0=c0, sq_h=sq_h: e.activation(
                            out=sq_h[0:64, :], in_=qe[64:128, c0:c0 + 512], func=AF.Square),
                            reads=[bqe], writes=[bsq_h])
                        p.op("pe", lambda e, c0=c0, sq_h=sq_h: e.matmul(
                            out=pY[0:1, (c0 % 1024):(c0 % 1024) + 512],
                            lhsT=cs["ones64"][:, :], rhs=sq_h[0:64, :],
                            start=True, stop=True),
                            reads=[bsq_h, b_cs["ones64"]], writes=[bY_h])
                        p.op("dve", lambda e, c0=c0, rk=rk: e.tensor_copy(
                            rk[:, :], pY[0:1, (c0 % 1024):(c0 % 1024) + 512]),
                            reads=[bY_h], writes=[brk])
                        p.op("act", lambda e, rk=rk: e.activation(
                            out=rk[:, :], in_=rk[:, :], func=AF.Ln),
                            writes=[brk])
                        p.op("act", lambda e, rk=rk, c0=c0: e.activation(
                            out=m_row[:, c0:c0 + 512], in_=rk[:, :],
                            func=AF.Exp, scale=0.5, bias=kmx[:, :]),
                            reads=[brk, b_kmx], writes=[b_mrow])
                        p.op("dve", lambda e, c0=c0: e.tensor_scalar(
                            out=m_row[:, c0:c0 + 512], in0=m_row[:, c0:c0 + 512],
                            scalar1=-MSHIFT, scalar2=None, op0=ALU.add),
                            writes=[b_mrow])
                    pc = 0
                    for J in range(16 if attn_lvl >= 2 else 0):
                        w_J = T - 128 * J
                        for (b0, b1) in splits(0, w_J, 1024):
                            lsx, blsx = ls1_[pc % 2], b_ls1_[pc % 2]
                            pc += 1
                            for (a0, a1) in splits(b0, b1):
                                ps, bps = PSMALL[(a0 // 512) % 2]
                                p.op("pe", lambda e, J=J, a0=a0, a1=a1, ps=ps, pr=pr, p6=p6: e.matmul(
                                    out=ps[:, 0:a1 - a0], lhsT=pr[:, J * 128:(J + 1) * 128],
                                    rhs=p6[:, J * 128 + a0:J * 128 + a1], start=True, stop=True),
                                    reads=[bpr, bp6], writes=[bps])
                                p.op("act", lambda e, a0=a0, a1=a1, b0=b0, ps=ps, lsx=lsx: e.activation(
                                    out=lsx[:, a0 - b0:a1 - b0], in_=ps[:, 0:a1 - a0],
                                    func=AF.Exp, scale=-1.0),
                                    reads=[bps], writes=[blsx])
                            p.op("act", lambda e, b0=b0, b1=b1, lsx=lsx: e.activation(
                                out=lsx[:, 0:b1 - b0], in_=lsx[:, 0:b1 - b0], func=AF.Ln,
                                bias=cs["fconst"][:, 3:4]),
                                reads=[b_cs["fconst"]], writes=[blsx])
                            p.op("pool", lambda e, J=J, b0=b0, b1=b1, slope=slope, lsx=lsx: e.tensor_scalar(
                                out=biasT[:, offJ[J] + b0:offJ[J] + b1], in0=lsx[:, 0:b1 - b0],
                                scalar1=-slope, scalar2=0.0, op0=ALU.mult, op1=ALU.add),
                                reads=[blsx], writes=[b_biasT[J]])
                        p.op("dve", lambda e, J=J: e.tensor_tensor(
                            out=biasT[:, offJ[J]:offJ[J] + 128], in0=biasT[:, offJ[J]:offJ[J] + 128],
                            in1=cs["utmask"][:, :], op=ALU.mult),
                            reads=[b_cs["utmask"]], writes=[b_biasT[J]])

                    for gq in range(4 if attn_lvl >= 3 else 0):
                        i0g = gq * 512
                        Jmax = 4 * gq + 3
                        p.op("pe", lambda e, i0g=i0g, Jmax=Jmax: [
                            e.matmul(out=pS1[0:16, max(0, 128 * J - i0g):512],
                                     lhsT=cs["ecols"][:, 16 * J:16 * J + 16],
                                     rhs=biasT[:, offJ[J] + max(0, i0g - 128 * J):offJ[J] + i0g + 512 - 128 * J],
                                     start=(J == 0), stop=(J == Jmax))
                            for J in range(Jmax + 1)][-1],
                            reads=[b_cs["ecols"]] + [b_biasT[J] for J in range(Jmax + 1)],
                            writes=[bufS1])
                        p.op("dve", lambda e: e.tensor_copy(csth[:, :], pS1[0:16, :]),
                             reads=[bufS1], writes=[b_csth])
                        p.op("dve", lambda e: e.tensor_tensor(
                            out=cstl[:, :], in0=pS1[0:16, :], in1=csth[:, :], op=ALU.subtract),
                            reads=[bufS1, b_csth], writes=[b_cstl])
                        p.op("pe", lambda e, i0g=i0g: [
                            e.matmul(out=pS2[0:16, :], lhsT=cs["l16"][:, :], rhs=csth[:, :],
                                     start=True, stop=False),
                            e.matmul(out=pS2[0:16, :], lhsT=cs["l16"][:, :], rhs=cstl[:, :],
                                     start=False, stop=False),
                            e.matmul(out=pS2[0:16, :], lhsT=cs["neg1"][:, :],
                                     rhs=m_row[:, i0g:i0g + 512], start=False, stop=True)][-1],
                            reads=[b_csth, b_cstl, b_mrow, b_cs["l16"], b_cs["neg1"]],
                            writes=[bufS2])
                        p.op("dve", lambda e, i0g=i0g, qe=qe: e.tensor_copy(
                            qe[0:16, i0g:i0g + 512], pS2[0:16, :]),
                            reads=[bufS2], writes=[bqe])
                        p.op("dve", lambda e, i0g=i0g, qe=qe: e.tensor_tensor(
                            out=qe[32:48, i0g:i0g + 512], in0=pS2[0:16, :],
                            in1=qe[0:16, i0g:i0g + 512], op=ALU.subtract),
                            reads=[bufS2], writes=[bqe])

                    SCF = [pr.bitcast(F32), p6.bitcast(F32)]
                    b_SCF = [bpr, bp6]
                    for H in range(2 if attn_lvl >= 4 else 0):
                        ib = H * 1024
                        Jlist = [J for J in range(16) if 128 * J < ib + 1024]
                        for J in Jlist:
                            i0 = max(128 * J, ib)
                            i1 = ib + 1024
                            pX = pA if J % 2 == 0 else pB
                            bpXb = [bufA0, bufA1] if J % 2 == 0 else [bufB0, bufB1]
                            for (a0, a1) in splits(i0 - ib, i1 - ib):
                                p.op("pe", lambda e, J=J, a0=a0, a1=a1, ib=ib, pX=pX, ke=ke, qe=qe: e.matmul(
                                    out=pX[:, a0:a1], lhsT=ke[:, J * 128:(J + 1) * 128],
                                    rhs=qe[:, ib + a0:ib + a1], start=True, stop=False),
                                    reads=[bke, bqe], writes=[bpXb[a0 // 512]])
                                p.op("pe", lambda e, J=J, a0=a0, a1=a1, ib=ib, pX=pX: e.matmul(
                                    out=pX[:, a0:a1], lhsT=cs["slmat"][:, :],
                                    rhs=biasT[:, offJ[J] + ib + a0 - 128 * J:offJ[J] + ib + a1 - 128 * J],
                                    start=False, stop=True),
                                    reads=[b_biasT[J], b_cs["slmat"]], writes=[bpXb[a0 // 512]])
                            et, bet = ET[J % 2], b_ET[J % 2]
                            scf, bscf = SCF[J % 2], b_SCF[J % 2]
                            p.op("dve", lambda e, i0=i0, i1=i1, ib=ib, pX=pX, scf=scf: e.tensor_copy(
                                scf[:, 0:i1 - i0], pX[:, i0 - ib:i1 - ib]),
                                reads=bpXb[(i0 - ib) // 512: (i1 - ib + 511) // 512],
                                writes=[bscf])
                            p.op("act", lambda e, i0=i0, i1=i1, et=et, scf=scf: e.activation(
                                out=et[:, 0:i1 - i0], in_=scf[:, 0:i1 - i0], func=AF.Exp),
                                reads=[bscf], writes=[bet])
                            if 128 * J >= ib:
                                d0 = 128 * J - i0
                                p.op("dve", lambda e, d0=d0, et=et: e.tensor_tensor(
                                    out=et[:, d0:d0 + 128], in0=et[:, d0:d0 + 128],
                                    in1=cs["utmask"][:, :], op=ALU.mult),
                                    reads=[b_cs["utmask"]], writes=[bet])
                            for (a0, a1) in splits(i0 - ib, i1 - ib):
                                lastJ_for_bank = max(Jx for Jx in Jlist if 128 * Jx < ib + (a0 // 512 + 1) * 512)
                                p.op("pe", lambda e, J=J, a0=a0, a1=a1, i0=i0, ib=ib, et=et, ve=ve,
                                     st=(J == 0), sp_=(J == lastJ_for_bank): e.matmul(
                                    out=pY[0:65, a0:a1], lhsT=ve[:, J * 68:J * 68 + 65],
                                    rhs=et[:, ib + a0 - i0:ib + a1 - i0], start=st, stop=sp_),
                                    reads=[bet, bve], writes=[bufY0 if a0 < 512 else bufY1])
                        p.op("dve", lambda e: e.reciprocal(rzrow[:, :], pY[64:65, :]),
                             reads=[bufY0, bufY1], writes=[b_rz])
                        p.op("dve", lambda e: e.tensor_copy(rzhi[:, :], rzrow[:, :]),
                             reads=[b_rz], writes=[b_rzhi])
                        p.op("dve", lambda e: e.tensor_tensor(
                            out=rzlo[:, :], in0=rzrow[:, :], in1=rzhi[:, :], op=ALU.subtract),
                            reads=[b_rz, b_rzhi], writes=[b_rzlo])
                        for (a0, a1) in ((0, 512), (512, 1024)):
                            ps, bps = PSMALL[(a0 // 512) % 2]
                            p.op("pe", lambda e, a0=a0, a1=a1, ps=ps: [
                                e.matmul(out=ps[0:64, :], lhsT=cs["ones1x64"][:, :],
                                         rhs=rzhi[:, a0:a1], start=True, stop=False),
                                e.matmul(out=ps[0:64, :], lhsT=cs["ones1x64"][:, :],
                                         rhs=rzlo[:, a0:a1], start=False, stop=True)][-1],
                                reads=[b_rzhi, b_rzlo, b_cs["ones1x64"]], writes=[bps])
                            p.op("dve", lambda e, a0=a0, a1=a1, ps=ps: e.tensor_copy(
                                rzb_sb[:, a0:a1], ps[0:64, :]),
                                reads=[bps], writes=[b_rzb])
                        if attn_lvl < 5:
                            continue
                        yst, byst = ystage[H], b_yst[H]
                        p.op("dve", lambda e, yst=yst: e.tensor_tensor(
                            out=yst[:, :], in0=pY[0:64, :], in1=rzb_sb[:, :], op=ALU.mult),
                            reads=[bufY0, bufY1, b_rzb], writes=[byst])
                        p.op("sp", lambda e, s=s, h=h, H=H, yst=yst: e.dma_start(
                            out=yscr[s, h, :, H * 1024:(H + 1) * 1024], in_=yst[:, :]),
                            reads=[byst], writes=[b_yscr], dma=True)
                    if ("scoresT" in tap_set) and h == 0 and s == 0:
                        tap("scoresT", lambda: ET[1], [128, 1024], BF16, deps=[b_ET[1].w])

            if attn_lvl < 6:
                continue
            p.op("sp", lambda e: e.dma_start(
                out=wstream.rearrange("p (kc m) -> p kc m", kc=NKC)[:, :, :],
                in_=wscr["wprojT"].rearrange("(kc p) m -> p kc m", p=128)[:, :, :]),
                writes=[b_wproj], reads=[b_wscr["wprojT"]],
                deps=[ev for b in b_wh for ev in ([b.w] + b.r)], dma=True)
            for g in range(4):
                for half2 in range(2):
                    p.op("sp", lambda e, s=s, g=g, half2=half2: e.dma_start(
                        out=protc[half2].rearrange("p (kc t) -> p kc t", kc=4)[:, :, :],
                        in_=yscr[s].rearrange("(k g2) d t -> (g2 d) k t", g2=2)[:, half2 * 4:half2 * 4 + 4, g * 512:(g + 1) * 512]),
                        reads=[b_yscr], writes=[b_pr[half2]], dma=True)

                def ybv(kc):
                    return protc[kc // 4][:, (kc % 4) * 512:(kc % 4 + 1) * 512]

                for m in range(NKC):
                    ps, bps = PSMALL[m % 2]
                    p.op("pe", lambda e, m=m, ps=ps, ybv=ybv: [
                        e.matmul(out=ps[:, :], lhsT=wstream[:, kc * 1024 + m * 128:kc * 1024 + (m + 1) * 128],
                                 rhs=ybv(kc),
                                 start=(kc == 0), stop=(kc == NKC - 1))
                        for kc in range(NKC)][-1],
                        reads=[b_wproj, b_pr[0], b_pr[1]], writes=[bps])
                    ao, bao = ao_st[0], b_ao[0]
                    p.op("dve", lambda e, ps=ps, ao=ao: e.tensor_copy(ao[:, :], ps[:, :]),
                         reads=[bps], writes=[bao])
                    p.op("sp", lambda e, m=m, s=s, g=g, ao=ao: e.dma_start(
                        out=accd[m, :, s * T + g * 512:s * T + (g + 1) * 512], in_=ao[:, :]),
                        reads=[bao], writes=[b_accd], dma=True)
        tap("attn_acc", lambda: accd, [NKC, 128, 2 * T], F32, deps=list(b_accd.wl))

        p.barrier()
        main_es.close()
        ffn_es = ExitStack()
        fsb = lambda name, shp, dt: ffn_es.enter_context(nc.sbuf_tensor(name, shp, dt)).ap()
        xnf = fsb("xnf", [128, NKC * 2 * T], BF16)
        b_xnf = Buf()
        gsl = [fsb(f"gsl{i}", [128, NKC * 512], BF16) for i in range(2)]
        usl = [fsb(f"usl{i}", [128, NKC * 512], BF16) for i in range(2)]
        dsl = [fsb(f"dsl{i}", [128, 4 * 1024], BF16) for i in range(2)]
        b_gsl = [Buf(), Buf()]
        b_usl = [Buf(), Buf()]
        b_dsl = [Buf(), Buf()]
        gub_s = fsb("gub_s", [128, 32], F32)
        gub1 = fsb("gub1", [128, 32], F32)
        b_gub, b_gub1 = Buf(), Buf()
        gwb = fsb("gwb", [128, 4 * 1024], F32)
        b_gwb = Buf()
        act_sb = [fsb(f"act_sb{i}", [128, 4 * 1024], BF16) for i in range(2)]
        b_act = [Buf(), Buf()]
        glu_t = [fsb(f"glu_t{i}", [128, 1024], F32) for i in range(2)]
        upt = [fsb(f"upt{i}", [128, 1024], F32) for i in range(2)]
        b_glu = [Buf(), Buf()]
        b_upt = [Buf(), Buf()]
        dstage = [fsb(f"dstage{i}", [128, 1024], F32) for i in range(2)]
        b_dst = [Buf(), Buf()]

        p.op("sp", lambda e: e.dma_start(out=xnf[:, :], in_=xnscr),
             reads=[b_xnscr], writes=[b_xnf], dma=True)

        def xnfv(kc, c0, c1):
            return xnf[:, kc * 2 * T + c0:kc * 2 * T + c1]

        it_list = [(e_, iq) for e_ in range(experts) for iq in range(4)]
        for it_idx, (ex, iq) in enumerate(it_list):
            par = it_idx % 2
            bg, bu, bd = b_gsl[par], b_usl[par], b_dsl[par]
            gs, us, ds = gsl[par], usl[par], dsl[par]
            p.op("sp", lambda e, ex=ex, iq=iq, gs=gs: e.dma_start(
                out=gs.rearrange("p (kc m) -> p kc m", kc=NKC)[:, :, :],
                in_=gupb[ex].rearrange("(kc p) m -> p kc m", p=128)[:, :, iq * 512:(iq + 1) * 512]),
                writes=[bg], reads=[b_gupb], dma=True)
            p.op("sp", lambda e, ex=ex, iq=iq, us=us: e.dma_start(
                out=us.rearrange("p (kc m) -> p kc m", kc=NKC)[:, :, :],
                in_=gupb[ex].rearrange("(kc p) m -> p kc m", p=128)[:, :, I + iq * 512:I + (iq + 1) * 512]),
                writes=[bu], reads=[b_gupb], dma=True)
            p.op("sp", lambda e, ex=ex, iq=iq, ds=ds: e.dma_start(
                out=ds.rearrange("p (ic h) -> p ic h", ic=4)[:, :, :],
                in_=dwnb[ex, iq * 512:(iq + 1) * 512, :].rearrange("(ic p) h -> p ic h", p=128)[:, :, :]),
                writes=[bd], reads=[b_dwnb], dma=True)
            if iq == 0:
                p.op("pool", lambda e, ex=ex: e.dma_start(out=gub_s[:, :], in_=gub_d[ex]),
                     writes=[b_gub], dma=True)
                p.op("dve", lambda e: e.tensor_scalar(
                    out=gub1[:, :], in0=gub_s[:, :], scalar1=1.0, scalar2=None, op0=ALU.add),
                    reads=[b_gub], writes=[b_gub1])
                for gg in range(NKC):
                    ps, bps = PSMALL[gg % 2]
                    p.op("pe", lambda e, ex=ex, gg=gg, ps=ps: [
                        e.matmul(out=ps[:, :], lhsT=cs["oneh8"][:, ex * 128:(ex + 1) * 128],
                                 rhs=gthi[:, gg * 512:(gg + 1) * 512], start=True, stop=False),
                        e.matmul(out=ps[:, :], lhsT=cs["oneh8"][:, ex * 128:(ex + 1) * 128],
                                 rhs=gtlo[:, gg * 512:(gg + 1) * 512], start=False, stop=True)][-1],
                        reads=[b_gthi, b_gtlo, b_cs["oneh8"]], writes=[bps])
                    p.op("dve", lambda e, gg=gg, ps=ps: e.tensor_copy(
                        gwb[:, gg * 512:(gg + 1) * 512], ps[:, :]),
                        reads=[bps], writes=[b_gwb])
            for tgi in range(5):
                if tgi < 4:
                    tg = tgi
                    tc0 = tg * 1024
                    asb, bact = act_sb[tg % 2], b_act[tg % 2]
                    for pr_ in range(4):
                        ppar = pr_ % 2
                        glu, bglu = glu_t[ppar], b_glu[ppar]
                        up_t, bupt = upt[ppar], b_upt[ppar]
                        p.op("pe", lambda e, pr_=pr_, gs=gs, tc0=tc0: [
                            e.matmul(out=pA[:, gg * 512:(gg + 1) * 512],
                                     lhsT=gs[:, kc * 512 + pr_ * 128:kc * 512 + (pr_ + 1) * 128],
                                     rhs=xnfv(kc, tc0 + gg * 512, tc0 + (gg + 1) * 512),
                                     start=(kc == 0), stop=(kc == NKC - 1))
                            for kc in range(NKC) for gg in range(2)][-1],
                            reads=[bg, b_xnf], writes=[bufA0, bufA1])
                        p.op("dve", lambda e, pr_=pr_, iq=iq, glu=glu: e.tensor_scalar(
                            out=glu[:, :], in0=pA[:, :], scalar1=gub_s[:, iq * 4 + pr_:iq * 4 + pr_ + 1],
                            op0=ALU.add, scalar2=LIMIT, op1=ALU.min),
                            reads=[bufA0, bufA1, b_gub], writes=[bglu])
                        p.op("act", lambda e, glu=glu: e.activation(
                            out=glu[:, :], in_=glu[:, :], func=AF.Silu, scale=ALPHA),
                            writes=[bglu])
                        p.op("pe", lambda e, pr_=pr_, us=us, tc0=tc0: [
                            e.matmul(out=pB[:, gg * 512:(gg + 1) * 512],
                                     lhsT=us[:, kc * 512 + pr_ * 128:kc * 512 + (pr_ + 1) * 128],
                                     rhs=xnfv(kc, tc0 + gg * 512, tc0 + (gg + 1) * 512),
                                     start=(kc == 0), stop=(kc == NKC - 1))
                            for kc in range(NKC) for gg in range(2)][-1],
                            reads=[bu, b_xnf], writes=[bufB0, bufB1])
                        p.op("dve", lambda e, pr_=pr_, iq=iq, up_t=up_t: e.tensor_scalar(
                            out=up_t[:, :], in0=pB[:, :], scalar1=gub1[:, 16 + iq * 4 + pr_:16 + iq * 4 + pr_ + 1],
                            op0=ALU.add, scalar2=-(LIMIT - 1.0), op1=ALU.max),
                            reads=[bufB0, bufB1, b_gub1], writes=[bupt])
                        p.op("pool", lambda e, up_t=up_t: e.tensor_scalar(
                            out=up_t[:, :], in0=up_t[:, :], scalar1=LIMIT + 1.0, scalar2=-3.0e38,
                            op0=ALU.min, op1=ALU.max),
                            writes=[bupt])
                        p.op("pool" if pr_ % 2 == 0 else "dve",
                             lambda e, pr_=pr_, asb=asb, up_t=up_t, glu=glu: e.tensor_tensor(
                            out=asb[:, pr_ * 1024:(pr_ + 1) * 1024], in0=up_t[:, :], in1=glu[:, :],
                            op=ALU.mult),
                            reads=[bupt, bglu], writes=[bact])
                if tgi >= 1:
                    tg = tgi - 1
                    tc0 = tg * 1024
                    asb, bact = act_sb[tg % 2], b_act[tg % 2]
                    for hm in range(NKC):
                        if hm % 2 == 0:
                            outs = [(pY[:, 0:512], bufY0), (pY[:, 512:1024], bufY1)]
                        else:
                            outs = [(pS1[:, :], bufS1), (pS2[:, :], bufS2)]
                        p.op("pe", lambda e, hm=hm, ds=ds, asb=asb, outs=outs: [
                            e.matmul(out=outs[gg][0],
                                     lhsT=ds[:, ic * 1024 + hm * 128:ic * 1024 + (hm + 1) * 128],
                                     rhs=asb[:, ic * 1024 + gg * 512:ic * 1024 + (gg + 1) * 512],
                                     start=(ic == 0), stop=(ic == 3))
                            for ic in range(4) for gg in range(2)][-1],
                            reads=[bd, bact], writes=[outs[0][1], outs[1][1]])
                        dst, bdst2 = dstage[hm % 2], b_dst[hm % 2]
                        p.op("dve", lambda e, tc0=tc0, outs=outs, dst=dst: [
                            e.tensor_tensor(out=dst[:, gg * 512:(gg + 1) * 512], in0=outs[gg][0],
                                            in1=gwb[:, tc0 + gg * 512:tc0 + (gg + 1) * 512],
                                            op=ALU.mult)
                            for gg in range(2)][-1],
                            reads=[outs[0][1], outs[1][1], b_gwb], writes=[bdst2])
                        p.op("pool", lambda e, hm=hm, tc0=tc0, dst=dst: e.dma_start(
                            out=accd[hm, :, tc0:tc0 + 1024], in_=dst[:, :], accum_op=ALU.add),
                            reads=[bdst2], writes=[b_accd], dma=True)
        for hm in range(NKC):
            for g in range(8):
                ps, bps = PSMALL[g % 2]
                p.op("pe", lambda e, hm=hm, g=g, ps=ps: [
                    e.matmul(out=ps[:, :], lhsT=db_sb[:, hm * 128:(hm + 1) * 128],
                             rhs=gthi[:, g * 512:(g + 1) * 512], start=True, stop=False),
                    e.matmul(out=ps[:, :], lhsT=db_sb[:, hm * 128:(hm + 1) * 128],
                             rhs=gtlo[:, g * 512:(g + 1) * 512], start=False, stop=True)][-1],
                    reads=[b_db, b_gthi, b_gtlo], writes=[bps])
                dst, bdst2 = dstage[g % 2], b_dst[g % 2]
                p.op("dve", lambda e, ps=ps, dst=dst: e.tensor_copy(dst[:, 0:512], ps[:, :]),
                     reads=[bps], writes=[bdst2])
                p.op("pool", lambda e, hm=hm, g=g, dst=dst: e.dma_start(
                    out=accd[hm, :, g * 512:(g + 1) * 512], in_=dst[:, 0:512], accum_op=ALU.add),
                    reads=[bdst2], writes=[b_accd], dma=True)

        xr = [gsl[0].bitcast(F32)[:, 0:1024], gsl[0].bitcast(F32)[:, 1024:2048]]
        b_xr = [b_gsl[0], b_gsl[0]]
        astage = [usl[0].bitcast(F32)[:, 0:1024], usl[0].bitcast(F32)[:, 1024:2048]]
        b_ast = [b_usl[0], b_usl[0]]
        ostage = [dsl[0].bitcast(F32)[:, 0:1024], dsl[0].bitcast(F32)[:, 1024:2048]]
        b_ost = [b_dsl[0], b_dsl[0]]
        out_evs = []
        for s in range(2):
            for tt in range(NTT):
                par = tt % 2
                gc0 = s * T + tt * 128
                xrr, bxr = xr[par], b_xr[par]
                p.op("sp", lambda e, s=s, tt=tt, xrr=xrr: e.dma_start(
                    out=xrr[:, :], in_=x_d[s, tt * 128:(tt + 1) * 128, :]),
                    writes=[bxr], dma=True)
                ast, bast = astage[par], b_ast[par]
                p.op("sp", lambda e, gc0=gc0, ast=ast: e.dma_start(
                    out=ast.rearrange("p (kc t) -> p kc t", kc=NKC)[:, :, :],
                    in_=accd.rearrange("kc p t -> p kc t")[:, :, gc0:gc0 + 128]),
                    reads=[b_accd], writes=[bast], dma=True)
                for half in range(2):
                    ps, bps = PSMALL[half]
                    p.op("pe", lambda e, half=half, ast=ast, ps=ps: [
                        e.transpose(out=ps[:, q * 128:(q + 1) * 128],
                                    in_=ast[:, (half * 4 + q) * 128:(half * 4 + q + 1) * 128],
                                    identity=cs["ident_f32"][:, :])
                        for q in range(4)][-1],
                        reads=[bast, b_cs["ident_f32"]], writes=[bps])
                    ost, bost = ostage[par], b_ost[par]
                    p.op("dve", lambda e, half=half, xrr=xrr, ps=ps, ost=ost: e.tensor_tensor(
                        out=ost[:, half * 512:(half + 1) * 512], in0=ps[:, :],
                        in1=xrr[:, half * 512:(half + 1) * 512], op=ALU.add),
                        reads=[bps, bxr], writes=[bost])
                ost, bost = ostage[par], b_ost[par]
                ev = p.op("sp", lambda e, s=s, tt=tt, ost=ost: e.dma_start(
                    out=out_d[s, tt * 128:(tt + 1) * 128, :], in_=ost[:, :]),
                    reads=[bost], writes=[b_out], dma=True)
                out_evs.append(ev)
        p.op("sp", lambda e: e.nop(), deps=out_evs)

        with nc.Block() as block:
            counts = p.emit(nc, block, sems, dma_sems)
        ffn_es.close()
    return nc, counts, tap_out


_BUILD_CACHE = {}


def _get_nc():
    key = "full"
    if key not in _BUILD_CACHE:
        _BUILD_CACHE[key] = build_nc()
    return _BUILD_CACHE[key]


def kernel(L0_attn_x, L0_attn_Wq, L0_attn_Wk, L0_attn_Wv, L0_attn_Wp, L0_attn_Wproj,
           L0_attn_q_rms_w, L0_attn_k_rms_w, L0_ffn_router_weight, L0_ffn_router_bias,
           L0_ffn_gate_up_proj, L0_ffn_gate_up_bias, L0_ffn_down_proj, L0_ffn_down_bias):
    f32 = np.float32
    x = np.ascontiguousarray(np.asarray(L0_attn_x, dtype=f32))
    base = {
        "wqT": np.ascontiguousarray(np.asarray(L0_attn_Wq, f32).T),
        "wkT": np.ascontiguousarray(np.asarray(L0_attn_Wk, f32).T),
        "wvT": np.ascontiguousarray(np.asarray(L0_attn_Wv, f32).T),
        "wpT": np.ascontiguousarray(np.asarray(L0_attn_Wp, f32).T),
        "wprojT": np.ascontiguousarray(np.asarray(L0_attn_Wproj, f32).T),
        "qrw": np.ascontiguousarray(np.asarray(L0_attn_q_rms_w, f32).reshape(1, 64)),
        "krw": np.ascontiguousarray(np.asarray(L0_attn_k_rms_w, f32).reshape(1, 64)),
        "rw_l": np.ascontiguousarray(
            np.asarray(L0_ffn_router_weight, f32).T.reshape(8, 128, 8).transpose(1, 0, 2).reshape(128, 64)),
        "rb": np.ascontiguousarray(np.asarray(L0_ffn_router_bias, f32).reshape(1, 8)),
        "gup": np.ascontiguousarray(np.asarray(L0_ffn_gate_up_proj, f32)),
        "gub_l": np.ascontiguousarray(
            np.asarray(L0_ffn_gate_up_bias, f32).reshape(8, 32, 128).transpose(0, 2, 1)),
        "dwn": np.ascontiguousarray(np.asarray(L0_ffn_down_proj, f32)),
        "dbias": np.ascontiguousarray(np.asarray(L0_ffn_down_bias, f32)),
    }
    base.update(host_consts())
    nc, counts, _ = _get_nc()
    in_maps = []
    for c in range(8):
        m = dict(base)
        m["x"] = np.ascontiguousarray(x[2 * c:2 * c + 2])
        in_maps.append(m)
    res = run_bass_kernel_spmd(nc, in_maps, core_ids=list(range(8)))
    out = np.concatenate([np.asarray(r["out"]).reshape(2, T, C) for r in res.results], axis=0)
    return out.astype(np.float32)


# version 16
# speedup vs baseline: 1.1343x; 1.1343x over previous
import math
from contextlib import ExitStack

import numpy as np
import ml_dtypes
import concourse.bass as bass
import concourse.mybir as mybir
from concourse.bass_utils import run_bass_kernel_spmd

F32 = mybir.dt.float32
BF16 = mybir.dt.bfloat16
AF = mybir.ActivationFunctionType
ALU = mybir.AluOpType
AX = mybir.AxisListType.X

T = 2048
C = 1024
NH = 16
HD = 64
NKC = C // 128
NTT = T // 128
I = 2048
NE = 8
RMS_EPS = 1e-5
FRMS_EPS = 1.1920929e-07
LN_EPS = 1e-5
ALPHA = 1.702
LIMIT = 7.0
MSHIFT = 60.0
DBG = {}

ENGINES = ("pe", "act", "dve", "pool", "sp")
BLOCK_ATTR = {"pe": "tensor", "act": "scalar", "dve": "vector", "pool": "gpsimd", "sp": "sync"}


class Buf:
    __slots__ = ("w", "r", "mw", "wl")
    def __init__(self, mw=False):
        self.w = None
        self.r = []
        self.mw = mw
        self.wl = []


class Prog:
    DMA_POOL = 20

    def __init__(self):
        self.eng_ops = {e: [] for e in ENGINES}
        self.dma_count = {"sp": 0, "pool": 0}

    def op(self, eng, fn, reads=(), writes=(), deps=(), dma=False):
        dep_list = [d for d in deps if d is not None]
        for b in reads:
            if b.mw:
                dep_list.extend(b.wl)
            elif b.w is not None:
                dep_list.append(b.w)
        for b in writes:
            dep_list.extend(b.r)
            if b.mw:
                pass
            elif b.w is not None:
                dep_list.append(b.w)
        rec = {"eng": eng, "fn": fn, "deps": tuple(dep_list), "dma": dma,
               "needed": False, "val": None, "dma_idx": None}
        if dma:
            rec["dma_idx"] = self.dma_count[eng]
            self.dma_count[eng] += 1
        self.eng_ops[eng].append(rec)
        for b in reads:
            b.r.append(rec)
        for b in writes:
            if b.mw:
                b.wl.append(rec)
            else:
                b.w = rec
                b.r = []
        return rec

    def barrier(self):
        lasts = [self.eng_ops[e][-1] for e in ENGINES if self.eng_ops[e]]
        for q in ("sp", "pool"):
            slot_last = {}
            for r in self.eng_ops[q]:
                if r["dma"]:
                    slot_last[r["dma_idx"] % self.DMA_POOL] = r
            lasts.extend(slot_last.values())
        nops = []
        for e in ENGINES:
            nops.append(self.op(e, lambda eng: eng.nop(), deps=lasts))
        return nops

    def emit(self, nc, block, sems, dma_sems):
        for e in ENGINES:
            for r in self.eng_ops[e]:
                for d in r["deps"]:
                    d["needed"] = True
        for e in ENGINES:
            cum = 0
            for r in self.eng_ops[e]:
                if r["dma"]:
                    k = r["dma_idx"]
                    r["val"] = (k // self.DMA_POOL + 1) * 16
                elif r["needed"]:
                    cum += 1
                    r["val"] = cum
        counts = {e: sum(1 for r in self.eng_ops[e]) for e in ENGINES}

        def sem_of(r):
            if r["dma"]:
                return dma_sems[r["eng"]][r["dma_idx"] % self.DMA_POOL]
            return sems[r["eng"]]

        def body(eng_name):
            def go(eng):
                lastwait = {}
                sem = sems[eng_name]
                for r in self.eng_ops[eng_name]:
                    if r["dma"] and r["dma_idx"] >= self.DMA_POOL:
                        dsem = dma_sems[eng_name][r["dma_idx"] % self.DMA_POOL]
                        v = (r["dma_idx"] // self.DMA_POOL) * 16
                        key = id(dsem)
                        if lastwait.get(key, -1) < v:
                            eng.wait_ge(dsem, v)
                            lastwait[key] = v
                    for d in r["deps"]:
                        dsem = sem_of(d)
                        v = d["val"]
                        if v is None:
                            raise RuntimeError("dep without value")
                        key = id(dsem)
                        if lastwait.get(key, -1) < v:
                            eng.wait_ge(dsem, v)
                            lastwait[key] = v
                    instr = r["fn"](eng)
                    if r["dma"]:
                        instr.then_inc(sem_of(r), 16)
                    elif r["needed"]:
                        instr.then_inc(sem, 1)
            return go

        for e in ENGINES:
            getattr(block, BLOCK_ATTR[e])(body(e))
        return counts


def splits(lo, hi, align=512):
    out = []
    a = lo
    while a < hi:
        b = min(hi, (a // align + 1) * align)
        out.append((a, b))
        a = b
    return out


def host_consts():
    bf = ml_dtypes.bfloat16
    c = {}
    c["ident_bf"] = np.eye(128, dtype=bf)
    c["ident_f32"] = np.eye(128, dtype=np.float32)
    r = np.arange(128)
    c["slmat"] = (r[:, None] > r[None, :]).astype(bf)
    c["utmask"] = (r[:, None] <= r[None, :]).astype(bf)
    kc = np.zeros((64, T), dtype=np.float32)
    ind = (np.arange(T)[None, :] // 128 == np.arange(16)[:, None]).astype(np.float32)
    kc[0:16] = ind
    kc[32:48] = ind
    c["kconst"] = kc.astype(bf)
    ec = np.zeros((128, 256), dtype=np.float32)
    for J in range(16):
        ec[:, 16 * J + J] = 1.0
    c["ecols"] = ec.astype(bf)
    r16 = np.arange(16)
    c["l16"] = (r16[:, None] > r16[None, :]).astype(bf)
    c["neg1"] = np.full((1, 16), -1.0, dtype=bf)
    c["ones64"] = np.ones((64, 1), dtype=bf)
    op_ = np.zeros((128, 65), dtype=np.float32)
    op_[0:64, 0] = 1.0
    op_[64:128, 64] = 1.0
    c["onespair"] = op_.astype(bf)
    c["ones1x64"] = np.ones((1, 64), dtype=bf)
    c["onesrow"] = np.ones((1, 128), dtype=bf)
    oh = np.zeros((8, 8 * 128), dtype=np.float32)
    for e in range(8):
        oh[e, 128 * e:128 * (e + 1)] = 1.0 / ALPHA
    c["oneh8"] = oh.astype(bf)
    fc = np.zeros((128, 4), dtype=np.float32)
    fc[:, 0] = LN_EPS
    fc[:, 1] = FRMS_EPS
    fc[:, 2] = -math.log(8.0)
    fc[:, 3] = 1.0
    c["fconst"] = fc
    t = np.arange(T, dtype=np.float64)
    c["cosb64"] = np.tile(np.cos(t)[None, :], (64, 1)).astype(bf)
    sb64 = np.zeros((64, T), dtype=np.float64)
    sb64[0:32] = -np.sin(t)[None, :]
    sb64[32:64] = np.sin(t)[None, :]
    c["sinb64"] = sb64.astype(bf)
    return c


def build_nc(taps=(), heads=NH, experts=NE, seqs=(0, 1), attn_lvl=6):
    nc = bass.Bass("TRN2", target_bir_lowering=False)
    tap_set = set(taps)

    x_d = nc.dram_tensor("x", [2, T, C], F32, kind="ExternalInput").ap()
    wT = {}
    for nm in ("wqT", "wkT", "wvT", "wpT", "wprojT"):
        wT[nm] = nc.dram_tensor(nm, [C, C], F32, kind="ExternalInput").ap()
    qrw_d = nc.dram_tensor("qrw", [1, 64], F32, kind="ExternalInput").ap()
    krw_d = nc.dram_tensor("krw", [1, 64], F32, kind="ExternalInput").ap()
    rwl_d = nc.dram_tensor("rw_l", [128, 64], F32, kind="ExternalInput").ap()
    rb_d = nc.dram_tensor("rb", [1, 8], F32, kind="ExternalInput").ap()
    gup_d = nc.dram_tensor("gup", [NE, C, 2 * I], F32, kind="ExternalInput").ap()
    gub_d = nc.dram_tensor("gub_l", [NE, 128, 32], F32, kind="ExternalInput").ap()
    dwn_d = nc.dram_tensor("dwn", [NE, I, C], F32, kind="ExternalInput").ap()
    dbias_d = nc.dram_tensor("dbias", [NE, C], F32, kind="ExternalInput").ap()
    cst_d = {}
    cshapes = {"ident_bf": ([128, 128], BF16), "ident_f32": ([128, 128], F32),
               "slmat": ([128, 128], BF16), "utmask": ([128, 128], BF16),
               "kconst": ([64, T], BF16), "ecols": ([128, 256], BF16),
               "l16": ([16, 16], BF16), "neg1": ([1, 16], BF16),
               "ones64": ([64, 1], BF16), "ones1x64": ([1, 64], BF16),
               "onespair": ([128, 65], BF16),
               "onesrow": ([1, 128], BF16), "oneh8": ([8, 8 * 128], BF16),
               "cosb64": ([64, T], BF16), "sinb64": ([64, T], BF16),
               "fconst": ([128, 4], F32)}
    for nm, (shp, dt) in cshapes.items():
        cst_d[nm] = nc.dram_tensor(nm, shp, dt, kind="ExternalInput").ap()
    out_d = nc.dram_tensor("out", [2, T, C], F32, kind="ExternalOutput").ap()
    xnscr = nc.dram_tensor("xnscr", [128, NKC * 2 * T], BF16, kind="Internal").ap()
    wscr = {nm: nc.dram_tensor("wscr_" + nm, [C, C], BF16, kind="Internal").ap()
            for nm in ("wqT", "wkT", "wvT", "wpT", "wprojT")}
    gupb = nc.dram_tensor("gupb", [NE, C, 2 * I], BF16, kind="Internal").ap()
    dwnb = nc.dram_tensor("dwnb", [NE, I, C], BF16, kind="Internal").ap()
    accd = nc.dram_tensor("accd", [NKC, 128, 2 * T], F32, kind="Internal").ap()
    yscr = nc.dram_tensor("yscr", [2, NH, 64, T], BF16, kind="Internal").ap()

    tap_out = {}
    p = Prog()

    def tap(name, ap_fn, shape, dtype, deps):
        if name not in tap_set:
            return
        td = nc.dram_tensor("tap_" + name, shape, F32, kind="ExternalOutput").ap()
        tap_out[name] = td
        p.op("pool", lambda e, td=td, ap_fn=ap_fn: e.dma_start(out=td, in_=ap_fn()),
             deps=deps, dma=True)

    with ExitStack() as es:
        sems = {}
        for e in ENGINES:
            sems[e] = es.enter_context(nc.semaphore("sem_" + e))
        dma_sems = {q: [es.enter_context(nc.semaphore(f"dsem_{q}_{i}"))
                        for i in range(Prog.DMA_POOL)] for q in ("sp", "pool")}

        pA = es.enter_context(nc.psum_tensor("pA", [128, 1024], F32)).ap()
        pB = es.enter_context(nc.psum_tensor("pB", [128, 1024], F32)).ap()
        pY = es.enter_context(nc.psum_tensor("pY", [128, 1024], F32)).ap()
        pS1 = es.enter_context(nc.psum_tensor("pS1", [128, 512], F32)).ap()
        pS2 = es.enter_context(nc.psum_tensor("pS2", [128, 512], F32)).ap()
        bufA0, bufA1, bufB0, bufB1 = Buf(), Buf(), Buf(), Buf()
        bufY, bufS1, bufS2 = Buf(), Buf(), Buf()

        sb = lambda name, shp, dt: es.enter_context(nc.sbuf_tensor(name, shp, dt)).ap()
        cs = {nm: sb("sb_" + nm, shp, dt) for nm, (shp, dt) in cshapes.items() if nm != "kconst"}
        qrw_b = sb("qrw_b", [65, 64], BF16)
        krw_b = sb("krw_b", [65, 64], BF16)
        onesp64 = sb("onesp64", [65, 64], BF16)
        rw_sb = sb("rw_sb", [128, 128], BF16)
        rw_f = sb("rw_f", [128, 64], F32)
        rb_b = sb("rb_b", [1, 8], BF16)
        db_sb = sb("db_sb", [8, C], BF16)
        gthi = sb("gthi", [8, 2 * T], BF16)
        gtlo = sb("gtlo", [8, 2 * T], BF16)
        b_cs = {nm: Buf() for nm in cshapes}
        b_qrw, b_krw, b_rw, b_rb, b_db, b_rwf = Buf(), Buf(), Buf(), Buf(), Buf(), Buf()
        b_gthi, b_gtlo = Buf(mw=True), Buf(mw=True)
        b_xnscr, b_accd, b_yscr, b_out = Buf(mw=True), Buf(mw=True), Buf(mw=True), Buf(mw=True)

        b_wscr = {nm: Buf(mw=True) for nm in wscr}
        b_gupb = Buf(mw=True)
        b_dwnb = Buf(mw=True)
        for nm in wscr:
            p.op("pool", lambda e, nm=nm: e.dma_start(out=wscr[nm], in_=wT[nm]),
                 writes=[b_wscr[nm]], dma=True)
        for ex in range(NE):
            p.op("pool", lambda e, ex=ex: e.dma_start(out=gupb[ex], in_=gup_d[ex]),
                 writes=[b_gupb], dma=True)
            p.op("pool", lambda e, ex=ex: e.dma_start(out=dwnb[ex], in_=dwn_d[ex]),
                 writes=[b_dwnb], dma=True)
        for nm in cs:
            pass
        for nm in [n for n in cshapes if n != "kconst"]:
            p.op("sp", lambda e, nm=nm: e.dma_start(out=cs[nm][:, :], in_=cst_d[nm]),
                 writes=[b_cs[nm]], dma=True)
        p.op("pool", lambda e: e.dma_start(out=qrw_b[0:1, :], in_=qrw_d), writes=[b_qrw], dma=True)
        p.op("pool", lambda e: e.dma_start(out=qrw_b[64:65, :], in_=qrw_d), writes=[b_qrw], dma=True)
        p.op("pool", lambda e: e.dma_start(out=krw_b[0:1, :], in_=krw_d), writes=[b_krw], dma=True)
        p.op("pool", lambda e: e.dma_start(out=krw_b[64:65, :], in_=krw_d), writes=[b_krw], dma=True)
        p.op("sp", lambda e: e.dma_start(out=onesp64[0:1, :], in_=cst_d["ones1x64"]),
             writes=[b_cs["ones1x64"]], dma=True)
        p.op("sp", lambda e: e.dma_start(out=onesp64[64:65, :], in_=cst_d["ones1x64"]),
             writes=[b_cs["ones1x64"]], dma=True)
        p.op("pool", lambda e: e.dma_start(out=rw_f[:, :], in_=rwl_d), writes=[b_rwf], dma=True)
        p.op("dve", lambda e: e.tensor_copy(rw_sb[:, 0:64], rw_f[:, :]), reads=[b_rwf], writes=[b_rw])
        p.op("dve", lambda e: e.tensor_tensor(out=rw_sb[:, 64:128], in0=rw_f[:, :],
                                              in1=rw_sb[:, 0:64], op=ALU.subtract),
             reads=[b_rwf], writes=[b_rw])
        p.op("pool", lambda e: e.dma_start(out=rb_b[:, :], in_=rb_d), writes=[b_rb], dma=True)
        p.op("pool", lambda e: e.dma_start(out=db_sb[:, :], in_=dbias_d), writes=[b_db], dma=True)

        main_es = ExitStack()
        msb = lambda name, shp, dt: main_es.enter_context(nc.sbuf_tensor(name, shp, dt)).ap()
        lsb = msb

        xstage = [lsb(f"xstage{i}", [128, C], F32) for i in range(2)]
        b_xstage = [Buf(), Buf()]
        xntm = [lsb(f"xntm{i}", [128, C], BF16) for i in range(2)]
        b_xntm = [Buf(), Buf()]
        xnfst = [lsb(f"xnfst{i}", [128, C], BF16) for i in range(2)]
        b_xnfst = [Buf(), Buf()]
        dump = lsb("dump", [128, C], BF16)
        b_dump = Buf()
        nmean = [lsb(f"nmean{i}", [128, 1], F32) for i in range(2)]
        b_nmean = [Buf(), Buf()]
        vsum = [lsb(f"vsum{i}", [128, 1], F32) for i in range(2)]
        b_vsum = [Buf(), Buf()]
        rstd_ln = [lsb(f"rstdln{i}", [128, 1], F32) for i in range(2)]
        b_rstd_ln = [Buf(), Buf()]
        sm8 = lsb("sm8", [128, 8], F32)
        m8a = lsb("m8a", [128, 1], F32)
        m8b = lsb("m8b", [128, 8], F32)
        g8 = lsb("g8", [128, 8], F32)
        b_sm8, b_m8a, b_m8b, b_g8 = Buf(), Buf(), Buf(), Buf()

        for s in range(2):
            for tt in range(NTT):
                par = tt % 2
                gc0 = s * T + tt * 128
                xs, bx = xstage[par], b_xstage[par]
                p.op("sp", lambda e, s=s, tt=tt, xs=xs: e.dma_start(
                    out=xs[:, :], in_=x_d[s, tt * 128:(tt + 1) * 128, :]),
                    writes=[bx], dma=True)
                nm, bnm = nmean[par], b_nmean[par]
                p.op("dve", lambda e, xs=xs, nm=nm: e.reduce_sum(nm[:, :], xs[:, :], axis=AX),
                     reads=[bx], writes=[bnm])
                p.op("dve", lambda e, nm=nm: e.tensor_scalar(
                    out=nm[:, :], in0=nm[:, :], scalar1=-1.0 / C, scalar2=None, op0=ALU.mult),
                    writes=[bnm])
                vs, bvs = vsum[par], b_vsum[par]
                p.op("act", lambda e, xs=xs, nm=nm, vs=vs: e.activation(
                    out=dump[:, :], in_=xs[:, :], func=AF.Square, bias=nm[:, :], accum_out=vs[:, :]),
                    reads=[bx, bnm], writes=[b_dump, bvs])
                rl, brl = rstd_ln[par], b_rstd_ln[par]
                p.op("act", lambda e, vs=vs, rl=rl: e.activation(
                    out=rl[:, :], in_=vs[:, :], func=AF.Ln, scale=1.0 / C,
                    bias=cs["fconst"][:, 0:1]),
                    reads=[bvs, b_cs["fconst"]], writes=[brl])
                p.op("act", lambda e, rl=rl: e.activation(
                    out=rl[:, :], in_=rl[:, :], func=AF.Exp, scale=-0.5),
                    writes=[brl])
                xt, bxt = xntm[par], b_xntm[par]
                p.op("dve", lambda e, xs=xs, nm=nm, rl=rl, xt=xt: e.tensor_scalar(
                    out=xt[:, :], in0=xs[:, :], scalar1=nm[:, :], op0=ALU.add,
                    scalar2=rl[:, :], op1=ALU.mult),
                    reads=[bx, bnm, brl], writes=[bxt])
                if "xn_tm" in tap_set:
                    if "xn_tm" not in tap_out:
                        tap_out["xn_tm"] = nc.dram_tensor("tap_xn_tm", [2 * NTT, 128, C], F32,
                                                          kind="ExternalOutput").ap()
                    p.op("pool", lambda e, s=s, tt=tt, xt=xt: e.dma_start(
                        out=tap_out["xn_tm"][s * NTT + tt], in_=xt[:, :]),
                        reads=[bxt], dma=True)
                pT = pS2.bitcast(BF16)
                p.op("pe", lambda e, xt=xt, pT=pT: [
                    e.transpose(out=pT[:, kc * 128:(kc + 1) * 128],
                                in_=xt[:, kc * 128:(kc + 1) * 128],
                                identity=cs["ident_bf"][:, :])
                    for kc in range(NKC)][-1],
                    reads=[bxt, b_cs["ident_bf"]], writes=[bufS2])
                xf, bxf = xnfst[par], b_xnfst[par]
                p.op("dve", lambda e, pT=pT, xf=xf: e.tensor_copy(xf[:, :], pT[:, :]),
                     reads=[bufS2], writes=[bxf])
                p.op("sp", lambda e, xf=xf, gc0=gc0: e.dma_start(
                    out=xnscr.rearrange("p (kc t) -> p kc t", kc=NKC)[:, :, gc0:gc0 + 128],
                    in_=xf.rearrange("p (kc t) -> p kc t", kc=NKC)[:, :, :]),
                    reads=[bxf], writes=[b_xnscr], dma=True)
                p.op("pe", lambda e, xf=xf: [
                    e.matmul(out=pS1[:, 0:8], lhsT=xf[:, kc * 128:(kc + 1) * 128],
                             rhs=rw_sb[:, hl * 64 + kc * 8:hl * 64 + (kc + 1) * 8],
                             start=(kc == 0 and hl == 0), stop=False)
                    for hl in range(2) for kc in range(NKC)][-1],
                    reads=[bxf, b_rw], writes=[bufS1])
                p.op("pe", lambda e: e.matmul(out=pS1[:, 0:8], lhsT=cs["onesrow"][:, :],
                                              rhs=rb_b[:, :], start=False, stop=True),
                     reads=[b_cs["onesrow"], b_rb], writes=[bufS1])
                p.op("dve", lambda e: e.reduce_max(m8a[:, :], pS1[:, 0:8], axis=AX),
                     reads=[bufS1], writes=[b_m8a])
                p.op("dve", lambda e: e.tensor_scalar(
                    out=m8a[:, :], in0=m8a[:, :], scalar1=-1.0, scalar2=None, op0=ALU.mult),
                    writes=[b_m8a])
                p.op("dve", lambda e: e.tensor_copy(sm8[:, :], pS1[:, 0:8]),
                     reads=[bufS1], writes=[b_sm8])
                p.op("act", lambda e: e.activation(out=sm8[:, :], in_=sm8[:, :],
                                                   func=AF.Exp, bias=m8a[:, :]),
                     reads=[b_m8a], writes=[b_sm8])
                p.op("dve", lambda e: e.tensor_scalar(
                    out=dump[:, 0:8], in0=sm8[:, :], scalar1=1.0, scalar2=None,
                    op0=ALU.mult, op1=ALU.add, accum_out=m8a[:, :]),
                    reads=[b_sm8], writes=[b_dump, b_m8a])
                p.op("dve", lambda e: e.reciprocal(m8a[:, :], m8a[:, :]), writes=[b_m8a])
                p.op("dve", lambda e: e.tensor_scalar(
                    out=sm8[:, :], in0=sm8[:, :], scalar1=m8a[:, :], scalar2=None, op0=ALU.mult),
                    reads=[b_m8a], writes=[b_sm8])
                p.op("dve", lambda e: e.reduce_max(m8a[:, :], sm8[:, :], axis=AX),
                     reads=[b_sm8], writes=[b_m8a])
                p.op("dve", lambda e: e.tensor_scalar(
                    out=m8b[:, :], in0=sm8[:, :], scalar1=m8a[:, :], scalar2=None, op0=ALU.is_ge),
                    reads=[b_sm8, b_m8a], writes=[b_m8b])
                p.op("dve", lambda e: e.scalar_tensor_tensor(
                    out=m8b[:, :], in0=m8b[:, :], scalar=-4.0, in1=sm8[:, :],
                    op0=ALU.mult, op1=ALU.add),
                    reads=[b_sm8], writes=[b_m8b])
                p.op("dve", lambda e: e.reduce_max(m8a[:, :], m8b[:, :], axis=AX),
                     reads=[b_m8b], writes=[b_m8a])
                p.op("dve", lambda e: e.scalar_tensor_tensor(
                    out=g8[:, :], in0=sm8[:, :], scalar=m8a[:, :], in1=sm8[:, :],
                    op0=ALU.is_ge, op1=ALU.mult),
                    reads=[b_sm8, b_m8a], writes=[b_g8])
                p.op("pe", lambda e: e.transpose(out=pS2[0:8, 0:128], in_=g8[:, :],
                                                 identity=cs["ident_f32"][:, :]),
                     reads=[b_g8, b_cs["ident_f32"]], writes=[bufS2])
                p.op("dve", lambda e, gc0=gc0: e.tensor_copy(
                    gthi[:, gc0:gc0 + 128], pS2[0:8, 0:128]),
                    reads=[bufS2], writes=[b_gthi])
                p.op("dve", lambda e, gc0=gc0: e.tensor_tensor(
                    out=gtlo[:, gc0:gc0 + 128], in0=pS2[0:8, 0:128],
                    in1=gthi[:, gc0:gc0 + 128], op=ALU.subtract),
                    reads=[bufS2, b_gthi], writes=[b_gtlo])
        tap("xn_fm", lambda: xnscr, [128, NKC * 2 * T], BF16, deps=list(b_xnscr.wl))
        tap("g8", lambda: gthi, [8, 2 * T], BF16, deps=list(b_gthi.wl))
        p.barrier()

        xng = [msb(f"xng{i}", [128, NKC * 512], BF16) for i in range(2)]
        b_xng = [Buf(), Buf()]
        wstream = msb("wstream", [128, 8192], BF16)
        b_wh = [Buf(), Buf()]
        b_wproj = Buf()
        biasT = msb("biasT", [128, 17408], BF16)
        b_biasT = [Buf() for _ in range(16)]
        offJ = [2048 * J - 64 * J * (J - 1) for J in range(16)]
        ls1_ = [msb(f"ls1_{i}", [128, 1024], F32) for i in range(2)]
        b_ls1_ = [Buf(), Buf()]
        ls1, b_ls1 = ls1_[0], b_ls1_[0]
        bufY0, bufY1 = Buf(), Buf()
        q_ext = [msb(f"q_ext{i}", [128, T], BF16) for i in range(2)]
        k_ext = [msb(f"k_ext{i}", [128, T], BF16) for i in range(2)]
        protc = [msb(f"protc{i}", [128, T], BF16) for i in range(2)]
        p64x2 = [msb(f"p64x2{i}", [128, T], BF16) for i in range(2)]
        v_ext = [msb(f"v_ext{i}", [128, 16 * 68], BF16) for i in range(2)]
        b_qe = [Buf(), Buf()]
        b_ke = [Buf(), Buf()]
        b_pr = [Buf(), Buf()]
        b_p6 = [Buf(), Buf()]
        b_ve = [Buf(), Buf()]
        Tf = [msb(f"Tf{i}", [128, 512], F32) for i in range(3)]
        b_Tf = [Buf(), Buf(), Buf()]
        sqb = [msb(f"sqb{i}", [128, 512], BF16) for i in range(3)]
        b_sq = [Buf(), Buf(), Buf()]
        v_fm = msb("v_fm", [128, T], BF16)
        b_vfm = Buf()
        ET = [msb(f"ET{i}", [128, 1024], BF16) for i in range(2)]
        b_ET = [Buf(), Buf()]
        csth = msb("csth", [16, 512], BF16)
        cstl = msb("cstl", [16, 512], BF16)
        b_csth, b_cstl = Buf(), Buf()
        rows65 = [msb(f"rows65_{i}", [65, 512], F32) for i in range(2)]
        b_rows65 = [Buf(), Buf()]
        rstd65 = [msb(f"rstd65_{i}", [65, 512], BF16) for i in range(3)]
        b_rstd65 = [Buf(), Buf(), Buf()]
        rowkq = [msb(f"rowkq{i}", [1, 512], F32) for i in range(2)]
        b_rowkq = [Buf(), Buf()]
        kmx = msb("kmx", [1, 1], F32)
        kmg = msb("kmg", [1, 1], F32)
        b_kmx, b_kmg = Buf(), Buf()
        m_row = msb("m_row", [1, T], BF16)
        b_mrow = Buf()
        rzrow = msb("rzrow", [1, 1024], F32)
        rzhi = msb("rzhi", [1, 1024], BF16)
        rzlo = msb("rzlo", [1, 1024], BF16)
        b_rz, b_rzhi, b_rzlo = Buf(), Buf(), Buf()
        rzb_sb = msb("rzb_sb", [64, 1024], F32)
        b_rzb = Buf()
        ystage = [rows65[i].bitcast(BF16)[0:64, 0:1024] for i in range(2)]
        b_yst = b_rows65
        ao_st = [p64x2[0].bitcast(F32)[:, 0:512]]
        b_ao = [b_p6[0]]

        for i in range(2):
            p.op("dve", lambda e, i=i: e.memset(q_ext[i][:, :], 0.0), writes=[b_qe[i]])
            p.op("sp", lambda e, i=i: e.dma_start(out=k_ext[i][0:64, :], in_=cst_d["kconst"]),
                 writes=[b_ke[i]], dma=True)
            p.op("dve", lambda e, i=i: e.memset(v_ext[i][:, :], 0.0), writes=[b_ve[i]])
            p.op("dve", lambda e, i=i: e.memset(
                v_ext[i].rearrange("p (b c) -> p b c", c=68)[:, :, 64:65], 1.0),
                writes=[b_ve[i]])

        PSMALL = [(pS1, bufS1), (pS2, bufS2)]

        for s in seqs:
            sc0 = s * T
            for hp in range(heads // 2):
                wpar = hp % 2
                wof = wpar * 4096
                bw = b_wh[wpar]
                for ti, nm in enumerate(("wkT", "wqT", "wpT", "wvT")):
                    p.op("sp", lambda e, nm=nm, ti=ti, hp=hp, wof=wof: e.dma_start(
                        out=wstream[:, wof + ti * 1024: wof + (ti + 1) * 1024].rearrange("p (kc m) -> p kc m", kc=NKC)[:, :, :],
                        in_=wscr[nm].rearrange("(kc p) m -> p kc m", p=128)[:, :, hp * 128:(hp + 1) * 128]),
                        writes=[bw], reads=[b_wscr[nm]],
                        deps=([b_wproj.w] if b_wproj.w is not None else []) + list(b_wproj.r), dma=True)

                for g in range(4):
                    c0 = g * 512
                    xpar = g % 2
                    xn_g, bxn = xng[xpar], b_xng[xpar]
                    p.op("sp", lambda e, sc0=sc0, c0=c0, xn_g=xn_g: e.dma_start(
                        out=xn_g.rearrange("p (kc c) -> p kc c", kc=NKC)[:, :, :],
                        in_=xnscr.rearrange("p (kc t) -> p kc t", kc=NKC)[:, :, sc0 + c0:sc0 + c0 + 512]),
                        reads=[b_xnscr], writes=[bxn], dma=True)
                    pbank = {"k": (pA[:, 0:512], bufA0), "q": (pA[:, 512:1024], bufA1),
                             "p": (pB[:, 0:512], bufB0), "v": (pB[:, 512:1024], bufB1)}
                    sbank = {"k": (pS1, bufS1), "q": (pS2, bufS2), "p": (pS1, bufS1)}
                    tfm = {"k": 0, "q": 1, "p": 2}
                    if attn_lvl < 0.2:
                        continue
                    for ti, tn in enumerate(("k", "q", "p", "v")):
                        pp_, bpp = pbank[tn]
                        wslice = lambda kc, ti=ti, wof=wof: wstream[:, wof + ti * 1024 + kc * 128: wof + ti * 1024 + (kc + 1) * 128]
                        p.op("pe", lambda e, pp_=pp_, wslice=wslice, xn_g=xn_g: [
                            e.matmul(out=pp_, lhsT=wslice(kc), rhs=xn_g[:, kc * 512:(kc + 1) * 512],
                                     start=(kc == 0), stop=(kc == NKC - 1))
                            for kc in range(NKC)][-1],
                            reads=[bw, bxn], writes=[bpp])
                        if tn == "v":
                            p.op("dve", lambda e, pp_=pp_, c0=c0: e.tensor_copy(
                                v_fm[:, c0:c0 + 512], pp_),
                                reads=[bpp], writes=[b_vfm])
                        else:
                            tf, btf = Tf[tfm[tn]], b_Tf[tfm[tn]]
                            p.op("dve", lambda e, pp_=pp_, tf=tf: e.tensor_copy(
                                tf[:, :], pp_),
                                reads=[bpp], writes=[btf])
                            sq_h, bsq_h = sqb[tfm[tn]], b_sq[tfm[tn]]
                            p.op("act", lambda e, tf=tf, sq_h=sq_h: e.activation(
                                out=sq_h[:, :], in_=tf[:, :], func=AF.Square),
                                reads=[btf], writes=[bsq_h])
                    if attn_lvl < 0.25:
                        continue
                    for tn in ("k", "q", "p"):
                        ps_, bps_ = sbank[tn]
                        sq_h, bsq_h = sqb[tfm[tn]], b_sq[tfm[tn]]
                        p.op("pe", lambda e, ps_=ps_, sq_h=sq_h: e.matmul(
                            out=ps_[0:65, :], lhsT=cs["onespair"][:, :], rhs=sq_h[:, :],
                            start=True, stop=True),
                            reads=[bsq_h, b_cs["onespair"]], writes=[bps_])
                        r65, br65 = rows65[tfm[tn] % 2], b_rows65[tfm[tn] % 2]
                        rs65, brs65 = rstd65[tfm[tn]], b_rstd65[tfm[tn]]
                        epsc = 0 if tn in ("k", "q") else 1
                        p.op("dve", lambda e, ps_=ps_, r65=r65: e.tensor_copy(
                            r65[:, :], ps_[0:65, :]),
                            reads=[bps_], writes=[br65])
                        p.op("act", lambda e, epsc=epsc, r65=r65: e.activation(
                            out=r65[:, :], in_=r65[:, :],
                            func=AF.Ln, scale=1.0 / 64, bias=cs["fconst"][0:65, epsc:epsc + 1]),
                            reads=[b_cs["fconst"]], writes=[br65])
                        if tn == "q":
                            p.op("act", lambda e, r65=r65, rs65=rs65: e.activation(
                                out=rs65[:, :], in_=r65[:, :],
                                func=AF.Exp, scale=-0.5, bias=cs["fconst"][0:65, 2:3]),
                                reads=[br65, b_cs["fconst"]], writes=[brs65])
                        else:
                            p.op("act", lambda e, r65=r65, rs65=rs65: e.activation(
                                out=rs65[:, :], in_=r65[:, :], func=AF.Exp, scale=-0.5),
                                reads=[br65], writes=[brs65])
                    for tn in ("k", "q", "p"):
                        ps_, bps_ = sbank[tn]
                        tf, btf = Tf[tfm[tn]], b_Tf[tfm[tn]]
                        rs65, brs65 = rstd65[tfm[tn]], b_rstd65[tfm[tn]]
                        for hh in range(2):
                            p.op("pe", lambda e, hh=hh, ps_=ps_, rs65=rs65,
                                 lrow={"k": krw_b, "q": qrw_b, "p": onesp64}[tn]: e.matmul(
                                out=ps_[64 * hh:64 * hh + 64, :],
                                lhsT=lrow[64 * hh:64 * hh + 1, :],
                                rhs=rs65[64 * hh:64 * hh + 1, :],
                                start=True, stop=True),
                                reads=[brs65,
                                       {"k": b_krw, "q": b_qrw, "p": b_cs["ones1x64"]}[tn]],
                                writes=[bps_])
                            if tn in ("k", "q"):
                                dst, bdst = (k_ext[hh], b_ke[hh]) if tn == "k" else (q_ext[hh], b_qe[hh])
                                p.op("dve", lambda e, tf=tf, hh=hh, c0=c0, dst=dst, ps_=ps_: e.tensor_tensor(
                                    out=dst[64:128, c0:c0 + 512], in0=tf[64 * hh:64 * hh + 64, :],
                                    in1=ps_[64 * hh:64 * hh + 64, :], op=ALU.mult),
                                    reads=[btf, bps_], writes=[bdst])
                            else:
                                p.op("dve", lambda e, tf=tf, hh=hh, c0=c0, ps_=ps_: e.tensor_tensor(
                                    out=ls1[0:64, 0:512], in0=tf[64 * hh:64 * hh + 64, :],
                                    in1=ps_[64 * hh:64 * hh + 64, :], op=ALU.mult),
                                    reads=[btf, bps_], writes=[b_ls1])
                                pr, bpr = protc[hh], b_pr[hh]
                                p.op("dve", lambda e, c0=c0, pr=pr: e.tensor_tensor(
                                    out=pr[0:64, c0:c0 + 512], in0=ls1[0:64, 0:512],
                                    in1=cs["cosb64"][:, c0:c0 + 512], op=ALU.mult),
                                    reads=[b_ls1, b_cs["cosb64"]], writes=[bpr])
                                p.op("pool", lambda e, c0=c0, pr=pr: e.tensor_tensor(
                                    out=pr[64:96, c0:c0 + 512], in0=ls1[32:64, 0:512],
                                    in1=cs["sinb64"][32:64, c0:c0 + 512], op=ALU.mult),
                                    reads=[b_ls1, b_cs["sinb64"]], writes=[bpr])
                                p.op("pool", lambda e, c0=c0, pr=pr: e.tensor_tensor(
                                    out=pr[96:128, c0:c0 + 512], in0=ls1[0:32, 0:512],
                                    in1=cs["sinb64"][0:32, c0:c0 + 512], op=ALU.mult),
                                    reads=[b_ls1, b_cs["sinb64"]], writes=[bpr])
                                p6, bp6 = p64x2[hh], b_p6[hh]
                                p.op("dve", lambda e, tf=tf, hh=hh, c0=c0, p6=p6: e.tensor_scalar(
                                    out=p6[0:64, c0:c0 + 512], in0=tf[64 * hh:64 * hh + 64, :],
                                    scalar1=1.0 / 64, scalar2=None, op0=ALU.mult),
                                    reads=[btf], writes=[bp6])
                                p.op("pool", lambda e, tf=tf, hh=hh, c0=c0, p6=p6: e.tensor_scalar(
                                    out=p6[64:128, c0:c0 + 512], in0=tf[64 * hh:64 * hh + 64, :],
                                    scalar1=1.0 / 64, scalar2=None, op0=ALU.mult),
                                    reads=[btf], writes=[bp6])
                for jt in range(16 if attn_lvl >= 1 else 0):
                    pT = pS2.bitcast(BF16)
                    p.op("pe", lambda e, jt=jt, pT=pT: e.transpose(
                        out=pT[:, 0:128], in_=v_fm[:, jt * 128:(jt + 1) * 128],
                        identity=cs["ident_bf"][:, :]),
                        reads=[b_vfm, b_cs["ident_bf"]], writes=[bufS2])
                    for hh in range(2):
                        p.op("dve", lambda e, jt=jt, hh=hh, pT=pT: e.tensor_copy(
                            v_ext[hh][:, jt * 68:jt * 68 + 64], pT[:, 64 * hh:64 * hh + 64]),
                            reads=[bufS2], writes=[b_ve[hh]])

                for hh in range(2):
                    h = hp * 2 + hh
                    slope = 2.0 ** (-(h + 1) / 2.0)
                    qe, ke, pr, p6, ve = q_ext[hh], k_ext[hh], protc[hh], p64x2[hh], v_ext[hh]
                    bqe, bke, bpr, bp6, bve = b_qe[hh], b_ke[hh], b_pr[hh], b_p6[hh], b_ve[hh]

                    for g in range(4):
                        c0 = g * 512
                        sq_h, bsq_h = sqb[g % 2], b_sq[g % 2]
                        rk, brk = rowkq[g % 2], b_rowkq[g % 2]
                        bY_h = bufY0 if ((c0 % 1024) == 0) else bufY1
                        p.op("act", lambda e, ke=ke, c0=c0, sq_h=sq_h: e.activation(
                            out=sq_h[0:64, :], in_=ke[64:128, c0:c0 + 512], func=AF.Square),
                            reads=[bke], writes=[bsq_h])
                        p.op("pe", lambda e, c0=c0, sq_h=sq_h: e.matmul(
                            out=pY[0:1, (c0 % 1024):(c0 % 1024) + 512],
                            lhsT=cs["ones64"][:, :], rhs=sq_h[0:64, :],
                            start=True, stop=True),
                            reads=[bsq_h, b_cs["ones64"]], writes=[bY_h])
                        p.op("dve", lambda e, c0=c0, rk=rk: e.tensor_copy(
                            rk[:, :], pY[0:1, (c0 % 1024):(c0 % 1024) + 512]),
                            reads=[bY_h], writes=[brk])
                        if g == 0:
                            p.op("dve", lambda e, rk=rk: e.reduce_max(kmx[:, :], rk[:, :], axis=AX),
                                 reads=[brk], writes=[b_kmx])
                        else:
                            p.op("dve", lambda e, rk=rk: e.reduce_max(kmg[:, :], rk[:, :], axis=AX),
                                 reads=[brk], writes=[b_kmg])
                            p.op("dve", lambda e: e.tensor_tensor(
                                out=kmx[:, :], in0=kmg[:, :], in1=kmx[:, :], op=ALU.max),
                                reads=[b_kmg], writes=[b_kmx])
                    p.op("act", lambda e: e.activation(out=kmx[:, :], in_=kmx[:, :], func=AF.Ln),
                         writes=[b_kmx])
                    p.op("dve", lambda e: e.tensor_scalar(
                        out=kmx[:, :], in0=kmx[:, :], scalar1=0.5, scalar2=None, op0=ALU.mult),
                        writes=[b_kmx])
                    for g in range(4):
                        c0 = g * 512
                        sq_h, bsq_h = sqb[g % 2], b_sq[g % 2]
                        rk, brk = rowkq[g % 2], b_rowkq[g % 2]
                        bY_h = bufY0 if ((c0 % 1024) == 0) else bufY1
                        p.op("act", lambda e, qe=qe, c0=c0, sq_h=sq_h: e.activation(
                            out=sq_h[0:64, :], in_=qe[64:128, c0:c0 + 512], func=AF.Square),
                            reads=[bqe], writes=[bsq_h])
                        p.op("pe", lambda e, c0=c0, sq_h=sq_h: e.matmul(
                            out=pY[0:1, (c0 % 1024):(c0 % 1024) + 512],
                            lhsT=cs["ones64"][:, :], rhs=sq_h[0:64, :],
                            start=True, stop=True),
                            reads=[bsq_h, b_cs["ones64"]], writes=[bY_h])
                        p.op("dve", lambda e, c0=c0, rk=rk: e.tensor_copy(
                            rk[:, :], pY[0:1, (c0 % 1024):(c0 % 1024) + 512]),
                            reads=[bY_h], writes=[brk])
                        p.op("act", lambda e, rk=rk: e.activation(
                            out=rk[:, :], in_=rk[:, :], func=AF.Ln),
                            writes=[brk])
                        p.op("act", lambda e, rk=rk, c0=c0: e.activation(
                            out=m_row[:, c0:c0 + 512], in_=rk[:, :],
                            func=AF.Exp, scale=0.5, bias=kmx[:, :]),
                            reads=[brk, b_kmx], writes=[b_mrow])
                        p.op("dve", lambda e, c0=c0: e.tensor_scalar(
                            out=m_row[:, c0:c0 + 512], in0=m_row[:, c0:c0 + 512],
                            scalar1=-MSHIFT, scalar2=None, op0=ALU.add),
                            writes=[b_mrow])
                    pc = 0
                    for J in range(16 if attn_lvl >= 2 else 0):
                        w_J = T - 128 * J
                        for (b0, b1) in splits(0, w_J, 1024):
                            lsx, blsx = ls1_[pc % 2], b_ls1_[pc % 2]
                            pc += 1
                            for (a0, a1) in splits(b0, b1):
                                ps, bps = PSMALL[(a0 // 512) % 2]
                                p.op("pe", lambda e, J=J, a0=a0, a1=a1, ps=ps, pr=pr, p6=p6: e.matmul(
                                    out=ps[:, 0:a1 - a0], lhsT=pr[:, J * 128:(J + 1) * 128],
                                    rhs=p6[:, J * 128 + a0:J * 128 + a1], start=True, stop=True),
                                    reads=[bpr, bp6], writes=[bps])
                                p.op("act", lambda e, a0=a0, a1=a1, b0=b0, ps=ps, lsx=lsx: e.activation(
                                    out=lsx[:, a0 - b0:a1 - b0], in_=ps[:, 0:a1 - a0],
                                    func=AF.Exp, scale=-1.0),
                                    reads=[bps], writes=[blsx])
                            p.op("act", lambda e, b0=b0, b1=b1, lsx=lsx: e.activation(
                                out=lsx[:, 0:b1 - b0], in_=lsx[:, 0:b1 - b0], func=AF.Ln,
                                bias=cs["fconst"][:, 3:4]),
                                reads=[b_cs["fconst"]], writes=[blsx])
                            p.op("pool", lambda e, J=J, b0=b0, b1=b1, slope=slope, lsx=lsx: e.tensor_scalar(
                                out=biasT[:, offJ[J] + b0:offJ[J] + b1], in0=lsx[:, 0:b1 - b0],
                                scalar1=-slope, scalar2=0.0, op0=ALU.mult, op1=ALU.add),
                                reads=[blsx], writes=[b_biasT[J]])
                        p.op("dve", lambda e, J=J: e.tensor_tensor(
                            out=biasT[:, offJ[J]:offJ[J] + 128], in0=biasT[:, offJ[J]:offJ[J] + 128],
                            in1=cs["utmask"][:, :], op=ALU.mult),
                            reads=[b_cs["utmask"]], writes=[b_biasT[J]])

                    for gq in range(4 if attn_lvl >= 3 else 0):
                        i0g = gq * 512
                        Jmax = 4 * gq + 3
                        p.op("pe", lambda e, i0g=i0g, Jmax=Jmax: [
                            e.matmul(out=pS1[0:16, max(0, 128 * J - i0g):512],
                                     lhsT=cs["ecols"][:, 16 * J:16 * J + 16],
                                     rhs=biasT[:, offJ[J] + max(0, i0g - 128 * J):offJ[J] + i0g + 512 - 128 * J],
                                     start=(J == 0), stop=(J == Jmax))
                            for J in range(Jmax + 1)][-1],
                            reads=[b_cs["ecols"]] + [b_biasT[J] for J in range(Jmax + 1)],
                            writes=[bufS1])
                        p.op("dve", lambda e: e.tensor_copy(csth[:, :], pS1[0:16, :]),
                             reads=[bufS1], writes=[b_csth])
                        p.op("dve", lambda e: e.tensor_tensor(
                            out=cstl[:, :], in0=pS1[0:16, :], in1=csth[:, :], op=ALU.subtract),
                            reads=[bufS1, b_csth], writes=[b_cstl])
                        p.op("pe", lambda e, i0g=i0g: [
                            e.matmul(out=pS2[0:16, :], lhsT=cs["l16"][:, :], rhs=csth[:, :],
                                     start=True, stop=False),
                            e.matmul(out=pS2[0:16, :], lhsT=cs["l16"][:, :], rhs=cstl[:, :],
                                     start=False, stop=False),
                            e.matmul(out=pS2[0:16, :], lhsT=cs["neg1"][:, :],
                                     rhs=m_row[:, i0g:i0g + 512], start=False, stop=True)][-1],
                            reads=[b_csth, b_cstl, b_mrow, b_cs["l16"], b_cs["neg1"]],
                            writes=[bufS2])
                        p.op("dve", lambda e, i0g=i0g, qe=qe: e.tensor_copy(
                            qe[0:16, i0g:i0g + 512], pS2[0:16, :]),
                            reads=[bufS2], writes=[bqe])
                        p.op("dve", lambda e, i0g=i0g, qe=qe: e.tensor_tensor(
                            out=qe[32:48, i0g:i0g + 512], in0=pS2[0:16, :],
                            in1=qe[0:16, i0g:i0g + 512], op=ALU.subtract),
                            reads=[bufS2], writes=[bqe])

                    SCF = [pr.bitcast(F32), p6.bitcast(F32)]
                    b_SCF = [bpr, bp6]
                    for H in range(2 if attn_lvl >= 4 else 0):
                        ib = H * 1024
                        Jlist = [J for J in range(16) if 128 * J < ib + 1024]
                        pend_mm3 = []
                        for J in Jlist:
                            i0 = max(128 * J, ib)
                            i1 = ib + 1024
                            o0 = i0 - ib
                            pX = pA if J % 2 == 0 else pB
                            bpXb = [bufA0, bufA1] if J % 2 == 0 else [bufB0, bufB1]
                            et, bet = ET[J % 2], b_ET[J % 2]
                            scf, bscf = SCF[J % 2], b_SCF[J % 2]
                            chain = []
                            for (a0, a1) in splits(o0, 1024):
                                p.op("pe", lambda e, J=J, a0=a0, a1=a1, ib=ib, pX=pX, ke=ke, qe=qe: e.matmul(
                                    out=pX[:, a0:a1], lhsT=ke[:, J * 128:(J + 1) * 128],
                                    rhs=qe[:, ib + a0:ib + a1], start=True, stop=False),
                                    reads=[bke, bqe], writes=[bpXb[a0 // 512]])
                                p.op("pe", lambda e, J=J, a0=a0, a1=a1, ib=ib, pX=pX: e.matmul(
                                    out=pX[:, a0:a1], lhsT=cs["slmat"][:, :],
                                    rhs=biasT[:, offJ[J] + ib + a0 - 128 * J:offJ[J] + ib + a1 - 128 * J],
                                    start=False, stop=True),
                                    reads=[b_biasT[J], b_cs["slmat"]], writes=[bpXb[a0 // 512]])
                                p.op("dve", lambda e, a0=a0, a1=a1, o0=o0, pX=pX, scf=scf: e.tensor_copy(
                                    scf[:, a0 - o0:a1 - o0], pX[:, a0:a1]),
                                    reads=[bpXb[a0 // 512]], writes=[bscf])
                                p.op("act", lambda e, a0=a0, a1=a1, o0=o0, et=et, scf=scf: e.activation(
                                    out=et[:, a0 - o0:a1 - o0], in_=scf[:, a0 - o0:a1 - o0],
                                    func=AF.Exp),
                                    reads=[bscf], writes=[bet])
                                if 128 * J >= ib and a0 == o0:
                                    p.op("dve", lambda e, et=et: e.tensor_tensor(
                                        out=et[:, 0:128], in0=et[:, 0:128],
                                        in1=cs["utmask"][:, :], op=ALU.mult),
                                        reads=[b_cs["utmask"]], writes=[bet])
                                lastJ_for_bank = max(Jx for Jx in Jlist if 128 * Jx < ib + (a0 // 512 + 1) * 512)
                                chain.append((J, a0, a1, i0, et, J == 0, J == lastJ_for_bank))
                            pend_mm3.append((chain, bet, bve))
                            if len(pend_mm3) >= 2:
                                ch, betx, bvex = pend_mm3.pop(0)
                                for (Jx, a0, a1, i0x, etx, st, sp_) in ch:
                                    p.op("pe", lambda e, Jx=Jx, a0=a0, a1=a1, i0x=i0x, etx=etx, ve=ve,
                                         st=st, sp_=sp_, ib=ib: e.matmul(
                                        out=pY[0:65, a0:a1], lhsT=ve[:, Jx * 68:Jx * 68 + 65],
                                        rhs=etx[:, ib + a0 - i0x:ib + a1 - i0x], start=st, stop=sp_),
                                        reads=[betx, bvex], writes=[bufY0 if a0 < 512 else bufY1])
                        for (ch, betx, bvex) in pend_mm3:
                            for (Jx, a0, a1, i0x, etx, st, sp_) in ch:
                                p.op("pe", lambda e, Jx=Jx, a0=a0, a1=a1, i0x=i0x, etx=etx, ve=ve,
                                     st=st, sp_=sp_, ib=ib: e.matmul(
                                    out=pY[0:65, a0:a1], lhsT=ve[:, Jx * 68:Jx * 68 + 65],
                                    rhs=etx[:, ib + a0 - i0x:ib + a1 - i0x], start=st, stop=sp_),
                                    reads=[betx, bvex], writes=[bufY0 if a0 < 512 else bufY1])
                        p.op("dve", lambda e: e.reciprocal(rzrow[:, :], pY[64:65, :]),
                             reads=[bufY0, bufY1], writes=[b_rz])
                        p.op("dve", lambda e: e.tensor_copy(rzhi[:, :], rzrow[:, :]),
                             reads=[b_rz], writes=[b_rzhi])
                        p.op("dve", lambda e: e.tensor_tensor(
                            out=rzlo[:, :], in0=rzrow[:, :], in1=rzhi[:, :], op=ALU.subtract),
                            reads=[b_rz, b_rzhi], writes=[b_rzlo])
                        for (a0, a1) in ((0, 512), (512, 1024)):
                            ps, bps = PSMALL[(a0 // 512) % 2]
                            p.op("pe", lambda e, a0=a0, a1=a1, ps=ps: [
                                e.matmul(out=ps[0:64, :], lhsT=cs["ones1x64"][:, :],
                                         rhs=rzhi[:, a0:a1], start=True, stop=False),
                                e.matmul(out=ps[0:64, :], lhsT=cs["ones1x64"][:, :],
                                         rhs=rzlo[:, a0:a1], start=False, stop=True)][-1],
                                reads=[b_rzhi, b_rzlo, b_cs["ones1x64"]], writes=[bps])
                            p.op("dve", lambda e, a0=a0, a1=a1, ps=ps: e.tensor_copy(
                                rzb_sb[:, a0:a1], ps[0:64, :]),
                                reads=[bps], writes=[b_rzb])
                        if attn_lvl < 5:
                            continue
                        yst, byst = ystage[H], b_yst[H]
                        p.op("dve", lambda e, yst=yst: e.tensor_tensor(
                            out=yst[:, :], in0=pY[0:64, :], in1=rzb_sb[:, :], op=ALU.mult),
                            reads=[bufY0, bufY1, b_rzb], writes=[byst])
                        p.op("sp", lambda e, s=s, h=h, H=H, yst=yst: e.dma_start(
                            out=yscr[s, h, :, H * 1024:(H + 1) * 1024], in_=yst[:, :]),
                            reads=[byst], writes=[b_yscr], dma=True)
                    if ("scoresT" in tap_set) and h == 0 and s == 0:
                        tap("scoresT", lambda: ET[1], [128, 1024], BF16, deps=[b_ET[1].w])

            if attn_lvl < 6:
                continue
            p.op("sp", lambda e: e.dma_start(
                out=wstream.rearrange("p (kc m) -> p kc m", kc=NKC)[:, :, :],
                in_=wscr["wprojT"].rearrange("(kc p) m -> p kc m", p=128)[:, :, :]),
                writes=[b_wproj], reads=[b_wscr["wprojT"]],
                deps=[ev for b in b_wh for ev in ([b.w] + b.r)], dma=True)
            for g in range(4):
                for half2 in range(2):
                    p.op("sp", lambda e, s=s, g=g, half2=half2: e.dma_start(
                        out=protc[half2].rearrange("p (kc t) -> p kc t", kc=4)[:, :, :],
                        in_=yscr[s].rearrange("(k g2) d t -> (g2 d) k t", g2=2)[:, half2 * 4:half2 * 4 + 4, g * 512:(g + 1) * 512]),
                        reads=[b_yscr], writes=[b_pr[half2]], dma=True)

                def ybv(kc):
                    return protc[kc // 4][:, (kc % 4) * 512:(kc % 4 + 1) * 512]

                for m in range(NKC):
                    ps, bps = PSMALL[m % 2]
                    p.op("pe", lambda e, m=m, ps=ps, ybv=ybv: [
                        e.matmul(out=ps[:, :], lhsT=wstream[:, kc * 1024 + m * 128:kc * 1024 + (m + 1) * 128],
                                 rhs=ybv(kc),
                                 start=(kc == 0), stop=(kc == NKC - 1))
                        for kc in range(NKC)][-1],
                        reads=[b_wproj, b_pr[0], b_pr[1]], writes=[bps])
                    ao, bao = ao_st[0], b_ao[0]
                    p.op("dve", lambda e, ps=ps, ao=ao: e.tensor_copy(ao[:, :], ps[:, :]),
                         reads=[bps], writes=[bao])
                    p.op("sp", lambda e, m=m, s=s, g=g, ao=ao: e.dma_start(
                        out=accd[m, :, s * T + g * 512:s * T + (g + 1) * 512], in_=ao[:, :]),
                        reads=[bao], writes=[b_accd], dma=True)
        tap("attn_acc", lambda: accd, [NKC, 128, 2 * T], F32, deps=list(b_accd.wl))

        p.barrier()
        main_es.close()
        ffn_es = ExitStack()
        fsb = lambda name, shp, dt: ffn_es.enter_context(nc.sbuf_tensor(name, shp, dt)).ap()
        xnf = fsb("xnf", [128, NKC * 2 * T], BF16)
        b_xnf = Buf()
        gsl = [fsb(f"gsl{i}", [128, NKC * 512], BF16) for i in range(2)]
        usl = [fsb(f"usl{i}", [128, NKC * 512], BF16) for i in range(2)]
        dsl = [fsb(f"dsl{i}", [128, 4 * 1024], BF16) for i in range(2)]
        b_gsl = [Buf(), Buf()]
        b_usl = [Buf(), Buf()]
        b_dsl = [Buf(), Buf()]
        gub_s = fsb("gub_s", [128, 32], F32)
        gub1 = fsb("gub1", [128, 32], F32)
        b_gub, b_gub1 = Buf(), Buf()
        gwb = fsb("gwb", [128, 4 * 1024], F32)
        b_gwb = Buf()
        act_sb = [fsb(f"act_sb{i}", [128, 4 * 1024], BF16) for i in range(2)]
        b_act = [Buf(), Buf()]
        glu_t = [fsb(f"glu_t{i}", [128, 1024], F32) for i in range(2)]
        upt = [fsb(f"upt{i}", [128, 1024], F32) for i in range(2)]
        b_glu = [Buf(), Buf()]
        b_upt = [Buf(), Buf()]
        dstage = [fsb(f"dstage{i}", [128, 1024], F32) for i in range(2)]
        b_dst = [Buf(), Buf()]

        p.op("sp", lambda e: e.dma_start(out=xnf[:, :], in_=xnscr),
             reads=[b_xnscr], writes=[b_xnf], dma=True)

        def xnfv(kc, c0, c1):
            return xnf[:, kc * 2 * T + c0:kc * 2 * T + c1]

        it_list = [(e_, iq) for e_ in range(experts) for iq in range(4)]
        for it_idx, (ex, iq) in enumerate(it_list):
            par = it_idx % 2
            bg, bu, bd = b_gsl[par], b_usl[par], b_dsl[par]
            gs, us, ds = gsl[par], usl[par], dsl[par]
            p.op("sp", lambda e, ex=ex, iq=iq, gs=gs: e.dma_start(
                out=gs.rearrange("p (kc m) -> p kc m", kc=NKC)[:, :, :],
                in_=gupb[ex].rearrange("(kc p) m -> p kc m", p=128)[:, :, iq * 512:(iq + 1) * 512]),
                writes=[bg], reads=[b_gupb], dma=True)
            p.op("sp", lambda e, ex=ex, iq=iq, us=us: e.dma_start(
                out=us.rearrange("p (kc m) -> p kc m", kc=NKC)[:, :, :],
                in_=gupb[ex].rearrange("(kc p) m -> p kc m", p=128)[:, :, I + iq * 512:I + (iq + 1) * 512]),
                writes=[bu], reads=[b_gupb], dma=True)
            p.op("sp", lambda e, ex=ex, iq=iq, ds=ds: e.dma_start(
                out=ds.rearrange("p (ic h) -> p ic h", ic=4)[:, :, :],
                in_=dwnb[ex, iq * 512:(iq + 1) * 512, :].rearrange("(ic p) h -> p ic h", p=128)[:, :, :]),
                writes=[bd], reads=[b_dwnb], dma=True)
            if iq == 0:
                p.op("pool", lambda e, ex=ex: e.dma_start(out=gub_s[:, :], in_=gub_d[ex]),
                     writes=[b_gub], dma=True)
                p.op("dve", lambda e: e.tensor_scalar(
                    out=gub1[:, :], in0=gub_s[:, :], scalar1=1.0, scalar2=None, op0=ALU.add),
                    reads=[b_gub], writes=[b_gub1])
                for gg in range(NKC):
                    ps, bps = PSMALL[gg % 2]
                    p.op("pe", lambda e, ex=ex, gg=gg, ps=ps: [
                        e.matmul(out=ps[:, :], lhsT=cs["oneh8"][:, ex * 128:(ex + 1) * 128],
                                 rhs=gthi[:, gg * 512:(gg + 1) * 512], start=True, stop=False),
                        e.matmul(out=ps[:, :], lhsT=cs["oneh8"][:, ex * 128:(ex + 1) * 128],
                                 rhs=gtlo[:, gg * 512:(gg + 1) * 512], start=False, stop=True)][-1],
                        reads=[b_gthi, b_gtlo, b_cs["oneh8"]], writes=[bps])
                    p.op("dve", lambda e, gg=gg, ps=ps: e.tensor_copy(
                        gwb[:, gg * 512:(gg + 1) * 512], ps[:, :]),
                        reads=[bps], writes=[b_gwb])
            for tgi in range(5):
                if tgi < 4:
                    tg = tgi
                    tc0 = tg * 1024
                    asb, bact = act_sb[tg % 2], b_act[tg % 2]
                    for pr_ in range(4):
                        ppar = pr_ % 2
                        glu, bglu = glu_t[ppar], b_glu[ppar]
                        up_t, bupt = upt[ppar], b_upt[ppar]
                        p.op("pe", lambda e, pr_=pr_, gs=gs, tc0=tc0: [
                            e.matmul(out=pA[:, gg * 512:(gg + 1) * 512],
                                     lhsT=gs[:, kc * 512 + pr_ * 128:kc * 512 + (pr_ + 1) * 128],
                                     rhs=xnfv(kc, tc0 + gg * 512, tc0 + (gg + 1) * 512),
                                     start=(kc == 0), stop=(kc == NKC - 1))
                            for kc in range(NKC) for gg in range(2)][-1],
                            reads=[bg, b_xnf], writes=[bufA0, bufA1])
                        p.op("dve", lambda e, pr_=pr_, iq=iq, glu=glu: e.tensor_scalar(
                            out=glu[:, :], in0=pA[:, :], scalar1=gub_s[:, iq * 4 + pr_:iq * 4 + pr_ + 1],
                            op0=ALU.add, scalar2=LIMIT, op1=ALU.min),
                            reads=[bufA0, bufA1, b_gub], writes=[bglu])
                        p.op("act", lambda e, glu=glu: e.activation(
                            out=glu[:, :], in_=glu[:, :], func=AF.Silu, scale=ALPHA),
                            writes=[bglu])
                        p.op("pe", lambda e, pr_=pr_, us=us, tc0=tc0: [
                            e.matmul(out=pB[:, gg * 512:(gg + 1) * 512],
                                     lhsT=us[:, kc * 512 + pr_ * 128:kc * 512 + (pr_ + 1) * 128],
                                     rhs=xnfv(kc, tc0 + gg * 512, tc0 + (gg + 1) * 512),
                                     start=(kc == 0), stop=(kc == NKC - 1))
                            for kc in range(NKC) for gg in range(2)][-1],
                            reads=[bu, b_xnf], writes=[bufB0, bufB1])
                        p.op("dve", lambda e, pr_=pr_, iq=iq, up_t=up_t: e.tensor_scalar(
                            out=up_t[:, :], in0=pB[:, :], scalar1=gub1[:, 16 + iq * 4 + pr_:16 + iq * 4 + pr_ + 1],
                            op0=ALU.add, scalar2=-(LIMIT - 1.0), op1=ALU.max),
                            reads=[bufB0, bufB1, b_gub1], writes=[bupt])
                        p.op("pool", lambda e, up_t=up_t: e.tensor_scalar(
                            out=up_t[:, :], in0=up_t[:, :], scalar1=LIMIT + 1.0, scalar2=-3.0e38,
                            op0=ALU.min, op1=ALU.max),
                            writes=[bupt])
                        p.op("pool" if pr_ % 2 == 0 else "dve",
                             lambda e, pr_=pr_, asb=asb, up_t=up_t, glu=glu: e.tensor_tensor(
                            out=asb[:, pr_ * 1024:(pr_ + 1) * 1024], in0=up_t[:, :], in1=glu[:, :],
                            op=ALU.mult),
                            reads=[bupt, bglu], writes=[bact])
                if tgi >= 1:
                    tg = tgi - 1
                    tc0 = tg * 1024
                    asb, bact = act_sb[tg % 2], b_act[tg % 2]
                    for hm in range(NKC):
                        if hm % 2 == 0:
                            outs = [(pY[:, 0:512], bufY0), (pY[:, 512:1024], bufY1)]
                        else:
                            outs = [(pS1[:, :], bufS1), (pS2[:, :], bufS2)]
                        p.op("pe", lambda e, hm=hm, ds=ds, asb=asb, outs=outs: [
                            e.matmul(out=outs[gg][0],
                                     lhsT=ds[:, ic * 1024 + hm * 128:ic * 1024 + (hm + 1) * 128],
                                     rhs=asb[:, ic * 1024 + gg * 512:ic * 1024 + (gg + 1) * 512],
                                     start=(ic == 0), stop=(ic == 3))
                            for ic in range(4) for gg in range(2)][-1],
                            reads=[bd, bact], writes=[outs[0][1], outs[1][1]])
                        dst, bdst2 = dstage[hm % 2], b_dst[hm % 2]
                        p.op("dve", lambda e, tc0=tc0, outs=outs, dst=dst: [
                            e.tensor_tensor(out=dst[:, gg * 512:(gg + 1) * 512], in0=outs[gg][0],
                                            in1=gwb[:, tc0 + gg * 512:tc0 + (gg + 1) * 512],
                                            op=ALU.mult)
                            for gg in range(2)][-1],
                            reads=[outs[0][1], outs[1][1], b_gwb], writes=[bdst2])
                        p.op("pool", lambda e, hm=hm, tc0=tc0, dst=dst: e.dma_start(
                            out=accd[hm, :, tc0:tc0 + 1024], in_=dst[:, :], accum_op=ALU.add),
                            reads=[bdst2], writes=[b_accd], dma=True)
        for hm in range(NKC):
            for g in range(8):
                ps, bps = PSMALL[g % 2]
                p.op("pe", lambda e, hm=hm, g=g, ps=ps: [
                    e.matmul(out=ps[:, :], lhsT=db_sb[:, hm * 128:(hm + 1) * 128],
                             rhs=gthi[:, g * 512:(g + 1) * 512], start=True, stop=False),
                    e.matmul(out=ps[:, :], lhsT=db_sb[:, hm * 128:(hm + 1) * 128],
                             rhs=gtlo[:, g * 512:(g + 1) * 512], start=False, stop=True)][-1],
                    reads=[b_db, b_gthi, b_gtlo], writes=[bps])
                dst, bdst2 = dstage[g % 2], b_dst[g % 2]
                p.op("dve", lambda e, ps=ps, dst=dst: e.tensor_copy(dst[:, 0:512], ps[:, :]),
                     reads=[bps], writes=[bdst2])
                p.op("pool", lambda e, hm=hm, g=g, dst=dst: e.dma_start(
                    out=accd[hm, :, g * 512:(g + 1) * 512], in_=dst[:, 0:512], accum_op=ALU.add),
                    reads=[bdst2], writes=[b_accd], dma=True)

        xr = [gsl[0].bitcast(F32)[:, 0:1024], gsl[0].bitcast(F32)[:, 1024:2048]]
        b_xr = [b_gsl[0], b_gsl[0]]
        astage = [usl[0].bitcast(F32)[:, 0:1024], usl[0].bitcast(F32)[:, 1024:2048]]
        b_ast = [b_usl[0], b_usl[0]]
        ostage = [dsl[0].bitcast(F32)[:, 0:1024], dsl[0].bitcast(F32)[:, 1024:2048]]
        b_ost = [b_dsl[0], b_dsl[0]]
        out_evs = []
        for s in range(2):
            for tt in range(NTT):
                par = tt % 2
                gc0 = s * T + tt * 128
                xrr, bxr = xr[par], b_xr[par]
                p.op("sp", lambda e, s=s, tt=tt, xrr=xrr: e.dma_start(
                    out=xrr[:, :], in_=x_d[s, tt * 128:(tt + 1) * 128, :]),
                    writes=[bxr], dma=True)
                ast, bast = astage[par], b_ast[par]
                p.op("sp", lambda e, gc0=gc0, ast=ast: e.dma_start(
                    out=ast.rearrange("p (kc t) -> p kc t", kc=NKC)[:, :, :],
                    in_=accd.rearrange("kc p t -> p kc t")[:, :, gc0:gc0 + 128]),
                    reads=[b_accd], writes=[bast], dma=True)
                for half in range(2):
                    ps, bps = PSMALL[half]
                    p.op("pe", lambda e, half=half, ast=ast, ps=ps: [
                        e.transpose(out=ps[:, q * 128:(q + 1) * 128],
                                    in_=ast[:, (half * 4 + q) * 128:(half * 4 + q + 1) * 128],
                                    identity=cs["ident_f32"][:, :])
                        for q in range(4)][-1],
                        reads=[bast, b_cs["ident_f32"]], writes=[bps])
                    ost, bost = ostage[par], b_ost[par]
                    p.op("dve", lambda e, half=half, xrr=xrr, ps=ps, ost=ost: e.tensor_tensor(
                        out=ost[:, half * 512:(half + 1) * 512], in0=ps[:, :],
                        in1=xrr[:, half * 512:(half + 1) * 512], op=ALU.add),
                        reads=[bps, bxr], writes=[bost])
                ost, bost = ostage[par], b_ost[par]
                ev = p.op("sp", lambda e, s=s, tt=tt, ost=ost: e.dma_start(
                    out=out_d[s, tt * 128:(tt + 1) * 128, :], in_=ost[:, :]),
                    reads=[bost], writes=[b_out], dma=True)
                out_evs.append(ev)
        p.op("sp", lambda e: e.nop(), deps=out_evs)

        with nc.Block() as block:
            counts = p.emit(nc, block, sems, dma_sems)
        ffn_es.close()
    return nc, counts, tap_out


_BUILD_CACHE = {}


def _get_nc():
    key = "full"
    if key not in _BUILD_CACHE:
        _BUILD_CACHE[key] = build_nc()
    return _BUILD_CACHE[key]


def kernel(L0_attn_x, L0_attn_Wq, L0_attn_Wk, L0_attn_Wv, L0_attn_Wp, L0_attn_Wproj,
           L0_attn_q_rms_w, L0_attn_k_rms_w, L0_ffn_router_weight, L0_ffn_router_bias,
           L0_ffn_gate_up_proj, L0_ffn_gate_up_bias, L0_ffn_down_proj, L0_ffn_down_bias):
    f32 = np.float32
    x = np.ascontiguousarray(np.asarray(L0_attn_x, dtype=f32))
    base = {
        "wqT": np.ascontiguousarray(np.asarray(L0_attn_Wq, f32).T),
        "wkT": np.ascontiguousarray(np.asarray(L0_attn_Wk, f32).T),
        "wvT": np.ascontiguousarray(np.asarray(L0_attn_Wv, f32).T),
        "wpT": np.ascontiguousarray(np.asarray(L0_attn_Wp, f32).T),
        "wprojT": np.ascontiguousarray(np.asarray(L0_attn_Wproj, f32).T),
        "qrw": np.ascontiguousarray(np.asarray(L0_attn_q_rms_w, f32).reshape(1, 64)),
        "krw": np.ascontiguousarray(np.asarray(L0_attn_k_rms_w, f32).reshape(1, 64)),
        "rw_l": np.ascontiguousarray(
            np.asarray(L0_ffn_router_weight, f32).T.reshape(8, 128, 8).transpose(1, 0, 2).reshape(128, 64)),
        "rb": np.ascontiguousarray(np.asarray(L0_ffn_router_bias, f32).reshape(1, 8)),
        "gup": np.ascontiguousarray(np.asarray(L0_ffn_gate_up_proj, f32)),
        "gub_l": np.ascontiguousarray(
            np.asarray(L0_ffn_gate_up_bias, f32).reshape(8, 32, 128).transpose(0, 2, 1)),
        "dwn": np.ascontiguousarray(np.asarray(L0_ffn_down_proj, f32)),
        "dbias": np.ascontiguousarray(np.asarray(L0_ffn_down_bias, f32)),
    }
    base.update(host_consts())
    nc, counts, _ = _get_nc()
    in_maps = []
    for c in range(8):
        m = dict(base)
        m["x"] = np.ascontiguousarray(x[2 * c:2 * c + 2])
        in_maps.append(m)
    res = run_bass_kernel_spmd(nc, in_maps, core_ids=list(range(8)))
    out = np.concatenate([np.asarray(r["out"]).reshape(2, T, C) for r in res.results], axis=0)
    return out.astype(np.float32)


# version 17 (final)
# speedup vs baseline: 1.1415x; 1.0063x over previous
import math
from contextlib import ExitStack

import numpy as np
import ml_dtypes
import concourse.bass as bass
import concourse.mybir as mybir
from concourse.bass_utils import run_bass_kernel_spmd

F32 = mybir.dt.float32
BF16 = mybir.dt.bfloat16
AF = mybir.ActivationFunctionType
ALU = mybir.AluOpType
AX = mybir.AxisListType.X

T = 2048
C = 1024
NH = 16
HD = 64
NKC = C // 128
NTT = T // 128
I = 2048
NE = 8
RMS_EPS = 1e-5
FRMS_EPS = 1.1920929e-07
LN_EPS = 1e-5
ALPHA = 1.702
LIMIT = 7.0
MSHIFT = 60.0
DBG = {}

ENGINES = ("pe", "act", "dve", "pool", "sp")
BLOCK_ATTR = {"pe": "tensor", "act": "scalar", "dve": "vector", "pool": "gpsimd", "sp": "sync"}


class Buf:
    __slots__ = ("w", "r", "mw", "wl")
    def __init__(self, mw=False):
        self.w = None
        self.r = []
        self.mw = mw
        self.wl = []


class Prog:
    DMA_POOL = 20

    def __init__(self):
        self.eng_ops = {e: [] for e in ENGINES}
        self.dma_count = {"sp": 0, "pool": 0}

    def op(self, eng, fn, reads=(), writes=(), deps=(), dma=False):
        dep_list = [d for d in deps if d is not None]
        for b in reads:
            if b.mw:
                dep_list.extend(b.wl)
            elif b.w is not None:
                dep_list.append(b.w)
        for b in writes:
            dep_list.extend(b.r)
            if b.mw:
                pass
            elif b.w is not None:
                dep_list.append(b.w)
        rec = {"eng": eng, "fn": fn, "deps": tuple(dep_list), "dma": dma,
               "needed": False, "val": None, "dma_idx": None}
        if dma:
            rec["dma_idx"] = self.dma_count[eng]
            self.dma_count[eng] += 1
        self.eng_ops[eng].append(rec)
        for b in reads:
            b.r.append(rec)
        for b in writes:
            if b.mw:
                b.wl.append(rec)
            else:
                b.w = rec
                b.r = []
        return rec

    def barrier(self):
        lasts = [self.eng_ops[e][-1] for e in ENGINES if self.eng_ops[e]]
        for q in ("sp", "pool"):
            slot_last = {}
            for r in self.eng_ops[q]:
                if r["dma"]:
                    slot_last[r["dma_idx"] % self.DMA_POOL] = r
            lasts.extend(slot_last.values())
        nops = []
        for e in ENGINES:
            nops.append(self.op(e, lambda eng: eng.nop(), deps=lasts))
        return nops

    def emit(self, nc, block, sems, dma_sems):
        for e in ENGINES:
            for r in self.eng_ops[e]:
                for d in r["deps"]:
                    d["needed"] = True
        for e in ENGINES:
            cum = 0
            for r in self.eng_ops[e]:
                if r["dma"]:
                    k = r["dma_idx"]
                    r["val"] = (k // self.DMA_POOL + 1) * 16
                elif r["needed"]:
                    cum += 1
                    r["val"] = cum
        counts = {e: sum(1 for r in self.eng_ops[e]) for e in ENGINES}

        def sem_of(r):
            if r["dma"]:
                return dma_sems[r["eng"]][r["dma_idx"] % self.DMA_POOL]
            return sems[r["eng"]]

        def body(eng_name):
            def go(eng):
                lastwait = {}
                sem = sems[eng_name]
                for r in self.eng_ops[eng_name]:
                    if r["dma"] and r["dma_idx"] >= self.DMA_POOL:
                        dsem = dma_sems[eng_name][r["dma_idx"] % self.DMA_POOL]
                        v = (r["dma_idx"] // self.DMA_POOL) * 16
                        key = id(dsem)
                        if lastwait.get(key, -1) < v:
                            eng.wait_ge(dsem, v)
                            lastwait[key] = v
                    for d in r["deps"]:
                        dsem = sem_of(d)
                        v = d["val"]
                        if v is None:
                            raise RuntimeError("dep without value")
                        key = id(dsem)
                        if lastwait.get(key, -1) < v:
                            eng.wait_ge(dsem, v)
                            lastwait[key] = v
                    instr = r["fn"](eng)
                    if r["dma"]:
                        instr.then_inc(sem_of(r), 16)
                    elif r["needed"]:
                        instr.then_inc(sem, 1)
            return go

        for e in ENGINES:
            getattr(block, BLOCK_ATTR[e])(body(e))
        return counts


def splits(lo, hi, align=512):
    out = []
    a = lo
    while a < hi:
        b = min(hi, (a // align + 1) * align)
        out.append((a, b))
        a = b
    return out


def host_consts():
    bf = ml_dtypes.bfloat16
    c = {}
    c["ident_bf"] = np.eye(128, dtype=bf)
    c["ident_f32"] = np.eye(128, dtype=np.float32)
    r = np.arange(128)
    c["slmat"] = (r[:, None] > r[None, :]).astype(bf)
    c["utmask"] = (r[:, None] <= r[None, :]).astype(bf)
    kc = np.zeros((64, T), dtype=np.float32)
    ind = (np.arange(T)[None, :] // 128 == np.arange(16)[:, None]).astype(np.float32)
    kc[0:16] = ind
    kc[32:48] = ind
    c["kconst"] = kc.astype(bf)
    ec = np.zeros((128, 256), dtype=np.float32)
    for J in range(16):
        ec[:, 16 * J + J] = 1.0
    c["ecols"] = ec.astype(bf)
    r16 = np.arange(16)
    c["l16"] = (r16[:, None] > r16[None, :]).astype(bf)
    c["neg1"] = np.full((1, 16), -1.0, dtype=bf)
    c["ones64"] = np.ones((64, 1), dtype=bf)
    op_ = np.zeros((128, 65), dtype=np.float32)
    op_[0:64, 0] = 1.0
    op_[64:128, 64] = 1.0
    c["onespair"] = op_.astype(bf)
    c["ones1x64"] = np.ones((1, 64), dtype=bf)
    c["onesrow"] = np.ones((1, 128), dtype=bf)
    oh = np.zeros((8, 8 * 128), dtype=np.float32)
    for e in range(8):
        oh[e, 128 * e:128 * (e + 1)] = 1.0 / ALPHA
    c["oneh8"] = oh.astype(bf)
    fc = np.zeros((128, 4), dtype=np.float32)
    fc[:, 0] = LN_EPS
    fc[:, 1] = FRMS_EPS
    fc[:, 2] = -math.log(8.0)
    fc[:, 3] = 1.0
    c["fconst"] = fc
    t = np.arange(T, dtype=np.float64)
    c["cosb64"] = np.tile(np.cos(t)[None, :], (64, 1)).astype(bf)
    sb64 = np.zeros((64, T), dtype=np.float64)
    sb64[0:32] = -np.sin(t)[None, :]
    sb64[32:64] = np.sin(t)[None, :]
    c["sinb64"] = sb64.astype(bf)
    return c


def build_nc(taps=(), heads=NH, experts=NE, seqs=(0, 1), attn_lvl=6):
    nc = bass.Bass("TRN2", target_bir_lowering=False)
    tap_set = set(taps)

    x_d = nc.dram_tensor("x", [2, T, C], F32, kind="ExternalInput").ap()
    wT = {}
    for nm in ("wqT", "wkT", "wvT", "wpT", "wprojT"):
        wT[nm] = nc.dram_tensor(nm, [C, C], F32, kind="ExternalInput").ap()
    qrw_d = nc.dram_tensor("qrw", [1, 64], F32, kind="ExternalInput").ap()
    krw_d = nc.dram_tensor("krw", [1, 64], F32, kind="ExternalInput").ap()
    rwl_d = nc.dram_tensor("rw_l", [128, 64], F32, kind="ExternalInput").ap()
    rb_d = nc.dram_tensor("rb", [1, 8], F32, kind="ExternalInput").ap()
    gup_d = nc.dram_tensor("gup", [NE, C, 2 * I], F32, kind="ExternalInput").ap()
    gub_d = nc.dram_tensor("gub_l", [NE, 128, 32], F32, kind="ExternalInput").ap()
    dwn_d = nc.dram_tensor("dwn", [NE, I, C], F32, kind="ExternalInput").ap()
    dbias_d = nc.dram_tensor("dbias", [NE, C], F32, kind="ExternalInput").ap()
    cst_d = {}
    cshapes = {"ident_bf": ([128, 128], BF16), "ident_f32": ([128, 128], F32),
               "slmat": ([128, 128], BF16), "utmask": ([128, 128], BF16),
               "kconst": ([64, T], BF16), "ecols": ([128, 256], BF16),
               "l16": ([16, 16], BF16), "neg1": ([1, 16], BF16),
               "ones64": ([64, 1], BF16), "ones1x64": ([1, 64], BF16),
               "onespair": ([128, 65], BF16),
               "onesrow": ([1, 128], BF16), "oneh8": ([8, 8 * 128], BF16),
               "cosb64": ([64, T], BF16), "sinb64": ([64, T], BF16),
               "fconst": ([128, 4], F32)}
    for nm, (shp, dt) in cshapes.items():
        cst_d[nm] = nc.dram_tensor(nm, shp, dt, kind="ExternalInput").ap()
    out_d = nc.dram_tensor("out", [2, T, C], F32, kind="ExternalOutput").ap()
    xnscr = nc.dram_tensor("xnscr", [128, NKC * 2 * T], BF16, kind="Internal").ap()
    wscr = {nm: nc.dram_tensor("wscr_" + nm, [C, C], BF16, kind="Internal").ap()
            for nm in ("wqT", "wkT", "wvT", "wpT", "wprojT")}
    gupb = nc.dram_tensor("gupb", [NE, C, 2 * I], BF16, kind="Internal").ap()
    dwnb = nc.dram_tensor("dwnb", [NE, I, C], BF16, kind="Internal").ap()
    accd = nc.dram_tensor("accd", [NKC, 128, 2 * T], F32, kind="Internal").ap()
    yscr = nc.dram_tensor("yscr", [2, NH, 64, T], BF16, kind="Internal").ap()

    tap_out = {}
    p = Prog()

    def tap(name, ap_fn, shape, dtype, deps):
        if name not in tap_set:
            return
        td = nc.dram_tensor("tap_" + name, shape, F32, kind="ExternalOutput").ap()
        tap_out[name] = td
        p.op("pool", lambda e, td=td, ap_fn=ap_fn: e.dma_start(out=td, in_=ap_fn()),
             deps=deps, dma=True)

    with ExitStack() as es:
        sems = {}
        for e in ENGINES:
            sems[e] = es.enter_context(nc.semaphore("sem_" + e))
        dma_sems = {q: [es.enter_context(nc.semaphore(f"dsem_{q}_{i}"))
                        for i in range(Prog.DMA_POOL)] for q in ("sp", "pool")}

        pA = es.enter_context(nc.psum_tensor("pA", [128, 1024], F32)).ap()
        pB = es.enter_context(nc.psum_tensor("pB", [128, 1024], F32)).ap()
        pY = es.enter_context(nc.psum_tensor("pY", [128, 1024], F32)).ap()
        pS1 = es.enter_context(nc.psum_tensor("pS1", [128, 512], F32)).ap()
        pS2 = es.enter_context(nc.psum_tensor("pS2", [128, 512], F32)).ap()
        bufA0, bufA1, bufB0, bufB1 = Buf(), Buf(), Buf(), Buf()
        bufY, bufS1, bufS2 = Buf(), Buf(), Buf()

        sb = lambda name, shp, dt: es.enter_context(nc.sbuf_tensor(name, shp, dt)).ap()
        cs = {nm: sb("sb_" + nm, shp, dt) for nm, (shp, dt) in cshapes.items() if nm != "kconst"}
        qrw_b = sb("qrw_b", [65, 64], BF16)
        krw_b = sb("krw_b", [65, 64], BF16)
        onesp64 = sb("onesp64", [65, 64], BF16)
        rw_sb = sb("rw_sb", [128, 128], BF16)
        rw_f = sb("rw_f", [128, 64], F32)
        rb_b = sb("rb_b", [1, 8], BF16)
        db_sb = sb("db_sb", [8, C], BF16)
        gthi = sb("gthi", [8, 2 * T], BF16)
        gtlo = sb("gtlo", [8, 2 * T], BF16)
        b_cs = {nm: Buf() for nm in cshapes}
        b_qrw, b_krw, b_rw, b_rb, b_db, b_rwf = Buf(), Buf(), Buf(), Buf(), Buf(), Buf()
        b_gthi, b_gtlo = Buf(mw=True), Buf(mw=True)
        b_xnscr, b_accd, b_yscr, b_out = Buf(mw=True), Buf(mw=True), Buf(mw=True), Buf(mw=True)

        b_wscr = {nm: Buf(mw=True) for nm in wscr}
        b_gupb = Buf(mw=True)
        b_dwnb = Buf(mw=True)
        for nm in wscr:
            p.op("pool", lambda e, nm=nm: e.dma_start(out=wscr[nm], in_=wT[nm]),
                 writes=[b_wscr[nm]], dma=True)
        for ex in range(NE):
            p.op("pool", lambda e, ex=ex: e.dma_start(out=gupb[ex], in_=gup_d[ex]),
                 writes=[b_gupb], dma=True)
            p.op("pool", lambda e, ex=ex: e.dma_start(out=dwnb[ex], in_=dwn_d[ex]),
                 writes=[b_dwnb], dma=True)
        for nm in cs:
            pass
        for nm in [n for n in cshapes if n != "kconst"]:
            p.op("sp", lambda e, nm=nm: e.dma_start(out=cs[nm][:, :], in_=cst_d[nm]),
                 writes=[b_cs[nm]], dma=True)
        p.op("pool", lambda e: e.dma_start(out=qrw_b[0:1, :], in_=qrw_d), writes=[b_qrw], dma=True)
        p.op("pool", lambda e: e.dma_start(out=qrw_b[64:65, :], in_=qrw_d), writes=[b_qrw], dma=True)
        p.op("pool", lambda e: e.dma_start(out=krw_b[0:1, :], in_=krw_d), writes=[b_krw], dma=True)
        p.op("pool", lambda e: e.dma_start(out=krw_b[64:65, :], in_=krw_d), writes=[b_krw], dma=True)
        p.op("sp", lambda e: e.dma_start(out=onesp64[0:1, :], in_=cst_d["ones1x64"]),
             writes=[b_cs["ones1x64"]], dma=True)
        p.op("sp", lambda e: e.dma_start(out=onesp64[64:65, :], in_=cst_d["ones1x64"]),
             writes=[b_cs["ones1x64"]], dma=True)
        p.op("pool", lambda e: e.dma_start(out=rw_f[:, :], in_=rwl_d), writes=[b_rwf], dma=True)
        p.op("dve", lambda e: e.tensor_copy(rw_sb[:, 0:64], rw_f[:, :]), reads=[b_rwf], writes=[b_rw])
        p.op("dve", lambda e: e.tensor_tensor(out=rw_sb[:, 64:128], in0=rw_f[:, :],
                                              in1=rw_sb[:, 0:64], op=ALU.subtract),
             reads=[b_rwf], writes=[b_rw])
        p.op("pool", lambda e: e.dma_start(out=rb_b[:, :], in_=rb_d), writes=[b_rb], dma=True)
        p.op("pool", lambda e: e.dma_start(out=db_sb[:, :], in_=dbias_d), writes=[b_db], dma=True)

        main_es = ExitStack()
        msb = lambda name, shp, dt: main_es.enter_context(nc.sbuf_tensor(name, shp, dt)).ap()
        lsb = msb

        xstage = [lsb(f"xstage{i}", [128, C], F32) for i in range(2)]
        b_xstage = [Buf(), Buf()]
        xntm = [lsb(f"xntm{i}", [128, C], BF16) for i in range(2)]
        b_xntm = [Buf(), Buf()]
        xnfst = [lsb(f"xnfst{i}", [128, C], BF16) for i in range(2)]
        b_xnfst = [Buf(), Buf()]
        dump = lsb("dump", [128, C], BF16)
        b_dump = Buf()
        nmean = [lsb(f"nmean{i}", [128, 1], F32) for i in range(2)]
        b_nmean = [Buf(), Buf()]
        vsum = [lsb(f"vsum{i}", [128, 1], F32) for i in range(2)]
        b_vsum = [Buf(), Buf()]
        rstd_ln = [lsb(f"rstdln{i}", [128, 1], F32) for i in range(2)]
        b_rstd_ln = [Buf(), Buf()]
        sm8 = lsb("sm8", [128, 8], F32)
        m8a = lsb("m8a", [128, 1], F32)
        m8b = lsb("m8b", [128, 8], F32)
        g8 = lsb("g8", [128, 8], F32)
        b_sm8, b_m8a, b_m8b, b_g8 = Buf(), Buf(), Buf(), Buf()

        for s in range(2):
            for tt in range(NTT):
                par = tt % 2
                gc0 = s * T + tt * 128
                xs, bx = xstage[par], b_xstage[par]
                p.op("sp", lambda e, s=s, tt=tt, xs=xs: e.dma_start(
                    out=xs[:, :], in_=x_d[s, tt * 128:(tt + 1) * 128, :]),
                    writes=[bx], dma=True)
                nm, bnm = nmean[par], b_nmean[par]
                p.op("dve", lambda e, xs=xs, nm=nm: e.reduce_sum(nm[:, :], xs[:, :], axis=AX),
                     reads=[bx], writes=[bnm])
                p.op("dve", lambda e, nm=nm: e.tensor_scalar(
                    out=nm[:, :], in0=nm[:, :], scalar1=-1.0 / C, scalar2=None, op0=ALU.mult),
                    writes=[bnm])
                vs, bvs = vsum[par], b_vsum[par]
                p.op("act", lambda e, xs=xs, nm=nm, vs=vs: e.activation(
                    out=dump[:, :], in_=xs[:, :], func=AF.Square, bias=nm[:, :], accum_out=vs[:, :]),
                    reads=[bx, bnm], writes=[b_dump, bvs])
                rl, brl = rstd_ln[par], b_rstd_ln[par]
                p.op("act", lambda e, vs=vs, rl=rl: e.activation(
                    out=rl[:, :], in_=vs[:, :], func=AF.Ln, scale=1.0 / C,
                    bias=cs["fconst"][:, 0:1]),
                    reads=[bvs, b_cs["fconst"]], writes=[brl])
                p.op("act", lambda e, rl=rl: e.activation(
                    out=rl[:, :], in_=rl[:, :], func=AF.Exp, scale=-0.5),
                    writes=[brl])
                xt, bxt = xntm[par], b_xntm[par]
                p.op("dve", lambda e, xs=xs, nm=nm, rl=rl, xt=xt: e.tensor_scalar(
                    out=xt[:, :], in0=xs[:, :], scalar1=nm[:, :], op0=ALU.add,
                    scalar2=rl[:, :], op1=ALU.mult),
                    reads=[bx, bnm, brl], writes=[bxt])
                if "xn_tm" in tap_set:
                    if "xn_tm" not in tap_out:
                        tap_out["xn_tm"] = nc.dram_tensor("tap_xn_tm", [2 * NTT, 128, C], F32,
                                                          kind="ExternalOutput").ap()
                    p.op("pool", lambda e, s=s, tt=tt, xt=xt: e.dma_start(
                        out=tap_out["xn_tm"][s * NTT + tt], in_=xt[:, :]),
                        reads=[bxt], dma=True)
                pT = pS2.bitcast(BF16)
                p.op("pe", lambda e, xt=xt, pT=pT: [
                    e.transpose(out=pT[:, kc * 128:(kc + 1) * 128],
                                in_=xt[:, kc * 128:(kc + 1) * 128],
                                identity=cs["ident_bf"][:, :])
                    for kc in range(NKC)][-1],
                    reads=[bxt, b_cs["ident_bf"]], writes=[bufS2])
                xf, bxf = xnfst[par], b_xnfst[par]
                p.op("dve", lambda e, pT=pT, xf=xf: e.tensor_copy(xf[:, :], pT[:, :]),
                     reads=[bufS2], writes=[bxf])
                p.op("sp", lambda e, xf=xf, gc0=gc0: e.dma_start(
                    out=xnscr.rearrange("p (kc t) -> p kc t", kc=NKC)[:, :, gc0:gc0 + 128],
                    in_=xf.rearrange("p (kc t) -> p kc t", kc=NKC)[:, :, :]),
                    reads=[bxf], writes=[b_xnscr], dma=True)
                p.op("pe", lambda e, xf=xf: [
                    e.matmul(out=pS1[:, 0:8], lhsT=xf[:, kc * 128:(kc + 1) * 128],
                             rhs=rw_sb[:, hl * 64 + kc * 8:hl * 64 + (kc + 1) * 8],
                             start=(kc == 0 and hl == 0), stop=False)
                    for hl in range(2) for kc in range(NKC)][-1],
                    reads=[bxf, b_rw], writes=[bufS1])
                p.op("pe", lambda e: e.matmul(out=pS1[:, 0:8], lhsT=cs["onesrow"][:, :],
                                              rhs=rb_b[:, :], start=False, stop=True),
                     reads=[b_cs["onesrow"], b_rb], writes=[bufS1])
                p.op("dve", lambda e: e.reduce_max(m8a[:, :], pS1[:, 0:8], axis=AX),
                     reads=[bufS1], writes=[b_m8a])
                p.op("dve", lambda e: e.tensor_scalar(
                    out=m8a[:, :], in0=m8a[:, :], scalar1=-1.0, scalar2=None, op0=ALU.mult),
                    writes=[b_m8a])
                p.op("dve", lambda e: e.tensor_copy(sm8[:, :], pS1[:, 0:8]),
                     reads=[bufS1], writes=[b_sm8])
                p.op("act", lambda e: e.activation(out=sm8[:, :], in_=sm8[:, :],
                                                   func=AF.Exp, bias=m8a[:, :]),
                     reads=[b_m8a], writes=[b_sm8])
                p.op("dve", lambda e: e.tensor_scalar(
                    out=dump[:, 0:8], in0=sm8[:, :], scalar1=1.0, scalar2=None,
                    op0=ALU.mult, op1=ALU.add, accum_out=m8a[:, :]),
                    reads=[b_sm8], writes=[b_dump, b_m8a])
                p.op("dve", lambda e: e.reciprocal(m8a[:, :], m8a[:, :]), writes=[b_m8a])
                p.op("dve", lambda e: e.tensor_scalar(
                    out=sm8[:, :], in0=sm8[:, :], scalar1=m8a[:, :], scalar2=None, op0=ALU.mult),
                    reads=[b_m8a], writes=[b_sm8])
                p.op("dve", lambda e: e.reduce_max(m8a[:, :], sm8[:, :], axis=AX),
                     reads=[b_sm8], writes=[b_m8a])
                p.op("dve", lambda e: e.tensor_scalar(
                    out=m8b[:, :], in0=sm8[:, :], scalar1=m8a[:, :], scalar2=None, op0=ALU.is_ge),
                    reads=[b_sm8, b_m8a], writes=[b_m8b])
                p.op("dve", lambda e: e.scalar_tensor_tensor(
                    out=m8b[:, :], in0=m8b[:, :], scalar=-4.0, in1=sm8[:, :],
                    op0=ALU.mult, op1=ALU.add),
                    reads=[b_sm8], writes=[b_m8b])
                p.op("dve", lambda e: e.reduce_max(m8a[:, :], m8b[:, :], axis=AX),
                     reads=[b_m8b], writes=[b_m8a])
                p.op("dve", lambda e: e.scalar_tensor_tensor(
                    out=g8[:, :], in0=sm8[:, :], scalar=m8a[:, :], in1=sm8[:, :],
                    op0=ALU.is_ge, op1=ALU.mult),
                    reads=[b_sm8, b_m8a], writes=[b_g8])
                p.op("pe", lambda e: e.transpose(out=pS2[0:8, 0:128], in_=g8[:, :],
                                                 identity=cs["ident_f32"][:, :]),
                     reads=[b_g8, b_cs["ident_f32"]], writes=[bufS2])
                p.op("dve", lambda e, gc0=gc0: e.tensor_copy(
                    gthi[:, gc0:gc0 + 128], pS2[0:8, 0:128]),
                    reads=[bufS2], writes=[b_gthi])
                p.op("dve", lambda e, gc0=gc0: e.tensor_tensor(
                    out=gtlo[:, gc0:gc0 + 128], in0=pS2[0:8, 0:128],
                    in1=gthi[:, gc0:gc0 + 128], op=ALU.subtract),
                    reads=[bufS2, b_gthi], writes=[b_gtlo])
        tap("xn_fm", lambda: xnscr, [128, NKC * 2 * T], BF16, deps=list(b_xnscr.wl))
        tap("g8", lambda: gthi, [8, 2 * T], BF16, deps=list(b_gthi.wl))
        p.barrier()

        xng = [msb(f"xng{i}", [128, NKC * 512], BF16) for i in range(2)]
        b_xng = [Buf(), Buf()]
        wstream = msb("wstream", [128, 8192], BF16)
        b_wh = [Buf(), Buf()]
        b_wproj = Buf()
        biasT = msb("biasT", [128, 17408], BF16)
        b_biasT = [Buf() for _ in range(16)]
        offJ = [2048 * J - 64 * J * (J - 1) for J in range(16)]
        ls1_ = [msb(f"ls1_{i}", [128, 1024], F32) for i in range(2)]
        b_ls1_ = [Buf(), Buf()]
        ls1, b_ls1 = ls1_[0], b_ls1_[0]
        bufY0, bufY1 = Buf(), Buf()
        q_ext = [msb(f"q_ext{i}", [128, T], BF16) for i in range(2)]
        k_ext = [msb(f"k_ext{i}", [128, T], BF16) for i in range(2)]
        protc = [msb(f"protc{i}", [128, T], BF16) for i in range(2)]
        p64x2 = [msb(f"p64x2{i}", [128, T], BF16) for i in range(2)]
        v_ext = [msb(f"v_ext{i}", [128, 16 * 68], BF16) for i in range(2)]
        b_qe = [Buf(), Buf()]
        b_ke = [Buf(), Buf()]
        b_pr = [Buf(), Buf()]
        b_p6 = [Buf(), Buf()]
        b_ve = [Buf(), Buf()]
        Tf = [msb(f"Tf{i}", [128, 512], F32) for i in range(3)]
        b_Tf = [Buf(), Buf(), Buf()]
        sqb = [msb(f"sqb{i}", [128, 512], BF16) for i in range(3)]
        b_sq = [Buf(), Buf(), Buf()]
        v_fm = msb("v_fm", [128, T], BF16)
        b_vfm = Buf()
        ET = [msb(f"ET{i}", [128, 1024], BF16) for i in range(2)]
        b_ET = [Buf(), Buf()]
        csth = msb("csth", [16, 512], BF16)
        cstl = msb("cstl", [16, 512], BF16)
        b_csth, b_cstl = Buf(), Buf()
        rows65 = [msb(f"rows65_{i}", [65, 512], F32) for i in range(2)]
        b_rows65 = [Buf(), Buf()]
        rstd65 = [msb(f"rstd65_{i}", [65, 512], BF16) for i in range(3)]
        b_rstd65 = [Buf(), Buf(), Buf()]
        rowkq = [msb(f"rowkq{i}", [1, 512], F32) for i in range(2)]
        b_rowkq = [Buf(), Buf()]
        kmx = msb("kmx", [1, 1], F32)
        kmg = msb("kmg", [1, 1], F32)
        b_kmx, b_kmg = Buf(), Buf()
        m_row = msb("m_row", [1, T], BF16)
        b_mrow = Buf()
        rzrow = msb("rzrow", [1, 1024], F32)
        rzhi = msb("rzhi", [1, 1024], BF16)
        rzlo = msb("rzlo", [1, 1024], BF16)
        b_rz, b_rzhi, b_rzlo = Buf(), Buf(), Buf()
        rzb_sb = msb("rzb_sb", [64, 1024], F32)
        b_rzb = Buf()
        ystage = [rows65[i].bitcast(BF16)[0:64, 0:1024] for i in range(2)]
        b_yst = b_rows65
        ao_st = [p64x2[0].bitcast(F32)[:, 0:512]]
        b_ao = [b_p6[0]]

        for i in range(2):
            p.op("dve", lambda e, i=i: e.memset(q_ext[i][:, :], 0.0), writes=[b_qe[i]])
            p.op("sp", lambda e, i=i: e.dma_start(out=k_ext[i][0:64, :], in_=cst_d["kconst"]),
                 writes=[b_ke[i]], dma=True)
            p.op("dve", lambda e, i=i: e.memset(v_ext[i][:, :], 0.0), writes=[b_ve[i]])
            p.op("dve", lambda e, i=i: e.memset(
                v_ext[i].rearrange("p (b c) -> p b c", c=68)[:, :, 64:65], 1.0),
                writes=[b_ve[i]])

        PSMALL = [(pS1, bufS1), (pS2, bufS2)]

        for s in seqs:
            sc0 = s * T
            def emit_wload(hp_):
                wof_ = (hp_ % 2) * 4096
                bw_ = b_wh[hp_ % 2]
                for ti, nm in enumerate(("wkT", "wqT", "wpT", "wvT")):
                    p.op("sp", lambda e, nm=nm, ti=ti, hp_=hp_, wof_=wof_: e.dma_start(
                        out=wstream[:, wof_ + ti * 1024: wof_ + (ti + 1) * 1024].rearrange("p (kc m) -> p kc m", kc=NKC)[:, :, :],
                        in_=wscr[nm].rearrange("(kc p) m -> p kc m", p=128)[:, :, hp_ * 128:(hp_ + 1) * 128]),
                        writes=[bw_], reads=[b_wscr[nm]],
                        deps=([b_wproj.w] if b_wproj.w is not None else []) + list(b_wproj.r), dma=True)

            def emit_xload(g_):
                xn_g_, bxn_ = xng[g_ % 2], b_xng[g_ % 2]
                c0_ = g_ * 512
                p.op("sp", lambda e, sc0=sc0, c0_=c0_, xn_g_=xn_g_: e.dma_start(
                    out=xn_g_.rearrange("p (kc c) -> p kc c", kc=NKC)[:, :, :],
                    in_=xnscr.rearrange("p (kc t) -> p kc t", kc=NKC)[:, :, sc0 + c0_:sc0 + c0_ + 512]),
                    reads=[b_xnscr], writes=[bxn_], dma=True)

            for hp in range(heads // 2):
                wpar = hp % 2
                wof = wpar * 4096
                bw = b_wh[wpar]
                if hp == 0:
                    emit_wload(0)

                for g in range(4):
                    c0 = g * 512
                    xpar = g % 2
                    xn_g, bxn = xng[xpar], b_xng[xpar]
                    if not (hp > 0 and g == 0):
                        emit_xload(g)
                    pbank = {"k": (pA[:, 0:512], bufA0), "q": (pA[:, 512:1024], bufA1),
                             "p": (pB[:, 0:512], bufB0), "v": (pB[:, 512:1024], bufB1)}
                    sbank = {"k": (pS1, bufS1), "q": (pS2, bufS2), "p": (pS1, bufS1)}
                    tfm = {"k": 0, "q": 1, "p": 2}
                    if attn_lvl < 0.2:
                        continue
                    for ti, tn in enumerate(("k", "q", "p", "v")):
                        pp_, bpp = pbank[tn]
                        wslice = lambda kc, ti=ti, wof=wof: wstream[:, wof + ti * 1024 + kc * 128: wof + ti * 1024 + (kc + 1) * 128]
                        p.op("pe", lambda e, pp_=pp_, wslice=wslice, xn_g=xn_g: [
                            e.matmul(out=pp_, lhsT=wslice(kc), rhs=xn_g[:, kc * 512:(kc + 1) * 512],
                                     start=(kc == 0), stop=(kc == NKC - 1))
                            for kc in range(NKC)][-1],
                            reads=[bw, bxn], writes=[bpp])
                        if tn == "v":
                            p.op("dve", lambda e, pp_=pp_, c0=c0: e.tensor_copy(
                                v_fm[:, c0:c0 + 512], pp_),
                                reads=[bpp], writes=[b_vfm])
                        else:
                            tf, btf = Tf[tfm[tn]], b_Tf[tfm[tn]]
                            p.op("dve", lambda e, pp_=pp_, tf=tf: e.tensor_copy(
                                tf[:, :], pp_),
                                reads=[bpp], writes=[btf])
                            sq_h, bsq_h = sqb[tfm[tn]], b_sq[tfm[tn]]
                            p.op("act", lambda e, tf=tf, sq_h=sq_h: e.activation(
                                out=sq_h[:, :], in_=tf[:, :], func=AF.Square),
                                reads=[btf], writes=[bsq_h])
                    if attn_lvl < 0.25:
                        continue
                    for tn in ("k", "q", "p"):
                        ps_, bps_ = sbank[tn]
                        sq_h, bsq_h = sqb[tfm[tn]], b_sq[tfm[tn]]
                        p.op("pe", lambda e, ps_=ps_, sq_h=sq_h: e.matmul(
                            out=ps_[0:65, :], lhsT=cs["onespair"][:, :], rhs=sq_h[:, :],
                            start=True, stop=True),
                            reads=[bsq_h, b_cs["onespair"]], writes=[bps_])
                        r65, br65 = rows65[tfm[tn] % 2], b_rows65[tfm[tn] % 2]
                        rs65, brs65 = rstd65[tfm[tn]], b_rstd65[tfm[tn]]
                        epsc = 0 if tn in ("k", "q") else 1
                        p.op("dve", lambda e, ps_=ps_, r65=r65: e.tensor_copy(
                            r65[:, :], ps_[0:65, :]),
                            reads=[bps_], writes=[br65])
                        p.op("act", lambda e, epsc=epsc, r65=r65: e.activation(
                            out=r65[:, :], in_=r65[:, :],
                            func=AF.Ln, scale=1.0 / 64, bias=cs["fconst"][0:65, epsc:epsc + 1]),
                            reads=[b_cs["fconst"]], writes=[br65])
                        if tn == "q":
                            p.op("act", lambda e, r65=r65, rs65=rs65: e.activation(
                                out=rs65[:, :], in_=r65[:, :],
                                func=AF.Exp, scale=-0.5, bias=cs["fconst"][0:65, 2:3]),
                                reads=[br65, b_cs["fconst"]], writes=[brs65])
                        else:
                            p.op("act", lambda e, r65=r65, rs65=rs65: e.activation(
                                out=rs65[:, :], in_=r65[:, :], func=AF.Exp, scale=-0.5),
                                reads=[br65], writes=[brs65])
                    for tn in ("k", "q", "p"):
                        ps_, bps_ = sbank[tn]
                        tf, btf = Tf[tfm[tn]], b_Tf[tfm[tn]]
                        rs65, brs65 = rstd65[tfm[tn]], b_rstd65[tfm[tn]]
                        for hh in range(2):
                            p.op("pe", lambda e, hh=hh, ps_=ps_, rs65=rs65,
                                 lrow={"k": krw_b, "q": qrw_b, "p": onesp64}[tn]: e.matmul(
                                out=ps_[64 * hh:64 * hh + 64, :],
                                lhsT=lrow[64 * hh:64 * hh + 1, :],
                                rhs=rs65[64 * hh:64 * hh + 1, :],
                                start=True, stop=True),
                                reads=[brs65,
                                       {"k": b_krw, "q": b_qrw, "p": b_cs["ones1x64"]}[tn]],
                                writes=[bps_])
                            if tn in ("k", "q"):
                                dst, bdst = (k_ext[hh], b_ke[hh]) if tn == "k" else (q_ext[hh], b_qe[hh])
                                p.op("dve", lambda e, tf=tf, hh=hh, c0=c0, dst=dst, ps_=ps_: e.tensor_tensor(
                                    out=dst[64:128, c0:c0 + 512], in0=tf[64 * hh:64 * hh + 64, :],
                                    in1=ps_[64 * hh:64 * hh + 64, :], op=ALU.mult),
                                    reads=[btf, bps_], writes=[bdst])
                            else:
                                p.op("dve", lambda e, tf=tf, hh=hh, c0=c0, ps_=ps_: e.tensor_tensor(
                                    out=ls1[0:64, 0:512], in0=tf[64 * hh:64 * hh + 64, :],
                                    in1=ps_[64 * hh:64 * hh + 64, :], op=ALU.mult),
                                    reads=[btf, bps_], writes=[b_ls1])
                                pr, bpr = protc[hh], b_pr[hh]
                                p.op("dve", lambda e, c0=c0, pr=pr: e.tensor_tensor(
                                    out=pr[0:64, c0:c0 + 512], in0=ls1[0:64, 0:512],
                                    in1=cs["cosb64"][:, c0:c0 + 512], op=ALU.mult),
                                    reads=[b_ls1, b_cs["cosb64"]], writes=[bpr])
                                p.op("pool", lambda e, c0=c0, pr=pr: e.tensor_tensor(
                                    out=pr[64:96, c0:c0 + 512], in0=ls1[32:64, 0:512],
                                    in1=cs["sinb64"][32:64, c0:c0 + 512], op=ALU.mult),
                                    reads=[b_ls1, b_cs["sinb64"]], writes=[bpr])
                                p.op("pool", lambda e, c0=c0, pr=pr: e.tensor_tensor(
                                    out=pr[96:128, c0:c0 + 512], in0=ls1[0:32, 0:512],
                                    in1=cs["sinb64"][0:32, c0:c0 + 512], op=ALU.mult),
                                    reads=[b_ls1, b_cs["sinb64"]], writes=[bpr])
                                p6, bp6 = p64x2[hh], b_p6[hh]
                                p.op("dve", lambda e, tf=tf, hh=hh, c0=c0, p6=p6: e.tensor_scalar(
                                    out=p6[0:64, c0:c0 + 512], in0=tf[64 * hh:64 * hh + 64, :],
                                    scalar1=1.0 / 64, scalar2=None, op0=ALU.mult),
                                    reads=[btf], writes=[bp6])
                                p.op("pool", lambda e, tf=tf, hh=hh, c0=c0, p6=p6: e.tensor_scalar(
                                    out=p6[64:128, c0:c0 + 512], in0=tf[64 * hh:64 * hh + 64, :],
                                    scalar1=1.0 / 64, scalar2=None, op0=ALU.mult),
                                    reads=[btf], writes=[bp6])
                for jt in range(16 if attn_lvl >= 1 else 0):
                    pT = pS2.bitcast(BF16)
                    p.op("pe", lambda e, jt=jt, pT=pT: e.transpose(
                        out=pT[:, 0:128], in_=v_fm[:, jt * 128:(jt + 1) * 128],
                        identity=cs["ident_bf"][:, :]),
                        reads=[b_vfm, b_cs["ident_bf"]], writes=[bufS2])
                    for hh in range(2):
                        p.op("dve", lambda e, jt=jt, hh=hh, pT=pT: e.tensor_copy(
                            v_ext[hh][:, jt * 68:jt * 68 + 64], pT[:, 64 * hh:64 * hh + 64]),
                            reads=[bufS2], writes=[b_ve[hh]])

                if hp + 1 < heads // 2:
                    emit_wload(hp + 1)
                    emit_xload(0)

                for hh in range(2):
                    h = hp * 2 + hh
                    slope = 2.0 ** (-(h + 1) / 2.0)
                    qe, ke, pr, p6, ve = q_ext[hh], k_ext[hh], protc[hh], p64x2[hh], v_ext[hh]
                    bqe, bke, bpr, bp6, bve = b_qe[hh], b_ke[hh], b_pr[hh], b_p6[hh], b_ve[hh]

                    for g in range(4):
                        c0 = g * 512
                        sq_h, bsq_h = sqb[g % 2], b_sq[g % 2]
                        rk, brk = rowkq[g % 2], b_rowkq[g % 2]
                        bY_h = bufY0 if ((c0 % 1024) == 0) else bufY1
                        p.op("act", lambda e, ke=ke, c0=c0, sq_h=sq_h: e.activation(
                            out=sq_h[0:64, :], in_=ke[64:128, c0:c0 + 512], func=AF.Square),
                            reads=[bke], writes=[bsq_h])
                        p.op("pe", lambda e, c0=c0, sq_h=sq_h: e.matmul(
                            out=pY[0:1, (c0 % 1024):(c0 % 1024) + 512],
                            lhsT=cs["ones64"][:, :], rhs=sq_h[0:64, :],
                            start=True, stop=True),
                            reads=[bsq_h, b_cs["ones64"]], writes=[bY_h])
                        p.op("dve", lambda e, c0=c0, rk=rk: e.tensor_copy(
                            rk[:, :], pY[0:1, (c0 % 1024):(c0 % 1024) + 512]),
                            reads=[bY_h], writes=[brk])
                        if g == 0:
                            p.op("dve", lambda e, rk=rk: e.reduce_max(kmx[:, :], rk[:, :], axis=AX),
                                 reads=[brk], writes=[b_kmx])
                        else:
                            p.op("dve", lambda e, rk=rk: e.reduce_max(kmg[:, :], rk[:, :], axis=AX),
                                 reads=[brk], writes=[b_kmg])
                            p.op("dve", lambda e: e.tensor_tensor(
                                out=kmx[:, :], in0=kmg[:, :], in1=kmx[:, :], op=ALU.max),
                                reads=[b_kmg], writes=[b_kmx])
                    p.op("act", lambda e: e.activation(out=kmx[:, :], in_=kmx[:, :], func=AF.Ln),
                         writes=[b_kmx])
                    p.op("dve", lambda e: e.tensor_scalar(
                        out=kmx[:, :], in0=kmx[:, :], scalar1=0.5, scalar2=None, op0=ALU.mult),
                        writes=[b_kmx])
                    for g in range(4):
                        c0 = g * 512
                        sq_h, bsq_h = sqb[g % 2], b_sq[g % 2]
                        rk, brk = rowkq[g % 2], b_rowkq[g % 2]
                        bY_h = bufY0 if ((c0 % 1024) == 0) else bufY1
                        p.op("act", lambda e, qe=qe, c0=c0, sq_h=sq_h: e.activation(
                            out=sq_h[0:64, :], in_=qe[64:128, c0:c0 + 512], func=AF.Square),
                            reads=[bqe], writes=[bsq_h])
                        p.op("pe", lambda e, c0=c0, sq_h=sq_h: e.matmul(
                            out=pY[0:1, (c0 % 1024):(c0 % 1024) + 512],
                            lhsT=cs["ones64"][:, :], rhs=sq_h[0:64, :],
                            start=True, stop=True),
                            reads=[bsq_h, b_cs["ones64"]], writes=[bY_h])
                        p.op("dve", lambda e, c0=c0, rk=rk: e.tensor_copy(
                            rk[:, :], pY[0:1, (c0 % 1024):(c0 % 1024) + 512]),
                            reads=[bY_h], writes=[brk])
                        p.op("act", lambda e, rk=rk: e.activation(
                            out=rk[:, :], in_=rk[:, :], func=AF.Ln),
                            writes=[brk])
                        p.op("act", lambda e, rk=rk, c0=c0: e.activation(
                            out=m_row[:, c0:c0 + 512], in_=rk[:, :],
                            func=AF.Exp, scale=0.5, bias=kmx[:, :]),
                            reads=[brk, b_kmx], writes=[b_mrow])
                        p.op("dve", lambda e, c0=c0: e.tensor_scalar(
                            out=m_row[:, c0:c0 + 512], in0=m_row[:, c0:c0 + 512],
                            scalar1=-MSHIFT, scalar2=None, op0=ALU.add),
                            writes=[b_mrow])
                    pc = 0
                    for J in range(16 if attn_lvl >= 2 else 0):
                        w_J = T - 128 * J
                        for (b0, b1) in splits(0, w_J, 1024):
                            lsx, blsx = ls1_[pc % 2], b_ls1_[pc % 2]
                            pc += 1
                            for (a0, a1) in splits(b0, b1):
                                ps, bps = PSMALL[(a0 // 512) % 2]
                                p.op("pe", lambda e, J=J, a0=a0, a1=a1, ps=ps, pr=pr, p6=p6: e.matmul(
                                    out=ps[:, 0:a1 - a0], lhsT=pr[:, J * 128:(J + 1) * 128],
                                    rhs=p6[:, J * 128 + a0:J * 128 + a1], start=True, stop=True),
                                    reads=[bpr, bp6], writes=[bps])
                                p.op("act", lambda e, a0=a0, a1=a1, b0=b0, ps=ps, lsx=lsx: e.activation(
                                    out=lsx[:, a0 - b0:a1 - b0], in_=ps[:, 0:a1 - a0],
                                    func=AF.Exp, scale=-1.0),
                                    reads=[bps], writes=[blsx])
                            p.op("act", lambda e, b0=b0, b1=b1, lsx=lsx: e.activation(
                                out=lsx[:, 0:b1 - b0], in_=lsx[:, 0:b1 - b0], func=AF.Ln,
                                bias=cs["fconst"][:, 3:4]),
                                reads=[b_cs["fconst"]], writes=[blsx])
                            p.op("pool", lambda e, J=J, b0=b0, b1=b1, slope=slope, lsx=lsx: e.tensor_scalar(
                                out=biasT[:, offJ[J] + b0:offJ[J] + b1], in0=lsx[:, 0:b1 - b0],
                                scalar1=-slope, scalar2=0.0, op0=ALU.mult, op1=ALU.add),
                                reads=[blsx], writes=[b_biasT[J]])
                        p.op("dve", lambda e, J=J: e.tensor_tensor(
                            out=biasT[:, offJ[J]:offJ[J] + 128], in0=biasT[:, offJ[J]:offJ[J] + 128],
                            in1=cs["utmask"][:, :], op=ALU.mult),
                            reads=[b_cs["utmask"]], writes=[b_biasT[J]])

                    for gq in range(4 if attn_lvl >= 3 else 0):
                        i0g = gq * 512
                        Jmax = 4 * gq + 3
                        p.op("pe", lambda e, i0g=i0g, Jmax=Jmax: [
                            e.matmul(out=pS1[0:16, max(0, 128 * J - i0g):512],
                                     lhsT=cs["ecols"][:, 16 * J:16 * J + 16],
                                     rhs=biasT[:, offJ[J] + max(0, i0g - 128 * J):offJ[J] + i0g + 512 - 128 * J],
                                     start=(J == 0), stop=(J == Jmax))
                            for J in range(Jmax + 1)][-1],
                            reads=[b_cs["ecols"]] + [b_biasT[J] for J in range(Jmax + 1)],
                            writes=[bufS1])
                        p.op("dve", lambda e: e.tensor_copy(csth[:, :], pS1[0:16, :]),
                             reads=[bufS1], writes=[b_csth])
                        p.op("dve", lambda e: e.tensor_tensor(
                            out=cstl[:, :], in0=pS1[0:16, :], in1=csth[:, :], op=ALU.subtract),
                            reads=[bufS1, b_csth], writes=[b_cstl])
                        p.op("pe", lambda e, i0g=i0g: [
                            e.matmul(out=pS2[0:16, :], lhsT=cs["l16"][:, :], rhs=csth[:, :],
                                     start=True, stop=False),
                            e.matmul(out=pS2[0:16, :], lhsT=cs["l16"][:, :], rhs=cstl[:, :],
                                     start=False, stop=False),
                            e.matmul(out=pS2[0:16, :], lhsT=cs["neg1"][:, :],
                                     rhs=m_row[:, i0g:i0g + 512], start=False, stop=True)][-1],
                            reads=[b_csth, b_cstl, b_mrow, b_cs["l16"], b_cs["neg1"]],
                            writes=[bufS2])
                        p.op("dve", lambda e, i0g=i0g, qe=qe: e.tensor_copy(
                            qe[0:16, i0g:i0g + 512], pS2[0:16, :]),
                            reads=[bufS2], writes=[bqe])
                        p.op("dve", lambda e, i0g=i0g, qe=qe: e.tensor_tensor(
                            out=qe[32:48, i0g:i0g + 512], in0=pS2[0:16, :],
                            in1=qe[0:16, i0g:i0g + 512], op=ALU.subtract),
                            reads=[bufS2], writes=[bqe])

                    SCF = [pr.bitcast(F32), p6.bitcast(F32)]
                    b_SCF = [bpr, bp6]
                    for H in range(2 if attn_lvl >= 4 else 0):
                        ib = H * 1024
                        Jlist = [J for J in range(16) if 128 * J < ib + 1024]
                        pend_mm3 = []
                        for J in Jlist:
                            i0 = max(128 * J, ib)
                            i1 = ib + 1024
                            o0 = i0 - ib
                            pX = pA if J % 2 == 0 else pB
                            bpXb = [bufA0, bufA1] if J % 2 == 0 else [bufB0, bufB1]
                            et, bet = ET[J % 2], b_ET[J % 2]
                            scf, bscf = SCF[J % 2], b_SCF[J % 2]
                            chain = []
                            for (a0, a1) in splits(o0, 1024):
                                p.op("pe", lambda e, J=J, a0=a0, a1=a1, ib=ib, pX=pX, ke=ke, qe=qe: e.matmul(
                                    out=pX[:, a0:a1], lhsT=ke[:, J * 128:(J + 1) * 128],
                                    rhs=qe[:, ib + a0:ib + a1], start=True, stop=False),
                                    reads=[bke, bqe], writes=[bpXb[a0 // 512]])
                                p.op("pe", lambda e, J=J, a0=a0, a1=a1, ib=ib, pX=pX: e.matmul(
                                    out=pX[:, a0:a1], lhsT=cs["slmat"][:, :],
                                    rhs=biasT[:, offJ[J] + ib + a0 - 128 * J:offJ[J] + ib + a1 - 128 * J],
                                    start=False, stop=True),
                                    reads=[b_biasT[J], b_cs["slmat"]], writes=[bpXb[a0 // 512]])
                                p.op("dve", lambda e, a0=a0, a1=a1, o0=o0, pX=pX, scf=scf: e.tensor_copy(
                                    scf[:, a0 - o0:a1 - o0], pX[:, a0:a1]),
                                    reads=[bpXb[a0 // 512]], writes=[bscf])
                                p.op("act", lambda e, a0=a0, a1=a1, o0=o0, et=et, scf=scf: e.activation(
                                    out=et[:, a0 - o0:a1 - o0], in_=scf[:, a0 - o0:a1 - o0],
                                    func=AF.Exp),
                                    reads=[bscf], writes=[bet])
                                if 128 * J >= ib and a0 == o0:
                                    p.op("dve", lambda e, et=et: e.tensor_tensor(
                                        out=et[:, 0:128], in0=et[:, 0:128],
                                        in1=cs["utmask"][:, :], op=ALU.mult),
                                        reads=[b_cs["utmask"]], writes=[bet])
                                lastJ_for_bank = max(Jx for Jx in Jlist if 128 * Jx < ib + (a0 // 512 + 1) * 512)
                                chain.append((J, a0, a1, i0, et, J == 0, J == lastJ_for_bank))
                            pend_mm3.append((chain, bet, bve))
                            if len(pend_mm3) >= 2:
                                ch, betx, bvex = pend_mm3.pop(0)
                                for (Jx, a0, a1, i0x, etx, st, sp_) in ch:
                                    p.op("pe", lambda e, Jx=Jx, a0=a0, a1=a1, i0x=i0x, etx=etx, ve=ve,
                                         st=st, sp_=sp_, ib=ib: e.matmul(
                                        out=pY[0:65, a0:a1], lhsT=ve[:, Jx * 68:Jx * 68 + 65],
                                        rhs=etx[:, ib + a0 - i0x:ib + a1 - i0x], start=st, stop=sp_),
                                        reads=[betx, bvex], writes=[bufY0 if a0 < 512 else bufY1])
                        for (ch, betx, bvex) in pend_mm3:
                            for (Jx, a0, a1, i0x, etx, st, sp_) in ch:
                                p.op("pe", lambda e, Jx=Jx, a0=a0, a1=a1, i0x=i0x, etx=etx, ve=ve,
                                     st=st, sp_=sp_, ib=ib: e.matmul(
                                    out=pY[0:65, a0:a1], lhsT=ve[:, Jx * 68:Jx * 68 + 65],
                                    rhs=etx[:, ib + a0 - i0x:ib + a1 - i0x], start=st, stop=sp_),
                                    reads=[betx, bvex], writes=[bufY0 if a0 < 512 else bufY1])
                        p.op("dve", lambda e: e.reciprocal(rzrow[:, :], pY[64:65, :]),
                             reads=[bufY0, bufY1], writes=[b_rz])
                        p.op("dve", lambda e: e.tensor_copy(rzhi[:, :], rzrow[:, :]),
                             reads=[b_rz], writes=[b_rzhi])
                        p.op("dve", lambda e: e.tensor_tensor(
                            out=rzlo[:, :], in0=rzrow[:, :], in1=rzhi[:, :], op=ALU.subtract),
                            reads=[b_rz, b_rzhi], writes=[b_rzlo])
                        for (a0, a1) in ((0, 512), (512, 1024)):
                            ps, bps = PSMALL[(a0 // 512) % 2]
                            p.op("pe", lambda e, a0=a0, a1=a1, ps=ps: [
                                e.matmul(out=ps[0:64, :], lhsT=cs["ones1x64"][:, :],
                                         rhs=rzhi[:, a0:a1], start=True, stop=False),
                                e.matmul(out=ps[0:64, :], lhsT=cs["ones1x64"][:, :],
                                         rhs=rzlo[:, a0:a1], start=False, stop=True)][-1],
                                reads=[b_rzhi, b_rzlo, b_cs["ones1x64"]], writes=[bps])
                            p.op("dve", lambda e, a0=a0, a1=a1, ps=ps: e.tensor_copy(
                                rzb_sb[:, a0:a1], ps[0:64, :]),
                                reads=[bps], writes=[b_rzb])
                        if attn_lvl < 5:
                            continue
                        yst, byst = ystage[H], b_yst[H]
                        p.op("dve", lambda e, yst=yst: e.tensor_tensor(
                            out=yst[:, :], in0=pY[0:64, :], in1=rzb_sb[:, :], op=ALU.mult),
                            reads=[bufY0, bufY1, b_rzb], writes=[byst])
                        p.op("sp", lambda e, s=s, h=h, H=H, yst=yst: e.dma_start(
                            out=yscr[s, h, :, H * 1024:(H + 1) * 1024], in_=yst[:, :]),
                            reads=[byst], writes=[b_yscr], dma=True)
                    if ("scoresT" in tap_set) and h == 0 and s == 0:
                        tap("scoresT", lambda: ET[1], [128, 1024], BF16, deps=[b_ET[1].w])

            if attn_lvl < 6:
                continue
            p.op("sp", lambda e: e.dma_start(
                out=wstream.rearrange("p (kc m) -> p kc m", kc=NKC)[:, :, :],
                in_=wscr["wprojT"].rearrange("(kc p) m -> p kc m", p=128)[:, :, :]),
                writes=[b_wproj], reads=[b_wscr["wprojT"]],
                deps=[ev for b in b_wh for ev in ([b.w] + b.r)], dma=True)
            for g in range(4):
                for half2 in range(2):
                    p.op("sp", lambda e, s=s, g=g, half2=half2: e.dma_start(
                        out=protc[half2].rearrange("p (kc t) -> p kc t", kc=4)[:, :, :],
                        in_=yscr[s].rearrange("(k g2) d t -> (g2 d) k t", g2=2)[:, half2 * 4:half2 * 4 + 4, g * 512:(g + 1) * 512]),
                        reads=[b_yscr], writes=[b_pr[half2]], dma=True)

                def ybv(kc):
                    return protc[kc // 4][:, (kc % 4) * 512:(kc % 4 + 1) * 512]

                for m in range(NKC):
                    ps, bps = PSMALL[m % 2]
                    p.op("pe", lambda e, m=m, ps=ps, ybv=ybv: [
                        e.matmul(out=ps[:, :], lhsT=wstream[:, kc * 1024 + m * 128:kc * 1024 + (m + 1) * 128],
                                 rhs=ybv(kc),
                                 start=(kc == 0), stop=(kc == NKC - 1))
                        for kc in range(NKC)][-1],
                        reads=[b_wproj, b_pr[0], b_pr[1]], writes=[bps])
                    ao, bao = ao_st[0], b_ao[0]
                    p.op("dve", lambda e, ps=ps, ao=ao: e.tensor_copy(ao[:, :], ps[:, :]),
                         reads=[bps], writes=[bao])
                    p.op("sp", lambda e, m=m, s=s, g=g, ao=ao: e.dma_start(
                        out=accd[m, :, s * T + g * 512:s * T + (g + 1) * 512], in_=ao[:, :]),
                        reads=[bao], writes=[b_accd], dma=True)
        tap("attn_acc", lambda: accd, [NKC, 128, 2 * T], F32, deps=list(b_accd.wl))

        p.barrier()
        main_es.close()
        ffn_es = ExitStack()
        fsb = lambda name, shp, dt: ffn_es.enter_context(nc.sbuf_tensor(name, shp, dt)).ap()
        xnf = fsb("xnf", [128, NKC * 2 * T], BF16)
        b_xnf = Buf()
        gsl = [fsb(f"gsl{i}", [128, NKC * 512], BF16) for i in range(2)]
        usl = [fsb(f"usl{i}", [128, NKC * 512], BF16) for i in range(2)]
        dsl = [fsb(f"dsl{i}", [128, 4 * 1024], BF16) for i in range(2)]
        b_gsl = [Buf(), Buf()]
        b_usl = [Buf(), Buf()]
        b_dsl = [Buf(), Buf()]
        gub_s = fsb("gub_s", [128, 32], F32)
        gub1 = fsb("gub1", [128, 32], F32)
        b_gub, b_gub1 = Buf(), Buf()
        gwb = fsb("gwb", [128, 4 * 1024], F32)
        b_gwb = Buf()
        act_sb = [fsb(f"act_sb{i}", [128, 4 * 1024], BF16) for i in range(2)]
        b_act = [Buf(), Buf()]
        glu_t = [fsb(f"glu_t{i}", [128, 1024], F32) for i in range(2)]
        upt = [fsb(f"upt{i}", [128, 1024], F32) for i in range(2)]
        b_glu = [Buf(), Buf()]
        b_upt = [Buf(), Buf()]
        dstage = [fsb(f"dstage{i}", [128, 1024], F32) for i in range(2)]
        b_dst = [Buf(), Buf()]

        p.op("sp", lambda e: e.dma_start(out=xnf[:, :], in_=xnscr),
             reads=[b_xnscr], writes=[b_xnf], dma=True)

        def xnfv(kc, c0, c1):
            return xnf[:, kc * 2 * T + c0:kc * 2 * T + c1]

        it_list = [(e_, iq) for e_ in range(experts) for iq in range(4)]
        for it_idx, (ex, iq) in enumerate(it_list):
            par = it_idx % 2
            bg, bu, bd = b_gsl[par], b_usl[par], b_dsl[par]
            gs, us, ds = gsl[par], usl[par], dsl[par]
            p.op("sp", lambda e, ex=ex, iq=iq, gs=gs: e.dma_start(
                out=gs.rearrange("p (kc m) -> p kc m", kc=NKC)[:, :, :],
                in_=gupb[ex].rearrange("(kc p) m -> p kc m", p=128)[:, :, iq * 512:(iq + 1) * 512]),
                writes=[bg], reads=[b_gupb], dma=True)
            p.op("sp", lambda e, ex=ex, iq=iq, us=us: e.dma_start(
                out=us.rearrange("p (kc m) -> p kc m", kc=NKC)[:, :, :],
                in_=gupb[ex].rearrange("(kc p) m -> p kc m", p=128)[:, :, I + iq * 512:I + (iq + 1) * 512]),
                writes=[bu], reads=[b_gupb], dma=True)
            p.op("sp", lambda e, ex=ex, iq=iq, ds=ds: e.dma_start(
                out=ds.rearrange("p (ic h) -> p ic h", ic=4)[:, :, :],
                in_=dwnb[ex, iq * 512:(iq + 1) * 512, :].rearrange("(ic p) h -> p ic h", p=128)[:, :, :]),
                writes=[bd], reads=[b_dwnb], dma=True)
            if iq == 0:
                p.op("pool", lambda e, ex=ex: e.dma_start(out=gub_s[:, :], in_=gub_d[ex]),
                     writes=[b_gub], dma=True)
                p.op("dve", lambda e: e.tensor_scalar(
                    out=gub1[:, :], in0=gub_s[:, :], scalar1=1.0, scalar2=None, op0=ALU.add),
                    reads=[b_gub], writes=[b_gub1])
                for gg in range(NKC):
                    ps, bps = PSMALL[gg % 2]
                    p.op("pe", lambda e, ex=ex, gg=gg, ps=ps: [
                        e.matmul(out=ps[:, :], lhsT=cs["oneh8"][:, ex * 128:(ex + 1) * 128],
                                 rhs=gthi[:, gg * 512:(gg + 1) * 512], start=True, stop=False),
                        e.matmul(out=ps[:, :], lhsT=cs["oneh8"][:, ex * 128:(ex + 1) * 128],
                                 rhs=gtlo[:, gg * 512:(gg + 1) * 512], start=False, stop=True)][-1],
                        reads=[b_gthi, b_gtlo, b_cs["oneh8"]], writes=[bps])
                    p.op("dve", lambda e, gg=gg, ps=ps: e.tensor_copy(
                        gwb[:, gg * 512:(gg + 1) * 512], ps[:, :]),
                        reads=[bps], writes=[b_gwb])
            for tgi in range(5):
                if tgi < 4:
                    tg = tgi
                    tc0 = tg * 1024
                    asb, bact = act_sb[tg % 2], b_act[tg % 2]
                    for pr_ in range(4):
                        ppar = pr_ % 2
                        glu, bglu = glu_t[ppar], b_glu[ppar]
                        up_t, bupt = upt[ppar], b_upt[ppar]
                        p.op("pe", lambda e, pr_=pr_, gs=gs, tc0=tc0: [
                            e.matmul(out=pA[:, gg * 512:(gg + 1) * 512],
                                     lhsT=gs[:, kc * 512 + pr_ * 128:kc * 512 + (pr_ + 1) * 128],
                                     rhs=xnfv(kc, tc0 + gg * 512, tc0 + (gg + 1) * 512),
                                     start=(kc == 0), stop=(kc == NKC - 1))
                            for kc in range(NKC) for gg in range(2)][-1],
                            reads=[bg, b_xnf], writes=[bufA0, bufA1])
                        p.op("dve", lambda e, pr_=pr_, iq=iq, glu=glu: e.tensor_scalar(
                            out=glu[:, :], in0=pA[:, :], scalar1=gub_s[:, iq * 4 + pr_:iq * 4 + pr_ + 1],
                            op0=ALU.add, scalar2=LIMIT, op1=ALU.min),
                            reads=[bufA0, bufA1, b_gub], writes=[bglu])
                        p.op("act", lambda e, glu=glu: e.activation(
                            out=glu[:, :], in_=glu[:, :], func=AF.Silu, scale=ALPHA),
                            writes=[bglu])
                        p.op("pe", lambda e, pr_=pr_, us=us, tc0=tc0: [
                            e.matmul(out=pB[:, gg * 512:(gg + 1) * 512],
                                     lhsT=us[:, kc * 512 + pr_ * 128:kc * 512 + (pr_ + 1) * 128],
                                     rhs=xnfv(kc, tc0 + gg * 512, tc0 + (gg + 1) * 512),
                                     start=(kc == 0), stop=(kc == NKC - 1))
                            for kc in range(NKC) for gg in range(2)][-1],
                            reads=[bu, b_xnf], writes=[bufB0, bufB1])
                        p.op("dve", lambda e, pr_=pr_, iq=iq, up_t=up_t: e.tensor_scalar(
                            out=up_t[:, :], in0=pB[:, :], scalar1=gub1[:, 16 + iq * 4 + pr_:16 + iq * 4 + pr_ + 1],
                            op0=ALU.add, scalar2=-(LIMIT - 1.0), op1=ALU.max),
                            reads=[bufB0, bufB1, b_gub1], writes=[bupt])
                        p.op("pool", lambda e, up_t=up_t: e.tensor_scalar(
                            out=up_t[:, :], in0=up_t[:, :], scalar1=LIMIT + 1.0, scalar2=-3.0e38,
                            op0=ALU.min, op1=ALU.max),
                            writes=[bupt])
                        p.op("pool" if pr_ % 2 == 0 else "dve",
                             lambda e, pr_=pr_, asb=asb, up_t=up_t, glu=glu: e.tensor_tensor(
                            out=asb[:, pr_ * 1024:(pr_ + 1) * 1024], in0=up_t[:, :], in1=glu[:, :],
                            op=ALU.mult),
                            reads=[bupt, bglu], writes=[bact])
                if tgi >= 1:
                    tg = tgi - 1
                    tc0 = tg * 1024
                    asb, bact = act_sb[tg % 2], b_act[tg % 2]
                    for hm in range(NKC):
                        if hm % 2 == 0:
                            outs = [(pY[:, 0:512], bufY0), (pY[:, 512:1024], bufY1)]
                        else:
                            outs = [(pS1[:, :], bufS1), (pS2[:, :], bufS2)]
                        p.op("pe", lambda e, hm=hm, ds=ds, asb=asb, outs=outs: [
                            e.matmul(out=outs[gg][0],
                                     lhsT=ds[:, ic * 1024 + hm * 128:ic * 1024 + (hm + 1) * 128],
                                     rhs=asb[:, ic * 1024 + gg * 512:ic * 1024 + (gg + 1) * 512],
                                     start=(ic == 0), stop=(ic == 3))
                            for ic in range(4) for gg in range(2)][-1],
                            reads=[bd, bact], writes=[outs[0][1], outs[1][1]])
                        dst, bdst2 = dstage[hm % 2], b_dst[hm % 2]
                        p.op("dve", lambda e, tc0=tc0, outs=outs, dst=dst: [
                            e.tensor_tensor(out=dst[:, gg * 512:(gg + 1) * 512], in0=outs[gg][0],
                                            in1=gwb[:, tc0 + gg * 512:tc0 + (gg + 1) * 512],
                                            op=ALU.mult)
                            for gg in range(2)][-1],
                            reads=[outs[0][1], outs[1][1], b_gwb], writes=[bdst2])
                        p.op("pool", lambda e, hm=hm, tc0=tc0, dst=dst: e.dma_start(
                            out=accd[hm, :, tc0:tc0 + 1024], in_=dst[:, :], accum_op=ALU.add),
                            reads=[bdst2], writes=[b_accd], dma=True)
        for hm in range(NKC):
            for g in range(8):
                ps, bps = PSMALL[g % 2]
                p.op("pe", lambda e, hm=hm, g=g, ps=ps: [
                    e.matmul(out=ps[:, :], lhsT=db_sb[:, hm * 128:(hm + 1) * 128],
                             rhs=gthi[:, g * 512:(g + 1) * 512], start=True, stop=False),
                    e.matmul(out=ps[:, :], lhsT=db_sb[:, hm * 128:(hm + 1) * 128],
                             rhs=gtlo[:, g * 512:(g + 1) * 512], start=False, stop=True)][-1],
                    reads=[b_db, b_gthi, b_gtlo], writes=[bps])
                dst, bdst2 = dstage[g % 2], b_dst[g % 2]
                p.op("dve", lambda e, ps=ps, dst=dst: e.tensor_copy(dst[:, 0:512], ps[:, :]),
                     reads=[bps], writes=[bdst2])
                p.op("pool", lambda e, hm=hm, g=g, dst=dst: e.dma_start(
                    out=accd[hm, :, g * 512:(g + 1) * 512], in_=dst[:, 0:512], accum_op=ALU.add),
                    reads=[bdst2], writes=[b_accd], dma=True)

        xr = [gsl[0].bitcast(F32)[:, 0:1024], gsl[0].bitcast(F32)[:, 1024:2048]]
        b_xr = [b_gsl[0], b_gsl[0]]
        astage = [usl[0].bitcast(F32)[:, 0:1024], usl[0].bitcast(F32)[:, 1024:2048]]
        b_ast = [b_usl[0], b_usl[0]]
        ostage = [dsl[0].bitcast(F32)[:, 0:1024], dsl[0].bitcast(F32)[:, 1024:2048]]
        b_ost = [b_dsl[0], b_dsl[0]]
        out_evs = []
        for s in range(2):
            for tt in range(NTT):
                par = tt % 2
                gc0 = s * T + tt * 128
                xrr, bxr = xr[par], b_xr[par]
                p.op("sp", lambda e, s=s, tt=tt, xrr=xrr: e.dma_start(
                    out=xrr[:, :], in_=x_d[s, tt * 128:(tt + 1) * 128, :]),
                    writes=[bxr], dma=True)
                ast, bast = astage[par], b_ast[par]
                p.op("sp", lambda e, gc0=gc0, ast=ast: e.dma_start(
                    out=ast.rearrange("p (kc t) -> p kc t", kc=NKC)[:, :, :],
                    in_=accd.rearrange("kc p t -> p kc t")[:, :, gc0:gc0 + 128]),
                    reads=[b_accd], writes=[bast], dma=True)
                for half in range(2):
                    ps, bps = PSMALL[half]
                    p.op("pe", lambda e, half=half, ast=ast, ps=ps: [
                        e.transpose(out=ps[:, q * 128:(q + 1) * 128],
                                    in_=ast[:, (half * 4 + q) * 128:(half * 4 + q + 1) * 128],
                                    identity=cs["ident_f32"][:, :])
                        for q in range(4)][-1],
                        reads=[bast, b_cs["ident_f32"]], writes=[bps])
                    ost, bost = ostage[par], b_ost[par]
                    p.op("dve", lambda e, half=half, xrr=xrr, ps=ps, ost=ost: e.tensor_tensor(
                        out=ost[:, half * 512:(half + 1) * 512], in0=ps[:, :],
                        in1=xrr[:, half * 512:(half + 1) * 512], op=ALU.add),
                        reads=[bps, bxr], writes=[bost])
                ost, bost = ostage[par], b_ost[par]
                ev = p.op("sp", lambda e, s=s, tt=tt, ost=ost: e.dma_start(
                    out=out_d[s, tt * 128:(tt + 1) * 128, :], in_=ost[:, :]),
                    reads=[bost], writes=[b_out], dma=True)
                out_evs.append(ev)
        p.op("sp", lambda e: e.nop(), deps=out_evs)

        with nc.Block() as block:
            counts = p.emit(nc, block, sems, dma_sems)
        ffn_es.close()
    return nc, counts, tap_out


_BUILD_CACHE = {}


def _get_nc():
    key = "full"
    if key not in _BUILD_CACHE:
        _BUILD_CACHE[key] = build_nc()
    return _BUILD_CACHE[key]


def kernel(L0_attn_x, L0_attn_Wq, L0_attn_Wk, L0_attn_Wv, L0_attn_Wp, L0_attn_Wproj,
           L0_attn_q_rms_w, L0_attn_k_rms_w, L0_ffn_router_weight, L0_ffn_router_bias,
           L0_ffn_gate_up_proj, L0_ffn_gate_up_bias, L0_ffn_down_proj, L0_ffn_down_bias):
    f32 = np.float32
    x = np.ascontiguousarray(np.asarray(L0_attn_x, dtype=f32))
    base = {
        "wqT": np.ascontiguousarray(np.asarray(L0_attn_Wq, f32).T),
        "wkT": np.ascontiguousarray(np.asarray(L0_attn_Wk, f32).T),
        "wvT": np.ascontiguousarray(np.asarray(L0_attn_Wv, f32).T),
        "wpT": np.ascontiguousarray(np.asarray(L0_attn_Wp, f32).T),
        "wprojT": np.ascontiguousarray(np.asarray(L0_attn_Wproj, f32).T),
        "qrw": np.ascontiguousarray(np.asarray(L0_attn_q_rms_w, f32).reshape(1, 64)),
        "krw": np.ascontiguousarray(np.asarray(L0_attn_k_rms_w, f32).reshape(1, 64)),
        "rw_l": np.ascontiguousarray(
            np.asarray(L0_ffn_router_weight, f32).T.reshape(8, 128, 8).transpose(1, 0, 2).reshape(128, 64)),
        "rb": np.ascontiguousarray(np.asarray(L0_ffn_router_bias, f32).reshape(1, 8)),
        "gup": np.ascontiguousarray(np.asarray(L0_ffn_gate_up_proj, f32)),
        "gub_l": np.ascontiguousarray(
            np.asarray(L0_ffn_gate_up_bias, f32).reshape(8, 32, 128).transpose(0, 2, 1)),
        "dwn": np.ascontiguousarray(np.asarray(L0_ffn_down_proj, f32)),
        "dbias": np.ascontiguousarray(np.asarray(L0_ffn_down_bias, f32)),
    }
    base.update(host_consts())
    nc, counts, _ = _get_nc()
    in_maps = []
    for c in range(8):
        m = dict(base)
        m["x"] = np.ascontiguousarray(x[2 * c:2 * c + 2])
        in_maps.append(m)
    res = run_bass_kernel_spmd(nc, in_maps, core_ids=list(range(8)))
    out = np.concatenate([np.asarray(r["out"]).reshape(2, T, C) for r in res.results], axis=0)
    return out.astype(np.float32)


# version 18
# speedup vs baseline: 1.1618x; 1.0178x over previous
import math
from contextlib import ExitStack

import numpy as np
import ml_dtypes
import concourse.bass as bass
import concourse.mybir as mybir
from concourse.bass_utils import run_bass_kernel_spmd

F32 = mybir.dt.float32
BF16 = mybir.dt.bfloat16
AF = mybir.ActivationFunctionType
ALU = mybir.AluOpType
AX = mybir.AxisListType.X

T = 2048
C = 1024
NH = 16
HD = 64
NKC = C // 128
NTT = T // 128
I = 2048
NE = 8
RMS_EPS = 1e-5
FRMS_EPS = 1.1920929e-07
LN_EPS = 1e-5
ALPHA = 1.702
LIMIT = 7.0
MSHIFT = 60.0
DBG = {}

ENGINES = ("pe", "act", "dve", "pool", "sp")
BLOCK_ATTR = {"pe": "tensor", "act": "scalar", "dve": "vector", "pool": "gpsimd", "sp": "sync"}


class Buf:
    __slots__ = ("w", "r", "mw", "wl")
    def __init__(self, mw=False):
        self.w = None
        self.r = []
        self.mw = mw
        self.wl = []


class Prog:
    DMA_POOL = 20

    def __init__(self):
        self.eng_ops = {e: [] for e in ENGINES}
        self.dma_count = {"sp": 0, "pool": 0}

    def op(self, eng, fn, reads=(), writes=(), deps=(), dma=False):
        dep_list = [d for d in deps if d is not None]
        for b in reads:
            if b.mw:
                dep_list.extend(b.wl)
            elif b.w is not None:
                dep_list.append(b.w)
        for b in writes:
            dep_list.extend(b.r)
            if b.mw:
                pass
            elif b.w is not None:
                dep_list.append(b.w)
        rec = {"eng": eng, "fn": fn, "deps": tuple(dep_list), "dma": dma,
               "needed": False, "val": None, "dma_idx": None}
        if dma:
            rec["dma_idx"] = self.dma_count[eng]
            self.dma_count[eng] += 1
        self.eng_ops[eng].append(rec)
        for b in reads:
            b.r.append(rec)
        for b in writes:
            if b.mw:
                b.wl.append(rec)
            else:
                b.w = rec
                b.r = []
        return rec

    def barrier(self):
        lasts = [self.eng_ops[e][-1] for e in ENGINES if self.eng_ops[e]]
        for q in ("sp", "pool"):
            slot_last = {}
            for r in self.eng_ops[q]:
                if r["dma"]:
                    slot_last[r["dma_idx"] % self.DMA_POOL] = r
            lasts.extend(slot_last.values())
        nops = []
        for e in ENGINES:
            nops.append(self.op(e, lambda eng: eng.nop(), deps=lasts))
        return nops

    def emit(self, nc, block, sems, dma_sems):
        for e in ENGINES:
            for r in self.eng_ops[e]:
                for d in r["deps"]:
                    d["needed"] = True
        for e in ENGINES:
            cum = 0
            for r in self.eng_ops[e]:
                if r["dma"]:
                    k = r["dma_idx"]
                    r["val"] = (k // self.DMA_POOL + 1) * 16
                elif r["needed"]:
                    cum += 1
                    r["val"] = cum
        counts = {e: sum(1 for r in self.eng_ops[e]) for e in ENGINES}

        def sem_of(r):
            if r["dma"]:
                return dma_sems[r["eng"]][r["dma_idx"] % self.DMA_POOL]
            return sems[r["eng"]]

        def body(eng_name):
            def go(eng):
                lastwait = {}
                sem = sems[eng_name]
                for r in self.eng_ops[eng_name]:
                    if r["dma"] and r["dma_idx"] >= self.DMA_POOL:
                        dsem = dma_sems[eng_name][r["dma_idx"] % self.DMA_POOL]
                        v = (r["dma_idx"] // self.DMA_POOL) * 16
                        key = id(dsem)
                        if lastwait.get(key, -1) < v:
                            eng.wait_ge(dsem, v)
                            lastwait[key] = v
                    for d in r["deps"]:
                        dsem = sem_of(d)
                        v = d["val"]
                        if v is None:
                            raise RuntimeError("dep without value")
                        key = id(dsem)
                        if lastwait.get(key, -1) < v:
                            eng.wait_ge(dsem, v)
                            lastwait[key] = v
                    instr = r["fn"](eng)
                    if r["dma"]:
                        instr.then_inc(sem_of(r), 16)
                    elif r["needed"]:
                        instr.then_inc(sem, 1)
            return go

        for e in ENGINES:
            getattr(block, BLOCK_ATTR[e])(body(e))
        return counts


def splits(lo, hi, align=512):
    out = []
    a = lo
    while a < hi:
        b = min(hi, (a // align + 1) * align)
        out.append((a, b))
        a = b
    return out


def host_consts():
    bf = ml_dtypes.bfloat16
    c = {}
    c["ident_bf"] = np.eye(128, dtype=bf)
    c["ident_f32"] = np.eye(128, dtype=np.float32)
    r = np.arange(128)
    c["slmat"] = (r[:, None] > r[None, :]).astype(bf)
    c["utmask"] = (r[:, None] <= r[None, :]).astype(bf)
    kc = np.zeros((64, T), dtype=np.float32)
    ind = (np.arange(T)[None, :] // 128 == np.arange(16)[:, None]).astype(np.float32)
    kc[0:16] = ind
    kc[32:48] = ind
    c["kconst"] = kc.astype(bf)
    ec = np.zeros((128, 256), dtype=np.float32)
    for J in range(16):
        ec[:, 16 * J + J] = 1.0
    c["ecols"] = ec.astype(bf)
    r16 = np.arange(16)
    c["l16"] = (r16[:, None] > r16[None, :]).astype(bf)
    c["neg1"] = np.full((1, 16), -1.0, dtype=bf)
    c["ones64"] = np.ones((64, 1), dtype=bf)
    op_ = np.zeros((128, 65), dtype=np.float32)
    op_[0:64, 0] = 1.0
    op_[64:128, 64] = 1.0
    c["onespair"] = op_.astype(bf)
    c["ones1x64"] = np.ones((1, 64), dtype=bf)
    c["onesrow"] = np.ones((1, 128), dtype=bf)
    oh = np.zeros((8, 8 * 128), dtype=np.float32)
    for e in range(8):
        oh[e, 128 * e:128 * (e + 1)] = 1.0 / ALPHA
    c["oneh8"] = oh.astype(bf)
    fc = np.zeros((128, 4), dtype=np.float32)
    fc[:, 0] = LN_EPS
    fc[:, 1] = FRMS_EPS
    fc[:, 2] = -math.log(8.0)
    fc[:, 3] = 1.0
    c["fconst"] = fc
    t = np.arange(T, dtype=np.float64)
    c["cosb64"] = np.tile(np.cos(t)[None, :], (64, 1)).astype(bf)
    sb64 = np.zeros((64, T), dtype=np.float64)
    sb64[0:32] = -np.sin(t)[None, :]
    sb64[32:64] = np.sin(t)[None, :]
    c["sinb64"] = sb64.astype(bf)
    return c


def build_nc(taps=(), heads=NH, experts=NE, seqs=(0, 1), attn_lvl=6):
    nc = bass.Bass("TRN2", target_bir_lowering=False)
    tap_set = set(taps)

    x_d = nc.dram_tensor("x", [2, T, C], F32, kind="ExternalInput").ap()
    wT = {}
    for nm in ("wqT", "wkT", "wvT", "wpT", "wprojT"):
        wT[nm] = nc.dram_tensor(nm, [C, C], F32, kind="ExternalInput").ap()
    qrw_d = nc.dram_tensor("qrw", [1, 64], F32, kind="ExternalInput").ap()
    krw_d = nc.dram_tensor("krw", [1, 64], F32, kind="ExternalInput").ap()
    rwl_d = nc.dram_tensor("rw_l", [128, 64], F32, kind="ExternalInput").ap()
    rb_d = nc.dram_tensor("rb", [1, 8], F32, kind="ExternalInput").ap()
    gup_d = nc.dram_tensor("gup", [NE, C, 2 * I], F32, kind="ExternalInput").ap()
    gub_d = nc.dram_tensor("gub_l", [NE, 128, 32], F32, kind="ExternalInput").ap()
    dwn_d = nc.dram_tensor("dwn", [NE, I, C], F32, kind="ExternalInput").ap()
    dbias_d = nc.dram_tensor("dbias", [NE, C], F32, kind="ExternalInput").ap()
    cst_d = {}
    cshapes = {"ident_bf": ([128, 128], BF16), "ident_f32": ([128, 128], F32),
               "slmat": ([128, 128], BF16), "utmask": ([128, 128], BF16),
               "kconst": ([64, T], BF16), "ecols": ([128, 256], BF16),
               "l16": ([16, 16], BF16), "neg1": ([1, 16], BF16),
               "ones64": ([64, 1], BF16), "ones1x64": ([1, 64], BF16),
               "onespair": ([128, 65], BF16),
               "onesrow": ([1, 128], BF16), "oneh8": ([8, 8 * 128], BF16),
               "cosb64": ([64, T], BF16), "sinb64": ([64, T], BF16),
               "fconst": ([128, 4], F32)}
    for nm, (shp, dt) in cshapes.items():
        cst_d[nm] = nc.dram_tensor(nm, shp, dt, kind="ExternalInput").ap()
    out_d = nc.dram_tensor("out", [2, T, C], F32, kind="ExternalOutput").ap()
    xnscr = nc.dram_tensor("xnscr", [128, NKC * 2 * T], BF16, kind="Internal").ap()
    wscr = {nm: nc.dram_tensor("wscr_" + nm, [C, C], BF16, kind="Internal").ap()
            for nm in ("wqT", "wkT", "wvT", "wpT", "wprojT")}
    gupb = nc.dram_tensor("gupb", [NE, C, 2 * I], BF16, kind="Internal").ap()
    dwnb = nc.dram_tensor("dwnb", [NE, I, C], BF16, kind="Internal").ap()
    accd = nc.dram_tensor("accd", [NKC, 128, 2 * T], F32, kind="Internal").ap()
    yscr = nc.dram_tensor("yscr", [2, NH, 64, T], BF16, kind="Internal").ap()

    tap_out = {}
    p = Prog()

    def tap(name, ap_fn, shape, dtype, deps):
        if name not in tap_set:
            return
        td = nc.dram_tensor("tap_" + name, shape, F32, kind="ExternalOutput").ap()
        tap_out[name] = td
        p.op("pool", lambda e, td=td, ap_fn=ap_fn: e.dma_start(out=td, in_=ap_fn()),
             deps=deps, dma=True)

    with ExitStack() as es:
        sems = {}
        for e in ENGINES:
            sems[e] = es.enter_context(nc.semaphore("sem_" + e))
        dma_sems = {q: [es.enter_context(nc.semaphore(f"dsem_{q}_{i}"))
                        for i in range(Prog.DMA_POOL)] for q in ("sp", "pool")}

        pA = es.enter_context(nc.psum_tensor("pA", [128, 1024], F32)).ap()
        pB = es.enter_context(nc.psum_tensor("pB", [128, 1024], F32)).ap()
        pY = es.enter_context(nc.psum_tensor("pY", [128, 1024], F32)).ap()
        pS1 = es.enter_context(nc.psum_tensor("pS1", [128, 512], F32)).ap()
        pS2 = es.enter_context(nc.psum_tensor("pS2", [128, 512], F32)).ap()
        bufA0, bufA1, bufB0, bufB1 = Buf(), Buf(), Buf(), Buf()
        bufY, bufS1, bufS2 = Buf(), Buf(), Buf()

        sb = lambda name, shp, dt: es.enter_context(nc.sbuf_tensor(name, shp, dt)).ap()
        cs = {nm: sb("sb_" + nm, shp, dt) for nm, (shp, dt) in cshapes.items() if nm != "kconst"}
        qrw_b = sb("qrw_b", [65, 64], BF16)
        krw_b = sb("krw_b", [65, 64], BF16)
        onesp64 = sb("onesp64", [65, 64], BF16)
        rw_sb = sb("rw_sb", [128, 128], BF16)
        rw_f = sb("rw_f", [128, 64], F32)
        rb_b = sb("rb_b", [1, 8], BF16)
        db_sb = sb("db_sb", [8, C], BF16)
        gthi = sb("gthi", [8, 2 * T], BF16)
        gtlo = sb("gtlo", [8, 2 * T], BF16)
        b_cs = {nm: Buf() for nm in cshapes}
        b_qrw, b_krw, b_rw, b_rb, b_db, b_rwf = Buf(), Buf(), Buf(), Buf(), Buf(), Buf()
        b_gthi, b_gtlo = Buf(mw=True), Buf(mw=True)
        b_xnscr, b_accd, b_yscr, b_out = Buf(mw=True), Buf(mw=True), Buf(mw=True), Buf(mw=True)

        b_wscr = {nm: Buf(mw=True) for nm in wscr}
        b_gupb = Buf(mw=True)
        b_dwnb = Buf(mw=True)
        for nm in wscr:
            p.op("pool", lambda e, nm=nm: e.dma_start(out=wscr[nm], in_=wT[nm]),
                 writes=[b_wscr[nm]], dma=True)
        for ex in range(NE):
            p.op("pool", lambda e, ex=ex: e.dma_start(out=gupb[ex], in_=gup_d[ex]),
                 writes=[b_gupb], dma=True)
            p.op("pool", lambda e, ex=ex: e.dma_start(out=dwnb[ex], in_=dwn_d[ex]),
                 writes=[b_dwnb], dma=True)
        for nm in cs:
            pass
        for nm in [n for n in cshapes if n != "kconst"]:
            p.op("sp", lambda e, nm=nm: e.dma_start(out=cs[nm][:, :], in_=cst_d[nm]),
                 writes=[b_cs[nm]], dma=True)
        p.op("pool", lambda e: e.dma_start(out=qrw_b[0:1, :], in_=qrw_d), writes=[b_qrw], dma=True)
        p.op("pool", lambda e: e.dma_start(out=qrw_b[64:65, :], in_=qrw_d), writes=[b_qrw], dma=True)
        p.op("pool", lambda e: e.dma_start(out=krw_b[0:1, :], in_=krw_d), writes=[b_krw], dma=True)
        p.op("pool", lambda e: e.dma_start(out=krw_b[64:65, :], in_=krw_d), writes=[b_krw], dma=True)
        p.op("sp", lambda e: e.dma_start(out=onesp64[0:1, :], in_=cst_d["ones1x64"]),
             writes=[b_cs["ones1x64"]], dma=True)
        p.op("sp", lambda e: e.dma_start(out=onesp64[64:65, :], in_=cst_d["ones1x64"]),
             writes=[b_cs["ones1x64"]], dma=True)
        p.op("pool", lambda e: e.dma_start(out=rw_f[:, :], in_=rwl_d), writes=[b_rwf], dma=True)
        p.op("dve", lambda e: e.tensor_copy(rw_sb[:, 0:64], rw_f[:, :]), reads=[b_rwf], writes=[b_rw])
        p.op("dve", lambda e: e.tensor_tensor(out=rw_sb[:, 64:128], in0=rw_f[:, :],
                                              in1=rw_sb[:, 0:64], op=ALU.subtract),
             reads=[b_rwf], writes=[b_rw])
        p.op("pool", lambda e: e.dma_start(out=rb_b[:, :], in_=rb_d), writes=[b_rb], dma=True)
        p.op("pool", lambda e: e.dma_start(out=db_sb[:, :], in_=dbias_d), writes=[b_db], dma=True)

        main_es = ExitStack()
        msb = lambda name, shp, dt: main_es.enter_context(nc.sbuf_tensor(name, shp, dt)).ap()
        lsb = msb

        xstage = [lsb(f"xstage{i}", [128, C], F32) for i in range(2)]
        b_xstage = [Buf(), Buf()]
        xntm = [lsb(f"xntm{i}", [128, C], BF16) for i in range(2)]
        b_xntm = [Buf(), Buf()]
        xnfst = [lsb(f"xnfst{i}", [128, C], BF16) for i in range(2)]
        b_xnfst = [Buf(), Buf()]
        dump = lsb("dump", [128, C], BF16)
        b_dump = Buf()
        nmean = [lsb(f"nmean{i}", [128, 1], F32) for i in range(2)]
        b_nmean = [Buf(), Buf()]
        vsum = [lsb(f"vsum{i}", [128, 1], F32) for i in range(2)]
        b_vsum = [Buf(), Buf()]
        rstd_ln = [lsb(f"rstdln{i}", [128, 1], F32) for i in range(2)]
        b_rstd_ln = [Buf(), Buf()]
        sm8 = lsb("sm8", [128, 8], F32)
        m8a = lsb("m8a", [128, 1], F32)
        m8b = lsb("m8b", [128, 8], F32)
        g8 = lsb("g8", [128, 8], F32)
        b_sm8, b_m8a, b_m8b, b_g8 = Buf(), Buf(), Buf(), Buf()

        for s in range(2):
            for tt in range(NTT):
                par = tt % 2
                gc0 = s * T + tt * 128
                xs, bx = xstage[par], b_xstage[par]
                p.op("sp", lambda e, s=s, tt=tt, xs=xs: e.dma_start(
                    out=xs[:, :], in_=x_d[s, tt * 128:(tt + 1) * 128, :]),
                    writes=[bx], dma=True)
                nm, bnm = nmean[par], b_nmean[par]
                p.op("dve", lambda e, xs=xs, nm=nm: e.reduce_sum(nm[:, :], xs[:, :], axis=AX),
                     reads=[bx], writes=[bnm])
                p.op("dve", lambda e, nm=nm: e.tensor_scalar(
                    out=nm[:, :], in0=nm[:, :], scalar1=-1.0 / C, scalar2=None, op0=ALU.mult),
                    writes=[bnm])
                vs, bvs = vsum[par], b_vsum[par]
                p.op("act", lambda e, xs=xs, nm=nm, vs=vs: e.activation(
                    out=dump[:, :], in_=xs[:, :], func=AF.Square, bias=nm[:, :], accum_out=vs[:, :]),
                    reads=[bx, bnm], writes=[b_dump, bvs])
                rl, brl = rstd_ln[par], b_rstd_ln[par]
                p.op("act", lambda e, vs=vs, rl=rl: e.activation(
                    out=rl[:, :], in_=vs[:, :], func=AF.Ln, scale=1.0 / C,
                    bias=cs["fconst"][:, 0:1]),
                    reads=[bvs, b_cs["fconst"]], writes=[brl])
                p.op("act", lambda e, rl=rl: e.activation(
                    out=rl[:, :], in_=rl[:, :], func=AF.Exp, scale=-0.5),
                    writes=[brl])
                xt, bxt = xntm[par], b_xntm[par]
                p.op("dve", lambda e, xs=xs, nm=nm, rl=rl, xt=xt: e.tensor_scalar(
                    out=xt[:, :], in0=xs[:, :], scalar1=nm[:, :], op0=ALU.add,
                    scalar2=rl[:, :], op1=ALU.mult),
                    reads=[bx, bnm, brl], writes=[bxt])
                if "xn_tm" in tap_set:
                    if "xn_tm" not in tap_out:
                        tap_out["xn_tm"] = nc.dram_tensor("tap_xn_tm", [2 * NTT, 128, C], F32,
                                                          kind="ExternalOutput").ap()
                    p.op("pool", lambda e, s=s, tt=tt, xt=xt: e.dma_start(
                        out=tap_out["xn_tm"][s * NTT + tt], in_=xt[:, :]),
                        reads=[bxt], dma=True)
                pT = pS2.bitcast(BF16)
                p.op("pe", lambda e, xt=xt, pT=pT: [
                    e.transpose(out=pT[:, kc * 128:(kc + 1) * 128],
                                in_=xt[:, kc * 128:(kc + 1) * 128],
                                identity=cs["ident_bf"][:, :])
                    for kc in range(NKC)][-1],
                    reads=[bxt, b_cs["ident_bf"]], writes=[bufS2])
                xf, bxf = xnfst[par], b_xnfst[par]
                p.op("dve", lambda e, pT=pT, xf=xf: e.tensor_copy(xf[:, :], pT[:, :]),
                     reads=[bufS2], writes=[bxf])
                p.op("sp", lambda e, xf=xf, gc0=gc0: e.dma_start(
                    out=xnscr.rearrange("p (kc t) -> p kc t", kc=NKC)[:, :, gc0:gc0 + 128],
                    in_=xf.rearrange("p (kc t) -> p kc t", kc=NKC)[:, :, :]),
                    reads=[bxf], writes=[b_xnscr], dma=True)
                p.op("pe", lambda e, xf=xf: [
                    e.matmul(out=pS1[:, 0:8], lhsT=xf[:, kc * 128:(kc + 1) * 128],
                             rhs=rw_sb[:, hl * 64 + kc * 8:hl * 64 + (kc + 1) * 8],
                             start=(kc == 0 and hl == 0), stop=False)
                    for hl in range(2) for kc in range(NKC)][-1],
                    reads=[bxf, b_rw], writes=[bufS1])
                p.op("pe", lambda e: e.matmul(out=pS1[:, 0:8], lhsT=cs["onesrow"][:, :],
                                              rhs=rb_b[:, :], start=False, stop=True),
                     reads=[b_cs["onesrow"], b_rb], writes=[bufS1])
                p.op("dve", lambda e: e.reduce_max(m8a[:, :], pS1[:, 0:8], axis=AX),
                     reads=[bufS1], writes=[b_m8a])
                p.op("dve", lambda e: e.tensor_scalar(
                    out=m8a[:, :], in0=m8a[:, :], scalar1=-1.0, scalar2=None, op0=ALU.mult),
                    writes=[b_m8a])
                p.op("dve", lambda e: e.tensor_copy(sm8[:, :], pS1[:, 0:8]),
                     reads=[bufS1], writes=[b_sm8])
                p.op("act", lambda e: e.activation(out=sm8[:, :], in_=sm8[:, :],
                                                   func=AF.Exp, bias=m8a[:, :]),
                     reads=[b_m8a], writes=[b_sm8])
                p.op("dve", lambda e: e.tensor_scalar(
                    out=dump[:, 0:8], in0=sm8[:, :], scalar1=1.0, scalar2=None,
                    op0=ALU.mult, op1=ALU.add, accum_out=m8a[:, :]),
                    reads=[b_sm8], writes=[b_dump, b_m8a])
                p.op("dve", lambda e: e.reciprocal(m8a[:, :], m8a[:, :]), writes=[b_m8a])
                p.op("dve", lambda e: e.tensor_scalar(
                    out=sm8[:, :], in0=sm8[:, :], scalar1=m8a[:, :], scalar2=None, op0=ALU.mult),
                    reads=[b_m8a], writes=[b_sm8])
                p.op("dve", lambda e: e.reduce_max(m8a[:, :], sm8[:, :], axis=AX),
                     reads=[b_sm8], writes=[b_m8a])
                p.op("dve", lambda e: e.tensor_scalar(
                    out=m8b[:, :], in0=sm8[:, :], scalar1=m8a[:, :], scalar2=None, op0=ALU.is_ge),
                    reads=[b_sm8, b_m8a], writes=[b_m8b])
                p.op("dve", lambda e: e.scalar_tensor_tensor(
                    out=m8b[:, :], in0=m8b[:, :], scalar=-4.0, in1=sm8[:, :],
                    op0=ALU.mult, op1=ALU.add),
                    reads=[b_sm8], writes=[b_m8b])
                p.op("dve", lambda e: e.reduce_max(m8a[:, :], m8b[:, :], axis=AX),
                     reads=[b_m8b], writes=[b_m8a])
                p.op("dve", lambda e: e.scalar_tensor_tensor(
                    out=g8[:, :], in0=sm8[:, :], scalar=m8a[:, :], in1=sm8[:, :],
                    op0=ALU.is_ge, op1=ALU.mult),
                    reads=[b_sm8, b_m8a], writes=[b_g8])
                p.op("pe", lambda e: e.transpose(out=pS2[0:8, 0:128], in_=g8[:, :],
                                                 identity=cs["ident_f32"][:, :]),
                     reads=[b_g8, b_cs["ident_f32"]], writes=[bufS2])
                p.op("dve", lambda e, gc0=gc0: e.tensor_copy(
                    gthi[:, gc0:gc0 + 128], pS2[0:8, 0:128]),
                    reads=[bufS2], writes=[b_gthi])
                p.op("dve", lambda e, gc0=gc0: e.tensor_tensor(
                    out=gtlo[:, gc0:gc0 + 128], in0=pS2[0:8, 0:128],
                    in1=gthi[:, gc0:gc0 + 128], op=ALU.subtract),
                    reads=[bufS2, b_gthi], writes=[b_gtlo])
        tap("xn_fm", lambda: xnscr, [128, NKC * 2 * T], BF16, deps=list(b_xnscr.wl))
        tap("g8", lambda: gthi, [8, 2 * T], BF16, deps=list(b_gthi.wl))
        p.barrier()

        xng = [msb(f"xng{i}", [128, NKC * 512], BF16) for i in range(2)]
        b_xng = [Buf(), Buf()]
        wstream = msb("wstream", [128, 8192], BF16)
        b_wh = [Buf(), Buf()]
        b_wproj = Buf()
        biasT = msb("biasT", [128, 17408], BF16)
        b_biasT = [Buf() for _ in range(16)]
        offJ = [2048 * J - 64 * J * (J - 1) for J in range(16)]
        ls1_ = [msb(f"ls1_{i}", [128, 1024], F32) for i in range(2)]
        b_ls1_ = [Buf(), Buf()]
        ls1, b_ls1 = ls1_[0], b_ls1_[0]
        bufY0, bufY1 = Buf(), Buf()
        q_ext = [msb(f"q_ext{i}", [128, T], BF16) for i in range(2)]
        k_ext = [msb(f"k_ext{i}", [128, T], BF16) for i in range(2)]
        protc = [msb(f"protc{i}", [128, T], BF16) for i in range(2)]
        p64x2 = [msb(f"p64x2{i}", [128, T], BF16) for i in range(2)]
        v_ext = [msb(f"v_ext{i}", [128, 16 * 68], BF16) for i in range(2)]
        b_qe = [Buf(), Buf()]
        b_ke = [Buf(), Buf()]
        b_pr = [Buf(), Buf()]
        b_p6 = [Buf(), Buf()]
        b_ve = [Buf(), Buf()]
        Tf = [msb(f"Tf{i}", [128, 512], F32) for i in range(3)]
        b_Tf = [Buf(), Buf(), Buf()]
        sqb = [msb(f"sqb{i}", [128, 512], BF16) for i in range(3)]
        b_sq = [Buf(), Buf(), Buf()]
        v_fm = msb("v_fm", [128, T], BF16)
        b_vfm = Buf()
        ET = [msb(f"ET{i}", [128, 1024], BF16) for i in range(2)]
        b_ET = [Buf(), Buf()]
        csth = msb("csth", [16, 512], BF16)
        cstl = msb("cstl", [16, 512], BF16)
        b_csth, b_cstl = Buf(), Buf()
        rows65 = [msb(f"rows65_{i}", [65, 512], F32) for i in range(2)]
        b_rows65 = [Buf(), Buf()]
        rstd65 = [msb(f"rstd65_{i}", [65, 512], BF16) for i in range(3)]
        b_rstd65 = [Buf(), Buf(), Buf()]
        rowkq = [msb(f"rowkq{i}", [1, 512], F32) for i in range(2)]
        b_rowkq = [Buf(), Buf()]
        kmx = msb("kmx", [1, 1], F32)
        kmg = msb("kmg", [1, 1], F32)
        b_kmx, b_kmg = Buf(), Buf()
        m_row = msb("m_row", [1, T], BF16)
        b_mrow = Buf()
        rzrow = msb("rzrow", [1, 1024], F32)
        rzhi = msb("rzhi", [1, 1024], BF16)
        rzlo = msb("rzlo", [1, 1024], BF16)
        b_rz, b_rzhi, b_rzlo = Buf(), Buf(), Buf()
        rzb_sb = msb("rzb_sb", [64, 1024], F32)
        b_rzb = Buf()
        ystage = [rows65[i].bitcast(BF16)[0:64, 0:1024] for i in range(2)]
        b_yst = b_rows65
        ao_st = [p64x2[0].bitcast(F32)[:, 0:512]]
        b_ao = [b_p6[0]]

        for i in range(2):
            p.op("dve", lambda e, i=i: e.memset(q_ext[i][:, :], 0.0), writes=[b_qe[i]])
            p.op("sp", lambda e, i=i: e.dma_start(out=k_ext[i][0:64, :], in_=cst_d["kconst"]),
                 writes=[b_ke[i]], dma=True)
            p.op("dve", lambda e, i=i: e.memset(v_ext[i][:, :], 0.0), writes=[b_ve[i]])
            p.op("dve", lambda e, i=i: e.memset(
                v_ext[i].rearrange("p (b c) -> p b c", c=68)[:, :, 64:65], 1.0),
                writes=[b_ve[i]])

        PSMALL = [(pS1, bufS1), (pS2, bufS2)]

        for s in seqs:
            sc0 = s * T
            def emit_wload(hp_):
                wof_ = (hp_ % 2) * 4096
                bw_ = b_wh[hp_ % 2]
                for ti, nm in enumerate(("wkT", "wqT", "wpT", "wvT")):
                    p.op("sp", lambda e, nm=nm, ti=ti, hp_=hp_, wof_=wof_: e.dma_start(
                        out=wstream[:, wof_ + ti * 1024: wof_ + (ti + 1) * 1024].rearrange("p (kc m) -> p kc m", kc=NKC)[:, :, :],
                        in_=wscr[nm].rearrange("(kc p) m -> p kc m", p=128)[:, :, hp_ * 128:(hp_ + 1) * 128]),
                        writes=[bw_], reads=[b_wscr[nm]],
                        deps=([b_wproj.w] if b_wproj.w is not None else []) + list(b_wproj.r), dma=True)

            def emit_xload(g_):
                xn_g_, bxn_ = xng[g_ % 2], b_xng[g_ % 2]
                c0_ = g_ * 512
                p.op("sp", lambda e, sc0=sc0, c0_=c0_, xn_g_=xn_g_: e.dma_start(
                    out=xn_g_.rearrange("p (kc c) -> p kc c", kc=NKC)[:, :, :],
                    in_=xnscr.rearrange("p (kc t) -> p kc t", kc=NKC)[:, :, sc0 + c0_:sc0 + c0_ + 512]),
                    reads=[b_xnscr], writes=[bxn_], dma=True)

            for hp in range(heads // 2):
                wpar = hp % 2
                wof = wpar * 4096
                bw = b_wh[wpar]
                if hp == 0:
                    emit_wload(0)

                for g in range(4):
                    c0 = g * 512
                    xpar = g % 2
                    xn_g, bxn = xng[xpar], b_xng[xpar]
                    if not (hp > 0 and g == 0):
                        emit_xload(g)
                    pbank = {"k": (pA[:, 0:512], bufA0), "q": (pA[:, 512:1024], bufA1),
                             "p": (pB[:, 0:512], bufB0), "v": (pB[:, 512:1024], bufB1)}
                    sbank = {"k": (pS1, bufS1), "q": (pS2, bufS2), "p": (pS1, bufS1)}
                    tfm = {"k": 0, "q": 1, "p": 2}
                    if attn_lvl < 0.2:
                        continue
                    for ti, tn in enumerate(("k", "q", "p", "v")):
                        pp_, bpp = pbank[tn]
                        wslice = lambda kc, ti=ti, wof=wof: wstream[:, wof + ti * 1024 + kc * 128: wof + ti * 1024 + (kc + 1) * 128]
                        p.op("pe", lambda e, pp_=pp_, wslice=wslice, xn_g=xn_g: [
                            e.matmul(out=pp_, lhsT=wslice(kc), rhs=xn_g[:, kc * 512:(kc + 1) * 512],
                                     start=(kc == 0), stop=(kc == NKC - 1))
                            for kc in range(NKC)][-1],
                            reads=[bw, bxn], writes=[bpp])
                        if tn == "v":
                            p.op("dve", lambda e, pp_=pp_, c0=c0: e.tensor_copy(
                                v_fm[:, c0:c0 + 512], pp_),
                                reads=[bpp], writes=[b_vfm])
                        else:
                            tf, btf = Tf[tfm[tn]], b_Tf[tfm[tn]]
                            p.op("dve", lambda e, pp_=pp_, tf=tf: e.tensor_copy(
                                tf[:, :], pp_),
                                reads=[bpp], writes=[btf])
                            sq_h, bsq_h = sqb[tfm[tn]], b_sq[tfm[tn]]
                            p.op("act", lambda e, tf=tf, sq_h=sq_h: e.activation(
                                out=sq_h[:, :], in_=tf[:, :], func=AF.Square),
                                reads=[btf], writes=[bsq_h])
                    if attn_lvl < 0.25:
                        continue
                    for tn in ("k", "q", "p"):
                        ps_, bps_ = sbank[tn]
                        sq_h, bsq_h = sqb[tfm[tn]], b_sq[tfm[tn]]
                        p.op("pe", lambda e, ps_=ps_, sq_h=sq_h: e.matmul(
                            out=ps_[0:65, :], lhsT=cs["onespair"][:, :], rhs=sq_h[:, :],
                            start=True, stop=True),
                            reads=[bsq_h, b_cs["onespair"]], writes=[bps_])
                        r65, br65 = rows65[tfm[tn] % 2], b_rows65[tfm[tn] % 2]
                        rs65, brs65 = rstd65[tfm[tn]], b_rstd65[tfm[tn]]
                        epsc = 0 if tn in ("k", "q") else 1
                        p.op("dve", lambda e, ps_=ps_, r65=r65: e.tensor_copy(
                            r65[:, :], ps_[0:65, :]),
                            reads=[bps_], writes=[br65])
                        p.op("act", lambda e, epsc=epsc, r65=r65: e.activation(
                            out=r65[:, :], in_=r65[:, :],
                            func=AF.Ln, scale=1.0 / 64, bias=cs["fconst"][0:65, epsc:epsc + 1]),
                            reads=[b_cs["fconst"]], writes=[br65])
                        if tn == "q":
                            p.op("act", lambda e, r65=r65, rs65=rs65: e.activation(
                                out=rs65[:, :], in_=r65[:, :],
                                func=AF.Exp, scale=-0.5, bias=cs["fconst"][0:65, 2:3]),
                                reads=[br65, b_cs["fconst"]], writes=[brs65])
                        else:
                            p.op("act", lambda e, r65=r65, rs65=rs65: e.activation(
                                out=rs65[:, :], in_=r65[:, :], func=AF.Exp, scale=-0.5),
                                reads=[br65], writes=[brs65])
                    for tn in ("k", "q", "p"):
                        ps_, bps_ = sbank[tn]
                        tf, btf = Tf[tfm[tn]], b_Tf[tfm[tn]]
                        rs65, brs65 = rstd65[tfm[tn]], b_rstd65[tfm[tn]]
                        for hh in range(2):
                            p.op("pe", lambda e, hh=hh, ps_=ps_, rs65=rs65,
                                 lrow={"k": krw_b, "q": qrw_b, "p": onesp64}[tn]: e.matmul(
                                out=ps_[64 * hh:64 * hh + 64, :],
                                lhsT=lrow[64 * hh:64 * hh + 1, :],
                                rhs=rs65[64 * hh:64 * hh + 1, :],
                                start=True, stop=True),
                                reads=[brs65,
                                       {"k": b_krw, "q": b_qrw, "p": b_cs["ones1x64"]}[tn]],
                                writes=[bps_])
                            if tn in ("k", "q"):
                                dst, bdst = (k_ext[hh], b_ke[hh]) if tn == "k" else (q_ext[hh], b_qe[hh])
                                p.op("dve", lambda e, tf=tf, hh=hh, c0=c0, dst=dst, ps_=ps_: e.tensor_tensor(
                                    out=dst[64:128, c0:c0 + 512], in0=tf[64 * hh:64 * hh + 64, :],
                                    in1=ps_[64 * hh:64 * hh + 64, :], op=ALU.mult),
                                    reads=[btf, bps_], writes=[bdst])
                            else:
                                p.op("dve", lambda e, tf=tf, hh=hh, c0=c0, ps_=ps_: e.tensor_tensor(
                                    out=ls1[0:64, 0:512], in0=tf[64 * hh:64 * hh + 64, :],
                                    in1=ps_[64 * hh:64 * hh + 64, :], op=ALU.mult),
                                    reads=[btf, bps_], writes=[b_ls1])
                                pr, bpr = protc[hh], b_pr[hh]
                                p.op("dve", lambda e, c0=c0, pr=pr: e.tensor_tensor(
                                    out=pr[0:64, c0:c0 + 512], in0=ls1[0:64, 0:512],
                                    in1=cs["cosb64"][:, c0:c0 + 512], op=ALU.mult),
                                    reads=[b_ls1, b_cs["cosb64"]], writes=[bpr])
                                p.op("pool", lambda e, c0=c0, pr=pr: e.tensor_tensor(
                                    out=pr[64:96, c0:c0 + 512], in0=ls1[32:64, 0:512],
                                    in1=cs["sinb64"][32:64, c0:c0 + 512], op=ALU.mult),
                                    reads=[b_ls1, b_cs["sinb64"]], writes=[bpr])
                                p.op("pool", lambda e, c0=c0, pr=pr: e.tensor_tensor(
                                    out=pr[96:128, c0:c0 + 512], in0=ls1[0:32, 0:512],
                                    in1=cs["sinb64"][0:32, c0:c0 + 512], op=ALU.mult),
                                    reads=[b_ls1, b_cs["sinb64"]], writes=[bpr])
                                p6, bp6 = p64x2[hh], b_p6[hh]
                                p.op("dve", lambda e, tf=tf, hh=hh, c0=c0, p6=p6: e.tensor_scalar(
                                    out=p6[0:64, c0:c0 + 512], in0=tf[64 * hh:64 * hh + 64, :],
                                    scalar1=1.0 / 64, scalar2=None, op0=ALU.mult),
                                    reads=[btf], writes=[bp6])
                                p.op("pool", lambda e, tf=tf, hh=hh, c0=c0, p6=p6: e.tensor_scalar(
                                    out=p6[64:128, c0:c0 + 512], in0=tf[64 * hh:64 * hh + 64, :],
                                    scalar1=1.0 / 64, scalar2=None, op0=ALU.mult),
                                    reads=[btf], writes=[bp6])
                for jt in range(16 if attn_lvl >= 1 else 0):
                    pT = pS2.bitcast(BF16)
                    p.op("pe", lambda e, jt=jt, pT=pT: e.transpose(
                        out=pT[:, 0:128], in_=v_fm[:, jt * 128:(jt + 1) * 128],
                        identity=cs["ident_bf"][:, :]),
                        reads=[b_vfm, b_cs["ident_bf"]], writes=[bufS2])
                    for hh in range(2):
                        p.op("dve", lambda e, jt=jt, hh=hh, pT=pT: e.tensor_copy(
                            v_ext[hh][:, jt * 68:jt * 68 + 64], pT[:, 64 * hh:64 * hh + 64]),
                            reads=[bufS2], writes=[b_ve[hh]])

                if hp + 1 < heads // 2:
                    emit_wload(hp + 1)
                    emit_xload(0)

                for hh in range(2):
                    h = hp * 2 + hh
                    slope = 2.0 ** (-(h + 1) / 2.0)
                    qe, ke, pr, p6, ve = q_ext[hh], k_ext[hh], protc[hh], p64x2[hh], v_ext[hh]
                    bqe, bke, bpr, bp6, bve = b_qe[hh], b_ke[hh], b_pr[hh], b_p6[hh], b_ve[hh]

                    msq = [(sqb[0], b_sq[0]), (sqb[1], b_sq[1]), (sqb[2], b_sq[2]),
                           (ET[0], b_ET[0])]
                    mps = [(pY[0:1, 0:512], bufY0), (pY[0:1, 512:1024], bufY1),
                           (pS1[0:1, 0:512], bufS1), (pS2[0:1, 0:512], bufS2)]
                    for src, bsrc in ((ke, bke), (qe, bqe)):
                        for g in range(4):
                            sq_t, bsq_t = msq[g]
                            p.op("act", lambda e, src=src, g=g, sq_t=sq_t: e.activation(
                                out=sq_t[0:64, 0:512], in_=src[64:128, g * 512:g * 512 + 512],
                                func=AF.Square),
                                reads=[bsrc], writes=[bsq_t])
                        for g in range(4):
                            sq_t, bsq_t = msq[g]
                            pr_, bpr_ = mps[g]
                            p.op("pe", lambda e, sq_t=sq_t, pr_=pr_: e.matmul(
                                out=pr_, lhsT=cs["ones64"][:, :], rhs=sq_t[0:64, 0:512],
                                start=True, stop=True),
                                reads=[bsq_t, b_cs["ones64"]], writes=[bpr_])
                        if src is ke:
                            for g in range(4):
                                pr_, bpr_ = mps[g]
                                if g == 0:
                                    p.op("dve", lambda e, pr_=pr_: e.reduce_max(
                                        kmx[:, :], pr_, axis=AX),
                                        reads=[bpr_], writes=[b_kmx])
                                else:
                                    p.op("dve", lambda e, pr_=pr_: e.reduce_max(
                                        kmg[:, :], pr_, axis=AX),
                                        reads=[bpr_], writes=[b_kmg])
                                    p.op("dve", lambda e: e.tensor_tensor(
                                        out=kmx[:, :], in0=kmg[:, :], in1=kmx[:, :], op=ALU.max),
                                        reads=[b_kmg], writes=[b_kmx])
                            p.op("act", lambda e: e.activation(
                                out=kmx[:, :], in_=kmx[:, :], func=AF.Ln),
                                writes=[b_kmx])
                            p.op("dve", lambda e: e.tensor_scalar(
                                out=kmx[:, :], in0=kmx[:, :], scalar1=0.5, scalar2=None,
                                op0=ALU.mult),
                                writes=[b_kmx])
                        else:
                            for g in range(4):
                                c0 = g * 512
                                pr_, bpr_ = mps[g]
                                p.op("dve", lambda e, c0=c0, pr_=pr_: e.tensor_copy(
                                    m_row[:, c0:c0 + 512], pr_),
                                    reads=[bpr_], writes=[b_mrow])
                                p.op("act", lambda e, c0=c0: e.activation(
                                    out=m_row[:, c0:c0 + 512], in_=m_row[:, c0:c0 + 512],
                                    func=AF.Ln),
                                    writes=[b_mrow])
                                p.op("act", lambda e, c0=c0: e.activation(
                                    out=m_row[:, c0:c0 + 512], in_=m_row[:, c0:c0 + 512],
                                    func=AF.Exp, scale=0.5, bias=kmx[:, :]),
                                    reads=[b_kmx], writes=[b_mrow])
                                p.op("dve", lambda e, c0=c0: e.tensor_scalar(
                                    out=m_row[:, c0:c0 + 512], in0=m_row[:, c0:c0 + 512],
                                    scalar1=-MSHIFT, scalar2=None, op0=ALU.add),
                                    writes=[b_mrow])
                    pc = 0
                    for J in range(16 if attn_lvl >= 2 else 0):
                        w_J = T - 128 * J
                        for (b0, b1) in splits(0, w_J, 1024):
                            lsx, blsx = ls1_[pc % 2], b_ls1_[pc % 2]
                            pc += 1
                            for (a0, a1) in splits(b0, b1):
                                ps, bps = PSMALL[(a0 // 512) % 2]
                                p.op("pe", lambda e, J=J, a0=a0, a1=a1, ps=ps, pr=pr, p6=p6: e.matmul(
                                    out=ps[:, 0:a1 - a0], lhsT=pr[:, J * 128:(J + 1) * 128],
                                    rhs=p6[:, J * 128 + a0:J * 128 + a1], start=True, stop=True),
                                    reads=[bpr, bp6], writes=[bps])
                                p.op("act", lambda e, a0=a0, a1=a1, b0=b0, ps=ps, lsx=lsx: e.activation(
                                    out=lsx[:, a0 - b0:a1 - b0], in_=ps[:, 0:a1 - a0],
                                    func=AF.Exp, scale=-1.0),
                                    reads=[bps], writes=[blsx])
                            p.op("act", lambda e, b0=b0, b1=b1, lsx=lsx: e.activation(
                                out=lsx[:, 0:b1 - b0], in_=lsx[:, 0:b1 - b0], func=AF.Ln,
                                bias=cs["fconst"][:, 3:4]),
                                reads=[b_cs["fconst"]], writes=[blsx])
                            p.op("pool", lambda e, J=J, b0=b0, b1=b1, slope=slope, lsx=lsx: e.tensor_scalar(
                                out=biasT[:, offJ[J] + b0:offJ[J] + b1], in0=lsx[:, 0:b1 - b0],
                                scalar1=-slope, scalar2=0.0, op0=ALU.mult, op1=ALU.add),
                                reads=[blsx], writes=[b_biasT[J]])
                        p.op("dve", lambda e, J=J: e.tensor_tensor(
                            out=biasT[:, offJ[J]:offJ[J] + 128], in0=biasT[:, offJ[J]:offJ[J] + 128],
                            in1=cs["utmask"][:, :], op=ALU.mult),
                            reads=[b_cs["utmask"]], writes=[b_biasT[J]])

                    for gq in range(4 if attn_lvl >= 3 else 0):
                        i0g = gq * 512
                        Jmax = 4 * gq + 3
                        p.op("pe", lambda e, i0g=i0g, Jmax=Jmax: [
                            e.matmul(out=pS1[0:16, max(0, 128 * J - i0g):512],
                                     lhsT=cs["ecols"][:, 16 * J:16 * J + 16],
                                     rhs=biasT[:, offJ[J] + max(0, i0g - 128 * J):offJ[J] + i0g + 512 - 128 * J],
                                     start=(J == 0), stop=(J == Jmax))
                            for J in range(Jmax + 1)][-1],
                            reads=[b_cs["ecols"]] + [b_biasT[J] for J in range(Jmax + 1)],
                            writes=[bufS1])
                        p.op("dve", lambda e: e.tensor_copy(csth[:, :], pS1[0:16, :]),
                             reads=[bufS1], writes=[b_csth])
                        p.op("dve", lambda e: e.tensor_tensor(
                            out=cstl[:, :], in0=pS1[0:16, :], in1=csth[:, :], op=ALU.subtract),
                            reads=[bufS1, b_csth], writes=[b_cstl])
                        p.op("pe", lambda e, i0g=i0g: [
                            e.matmul(out=pS2[0:16, :], lhsT=cs["l16"][:, :], rhs=csth[:, :],
                                     start=True, stop=False),
                            e.matmul(out=pS2[0:16, :], lhsT=cs["l16"][:, :], rhs=cstl[:, :],
                                     start=False, stop=False),
                            e.matmul(out=pS2[0:16, :], lhsT=cs["neg1"][:, :],
                                     rhs=m_row[:, i0g:i0g + 512], start=False, stop=True)][-1],
                            reads=[b_csth, b_cstl, b_mrow, b_cs["l16"], b_cs["neg1"]],
                            writes=[bufS2])
                        p.op("dve", lambda e, i0g=i0g, qe=qe: e.tensor_copy(
                            qe[0:16, i0g:i0g + 512], pS2[0:16, :]),
                            reads=[bufS2], writes=[bqe])
                        p.op("dve", lambda e, i0g=i0g, qe=qe: e.tensor_tensor(
                            out=qe[32:48, i0g:i0g + 512], in0=pS2[0:16, :],
                            in1=qe[0:16, i0g:i0g + 512], op=ALU.subtract),
                            reads=[bufS2], writes=[bqe])

                    SCF = [pr.bitcast(F32), p6.bitcast(F32)]
                    b_SCF = [bpr, bp6]
                    for H in range(2 if attn_lvl >= 4 else 0):
                        ib = H * 1024
                        Jlist = [J for J in range(16) if 128 * J < ib + 1024]
                        pend_mm3 = []
                        for J in Jlist:
                            i0 = max(128 * J, ib)
                            i1 = ib + 1024
                            o0 = i0 - ib
                            pX = pA if J % 2 == 0 else pB
                            bpXb = [bufA0, bufA1] if J % 2 == 0 else [bufB0, bufB1]
                            et, bet = ET[J % 2], b_ET[J % 2]
                            scf, bscf = SCF[J % 2], b_SCF[J % 2]
                            chain = []
                            for (a0, a1) in splits(o0, 1024):
                                p.op("pe", lambda e, J=J, a0=a0, a1=a1, ib=ib, pX=pX, ke=ke, qe=qe: e.matmul(
                                    out=pX[:, a0:a1], lhsT=ke[:, J * 128:(J + 1) * 128],
                                    rhs=qe[:, ib + a0:ib + a1], start=True, stop=False),
                                    reads=[bke, bqe], writes=[bpXb[a0 // 512]])
                                p.op("pe", lambda e, J=J, a0=a0, a1=a1, ib=ib, pX=pX: e.matmul(
                                    out=pX[:, a0:a1], lhsT=cs["slmat"][:, :],
                                    rhs=biasT[:, offJ[J] + ib + a0 - 128 * J:offJ[J] + ib + a1 - 128 * J],
                                    start=False, stop=True),
                                    reads=[b_biasT[J], b_cs["slmat"]], writes=[bpXb[a0 // 512]])
                                p.op("dve", lambda e, a0=a0, a1=a1, o0=o0, pX=pX, scf=scf: e.tensor_copy(
                                    scf[:, a0 - o0:a1 - o0], pX[:, a0:a1]),
                                    reads=[bpXb[a0 // 512]], writes=[bscf])
                                p.op("act", lambda e, a0=a0, a1=a1, o0=o0, et=et, scf=scf: e.activation(
                                    out=et[:, a0 - o0:a1 - o0], in_=scf[:, a0 - o0:a1 - o0],
                                    func=AF.Exp),
                                    reads=[bscf], writes=[bet])
                                if 128 * J >= ib and a0 == o0:
                                    p.op("dve", lambda e, et=et: e.tensor_tensor(
                                        out=et[:, 0:128], in0=et[:, 0:128],
                                        in1=cs["utmask"][:, :], op=ALU.mult),
                                        reads=[b_cs["utmask"]], writes=[bet])
                                lastJ_for_bank = max(Jx for Jx in Jlist if 128 * Jx < ib + (a0 // 512 + 1) * 512)
                                chain.append((J, a0, a1, i0, et, J == 0, J == lastJ_for_bank))
                            pend_mm3.append((chain, bet, bve))
                            if len(pend_mm3) >= 2:
                                ch, betx, bvex = pend_mm3.pop(0)
                                for (Jx, a0, a1, i0x, etx, st, sp_) in ch:
                                    p.op("pe", lambda e, Jx=Jx, a0=a0, a1=a1, i0x=i0x, etx=etx, ve=ve,
                                         st=st, sp_=sp_, ib=ib: e.matmul(
                                        out=pY[0:65, a0:a1], lhsT=ve[:, Jx * 68:Jx * 68 + 65],
                                        rhs=etx[:, ib + a0 - i0x:ib + a1 - i0x], start=st, stop=sp_),
                                        reads=[betx, bvex], writes=[bufY0 if a0 < 512 else bufY1])
                        for (ch, betx, bvex) in pend_mm3:
                            for (Jx, a0, a1, i0x, etx, st, sp_) in ch:
                                p.op("pe", lambda e, Jx=Jx, a0=a0, a1=a1, i0x=i0x, etx=etx, ve=ve,
                                     st=st, sp_=sp_, ib=ib: e.matmul(
                                    out=pY[0:65, a0:a1], lhsT=ve[:, Jx * 68:Jx * 68 + 65],
                                    rhs=etx[:, ib + a0 - i0x:ib + a1 - i0x], start=st, stop=sp_),
                                    reads=[betx, bvex], writes=[bufY0 if a0 < 512 else bufY1])
                        p.op("dve", lambda e: e.reciprocal(rzrow[:, :], pY[64:65, :]),
                             reads=[bufY0, bufY1], writes=[b_rz])
                        p.op("dve", lambda e: e.tensor_copy(rzhi[:, :], rzrow[:, :]),
                             reads=[b_rz], writes=[b_rzhi])
                        p.op("dve", lambda e: e.tensor_tensor(
                            out=rzlo[:, :], in0=rzrow[:, :], in1=rzhi[:, :], op=ALU.subtract),
                            reads=[b_rz, b_rzhi], writes=[b_rzlo])
                        for (a0, a1) in ((0, 512), (512, 1024)):
                            ps, bps = PSMALL[(a0 // 512) % 2]
                            p.op("pe", lambda e, a0=a0, a1=a1, ps=ps: [
                                e.matmul(out=ps[0:64, :], lhsT=cs["ones1x64"][:, :],
                                         rhs=rzhi[:, a0:a1], start=True, stop=False),
                                e.matmul(out=ps[0:64, :], lhsT=cs["ones1x64"][:, :],
                                         rhs=rzlo[:, a0:a1], start=False, stop=True)][-1],
                                reads=[b_rzhi, b_rzlo, b_cs["ones1x64"]], writes=[bps])
                            p.op("dve", lambda e, a0=a0, a1=a1, ps=ps: e.tensor_copy(
                                rzb_sb[:, a0:a1], ps[0:64, :]),
                                reads=[bps], writes=[b_rzb])
                        if attn_lvl < 5:
                            continue
                        yst, byst = ystage[H], b_yst[H]
                        p.op("dve", lambda e, yst=yst: e.tensor_tensor(
                            out=yst[:, :], in0=pY[0:64, :], in1=rzb_sb[:, :], op=ALU.mult),
                            reads=[bufY0, bufY1, b_rzb], writes=[byst])
                        p.op("sp", lambda e, s=s, h=h, H=H, yst=yst: e.dma_start(
                            out=yscr[s, h, :, H * 1024:(H + 1) * 1024], in_=yst[:, :]),
                            reads=[byst], writes=[b_yscr], dma=True)
                    if ("scoresT" in tap_set) and h == 0 and s == 0:
                        tap("scoresT", lambda: ET[1], [128, 1024], BF16, deps=[b_ET[1].w])

            if attn_lvl < 6:
                continue
            p.op("sp", lambda e: e.dma_start(
                out=wstream.rearrange("p (kc m) -> p kc m", kc=NKC)[:, :, :],
                in_=wscr["wprojT"].rearrange("(kc p) m -> p kc m", p=128)[:, :, :]),
                writes=[b_wproj], reads=[b_wscr["wprojT"]],
                deps=[ev for b in b_wh for ev in ([b.w] + b.r)], dma=True)
            for g in range(4):
                for half2 in range(2):
                    p.op("sp", lambda e, s=s, g=g, half2=half2: e.dma_start(
                        out=protc[half2].rearrange("p (kc t) -> p kc t", kc=4)[:, :, :],
                        in_=yscr[s].rearrange("(k g2) d t -> (g2 d) k t", g2=2)[:, half2 * 4:half2 * 4 + 4, g * 512:(g + 1) * 512]),
                        reads=[b_yscr], writes=[b_pr[half2]], dma=True)

                def ybv(kc):
                    return protc[kc // 4][:, (kc % 4) * 512:(kc % 4 + 1) * 512]

                for m in range(NKC):
                    ps, bps = PSMALL[m % 2]
                    p.op("pe", lambda e, m=m, ps=ps, ybv=ybv: [
                        e.matmul(out=ps[:, :], lhsT=wstream[:, kc * 1024 + m * 128:kc * 1024 + (m + 1) * 128],
                                 rhs=ybv(kc),
                                 start=(kc == 0), stop=(kc == NKC - 1))
                        for kc in range(NKC)][-1],
                        reads=[b_wproj, b_pr[0], b_pr[1]], writes=[bps])
                    ao, bao = ao_st[0], b_ao[0]
                    p.op("dve", lambda e, ps=ps, ao=ao: e.tensor_copy(ao[:, :], ps[:, :]),
                         reads=[bps], writes=[bao])
                    p.op("sp", lambda e, m=m, s=s, g=g, ao=ao: e.dma_start(
                        out=accd[m, :, s * T + g * 512:s * T + (g + 1) * 512], in_=ao[:, :]),
                        reads=[bao], writes=[b_accd], dma=True)
        tap("attn_acc", lambda: accd, [NKC, 128, 2 * T], F32, deps=list(b_accd.wl))

        p.barrier()
        main_es.close()
        ffn_es = ExitStack()
        fsb = lambda name, shp, dt: ffn_es.enter_context(nc.sbuf_tensor(name, shp, dt)).ap()
        xnf = fsb("xnf", [128, NKC * 2 * T], BF16)
        b_xnf = Buf()
        gsl = [fsb(f"gsl{i}", [128, NKC * 512], BF16) for i in range(2)]
        usl = [fsb(f"usl{i}", [128, NKC * 512], BF16) for i in range(2)]
        dsl = [fsb(f"dsl{i}", [128, 4 * 1024], BF16) for i in range(2)]
        b_gsl = [Buf(), Buf()]
        b_usl = [Buf(), Buf()]
        b_dsl = [Buf(), Buf()]
        gub_s = fsb("gub_s", [128, 32], F32)
        gub1 = fsb("gub1", [128, 32], F32)
        b_gub, b_gub1 = Buf(), Buf()
        gwb = fsb("gwb", [128, 4 * 1024], F32)
        b_gwb = Buf()
        act_sb = [fsb(f"act_sb{i}", [128, 4 * 1024], BF16) for i in range(2)]
        b_act = [Buf(), Buf()]
        glu_t = [fsb(f"glu_t{i}", [128, 1024], F32) for i in range(2)]
        upt = [fsb(f"upt{i}", [128, 1024], F32) for i in range(2)]
        b_glu = [Buf(), Buf()]
        b_upt = [Buf(), Buf()]
        dstage = [fsb(f"dstage{i}", [128, 1024], F32) for i in range(2)]
        b_dst = [Buf(), Buf()]

        p.op("sp", lambda e: e.dma_start(out=xnf[:, :], in_=xnscr),
             reads=[b_xnscr], writes=[b_xnf], dma=True)

        def xnfv(kc, c0, c1):
            return xnf[:, kc * 2 * T + c0:kc * 2 * T + c1]

        it_list = [(e_, iq) for e_ in range(experts) for iq in range(4)]
        for it_idx, (ex, iq) in enumerate(it_list):
            par = it_idx % 2
            bg, bu, bd = b_gsl[par], b_usl[par], b_dsl[par]
            gs, us, ds = gsl[par], usl[par], dsl[par]
            p.op("sp", lambda e, ex=ex, iq=iq, gs=gs: e.dma_start(
                out=gs.rearrange("p (kc m) -> p kc m", kc=NKC)[:, :, :],
                in_=gupb[ex].rearrange("(kc p) m -> p kc m", p=128)[:, :, iq * 512:(iq + 1) * 512]),
                writes=[bg], reads=[b_gupb], dma=True)
            p.op("sp", lambda e, ex=ex, iq=iq, us=us: e.dma_start(
                out=us.rearrange("p (kc m) -> p kc m", kc=NKC)[:, :, :],
                in_=gupb[ex].rearrange("(kc p) m -> p kc m", p=128)[:, :, I + iq * 512:I + (iq + 1) * 512]),
                writes=[bu], reads=[b_gupb], dma=True)
            p.op("sp", lambda e, ex=ex, iq=iq, ds=ds: e.dma_start(
                out=ds.rearrange("p (ic h) -> p ic h", ic=4)[:, :, :],
                in_=dwnb[ex, iq * 512:(iq + 1) * 512, :].rearrange("(ic p) h -> p ic h", p=128)[:, :, :]),
                writes=[bd], reads=[b_dwnb], dma=True)
            if iq == 0:
                p.op("pool", lambda e, ex=ex: e.dma_start(out=gub_s[:, :], in_=gub_d[ex]),
                     writes=[b_gub], dma=True)
                p.op("dve", lambda e: e.tensor_scalar(
                    out=gub1[:, :], in0=gub_s[:, :], scalar1=1.0, scalar2=None, op0=ALU.add),
                    reads=[b_gub], writes=[b_gub1])
                for gg in range(NKC):
                    ps, bps = PSMALL[gg % 2]
                    p.op("pe", lambda e, ex=ex, gg=gg, ps=ps: [
                        e.matmul(out=ps[:, :], lhsT=cs["oneh8"][:, ex * 128:(ex + 1) * 128],
                                 rhs=gthi[:, gg * 512:(gg + 1) * 512], start=True, stop=False),
                        e.matmul(out=ps[:, :], lhsT=cs["oneh8"][:, ex * 128:(ex + 1) * 128],
                                 rhs=gtlo[:, gg * 512:(gg + 1) * 512], start=False, stop=True)][-1],
                        reads=[b_gthi, b_gtlo, b_cs["oneh8"]], writes=[bps])
                    p.op("dve", lambda e, gg=gg, ps=ps: e.tensor_copy(
                        gwb[:, gg * 512:(gg + 1) * 512], ps[:, :]),
                        reads=[bps], writes=[b_gwb])
            for tgi in range(5):
                if tgi < 4:
                    tg = tgi
                    tc0 = tg * 1024
                    asb, bact = act_sb[tg % 2], b_act[tg % 2]
                    for pr_ in range(4):
                        ppar = pr_ % 2
                        glu, bglu = glu_t[ppar], b_glu[ppar]
                        up_t, bupt = upt[ppar], b_upt[ppar]
                        p.op("pe", lambda e, pr_=pr_, gs=gs, tc0=tc0: [
                            e.matmul(out=pA[:, gg * 512:(gg + 1) * 512],
                                     lhsT=gs[:, kc * 512 + pr_ * 128:kc * 512 + (pr_ + 1) * 128],
                                     rhs=xnfv(kc, tc0 + gg * 512, tc0 + (gg + 1) * 512),
                                     start=(kc == 0), stop=(kc == NKC - 1))
                            for kc in range(NKC) for gg in range(2)][-1],
                            reads=[bg, b_xnf], writes=[bufA0, bufA1])
                        p.op("dve", lambda e, pr_=pr_, iq=iq, glu=glu: e.tensor_scalar(
                            out=glu[:, :], in0=pA[:, :], scalar1=gub_s[:, iq * 4 + pr_:iq * 4 + pr_ + 1],
                            op0=ALU.add, scalar2=LIMIT, op1=ALU.min),
                            reads=[bufA0, bufA1, b_gub], writes=[bglu])
                        p.op("act", lambda e, glu=glu: e.activation(
                            out=glu[:, :], in_=glu[:, :], func=AF.Silu, scale=ALPHA),
                            writes=[bglu])
                        p.op("pe", lambda e, pr_=pr_, us=us, tc0=tc0: [
                            e.matmul(out=pB[:, gg * 512:(gg + 1) * 512],
                                     lhsT=us[:, kc * 512 + pr_ * 128:kc * 512 + (pr_ + 1) * 128],
                                     rhs=xnfv(kc, tc0 + gg * 512, tc0 + (gg + 1) * 512),
                                     start=(kc == 0), stop=(kc == NKC - 1))
                            for kc in range(NKC) for gg in range(2)][-1],
                            reads=[bu, b_xnf], writes=[bufB0, bufB1])
                        p.op("dve", lambda e, pr_=pr_, iq=iq, up_t=up_t: e.tensor_scalar(
                            out=up_t[:, :], in0=pB[:, :], scalar1=gub1[:, 16 + iq * 4 + pr_:16 + iq * 4 + pr_ + 1],
                            op0=ALU.add, scalar2=-(LIMIT - 1.0), op1=ALU.max),
                            reads=[bufB0, bufB1, b_gub1], writes=[bupt])
                        p.op("pool", lambda e, up_t=up_t: e.tensor_scalar(
                            out=up_t[:, :], in0=up_t[:, :], scalar1=LIMIT + 1.0, scalar2=-3.0e38,
                            op0=ALU.min, op1=ALU.max),
                            writes=[bupt])
                        p.op("pool" if pr_ % 2 == 0 else "dve",
                             lambda e, pr_=pr_, asb=asb, up_t=up_t, glu=glu: e.tensor_tensor(
                            out=asb[:, pr_ * 1024:(pr_ + 1) * 1024], in0=up_t[:, :], in1=glu[:, :],
                            op=ALU.mult),
                            reads=[bupt, bglu], writes=[bact])
                if tgi >= 1:
                    tg = tgi - 1
                    tc0 = tg * 1024
                    asb, bact = act_sb[tg % 2], b_act[tg % 2]
                    for hm in range(NKC):
                        if hm % 2 == 0:
                            outs = [(pY[:, 0:512], bufY0), (pY[:, 512:1024], bufY1)]
                        else:
                            outs = [(pS1[:, :], bufS1), (pS2[:, :], bufS2)]
                        p.op("pe", lambda e, hm=hm, ds=ds, asb=asb, outs=outs: [
                            e.matmul(out=outs[gg][0],
                                     lhsT=ds[:, ic * 1024 + hm * 128:ic * 1024 + (hm + 1) * 128],
                                     rhs=asb[:, ic * 1024 + gg * 512:ic * 1024 + (gg + 1) * 512],
                                     start=(ic == 0), stop=(ic == 3))
                            for ic in range(4) for gg in range(2)][-1],
                            reads=[bd, bact], writes=[outs[0][1], outs[1][1]])
                        dst, bdst2 = dstage[hm % 2], b_dst[hm % 2]
                        p.op("dve", lambda e, tc0=tc0, outs=outs, dst=dst: [
                            e.tensor_tensor(out=dst[:, gg * 512:(gg + 1) * 512], in0=outs[gg][0],
                                            in1=gwb[:, tc0 + gg * 512:tc0 + (gg + 1) * 512],
                                            op=ALU.mult)
                            for gg in range(2)][-1],
                            reads=[outs[0][1], outs[1][1], b_gwb], writes=[bdst2])
                        p.op("pool", lambda e, hm=hm, tc0=tc0, dst=dst: e.dma_start(
                            out=accd[hm, :, tc0:tc0 + 1024], in_=dst[:, :], accum_op=ALU.add),
                            reads=[bdst2], writes=[b_accd], dma=True)
        for hm in range(NKC):
            for g in range(8):
                ps, bps = PSMALL[g % 2]
                p.op("pe", lambda e, hm=hm, g=g, ps=ps: [
                    e.matmul(out=ps[:, :], lhsT=db_sb[:, hm * 128:(hm + 1) * 128],
                             rhs=gthi[:, g * 512:(g + 1) * 512], start=True, stop=False),
                    e.matmul(out=ps[:, :], lhsT=db_sb[:, hm * 128:(hm + 1) * 128],
                             rhs=gtlo[:, g * 512:(g + 1) * 512], start=False, stop=True)][-1],
                    reads=[b_db, b_gthi, b_gtlo], writes=[bps])
                dst, bdst2 = dstage[g % 2], b_dst[g % 2]
                p.op("dve", lambda e, ps=ps, dst=dst: e.tensor_copy(dst[:, 0:512], ps[:, :]),
                     reads=[bps], writes=[bdst2])
                p.op("pool", lambda e, hm=hm, g=g, dst=dst: e.dma_start(
                    out=accd[hm, :, g * 512:(g + 1) * 512], in_=dst[:, 0:512], accum_op=ALU.add),
                    reads=[bdst2], writes=[b_accd], dma=True)

        xr = [gsl[0].bitcast(F32)[:, 0:1024], gsl[0].bitcast(F32)[:, 1024:2048]]
        b_xr = [b_gsl[0], b_gsl[0]]
        astage = [usl[0].bitcast(F32)[:, 0:1024], usl[0].bitcast(F32)[:, 1024:2048]]
        b_ast = [b_usl[0], b_usl[0]]
        ostage = [dsl[0].bitcast(F32)[:, 0:1024], dsl[0].bitcast(F32)[:, 1024:2048]]
        b_ost = [b_dsl[0], b_dsl[0]]
        out_evs = []
        for s in range(2):
            for tt in range(NTT):
                par = tt % 2
                gc0 = s * T + tt * 128
                xrr, bxr = xr[par], b_xr[par]
                p.op("sp", lambda e, s=s, tt=tt, xrr=xrr: e.dma_start(
                    out=xrr[:, :], in_=x_d[s, tt * 128:(tt + 1) * 128, :]),
                    writes=[bxr], dma=True)
                ast, bast = astage[par], b_ast[par]
                p.op("sp", lambda e, gc0=gc0, ast=ast: e.dma_start(
                    out=ast.rearrange("p (kc t) -> p kc t", kc=NKC)[:, :, :],
                    in_=accd.rearrange("kc p t -> p kc t")[:, :, gc0:gc0 + 128]),
                    reads=[b_accd], writes=[bast], dma=True)
                for half in range(2):
                    ps, bps = PSMALL[half]
                    p.op("pe", lambda e, half=half, ast=ast, ps=ps: [
                        e.transpose(out=ps[:, q * 128:(q + 1) * 128],
                                    in_=ast[:, (half * 4 + q) * 128:(half * 4 + q + 1) * 128],
                                    identity=cs["ident_f32"][:, :])
                        for q in range(4)][-1],
                        reads=[bast, b_cs["ident_f32"]], writes=[bps])
                    ost, bost = ostage[par], b_ost[par]
                    p.op("dve", lambda e, half=half, xrr=xrr, ps=ps, ost=ost: e.tensor_tensor(
                        out=ost[:, half * 512:(half + 1) * 512], in0=ps[:, :],
                        in1=xrr[:, half * 512:(half + 1) * 512], op=ALU.add),
                        reads=[bps, bxr], writes=[bost])
                ost, bost = ostage[par], b_ost[par]
                ev = p.op("sp", lambda e, s=s, tt=tt, ost=ost: e.dma_start(
                    out=out_d[s, tt * 128:(tt + 1) * 128, :], in_=ost[:, :]),
                    reads=[bost], writes=[b_out], dma=True)
                out_evs.append(ev)
        p.op("sp", lambda e: e.nop(), deps=out_evs)

        with nc.Block() as block:
            counts = p.emit(nc, block, sems, dma_sems)
        ffn_es.close()
    return nc, counts, tap_out


_BUILD_CACHE = {}


def _get_nc():
    key = "full"
    if key not in _BUILD_CACHE:
        _BUILD_CACHE[key] = build_nc()
    return _BUILD_CACHE[key]


def kernel(L0_attn_x, L0_attn_Wq, L0_attn_Wk, L0_attn_Wv, L0_attn_Wp, L0_attn_Wproj,
           L0_attn_q_rms_w, L0_attn_k_rms_w, L0_ffn_router_weight, L0_ffn_router_bias,
           L0_ffn_gate_up_proj, L0_ffn_gate_up_bias, L0_ffn_down_proj, L0_ffn_down_bias):
    f32 = np.float32
    x = np.ascontiguousarray(np.asarray(L0_attn_x, dtype=f32))
    base = {
        "wqT": np.ascontiguousarray(np.asarray(L0_attn_Wq, f32).T),
        "wkT": np.ascontiguousarray(np.asarray(L0_attn_Wk, f32).T),
        "wvT": np.ascontiguousarray(np.asarray(L0_attn_Wv, f32).T),
        "wpT": np.ascontiguousarray(np.asarray(L0_attn_Wp, f32).T),
        "wprojT": np.ascontiguousarray(np.asarray(L0_attn_Wproj, f32).T),
        "qrw": np.ascontiguousarray(np.asarray(L0_attn_q_rms_w, f32).reshape(1, 64)),
        "krw": np.ascontiguousarray(np.asarray(L0_attn_k_rms_w, f32).reshape(1, 64)),
        "rw_l": np.ascontiguousarray(
            np.asarray(L0_ffn_router_weight, f32).T.reshape(8, 128, 8).transpose(1, 0, 2).reshape(128, 64)),
        "rb": np.ascontiguousarray(np.asarray(L0_ffn_router_bias, f32).reshape(1, 8)),
        "gup": np.ascontiguousarray(np.asarray(L0_ffn_gate_up_proj, f32)),
        "gub_l": np.ascontiguousarray(
            np.asarray(L0_ffn_gate_up_bias, f32).reshape(8, 32, 128).transpose(0, 2, 1)),
        "dwn": np.ascontiguousarray(np.asarray(L0_ffn_down_proj, f32)),
        "dbias": np.ascontiguousarray(np.asarray(L0_ffn_down_bias, f32)),
    }
    base.update(host_consts())
    nc, counts, _ = _get_nc()
    in_maps = []
    for c in range(8):
        m = dict(base)
        m["x"] = np.ascontiguousarray(x[2 * c:2 * c + 2])
        in_maps.append(m)
    res = run_bass_kernel_spmd(nc, in_maps, core_ids=list(range(8)))
    out = np.concatenate([np.asarray(r["out"]).reshape(2, T, C) for r in res.results], axis=0)
    return out.astype(np.float32)


# version 19
# speedup vs baseline: 1.1735x; 1.0101x over previous
import math
from contextlib import ExitStack

import numpy as np
import ml_dtypes
import concourse.bass as bass
import concourse.mybir as mybir
from concourse.bass_utils import run_bass_kernel_spmd

F32 = mybir.dt.float32
BF16 = mybir.dt.bfloat16
AF = mybir.ActivationFunctionType
ALU = mybir.AluOpType
AX = mybir.AxisListType.X

T = 2048
C = 1024
NH = 16
HD = 64
NKC = C // 128
NTT = T // 128
I = 2048
NE = 8
RMS_EPS = 1e-5
FRMS_EPS = 1.1920929e-07
LN_EPS = 1e-5
ALPHA = 1.702
LIMIT = 7.0
MSHIFT = 60.0
DBG = {}

ENGINES = ("pe", "act", "dve", "pool", "sp")
BLOCK_ATTR = {"pe": "tensor", "act": "scalar", "dve": "vector", "pool": "gpsimd", "sp": "sync"}


class Buf:
    __slots__ = ("w", "r", "mw", "wl")
    def __init__(self, mw=False):
        self.w = None
        self.r = []
        self.mw = mw
        self.wl = []


class Prog:
    DMA_POOL = 20

    def __init__(self):
        self.eng_ops = {e: [] for e in ENGINES}
        self.dma_count = {"sp": 0, "pool": 0}

    def op(self, eng, fn, reads=(), writes=(), deps=(), dma=False):
        dep_list = [d for d in deps if d is not None]
        for b in reads:
            if b.mw:
                dep_list.extend(b.wl)
            elif b.w is not None:
                dep_list.append(b.w)
        for b in writes:
            dep_list.extend(b.r)
            if b.mw:
                pass
            elif b.w is not None:
                dep_list.append(b.w)
        rec = {"eng": eng, "fn": fn, "deps": tuple(dep_list), "dma": dma,
               "needed": False, "val": None, "dma_idx": None}
        if dma:
            rec["dma_idx"] = self.dma_count[eng]
            self.dma_count[eng] += 1
        self.eng_ops[eng].append(rec)
        for b in reads:
            b.r.append(rec)
        for b in writes:
            if b.mw:
                b.wl.append(rec)
            else:
                b.w = rec
                b.r = []
        return rec

    def barrier(self):
        lasts = [self.eng_ops[e][-1] for e in ENGINES if self.eng_ops[e]]
        for q in ("sp", "pool"):
            slot_last = {}
            for r in self.eng_ops[q]:
                if r["dma"]:
                    slot_last[r["dma_idx"] % self.DMA_POOL] = r
            lasts.extend(slot_last.values())
        nops = []
        for e in ENGINES:
            nops.append(self.op(e, lambda eng: eng.nop(), deps=lasts))
        return nops

    def emit(self, nc, block, sems, dma_sems):
        for e in ENGINES:
            for r in self.eng_ops[e]:
                for d in r["deps"]:
                    d["needed"] = True
        for e in ENGINES:
            cum = 0
            for r in self.eng_ops[e]:
                if r["dma"]:
                    k = r["dma_idx"]
                    r["val"] = (k // self.DMA_POOL + 1) * 16
                elif r["needed"]:
                    cum += 1
                    r["val"] = cum
        counts = {e: sum(1 for r in self.eng_ops[e]) for e in ENGINES}

        def sem_of(r):
            if r["dma"]:
                return dma_sems[r["eng"]][r["dma_idx"] % self.DMA_POOL]
            return sems[r["eng"]]

        def body(eng_name):
            def go(eng):
                lastwait = {}
                sem = sems[eng_name]
                for r in self.eng_ops[eng_name]:
                    if r["dma"] and r["dma_idx"] >= self.DMA_POOL:
                        dsem = dma_sems[eng_name][r["dma_idx"] % self.DMA_POOL]
                        v = (r["dma_idx"] // self.DMA_POOL) * 16
                        key = id(dsem)
                        if lastwait.get(key, -1) < v:
                            eng.wait_ge(dsem, v)
                            lastwait[key] = v
                    for d in r["deps"]:
                        dsem = sem_of(d)
                        v = d["val"]
                        if v is None:
                            raise RuntimeError("dep without value")
                        key = id(dsem)
                        if lastwait.get(key, -1) < v:
                            eng.wait_ge(dsem, v)
                            lastwait[key] = v
                    instr = r["fn"](eng)
                    if r["dma"]:
                        instr.then_inc(sem_of(r), 16)
                    elif r["needed"]:
                        instr.then_inc(sem, 1)
            return go

        for e in ENGINES:
            getattr(block, BLOCK_ATTR[e])(body(e))
        return counts


def splits(lo, hi, align=512):
    out = []
    a = lo
    while a < hi:
        b = min(hi, (a // align + 1) * align)
        out.append((a, b))
        a = b
    return out


def host_consts():
    bf = ml_dtypes.bfloat16
    c = {}
    c["ident_bf"] = np.eye(128, dtype=bf)
    c["ident_f32"] = np.eye(128, dtype=np.float32)
    r = np.arange(128)
    c["slmat"] = (r[:, None] > r[None, :]).astype(bf)
    c["utmask"] = (r[:, None] <= r[None, :]).astype(bf)
    kc = np.zeros((64, T), dtype=np.float32)
    ind = (np.arange(T)[None, :] // 128 == np.arange(16)[:, None]).astype(np.float32)
    kc[0:16] = ind
    kc[32:48] = ind
    c["kconst"] = kc.astype(bf)
    ec = np.zeros((128, 256), dtype=np.float32)
    for J in range(16):
        ec[:, 16 * J + J] = 1.0
    c["ecols"] = ec.astype(bf)
    r16 = np.arange(16)
    c["l16"] = (r16[:, None] > r16[None, :]).astype(bf)
    c["neg1"] = np.full((1, 16), -1.0, dtype=bf)
    c["ones64"] = np.ones((64, 1), dtype=bf)
    op_ = np.zeros((128, 65), dtype=np.float32)
    op_[0:64, 0] = 1.0
    op_[64:128, 64] = 1.0
    c["onespair"] = op_.astype(bf)
    c["ones1x64"] = np.ones((1, 64), dtype=bf)
    c["onesrow"] = np.ones((1, 128), dtype=bf)
    oh = np.zeros((8, 8 * 128), dtype=np.float32)
    for e in range(8):
        oh[e, 128 * e:128 * (e + 1)] = 1.0 / ALPHA
    c["oneh8"] = oh.astype(bf)
    fc = np.zeros((128, 4), dtype=np.float32)
    fc[:, 0] = LN_EPS
    fc[:, 1] = FRMS_EPS
    fc[:, 2] = -math.log(8.0)
    fc[:, 3] = 1.0
    c["fconst"] = fc
    t = np.arange(T, dtype=np.float64)
    c["cosb64"] = np.tile(np.cos(t)[None, :], (64, 1)).astype(bf)
    sb64 = np.zeros((64, T), dtype=np.float64)
    sb64[0:32] = -np.sin(t)[None, :]
    sb64[32:64] = np.sin(t)[None, :]
    c["sinb64"] = sb64.astype(bf)
    return c


def build_nc(taps=(), heads=NH, experts=NE, seqs=(0, 1), attn_lvl=6):
    nc = bass.Bass("TRN2", target_bir_lowering=False)
    tap_set = set(taps)

    x_d = nc.dram_tensor("x", [2, T, C], F32, kind="ExternalInput").ap()
    wT = {}
    for nm in ("wqT", "wkT", "wvT", "wpT", "wprojT"):
        wT[nm] = nc.dram_tensor(nm, [C, C], F32, kind="ExternalInput").ap()
    qrw_d = nc.dram_tensor("qrw", [1, 64], F32, kind="ExternalInput").ap()
    krw_d = nc.dram_tensor("krw", [1, 64], F32, kind="ExternalInput").ap()
    rwl_d = nc.dram_tensor("rw_l", [128, 64], F32, kind="ExternalInput").ap()
    rb_d = nc.dram_tensor("rb", [1, 8], F32, kind="ExternalInput").ap()
    gup_d = nc.dram_tensor("gup", [NE, C, 2 * I], F32, kind="ExternalInput").ap()
    gub_d = nc.dram_tensor("gub_l", [NE, 128, 32], F32, kind="ExternalInput").ap()
    dwn_d = nc.dram_tensor("dwn", [NE, I, C], F32, kind="ExternalInput").ap()
    dbias_d = nc.dram_tensor("dbias", [NE, C], F32, kind="ExternalInput").ap()
    cst_d = {}
    cshapes = {"ident_bf": ([128, 128], BF16), "ident_f32": ([128, 128], F32),
               "slmat": ([128, 128], BF16), "utmask": ([128, 128], BF16),
               "kconst": ([64, T], BF16), "ecols": ([128, 256], BF16),
               "l16": ([16, 16], BF16), "neg1": ([1, 16], BF16),
               "ones64": ([64, 1], BF16), "ones1x64": ([1, 64], BF16),
               "onespair": ([128, 65], BF16),
               "onesrow": ([1, 128], BF16), "oneh8": ([8, 8 * 128], BF16),
               "cosb64": ([64, T], BF16), "sinb64": ([64, T], BF16),
               "fconst": ([128, 4], F32)}
    for nm, (shp, dt) in cshapes.items():
        cst_d[nm] = nc.dram_tensor(nm, shp, dt, kind="ExternalInput").ap()
    out_d = nc.dram_tensor("out", [2, T, C], F32, kind="ExternalOutput").ap()
    xnscr = nc.dram_tensor("xnscr", [128, NKC * 2 * T], BF16, kind="Internal").ap()
    wscr = {nm: nc.dram_tensor("wscr_" + nm, [C, C], BF16, kind="Internal").ap()
            for nm in ("wqT", "wkT", "wvT", "wpT", "wprojT")}
    gupb = nc.dram_tensor("gupb", [NE, C, 2 * I], BF16, kind="Internal").ap()
    dwnb = nc.dram_tensor("dwnb", [NE, I, C], BF16, kind="Internal").ap()
    accd = nc.dram_tensor("accd", [NKC, 128, 2 * T], F32, kind="Internal").ap()
    yscr = nc.dram_tensor("yscr", [2, NH, 64, T], BF16, kind="Internal").ap()

    tap_out = {}
    p = Prog()

    def tap(name, ap_fn, shape, dtype, deps):
        if name not in tap_set:
            return
        td = nc.dram_tensor("tap_" + name, shape, F32, kind="ExternalOutput").ap()
        tap_out[name] = td
        p.op("pool", lambda e, td=td, ap_fn=ap_fn: e.dma_start(out=td, in_=ap_fn()),
             deps=deps, dma=True)

    with ExitStack() as es:
        sems = {}
        for e in ENGINES:
            sems[e] = es.enter_context(nc.semaphore("sem_" + e))
        dma_sems = {q: [es.enter_context(nc.semaphore(f"dsem_{q}_{i}"))
                        for i in range(Prog.DMA_POOL)] for q in ("sp", "pool")}

        pA = es.enter_context(nc.psum_tensor("pA", [128, 1024], F32)).ap()
        pB = es.enter_context(nc.psum_tensor("pB", [128, 1024], F32)).ap()
        pY = es.enter_context(nc.psum_tensor("pY", [128, 1024], F32)).ap()
        pS1 = es.enter_context(nc.psum_tensor("pS1", [128, 512], F32)).ap()
        pS2 = es.enter_context(nc.psum_tensor("pS2", [128, 512], F32)).ap()
        bufA0, bufA1, bufB0, bufB1 = Buf(), Buf(), Buf(), Buf()
        bufY, bufS1, bufS2 = Buf(), Buf(), Buf()

        sb = lambda name, shp, dt: es.enter_context(nc.sbuf_tensor(name, shp, dt)).ap()
        cs = {nm: sb("sb_" + nm, shp, dt) for nm, (shp, dt) in cshapes.items() if nm != "kconst"}
        qrw_b = sb("qrw_b", [65, 64], BF16)
        krw_b = sb("krw_b", [65, 64], BF16)
        onesp64 = sb("onesp64", [65, 64], BF16)
        rw_sb = sb("rw_sb", [128, 128], BF16)
        rw_f = sb("rw_f", [128, 64], F32)
        rb_b = sb("rb_b", [1, 8], BF16)
        db_sb = sb("db_sb", [8, C], BF16)
        gthi = sb("gthi", [8, 2 * T], BF16)
        gtlo = sb("gtlo", [8, 2 * T], BF16)
        b_cs = {nm: Buf() for nm in cshapes}
        b_qrw, b_krw, b_rw, b_rb, b_db, b_rwf = Buf(), Buf(), Buf(), Buf(), Buf(), Buf()
        b_gthi, b_gtlo = Buf(mw=True), Buf(mw=True)
        b_xnscr, b_accd, b_yscr, b_out = Buf(mw=True), Buf(mw=True), Buf(mw=True), Buf(mw=True)

        b_wscr = {nm: Buf(mw=True) for nm in wscr}
        b_gupb = Buf(mw=True)
        b_dwnb = Buf(mw=True)
        for nm in wscr:
            p.op("pool", lambda e, nm=nm: e.dma_start(out=wscr[nm], in_=wT[nm]),
                 writes=[b_wscr[nm]], dma=True)
        for ex in range(NE):
            p.op("pool", lambda e, ex=ex: e.dma_start(out=gupb[ex], in_=gup_d[ex]),
                 writes=[b_gupb], dma=True)
            p.op("pool", lambda e, ex=ex: e.dma_start(out=dwnb[ex], in_=dwn_d[ex]),
                 writes=[b_dwnb], dma=True)
        for nm in cs:
            pass
        for nm in [n for n in cshapes if n != "kconst"]:
            p.op("sp", lambda e, nm=nm: e.dma_start(out=cs[nm][:, :], in_=cst_d[nm]),
                 writes=[b_cs[nm]], dma=True)
        p.op("pool", lambda e: e.dma_start(out=qrw_b[0:1, :], in_=qrw_d), writes=[b_qrw], dma=True)
        p.op("pool", lambda e: e.dma_start(out=qrw_b[64:65, :], in_=qrw_d), writes=[b_qrw], dma=True)
        p.op("pool", lambda e: e.dma_start(out=krw_b[0:1, :], in_=krw_d), writes=[b_krw], dma=True)
        p.op("pool", lambda e: e.dma_start(out=krw_b[64:65, :], in_=krw_d), writes=[b_krw], dma=True)
        p.op("sp", lambda e: e.dma_start(out=onesp64[0:1, :], in_=cst_d["ones1x64"]),
             writes=[b_cs["ones1x64"]], dma=True)
        p.op("sp", lambda e: e.dma_start(out=onesp64[64:65, :], in_=cst_d["ones1x64"]),
             writes=[b_cs["ones1x64"]], dma=True)
        p.op("pool", lambda e: e.dma_start(out=rw_f[:, :], in_=rwl_d), writes=[b_rwf], dma=True)
        p.op("dve", lambda e: e.tensor_copy(rw_sb[:, 0:64], rw_f[:, :]), reads=[b_rwf], writes=[b_rw])
        p.op("dve", lambda e: e.tensor_tensor(out=rw_sb[:, 64:128], in0=rw_f[:, :],
                                              in1=rw_sb[:, 0:64], op=ALU.subtract),
             reads=[b_rwf], writes=[b_rw])
        p.op("pool", lambda e: e.dma_start(out=rb_b[:, :], in_=rb_d), writes=[b_rb], dma=True)
        p.op("pool", lambda e: e.dma_start(out=db_sb[:, :], in_=dbias_d), writes=[b_db], dma=True)

        main_es = ExitStack()
        msb = lambda name, shp, dt: main_es.enter_context(nc.sbuf_tensor(name, shp, dt)).ap()
        lsb = msb

        xstage = [lsb(f"xstage{i}", [128, C], F32) for i in range(2)]
        b_xstage = [Buf(), Buf()]
        xntm = [lsb(f"xntm{i}", [128, C], BF16) for i in range(2)]
        b_xntm = [Buf(), Buf()]
        xnfst = [lsb(f"xnfst{i}", [128, C], BF16) for i in range(2)]
        b_xnfst = [Buf(), Buf()]
        dump = lsb("dump", [128, C], BF16)
        b_dump = Buf()
        nmean = [lsb(f"nmean{i}", [128, 1], F32) for i in range(2)]
        b_nmean = [Buf(), Buf()]
        vsum = [lsb(f"vsum{i}", [128, 1], F32) for i in range(2)]
        b_vsum = [Buf(), Buf()]
        rstd_ln = [lsb(f"rstdln{i}", [128, 1], F32) for i in range(2)]
        b_rstd_ln = [Buf(), Buf()]
        sm8 = lsb("sm8", [128, 8], F32)
        m8a = lsb("m8a", [128, 1], F32)
        m8b = lsb("m8b", [128, 8], F32)
        g8 = lsb("g8", [128, 8], F32)
        b_sm8, b_m8a, b_m8b, b_g8 = Buf(), Buf(), Buf(), Buf()

        for s in range(2):
            for tt in range(NTT):
                par = tt % 2
                gc0 = s * T + tt * 128
                xs, bx = xstage[par], b_xstage[par]
                p.op("sp", lambda e, s=s, tt=tt, xs=xs: e.dma_start(
                    out=xs[:, :], in_=x_d[s, tt * 128:(tt + 1) * 128, :]),
                    writes=[bx], dma=True)
                nm, bnm = nmean[par], b_nmean[par]
                p.op("dve", lambda e, xs=xs, nm=nm: e.reduce_sum(nm[:, :], xs[:, :], axis=AX),
                     reads=[bx], writes=[bnm])
                p.op("dve", lambda e, nm=nm: e.tensor_scalar(
                    out=nm[:, :], in0=nm[:, :], scalar1=-1.0 / C, scalar2=None, op0=ALU.mult),
                    writes=[bnm])
                vs, bvs = vsum[par], b_vsum[par]
                p.op("act", lambda e, xs=xs, nm=nm, vs=vs: e.activation(
                    out=dump[:, :], in_=xs[:, :], func=AF.Square, bias=nm[:, :], accum_out=vs[:, :]),
                    reads=[bx, bnm], writes=[b_dump, bvs])
                rl, brl = rstd_ln[par], b_rstd_ln[par]
                p.op("act", lambda e, vs=vs, rl=rl: e.activation(
                    out=rl[:, :], in_=vs[:, :], func=AF.Ln, scale=1.0 / C,
                    bias=cs["fconst"][:, 0:1]),
                    reads=[bvs, b_cs["fconst"]], writes=[brl])
                p.op("act", lambda e, rl=rl: e.activation(
                    out=rl[:, :], in_=rl[:, :], func=AF.Exp, scale=-0.5),
                    writes=[brl])
                xt, bxt = xntm[par], b_xntm[par]
                p.op("dve", lambda e, xs=xs, nm=nm, rl=rl, xt=xt: e.tensor_scalar(
                    out=xt[:, :], in0=xs[:, :], scalar1=nm[:, :], op0=ALU.add,
                    scalar2=rl[:, :], op1=ALU.mult),
                    reads=[bx, bnm, brl], writes=[bxt])
                if "xn_tm" in tap_set:
                    if "xn_tm" not in tap_out:
                        tap_out["xn_tm"] = nc.dram_tensor("tap_xn_tm", [2 * NTT, 128, C], F32,
                                                          kind="ExternalOutput").ap()
                    p.op("pool", lambda e, s=s, tt=tt, xt=xt: e.dma_start(
                        out=tap_out["xn_tm"][s * NTT + tt], in_=xt[:, :]),
                        reads=[bxt], dma=True)
                pT = pS2.bitcast(BF16)
                p.op("pe", lambda e, xt=xt, pT=pT: [
                    e.transpose(out=pT[:, kc * 128:(kc + 1) * 128],
                                in_=xt[:, kc * 128:(kc + 1) * 128],
                                identity=cs["ident_bf"][:, :])
                    for kc in range(NKC)][-1],
                    reads=[bxt, b_cs["ident_bf"]], writes=[bufS2])
                xf, bxf = xnfst[par], b_xnfst[par]
                p.op("dve", lambda e, pT=pT, xf=xf: e.tensor_copy(xf[:, :], pT[:, :]),
                     reads=[bufS2], writes=[bxf])
                p.op("sp", lambda e, xf=xf, gc0=gc0: e.dma_start(
                    out=xnscr.rearrange("p (kc t) -> p kc t", kc=NKC)[:, :, gc0:gc0 + 128],
                    in_=xf.rearrange("p (kc t) -> p kc t", kc=NKC)[:, :, :]),
                    reads=[bxf], writes=[b_xnscr], dma=True)
                p.op("pe", lambda e, xf=xf: [
                    e.matmul(out=pS1[:, 0:8], lhsT=xf[:, kc * 128:(kc + 1) * 128],
                             rhs=rw_sb[:, hl * 64 + kc * 8:hl * 64 + (kc + 1) * 8],
                             start=(kc == 0 and hl == 0), stop=False)
                    for hl in range(2) for kc in range(NKC)][-1],
                    reads=[bxf, b_rw], writes=[bufS1])
                p.op("pe", lambda e: e.matmul(out=pS1[:, 0:8], lhsT=cs["onesrow"][:, :],
                                              rhs=rb_b[:, :], start=False, stop=True),
                     reads=[b_cs["onesrow"], b_rb], writes=[bufS1])
                p.op("dve", lambda e: e.reduce_max(m8a[:, :], pS1[:, 0:8], axis=AX),
                     reads=[bufS1], writes=[b_m8a])
                p.op("dve", lambda e: e.tensor_scalar(
                    out=m8a[:, :], in0=m8a[:, :], scalar1=-1.0, scalar2=None, op0=ALU.mult),
                    writes=[b_m8a])
                p.op("dve", lambda e: e.tensor_copy(sm8[:, :], pS1[:, 0:8]),
                     reads=[bufS1], writes=[b_sm8])
                p.op("act", lambda e: e.activation(out=sm8[:, :], in_=sm8[:, :],
                                                   func=AF.Exp, bias=m8a[:, :]),
                     reads=[b_m8a], writes=[b_sm8])
                p.op("dve", lambda e: e.tensor_scalar(
                    out=dump[:, 0:8], in0=sm8[:, :], scalar1=1.0, scalar2=None,
                    op0=ALU.mult, op1=ALU.add, accum_out=m8a[:, :]),
                    reads=[b_sm8], writes=[b_dump, b_m8a])
                p.op("dve", lambda e: e.reciprocal(m8a[:, :], m8a[:, :]), writes=[b_m8a])
                p.op("dve", lambda e: e.tensor_scalar(
                    out=sm8[:, :], in0=sm8[:, :], scalar1=m8a[:, :], scalar2=None, op0=ALU.mult),
                    reads=[b_m8a], writes=[b_sm8])
                p.op("dve", lambda e: e.reduce_max(m8a[:, :], sm8[:, :], axis=AX),
                     reads=[b_sm8], writes=[b_m8a])
                p.op("dve", lambda e: e.tensor_scalar(
                    out=m8b[:, :], in0=sm8[:, :], scalar1=m8a[:, :], scalar2=None, op0=ALU.is_ge),
                    reads=[b_sm8, b_m8a], writes=[b_m8b])
                p.op("dve", lambda e: e.scalar_tensor_tensor(
                    out=m8b[:, :], in0=m8b[:, :], scalar=-4.0, in1=sm8[:, :],
                    op0=ALU.mult, op1=ALU.add),
                    reads=[b_sm8], writes=[b_m8b])
                p.op("dve", lambda e: e.reduce_max(m8a[:, :], m8b[:, :], axis=AX),
                     reads=[b_m8b], writes=[b_m8a])
                p.op("dve", lambda e: e.scalar_tensor_tensor(
                    out=g8[:, :], in0=sm8[:, :], scalar=m8a[:, :], in1=sm8[:, :],
                    op0=ALU.is_ge, op1=ALU.mult),
                    reads=[b_sm8, b_m8a], writes=[b_g8])
                p.op("pe", lambda e: e.transpose(out=pS2[0:8, 0:128], in_=g8[:, :],
                                                 identity=cs["ident_f32"][:, :]),
                     reads=[b_g8, b_cs["ident_f32"]], writes=[bufS2])
                p.op("dve", lambda e, gc0=gc0: e.tensor_copy(
                    gthi[:, gc0:gc0 + 128], pS2[0:8, 0:128]),
                    reads=[bufS2], writes=[b_gthi])
                p.op("dve", lambda e, gc0=gc0: e.tensor_tensor(
                    out=gtlo[:, gc0:gc0 + 128], in0=pS2[0:8, 0:128],
                    in1=gthi[:, gc0:gc0 + 128], op=ALU.subtract),
                    reads=[bufS2, b_gthi], writes=[b_gtlo])
        tap("xn_fm", lambda: xnscr, [128, NKC * 2 * T], BF16, deps=list(b_xnscr.wl))
        tap("g8", lambda: gthi, [8, 2 * T], BF16, deps=list(b_gthi.wl))
        p.barrier()

        xng = [msb(f"xng{i}", [128, NKC * 512], BF16) for i in range(2)]
        b_xng = [Buf(), Buf()]
        wstream = msb("wstream", [128, 8192], BF16)
        b_wh = [Buf(), Buf()]
        b_wproj = Buf()
        biasT = msb("biasT", [128, 17408], BF16)
        b_biasT = [Buf() for _ in range(16)]
        offJ = [2048 * J - 64 * J * (J - 1) for J in range(16)]
        ls1_ = [msb(f"ls1_{i}", [128, 1024], F32) for i in range(2)]
        b_ls1_ = [Buf(), Buf()]
        ls1, b_ls1 = ls1_[0], b_ls1_[0]
        bufY0, bufY1 = Buf(), Buf()
        q_ext = [msb(f"q_ext{i}", [128, T], BF16) for i in range(2)]
        k_ext = [msb(f"k_ext{i}", [128, T], BF16) for i in range(2)]
        protc = [msb(f"protc{i}", [128, T], BF16) for i in range(2)]
        p64x2 = [msb(f"p64x2{i}", [128, T], BF16) for i in range(2)]
        v_ext = [msb(f"v_ext{i}", [128, 16 * 68], BF16) for i in range(2)]
        b_qe = [Buf(), Buf()]
        b_ke = [Buf(), Buf()]
        b_pr = [Buf(), Buf()]
        b_p6 = [Buf(), Buf()]
        b_ve = [Buf(), Buf()]
        Tf = [msb(f"Tf{i}", [128, 512], F32) for i in range(3)]
        b_Tf = [Buf(), Buf(), Buf()]
        sqb = [msb(f"sqb{i}", [128, 512], BF16) for i in range(3)]
        b_sq = [Buf(), Buf(), Buf()]
        v_fm = msb("v_fm", [128, T], BF16)
        b_vfm = Buf()
        ET = [msb(f"ET{i}", [128, 1024], BF16) for i in range(2)]
        b_ET = [Buf(), Buf()]
        csth = msb("csth", [16, 512], BF16)
        cstl = msb("cstl", [16, 512], BF16)
        b_csth, b_cstl = Buf(), Buf()
        rows65 = [msb(f"rows65_{i}", [65, 512], F32) for i in range(2)]
        b_rows65 = [Buf(), Buf()]
        rstd65 = [msb(f"rstd65_{i}", [65, 512], BF16) for i in range(3)]
        b_rstd65 = [Buf(), Buf(), Buf()]
        rowkq = [msb(f"rowkq{i}", [1, 512], F32) for i in range(2)]
        b_rowkq = [Buf(), Buf()]
        kmx = msb("kmx", [1, 1], F32)
        kmg = msb("kmg", [1, 1], F32)
        b_kmx, b_kmg = Buf(), Buf()
        m_row = msb("m_row", [1, T], BF16)
        b_mrow = Buf()
        rzrow = msb("rzrow", [1, 1024], F32)
        rzhi = msb("rzhi", [1, 1024], BF16)
        rzlo = msb("rzlo", [1, 1024], BF16)
        b_rz, b_rzhi, b_rzlo = Buf(), Buf(), Buf()
        rzb_sb = msb("rzb_sb", [64, 1024], F32)
        b_rzb = Buf()
        ystage = [rows65[i].bitcast(BF16)[0:64, 0:1024] for i in range(2)]
        b_yst = b_rows65
        ao_st = [p64x2[0].bitcast(F32)[:, 0:512]]
        b_ao = [b_p6[0]]

        for i in range(2):
            p.op("dve", lambda e, i=i: e.memset(q_ext[i][:, :], 0.0), writes=[b_qe[i]])
            p.op("sp", lambda e, i=i: e.dma_start(out=k_ext[i][0:64, :], in_=cst_d["kconst"]),
                 writes=[b_ke[i]], dma=True)
            p.op("dve", lambda e, i=i: e.memset(v_ext[i][:, :], 0.0), writes=[b_ve[i]])
            p.op("dve", lambda e, i=i: e.memset(
                v_ext[i].rearrange("p (b c) -> p b c", c=68)[:, :, 64:65], 1.0),
                writes=[b_ve[i]])

        PSMALL = [(pS1, bufS1), (pS2, bufS2)]

        for s in seqs:
            sc0 = s * T
            def emit_wload(hp_):
                wof_ = (hp_ % 2) * 4096
                bw_ = b_wh[hp_ % 2]
                for ti, nm in enumerate(("wkT", "wqT", "wpT", "wvT")):
                    p.op("sp", lambda e, nm=nm, ti=ti, hp_=hp_, wof_=wof_: e.dma_start(
                        out=wstream[:, wof_ + ti * 1024: wof_ + (ti + 1) * 1024].rearrange("p (kc m) -> p kc m", kc=NKC)[:, :, :],
                        in_=wscr[nm].rearrange("(kc p) m -> p kc m", p=128)[:, :, hp_ * 128:(hp_ + 1) * 128]),
                        writes=[bw_], reads=[b_wscr[nm]],
                        deps=([b_wproj.w] if b_wproj.w is not None else []) + list(b_wproj.r), dma=True)

            def emit_xload(g_):
                xn_g_, bxn_ = xng[g_ % 2], b_xng[g_ % 2]
                c0_ = g_ * 512
                p.op("sp", lambda e, sc0=sc0, c0_=c0_, xn_g_=xn_g_: e.dma_start(
                    out=xn_g_.rearrange("p (kc c) -> p kc c", kc=NKC)[:, :, :],
                    in_=xnscr.rearrange("p (kc t) -> p kc t", kc=NKC)[:, :, sc0 + c0_:sc0 + c0_ + 512]),
                    reads=[b_xnscr], writes=[bxn_], dma=True)

            for hp in range(heads // 2):
                wpar = hp % 2
                wof = wpar * 4096
                bw = b_wh[wpar]
                if hp == 0:
                    emit_wload(0)

                for g in range(4):
                    c0 = g * 512
                    xpar = g % 2
                    xn_g, bxn = xng[xpar], b_xng[xpar]
                    if not (hp > 0 and g == 0):
                        emit_xload(g)
                    pbank = {"k": (pA[:, 0:512], bufA0), "q": (pA[:, 512:1024], bufA1),
                             "p": (pB[:, 0:512], bufB0), "v": (pB[:, 512:1024], bufB1)}
                    sbank = {"k": (pS1, bufS1), "q": (pS2, bufS2), "p": (pS1, bufS1)}
                    tfm = {"k": 0, "q": 1, "p": 2}
                    if attn_lvl < 0.2:
                        continue
                    for ti, tn in enumerate(("k", "q", "p", "v")):
                        pp_, bpp = pbank[tn]
                        wslice = lambda kc, ti=ti, wof=wof: wstream[:, wof + ti * 1024 + kc * 128: wof + ti * 1024 + (kc + 1) * 128]
                        p.op("pe", lambda e, pp_=pp_, wslice=wslice, xn_g=xn_g: [
                            e.matmul(out=pp_, lhsT=wslice(kc), rhs=xn_g[:, kc * 512:(kc + 1) * 512],
                                     start=(kc == 0), stop=(kc == NKC - 1))
                            for kc in range(NKC)][-1],
                            reads=[bw, bxn], writes=[bpp])
                        if tn == "v":
                            p.op("dve", lambda e, pp_=pp_, c0=c0: e.tensor_copy(
                                v_fm[:, c0:c0 + 512], pp_),
                                reads=[bpp], writes=[b_vfm])
                        else:
                            tf, btf = Tf[tfm[tn]], b_Tf[tfm[tn]]
                            p.op("dve", lambda e, pp_=pp_, tf=tf: e.tensor_copy(
                                tf[:, :], pp_),
                                reads=[bpp], writes=[btf])
                            sq_h, bsq_h = sqb[tfm[tn]], b_sq[tfm[tn]]
                            p.op("act", lambda e, tf=tf, sq_h=sq_h: e.activation(
                                out=sq_h[:, :], in_=tf[:, :], func=AF.Square),
                                reads=[btf], writes=[bsq_h])
                    if attn_lvl < 0.25:
                        continue
                    for tn in ("k", "q", "p"):
                        ps_, bps_ = sbank[tn]
                        sq_h, bsq_h = sqb[tfm[tn]], b_sq[tfm[tn]]
                        p.op("pe", lambda e, ps_=ps_, sq_h=sq_h: e.matmul(
                            out=ps_[0:65, :], lhsT=cs["onespair"][:, :], rhs=sq_h[:, :],
                            start=True, stop=True),
                            reads=[bsq_h, b_cs["onespair"]], writes=[bps_])
                        r65, br65 = rows65[tfm[tn] % 2], b_rows65[tfm[tn] % 2]
                        rs65, brs65 = rstd65[tfm[tn]], b_rstd65[tfm[tn]]
                        epsc = 0 if tn in ("k", "q") else 1
                        p.op("dve", lambda e, ps_=ps_, r65=r65: e.tensor_copy(
                            r65[:, :], ps_[0:65, :]),
                            reads=[bps_], writes=[br65])
                        p.op("act", lambda e, epsc=epsc, r65=r65: e.activation(
                            out=r65[:, :], in_=r65[:, :],
                            func=AF.Ln, scale=1.0 / 64, bias=cs["fconst"][0:65, epsc:epsc + 1]),
                            reads=[b_cs["fconst"]], writes=[br65])
                        if tn == "q":
                            p.op("act", lambda e, r65=r65, rs65=rs65: e.activation(
                                out=rs65[:, :], in_=r65[:, :],
                                func=AF.Exp, scale=-0.5, bias=cs["fconst"][0:65, 2:3]),
                                reads=[br65, b_cs["fconst"]], writes=[brs65])
                        else:
                            p.op("act", lambda e, r65=r65, rs65=rs65: e.activation(
                                out=rs65[:, :], in_=r65[:, :], func=AF.Exp, scale=-0.5),
                                reads=[br65], writes=[brs65])
                    for tn in ("k", "q", "p"):
                        ps_, bps_ = sbank[tn]
                        tf, btf = Tf[tfm[tn]], b_Tf[tfm[tn]]
                        rs65, brs65 = rstd65[tfm[tn]], b_rstd65[tfm[tn]]
                        for hh in range(2):
                            p.op("pe", lambda e, hh=hh, ps_=ps_, rs65=rs65,
                                 lrow={"k": krw_b, "q": qrw_b, "p": onesp64}[tn]: e.matmul(
                                out=ps_[64 * hh:64 * hh + 64, :],
                                lhsT=lrow[64 * hh:64 * hh + 1, :],
                                rhs=rs65[64 * hh:64 * hh + 1, :],
                                start=True, stop=True),
                                reads=[brs65,
                                       {"k": b_krw, "q": b_qrw, "p": b_cs["ones1x64"]}[tn]],
                                writes=[bps_])
                            if tn in ("k", "q"):
                                dst, bdst = (k_ext[hh], b_ke[hh]) if tn == "k" else (q_ext[hh], b_qe[hh])
                                p.op("dve", lambda e, tf=tf, hh=hh, c0=c0, dst=dst, ps_=ps_: e.tensor_tensor(
                                    out=dst[64:128, c0:c0 + 512], in0=tf[64 * hh:64 * hh + 64, :],
                                    in1=ps_[64 * hh:64 * hh + 64, :], op=ALU.mult),
                                    reads=[btf, bps_], writes=[bdst])
                            else:
                                p.op("dve", lambda e, tf=tf, hh=hh, c0=c0, ps_=ps_: e.tensor_tensor(
                                    out=ls1[0:64, 0:512], in0=tf[64 * hh:64 * hh + 64, :],
                                    in1=ps_[64 * hh:64 * hh + 64, :], op=ALU.mult),
                                    reads=[btf, bps_], writes=[b_ls1])
                                pr, bpr = protc[hh], b_pr[hh]
                                p.op("dve", lambda e, c0=c0, pr=pr: e.tensor_tensor(
                                    out=pr[0:64, c0:c0 + 512], in0=ls1[0:64, 0:512],
                                    in1=cs["cosb64"][:, c0:c0 + 512], op=ALU.mult),
                                    reads=[b_ls1, b_cs["cosb64"]], writes=[bpr])
                                p.op("pool", lambda e, c0=c0, pr=pr: e.tensor_tensor(
                                    out=pr[64:96, c0:c0 + 512], in0=ls1[32:64, 0:512],
                                    in1=cs["sinb64"][32:64, c0:c0 + 512], op=ALU.mult),
                                    reads=[b_ls1, b_cs["sinb64"]], writes=[bpr])
                                p.op("pool", lambda e, c0=c0, pr=pr: e.tensor_tensor(
                                    out=pr[96:128, c0:c0 + 512], in0=ls1[0:32, 0:512],
                                    in1=cs["sinb64"][0:32, c0:c0 + 512], op=ALU.mult),
                                    reads=[b_ls1, b_cs["sinb64"]], writes=[bpr])
                                p6, bp6 = p64x2[hh], b_p6[hh]
                                p.op("dve", lambda e, tf=tf, hh=hh, c0=c0, p6=p6: e.tensor_scalar(
                                    out=p6[0:64, c0:c0 + 512], in0=tf[64 * hh:64 * hh + 64, :],
                                    scalar1=1.0 / 64, scalar2=None, op0=ALU.mult),
                                    reads=[btf], writes=[bp6])
                                p.op("pool", lambda e, tf=tf, hh=hh, c0=c0, p6=p6: e.tensor_scalar(
                                    out=p6[64:128, c0:c0 + 512], in0=tf[64 * hh:64 * hh + 64, :],
                                    scalar1=1.0 / 64, scalar2=None, op0=ALU.mult),
                                    reads=[btf], writes=[bp6])
                for jt in range(16 if attn_lvl >= 1 else 0):
                    pT = pS2.bitcast(BF16)
                    p.op("pe", lambda e, jt=jt, pT=pT: e.transpose(
                        out=pT[:, 0:128], in_=v_fm[:, jt * 128:(jt + 1) * 128],
                        identity=cs["ident_bf"][:, :]),
                        reads=[b_vfm, b_cs["ident_bf"]], writes=[bufS2])
                    for hh in range(2):
                        p.op("dve", lambda e, jt=jt, hh=hh, pT=pT: e.tensor_copy(
                            v_ext[hh][:, jt * 68:jt * 68 + 64], pT[:, 64 * hh:64 * hh + 64]),
                            reads=[bufS2], writes=[b_ve[hh]])

                if hp + 1 < heads // 2:
                    emit_wload(hp + 1)
                    emit_xload(0)

                for hh in range(2):
                    h = hp * 2 + hh
                    slope = 2.0 ** (-(h + 1) / 2.0)
                    qe, ke, pr, p6, ve = q_ext[hh], k_ext[hh], protc[hh], p64x2[hh], v_ext[hh]
                    bqe, bke, bpr, bp6, bve = b_qe[hh], b_ke[hh], b_pr[hh], b_p6[hh], b_ve[hh]

                    for g in range(4):
                        c0 = g * 512
                        sq_h, bsq_h = sqb[g % 2], b_sq[g % 2]
                        rk, brk = rowkq[g % 2], b_rowkq[g % 2]
                        bY_h = bufY0 if ((c0 % 1024) == 0) else bufY1
                        p.op("act", lambda e, ke=ke, c0=c0, sq_h=sq_h: e.activation(
                            out=sq_h[0:64, :], in_=ke[64:128, c0:c0 + 512], func=AF.Square),
                            reads=[bke], writes=[bsq_h])
                        p.op("pe", lambda e, c0=c0, sq_h=sq_h: e.matmul(
                            out=pY[0:1, (c0 % 1024):(c0 % 1024) + 512],
                            lhsT=cs["ones64"][:, :], rhs=sq_h[0:64, :],
                            start=True, stop=True),
                            reads=[bsq_h, b_cs["ones64"]], writes=[bY_h])
                        p.op("dve", lambda e, c0=c0, rk=rk: e.tensor_copy(
                            rk[:, :], pY[0:1, (c0 % 1024):(c0 % 1024) + 512]),
                            reads=[bY_h], writes=[brk])
                        if g == 0:
                            p.op("dve", lambda e, rk=rk: e.reduce_max(kmx[:, :], rk[:, :], axis=AX),
                                 reads=[brk], writes=[b_kmx])
                        else:
                            p.op("dve", lambda e, rk=rk: e.reduce_max(kmg[:, :], rk[:, :], axis=AX),
                                 reads=[brk], writes=[b_kmg])
                            p.op("dve", lambda e: e.tensor_tensor(
                                out=kmx[:, :], in0=kmg[:, :], in1=kmx[:, :], op=ALU.max),
                                reads=[b_kmg], writes=[b_kmx])
                    p.op("act", lambda e: e.activation(out=kmx[:, :], in_=kmx[:, :], func=AF.Ln),
                         writes=[b_kmx])
                    p.op("dve", lambda e: e.tensor_scalar(
                        out=kmx[:, :], in0=kmx[:, :], scalar1=0.5, scalar2=None, op0=ALU.mult),
                        writes=[b_kmx])
                    for g in range(4):
                        c0 = g * 512
                        sq_h, bsq_h = sqb[g % 2], b_sq[g % 2]
                        rk, brk = rowkq[g % 2], b_rowkq[g % 2]
                        bY_h = bufY0 if ((c0 % 1024) == 0) else bufY1
                        p.op("act", lambda e, qe=qe, c0=c0, sq_h=sq_h: e.activation(
                            out=sq_h[0:64, :], in_=qe[64:128, c0:c0 + 512], func=AF.Square),
                            reads=[bqe], writes=[bsq_h])
                        p.op("pe", lambda e, c0=c0, sq_h=sq_h: e.matmul(
                            out=pY[0:1, (c0 % 1024):(c0 % 1024) + 512],
                            lhsT=cs["ones64"][:, :], rhs=sq_h[0:64, :],
                            start=True, stop=True),
                            reads=[bsq_h, b_cs["ones64"]], writes=[bY_h])
                        p.op("dve", lambda e, c0=c0, rk=rk: e.tensor_copy(
                            rk[:, :], pY[0:1, (c0 % 1024):(c0 % 1024) + 512]),
                            reads=[bY_h], writes=[brk])
                        p.op("act", lambda e, rk=rk: e.activation(
                            out=rk[:, :], in_=rk[:, :], func=AF.Ln),
                            writes=[brk])
                        p.op("act", lambda e, rk=rk, c0=c0: e.activation(
                            out=m_row[:, c0:c0 + 512], in_=rk[:, :],
                            func=AF.Exp, scale=0.5, bias=kmx[:, :]),
                            reads=[brk, b_kmx], writes=[b_mrow])
                        p.op("dve", lambda e, c0=c0: e.tensor_scalar(
                            out=m_row[:, c0:c0 + 512], in0=m_row[:, c0:c0 + 512],
                            scalar1=-MSHIFT, scalar2=None, op0=ALU.add),
                            writes=[b_mrow])
                    pc = 0
                    for J in range(16 if attn_lvl >= 2 else 0):
                        w_J = T - 128 * J
                        for (b0, b1) in splits(0, w_J, 1024):
                            lsx, blsx = ls1_[pc % 2], b_ls1_[pc % 2]
                            pc += 1
                            for (a0, a1) in splits(b0, b1):
                                ps, bps = PSMALL[(a0 // 512) % 2]
                                p.op("pe", lambda e, J=J, a0=a0, a1=a1, ps=ps, pr=pr, p6=p6: e.matmul(
                                    out=ps[:, 0:a1 - a0], lhsT=pr[:, J * 128:(J + 1) * 128],
                                    rhs=p6[:, J * 128 + a0:J * 128 + a1], start=True, stop=True),
                                    reads=[bpr, bp6], writes=[bps])
                                p.op("act", lambda e, a0=a0, a1=a1, b0=b0, ps=ps, lsx=lsx: e.activation(
                                    out=lsx[:, a0 - b0:a1 - b0], in_=ps[:, 0:a1 - a0],
                                    func=AF.Exp, scale=-1.0),
                                    reads=[bps], writes=[blsx])
                            p.op("act", lambda e, b0=b0, b1=b1, lsx=lsx: e.activation(
                                out=lsx[:, 0:b1 - b0], in_=lsx[:, 0:b1 - b0], func=AF.Ln,
                                bias=cs["fconst"][:, 3:4]),
                                reads=[b_cs["fconst"]], writes=[blsx])
                            p.op("pool", lambda e, J=J, b0=b0, b1=b1, slope=slope, lsx=lsx: e.tensor_scalar(
                                out=biasT[:, offJ[J] + b0:offJ[J] + b1], in0=lsx[:, 0:b1 - b0],
                                scalar1=-slope, scalar2=0.0, op0=ALU.mult, op1=ALU.add),
                                reads=[blsx], writes=[b_biasT[J]])
                        p.op("dve", lambda e, J=J: e.tensor_tensor(
                            out=biasT[:, offJ[J]:offJ[J] + 128], in0=biasT[:, offJ[J]:offJ[J] + 128],
                            in1=cs["utmask"][:, :], op=ALU.mult),
                            reads=[b_cs["utmask"]], writes=[b_biasT[J]])

                    for gq in range(4 if attn_lvl >= 3 else 0):
                        i0g = gq * 512
                        Jmax = 4 * gq + 3
                        p.op("pe", lambda e, i0g=i0g, Jmax=Jmax: [
                            e.matmul(out=pS1[0:16, max(0, 128 * J - i0g):512],
                                     lhsT=cs["ecols"][:, 16 * J:16 * J + 16],
                                     rhs=biasT[:, offJ[J] + max(0, i0g - 128 * J):offJ[J] + i0g + 512 - 128 * J],
                                     start=(J == 0), stop=(J == Jmax))
                            for J in range(Jmax + 1)][-1],
                            reads=[b_cs["ecols"]] + [b_biasT[J] for J in range(Jmax + 1)],
                            writes=[bufS1])
                        p.op("dve", lambda e: e.tensor_copy(csth[:, :], pS1[0:16, :]),
                             reads=[bufS1], writes=[b_csth])
                        p.op("dve", lambda e: e.tensor_tensor(
                            out=cstl[:, :], in0=pS1[0:16, :], in1=csth[:, :], op=ALU.subtract),
                            reads=[bufS1, b_csth], writes=[b_cstl])
                        p.op("pe", lambda e, i0g=i0g: [
                            e.matmul(out=pS2[0:16, :], lhsT=cs["l16"][:, :], rhs=csth[:, :],
                                     start=True, stop=False),
                            e.matmul(out=pS2[0:16, :], lhsT=cs["l16"][:, :], rhs=cstl[:, :],
                                     start=False, stop=False),
                            e.matmul(out=pS2[0:16, :], lhsT=cs["neg1"][:, :],
                                     rhs=m_row[:, i0g:i0g + 512], start=False, stop=True)][-1],
                            reads=[b_csth, b_cstl, b_mrow, b_cs["l16"], b_cs["neg1"]],
                            writes=[bufS2])
                        p.op("dve", lambda e, i0g=i0g, qe=qe: e.tensor_copy(
                            qe[0:16, i0g:i0g + 512], pS2[0:16, :]),
                            reads=[bufS2], writes=[bqe])
                        p.op("dve", lambda e, i0g=i0g, qe=qe: e.tensor_tensor(
                            out=qe[32:48, i0g:i0g + 512], in0=pS2[0:16, :],
                            in1=qe[0:16, i0g:i0g + 512], op=ALU.subtract),
                            reads=[bufS2], writes=[bqe])

                    SCF = [pr.bitcast(F32), p6.bitcast(F32), ls1_[0]]
                    b_SCF = [bpr, bp6, b_ls1_[0]]
                    ET3 = [ET[0], ET[1], v_fm[:, 0:1024]]
                    b_ET3 = [b_ET[0], b_ET[1], b_vfm]
                    for H in range(2 if attn_lvl >= 4 else 0):
                        ib = H * 1024
                        Jlist = [J for J in range(16) if 128 * J < ib + 1024]
                        pend_mm3 = []
                        for J in Jlist:
                            i0 = max(128 * J, ib)
                            i1 = ib + 1024
                            o0 = i0 - ib
                            pX = pA if J % 2 == 0 else pB
                            bpXb = [bufA0, bufA1] if J % 2 == 0 else [bufB0, bufB1]
                            et, bet = ET3[J % 3], b_ET3[J % 3]
                            scf, bscf = SCF[J % 3], b_SCF[J % 3]
                            chain = []
                            for (a0, a1) in splits(o0, 1024):
                                p.op("pe", lambda e, J=J, a0=a0, a1=a1, ib=ib, pX=pX, ke=ke, qe=qe: e.matmul(
                                    out=pX[:, a0:a1], lhsT=ke[:, J * 128:(J + 1) * 128],
                                    rhs=qe[:, ib + a0:ib + a1], start=True, stop=False),
                                    reads=[bke, bqe], writes=[bpXb[a0 // 512]])
                                p.op("pe", lambda e, J=J, a0=a0, a1=a1, ib=ib, pX=pX: e.matmul(
                                    out=pX[:, a0:a1], lhsT=cs["slmat"][:, :],
                                    rhs=biasT[:, offJ[J] + ib + a0 - 128 * J:offJ[J] + ib + a1 - 128 * J],
                                    start=False, stop=True),
                                    reads=[b_biasT[J], b_cs["slmat"]], writes=[bpXb[a0 // 512]])
                                p.op("dve", lambda e, a0=a0, a1=a1, o0=o0, pX=pX, scf=scf: e.tensor_copy(
                                    scf[:, a0 - o0:a1 - o0], pX[:, a0:a1]),
                                    reads=[bpXb[a0 // 512]], writes=[bscf])
                                p.op("act", lambda e, a0=a0, a1=a1, o0=o0, et=et, scf=scf: e.activation(
                                    out=et[:, a0 - o0:a1 - o0], in_=scf[:, a0 - o0:a1 - o0],
                                    func=AF.Exp),
                                    reads=[bscf], writes=[bet])
                                if 128 * J >= ib and a0 == o0:
                                    p.op("dve", lambda e, et=et: e.tensor_tensor(
                                        out=et[:, 0:128], in0=et[:, 0:128],
                                        in1=cs["utmask"][:, :], op=ALU.mult),
                                        reads=[b_cs["utmask"]], writes=[bet])
                                lastJ_for_bank = max(Jx for Jx in Jlist if 128 * Jx < ib + (a0 // 512 + 1) * 512)
                                chain.append((J, a0, a1, i0, et, J == 0, J == lastJ_for_bank))
                            pend_mm3.append((chain, bet, bve))
                            if len(pend_mm3) >= 3:
                                ch, betx, bvex = pend_mm3.pop(0)
                                for (Jx, a0, a1, i0x, etx, st, sp_) in ch:
                                    p.op("pe", lambda e, Jx=Jx, a0=a0, a1=a1, i0x=i0x, etx=etx, ve=ve,
                                         st=st, sp_=sp_, ib=ib: e.matmul(
                                        out=pY[0:65, a0:a1], lhsT=ve[:, Jx * 68:Jx * 68 + 65],
                                        rhs=etx[:, ib + a0 - i0x:ib + a1 - i0x], start=st, stop=sp_),
                                        reads=[betx, bvex], writes=[bufY0 if a0 < 512 else bufY1])
                        for (ch, betx, bvex) in pend_mm3:
                            for (Jx, a0, a1, i0x, etx, st, sp_) in ch:
                                p.op("pe", lambda e, Jx=Jx, a0=a0, a1=a1, i0x=i0x, etx=etx, ve=ve,
                                     st=st, sp_=sp_, ib=ib: e.matmul(
                                    out=pY[0:65, a0:a1], lhsT=ve[:, Jx * 68:Jx * 68 + 65],
                                    rhs=etx[:, ib + a0 - i0x:ib + a1 - i0x], start=st, stop=sp_),
                                    reads=[betx, bvex], writes=[bufY0 if a0 < 512 else bufY1])
                        p.op("dve", lambda e: e.reciprocal(rzrow[:, :], pY[64:65, :]),
                             reads=[bufY0, bufY1], writes=[b_rz])
                        p.op("dve", lambda e: e.tensor_copy(rzhi[:, :], rzrow[:, :]),
                             reads=[b_rz], writes=[b_rzhi])
                        p.op("dve", lambda e: e.tensor_tensor(
                            out=rzlo[:, :], in0=rzrow[:, :], in1=rzhi[:, :], op=ALU.subtract),
                            reads=[b_rz, b_rzhi], writes=[b_rzlo])
                        for (a0, a1) in ((0, 512), (512, 1024)):
                            ps, bps = PSMALL[(a0 // 512) % 2]
                            p.op("pe", lambda e, a0=a0, a1=a1, ps=ps: [
                                e.matmul(out=ps[0:64, :], lhsT=cs["ones1x64"][:, :],
                                         rhs=rzhi[:, a0:a1], start=True, stop=False),
                                e.matmul(out=ps[0:64, :], lhsT=cs["ones1x64"][:, :],
                                         rhs=rzlo[:, a0:a1], start=False, stop=True)][-1],
                                reads=[b_rzhi, b_rzlo, b_cs["ones1x64"]], writes=[bps])
                            p.op("dve", lambda e, a0=a0, a1=a1, ps=ps: e.tensor_copy(
                                rzb_sb[:, a0:a1], ps[0:64, :]),
                                reads=[bps], writes=[b_rzb])
                        if attn_lvl < 5:
                            continue
                        yst, byst = ystage[H], b_yst[H]
                        p.op("dve", lambda e, yst=yst: e.tensor_tensor(
                            out=yst[:, :], in0=pY[0:64, :], in1=rzb_sb[:, :], op=ALU.mult),
                            reads=[bufY0, bufY1, b_rzb], writes=[byst])
                        p.op("sp", lambda e, s=s, h=h, H=H, yst=yst: e.dma_start(
                            out=yscr[s, h, :, H * 1024:(H + 1) * 1024], in_=yst[:, :]),
                            reads=[byst], writes=[b_yscr], dma=True)
                    if ("scoresT" in tap_set) and h == 0 and s == 0:
                        tap("scoresT", lambda: ET[1], [128, 1024], BF16, deps=[b_ET[1].w])

            if attn_lvl < 6:
                continue
            p.op("sp", lambda e: e.dma_start(
                out=wstream.rearrange("p (kc m) -> p kc m", kc=NKC)[:, :, :],
                in_=wscr["wprojT"].rearrange("(kc p) m -> p kc m", p=128)[:, :, :]),
                writes=[b_wproj], reads=[b_wscr["wprojT"]],
                deps=[ev for b in b_wh for ev in ([b.w] + b.r)], dma=True)
            for g in range(4):
                for half2 in range(2):
                    p.op("sp", lambda e, s=s, g=g, half2=half2: e.dma_start(
                        out=protc[half2].rearrange("p (kc t) -> p kc t", kc=4)[:, :, :],
                        in_=yscr[s].rearrange("(k g2) d t -> (g2 d) k t", g2=2)[:, half2 * 4:half2 * 4 + 4, g * 512:(g + 1) * 512]),
                        reads=[b_yscr], writes=[b_pr[half2]], dma=True)

                def ybv(kc):
                    return protc[kc // 4][:, (kc % 4) * 512:(kc % 4 + 1) * 512]

                for m in range(NKC):
                    ps, bps = PSMALL[m % 2]
                    p.op("pe", lambda e, m=m, ps=ps, ybv=ybv: [
                        e.matmul(out=ps[:, :], lhsT=wstream[:, kc * 1024 + m * 128:kc * 1024 + (m + 1) * 128],
                                 rhs=ybv(kc),
                                 start=(kc == 0), stop=(kc == NKC - 1))
                        for kc in range(NKC)][-1],
                        reads=[b_wproj, b_pr[0], b_pr[1]], writes=[bps])
                    ao, bao = ao_st[0], b_ao[0]
                    p.op("dve", lambda e, ps=ps, ao=ao: e.tensor_copy(ao[:, :], ps[:, :]),
                         reads=[bps], writes=[bao])
                    p.op("sp", lambda e, m=m, s=s, g=g, ao=ao: e.dma_start(
                        out=accd[m, :, s * T + g * 512:s * T + (g + 1) * 512], in_=ao[:, :]),
                        reads=[bao], writes=[b_accd], dma=True)
        tap("attn_acc", lambda: accd, [NKC, 128, 2 * T], F32, deps=list(b_accd.wl))

        p.barrier()
        main_es.close()
        ffn_es = ExitStack()
        fsb = lambda name, shp, dt: ffn_es.enter_context(nc.sbuf_tensor(name, shp, dt)).ap()
        xnf = fsb("xnf", [128, NKC * 2 * T], BF16)
        b_xnf = Buf()
        gsl = [fsb(f"gsl{i}", [128, NKC * 512], BF16) for i in range(2)]
        usl = [fsb(f"usl{i}", [128, NKC * 512], BF16) for i in range(2)]
        dsl = [fsb(f"dsl{i}", [128, 4 * 1024], BF16) for i in range(2)]
        b_gsl = [Buf(), Buf()]
        b_usl = [Buf(), Buf()]
        b_dsl = [Buf(), Buf()]
        gub_s = fsb("gub_s", [128, 32], F32)
        gub1 = fsb("gub1", [128, 32], F32)
        b_gub, b_gub1 = Buf(), Buf()
        gwb = fsb("gwb", [128, 4 * 1024], F32)
        b_gwb = Buf()
        act_sb = [fsb(f"act_sb{i}", [128, 4 * 1024], BF16) for i in range(2)]
        b_act = [Buf(), Buf()]
        glu_t = [fsb(f"glu_t{i}", [128, 1024], F32) for i in range(2)]
        upt = [fsb(f"upt{i}", [128, 1024], F32) for i in range(2)]
        b_glu = [Buf(), Buf()]
        b_upt = [Buf(), Buf()]
        dstage = [fsb(f"dstage{i}", [128, 1024], F32) for i in range(2)]
        b_dst = [Buf(), Buf()]

        p.op("sp", lambda e: e.dma_start(out=xnf[:, :], in_=xnscr),
             reads=[b_xnscr], writes=[b_xnf], dma=True)

        def xnfv(kc, c0, c1):
            return xnf[:, kc * 2 * T + c0:kc * 2 * T + c1]

        it_list = [(e_, iq) for e_ in range(experts) for iq in range(4)]
        for it_idx, (ex, iq) in enumerate(it_list):
            par = it_idx % 2
            bg, bu, bd = b_gsl[par], b_usl[par], b_dsl[par]
            gs, us, ds = gsl[par], usl[par], dsl[par]
            p.op("sp", lambda e, ex=ex, iq=iq, gs=gs: e.dma_start(
                out=gs.rearrange("p (kc m) -> p kc m", kc=NKC)[:, :, :],
                in_=gupb[ex].rearrange("(kc p) m -> p kc m", p=128)[:, :, iq * 512:(iq + 1) * 512]),
                writes=[bg], reads=[b_gupb], dma=True)
            p.op("sp", lambda e, ex=ex, iq=iq, us=us: e.dma_start(
                out=us.rearrange("p (kc m) -> p kc m", kc=NKC)[:, :, :],
                in_=gupb[ex].rearrange("(kc p) m -> p kc m", p=128)[:, :, I + iq * 512:I + (iq + 1) * 512]),
                writes=[bu], reads=[b_gupb], dma=True)
            p.op("sp", lambda e, ex=ex, iq=iq, ds=ds: e.dma_start(
                out=ds.rearrange("p (ic h) -> p ic h", ic=4)[:, :, :],
                in_=dwnb[ex, iq * 512:(iq + 1) * 512, :].rearrange("(ic p) h -> p ic h", p=128)[:, :, :]),
                writes=[bd], reads=[b_dwnb], dma=True)
            if iq == 0:
                p.op("pool", lambda e, ex=ex: e.dma_start(out=gub_s[:, :], in_=gub_d[ex]),
                     writes=[b_gub], dma=True)
                p.op("dve", lambda e: e.tensor_scalar(
                    out=gub1[:, :], in0=gub_s[:, :], scalar1=1.0, scalar2=None, op0=ALU.add),
                    reads=[b_gub], writes=[b_gub1])
                for gg in range(NKC):
                    ps, bps = PSMALL[gg % 2]
                    p.op("pe", lambda e, ex=ex, gg=gg, ps=ps: [
                        e.matmul(out=ps[:, :], lhsT=cs["oneh8"][:, ex * 128:(ex + 1) * 128],
                                 rhs=gthi[:, gg * 512:(gg + 1) * 512], start=True, stop=False),
                        e.matmul(out=ps[:, :], lhsT=cs["oneh8"][:, ex * 128:(ex + 1) * 128],
                                 rhs=gtlo[:, gg * 512:(gg + 1) * 512], start=False, stop=True)][-1],
                        reads=[b_gthi, b_gtlo, b_cs["oneh8"]], writes=[bps])
                    p.op("dve", lambda e, gg=gg, ps=ps: e.tensor_copy(
                        gwb[:, gg * 512:(gg + 1) * 512], ps[:, :]),
                        reads=[bps], writes=[b_gwb])
            for tgi in range(5):
                if tgi < 4:
                    tg = tgi
                    tc0 = tg * 1024
                    asb, bact = act_sb[tg % 2], b_act[tg % 2]
                    for pr_ in range(4):
                        ppar = pr_ % 2
                        glu, bglu = glu_t[ppar], b_glu[ppar]
                        up_t, bupt = upt[ppar], b_upt[ppar]
                        p.op("pe", lambda e, pr_=pr_, gs=gs, tc0=tc0: [
                            e.matmul(out=pA[:, gg * 512:(gg + 1) * 512],
                                     lhsT=gs[:, kc * 512 + pr_ * 128:kc * 512 + (pr_ + 1) * 128],
                                     rhs=xnfv(kc, tc0 + gg * 512, tc0 + (gg + 1) * 512),
                                     start=(kc == 0), stop=(kc == NKC - 1))
                            for kc in range(NKC) for gg in range(2)][-1],
                            reads=[bg, b_xnf], writes=[bufA0, bufA1])
                        p.op("dve", lambda e, pr_=pr_, iq=iq, glu=glu: e.tensor_scalar(
                            out=glu[:, :], in0=pA[:, :], scalar1=gub_s[:, iq * 4 + pr_:iq * 4 + pr_ + 1],
                            op0=ALU.add, scalar2=LIMIT, op1=ALU.min),
                            reads=[bufA0, bufA1, b_gub], writes=[bglu])
                        p.op("act", lambda e, glu=glu: e.activation(
                            out=glu[:, :], in_=glu[:, :], func=AF.Silu, scale=ALPHA),
                            writes=[bglu])
                        p.op("pe", lambda e, pr_=pr_, us=us, tc0=tc0: [
                            e.matmul(out=pB[:, gg * 512:(gg + 1) * 512],
                                     lhsT=us[:, kc * 512 + pr_ * 128:kc * 512 + (pr_ + 1) * 128],
                                     rhs=xnfv(kc, tc0 + gg * 512, tc0 + (gg + 1) * 512),
                                     start=(kc == 0), stop=(kc == NKC - 1))
                            for kc in range(NKC) for gg in range(2)][-1],
                            reads=[bu, b_xnf], writes=[bufB0, bufB1])
                        p.op("dve", lambda e, pr_=pr_, iq=iq, up_t=up_t: e.tensor_scalar(
                            out=up_t[:, :], in0=pB[:, :], scalar1=gub1[:, 16 + iq * 4 + pr_:16 + iq * 4 + pr_ + 1],
                            op0=ALU.add, scalar2=-(LIMIT - 1.0), op1=ALU.max),
                            reads=[bufB0, bufB1, b_gub1], writes=[bupt])
                        p.op("pool", lambda e, up_t=up_t: e.tensor_scalar(
                            out=up_t[:, :], in0=up_t[:, :], scalar1=LIMIT + 1.0, scalar2=-3.0e38,
                            op0=ALU.min, op1=ALU.max),
                            writes=[bupt])
                        p.op("pool" if pr_ % 2 == 0 else "dve",
                             lambda e, pr_=pr_, asb=asb, up_t=up_t, glu=glu: e.tensor_tensor(
                            out=asb[:, pr_ * 1024:(pr_ + 1) * 1024], in0=up_t[:, :], in1=glu[:, :],
                            op=ALU.mult),
                            reads=[bupt, bglu], writes=[bact])
                if tgi >= 1:
                    tg = tgi - 1
                    tc0 = tg * 1024
                    asb, bact = act_sb[tg % 2], b_act[tg % 2]
                    for hm in range(NKC):
                        if hm % 2 == 0:
                            outs = [(pY[:, 0:512], bufY0), (pY[:, 512:1024], bufY1)]
                        else:
                            outs = [(pS1[:, :], bufS1), (pS2[:, :], bufS2)]
                        p.op("pe", lambda e, hm=hm, ds=ds, asb=asb, outs=outs: [
                            e.matmul(out=outs[gg][0],
                                     lhsT=ds[:, ic * 1024 + hm * 128:ic * 1024 + (hm + 1) * 128],
                                     rhs=asb[:, ic * 1024 + gg * 512:ic * 1024 + (gg + 1) * 512],
                                     start=(ic == 0), stop=(ic == 3))
                            for ic in range(4) for gg in range(2)][-1],
                            reads=[bd, bact], writes=[outs[0][1], outs[1][1]])
                        dst, bdst2 = dstage[hm % 2], b_dst[hm % 2]
                        p.op("dve", lambda e, tc0=tc0, outs=outs, dst=dst: [
                            e.tensor_tensor(out=dst[:, gg * 512:(gg + 1) * 512], in0=outs[gg][0],
                                            in1=gwb[:, tc0 + gg * 512:tc0 + (gg + 1) * 512],
                                            op=ALU.mult)
                            for gg in range(2)][-1],
                            reads=[outs[0][1], outs[1][1], b_gwb], writes=[bdst2])
                        p.op("pool", lambda e, hm=hm, tc0=tc0, dst=dst: e.dma_start(
                            out=accd[hm, :, tc0:tc0 + 1024], in_=dst[:, :], accum_op=ALU.add),
                            reads=[bdst2], writes=[b_accd], dma=True)
        for hm in range(NKC):
            for g in range(8):
                ps, bps = PSMALL[g % 2]
                p.op("pe", lambda e, hm=hm, g=g, ps=ps: [
                    e.matmul(out=ps[:, :], lhsT=db_sb[:, hm * 128:(hm + 1) * 128],
                             rhs=gthi[:, g * 512:(g + 1) * 512], start=True, stop=False),
                    e.matmul(out=ps[:, :], lhsT=db_sb[:, hm * 128:(hm + 1) * 128],
                             rhs=gtlo[:, g * 512:(g + 1) * 512], start=False, stop=True)][-1],
                    reads=[b_db, b_gthi, b_gtlo], writes=[bps])
                dst, bdst2 = dstage[g % 2], b_dst[g % 2]
                p.op("dve", lambda e, ps=ps, dst=dst: e.tensor_copy(dst[:, 0:512], ps[:, :]),
                     reads=[bps], writes=[bdst2])
                p.op("pool", lambda e, hm=hm, g=g, dst=dst: e.dma_start(
                    out=accd[hm, :, g * 512:(g + 1) * 512], in_=dst[:, 0:512], accum_op=ALU.add),
                    reads=[bdst2], writes=[b_accd], dma=True)

        xr = [gsl[0].bitcast(F32)[:, 0:1024], gsl[0].bitcast(F32)[:, 1024:2048]]
        b_xr = [b_gsl[0], b_gsl[0]]
        astage = [usl[0].bitcast(F32)[:, 0:1024], usl[0].bitcast(F32)[:, 1024:2048]]
        b_ast = [b_usl[0], b_usl[0]]
        ostage = [dsl[0].bitcast(F32)[:, 0:1024], dsl[0].bitcast(F32)[:, 1024:2048]]
        b_ost = [b_dsl[0], b_dsl[0]]
        out_evs = []
        for s in range(2):
            for tt in range(NTT):
                par = tt % 2
                gc0 = s * T + tt * 128
                xrr, bxr = xr[par], b_xr[par]
                p.op("sp", lambda e, s=s, tt=tt, xrr=xrr: e.dma_start(
                    out=xrr[:, :], in_=x_d[s, tt * 128:(tt + 1) * 128, :]),
                    writes=[bxr], dma=True)
                ast, bast = astage[par], b_ast[par]
                p.op("sp", lambda e, gc0=gc0, ast=ast: e.dma_start(
                    out=ast.rearrange("p (kc t) -> p kc t", kc=NKC)[:, :, :],
                    in_=accd.rearrange("kc p t -> p kc t")[:, :, gc0:gc0 + 128]),
                    reads=[b_accd], writes=[bast], dma=True)
                for half in range(2):
                    ps, bps = PSMALL[half]
                    p.op("pe", lambda e, half=half, ast=ast, ps=ps: [
                        e.transpose(out=ps[:, q * 128:(q + 1) * 128],
                                    in_=ast[:, (half * 4 + q) * 128:(half * 4 + q + 1) * 128],
                                    identity=cs["ident_f32"][:, :])
                        for q in range(4)][-1],
                        reads=[bast, b_cs["ident_f32"]], writes=[bps])
                    ost, bost = ostage[par], b_ost[par]
                    p.op("dve", lambda e, half=half, xrr=xrr, ps=ps, ost=ost: e.tensor_tensor(
                        out=ost[:, half * 512:(half + 1) * 512], in0=ps[:, :],
                        in1=xrr[:, half * 512:(half + 1) * 512], op=ALU.add),
                        reads=[bps, bxr], writes=[bost])
                ost, bost = ostage[par], b_ost[par]
                ev = p.op("sp", lambda e, s=s, tt=tt, ost=ost: e.dma_start(
                    out=out_d[s, tt * 128:(tt + 1) * 128, :], in_=ost[:, :]),
                    reads=[bost], writes=[b_out], dma=True)
                out_evs.append(ev)
        p.op("sp", lambda e: e.nop(), deps=out_evs)

        with nc.Block() as block:
            counts = p.emit(nc, block, sems, dma_sems)
        ffn_es.close()
    return nc, counts, tap_out


_BUILD_CACHE = {}


def _get_nc():
    key = "full"
    if key not in _BUILD_CACHE:
        _BUILD_CACHE[key] = build_nc()
    return _BUILD_CACHE[key]


def kernel(L0_attn_x, L0_attn_Wq, L0_attn_Wk, L0_attn_Wv, L0_attn_Wp, L0_attn_Wproj,
           L0_attn_q_rms_w, L0_attn_k_rms_w, L0_ffn_router_weight, L0_ffn_router_bias,
           L0_ffn_gate_up_proj, L0_ffn_gate_up_bias, L0_ffn_down_proj, L0_ffn_down_bias):
    f32 = np.float32
    x = np.ascontiguousarray(np.asarray(L0_attn_x, dtype=f32))
    base = {
        "wqT": np.ascontiguousarray(np.asarray(L0_attn_Wq, f32).T),
        "wkT": np.ascontiguousarray(np.asarray(L0_attn_Wk, f32).T),
        "wvT": np.ascontiguousarray(np.asarray(L0_attn_Wv, f32).T),
        "wpT": np.ascontiguousarray(np.asarray(L0_attn_Wp, f32).T),
        "wprojT": np.ascontiguousarray(np.asarray(L0_attn_Wproj, f32).T),
        "qrw": np.ascontiguousarray(np.asarray(L0_attn_q_rms_w, f32).reshape(1, 64)),
        "krw": np.ascontiguousarray(np.asarray(L0_attn_k_rms_w, f32).reshape(1, 64)),
        "rw_l": np.ascontiguousarray(
            np.asarray(L0_ffn_router_weight, f32).T.reshape(8, 128, 8).transpose(1, 0, 2).reshape(128, 64)),
        "rb": np.ascontiguousarray(np.asarray(L0_ffn_router_bias, f32).reshape(1, 8)),
        "gup": np.ascontiguousarray(np.asarray(L0_ffn_gate_up_proj, f32)),
        "gub_l": np.ascontiguousarray(
            np.asarray(L0_ffn_gate_up_bias, f32).reshape(8, 32, 128).transpose(0, 2, 1)),
        "dwn": np.ascontiguousarray(np.asarray(L0_ffn_down_proj, f32)),
        "dbias": np.ascontiguousarray(np.asarray(L0_ffn_down_bias, f32)),
    }
    base.update(host_consts())
    nc, counts, _ = _get_nc()
    in_maps = []
    for c in range(8):
        m = dict(base)
        m["x"] = np.ascontiguousarray(x[2 * c:2 * c + 2])
        in_maps.append(m)
    res = run_bass_kernel_spmd(nc, in_maps, core_ids=list(range(8)))
    out = np.concatenate([np.asarray(r["out"]).reshape(2, T, C) for r in res.results], axis=0)
    return out.astype(np.float32)


# version 20
# speedup vs baseline: 1.1864x; 1.0110x over previous
import math
from contextlib import ExitStack

import numpy as np
import ml_dtypes
import concourse.bass as bass
import concourse.mybir as mybir
from concourse.bass_utils import run_bass_kernel_spmd

F32 = mybir.dt.float32
BF16 = mybir.dt.bfloat16
AF = mybir.ActivationFunctionType
ALU = mybir.AluOpType
AX = mybir.AxisListType.X

T = 2048
C = 1024
NH = 16
HD = 64
NKC = C // 128
NTT = T // 128
I = 2048
NE = 8
RMS_EPS = 1e-5
FRMS_EPS = 1.1920929e-07
LN_EPS = 1e-5
ALPHA = 1.702
LIMIT = 7.0
MSHIFT = 60.0
DBG = {}

ENGINES = ("pe", "act", "dve", "pool", "sp")
BLOCK_ATTR = {"pe": "tensor", "act": "scalar", "dve": "vector", "pool": "gpsimd", "sp": "sync"}


class Buf:
    __slots__ = ("w", "r", "mw", "wl")
    def __init__(self, mw=False):
        self.w = None
        self.r = []
        self.mw = mw
        self.wl = []


class Prog:
    DMA_POOL = 20

    def __init__(self):
        self.eng_ops = {e: [] for e in ENGINES}
        self.dma_count = {"sp": 0, "pool": 0}

    def op(self, eng, fn, reads=(), writes=(), deps=(), dma=False):
        dep_list = [d for d in deps if d is not None]
        for b in reads:
            if b.mw:
                dep_list.extend(b.wl)
            elif b.w is not None:
                dep_list.append(b.w)
        for b in writes:
            dep_list.extend(b.r)
            if b.mw:
                pass
            elif b.w is not None:
                dep_list.append(b.w)
        rec = {"eng": eng, "fn": fn, "deps": tuple(dep_list), "dma": dma,
               "needed": False, "val": None, "dma_idx": None}
        if dma:
            rec["dma_idx"] = self.dma_count[eng]
            self.dma_count[eng] += 1
        self.eng_ops[eng].append(rec)
        for b in reads:
            b.r.append(rec)
        for b in writes:
            if b.mw:
                b.wl.append(rec)
            else:
                b.w = rec
                b.r = []
        return rec

    def barrier(self):
        lasts = [self.eng_ops[e][-1] for e in ENGINES if self.eng_ops[e]]
        for q in ("sp", "pool"):
            slot_last = {}
            for r in self.eng_ops[q]:
                if r["dma"]:
                    slot_last[r["dma_idx"] % self.DMA_POOL] = r
            lasts.extend(slot_last.values())
        nops = []
        for e in ENGINES:
            nops.append(self.op(e, lambda eng: eng.nop(), deps=lasts))
        return nops

    def emit(self, nc, block, sems, dma_sems):
        for e in ENGINES:
            for r in self.eng_ops[e]:
                for d in r["deps"]:
                    d["needed"] = True
        for e in ENGINES:
            cum = 0
            for r in self.eng_ops[e]:
                if r["dma"]:
                    k = r["dma_idx"]
                    r["val"] = (k // self.DMA_POOL + 1) * 16
                elif r["needed"]:
                    cum += 1
                    r["val"] = cum
        counts = {e: sum(1 for r in self.eng_ops[e]) for e in ENGINES}

        def sem_of(r):
            if r["dma"]:
                return dma_sems[r["eng"]][r["dma_idx"] % self.DMA_POOL]
            return sems[r["eng"]]

        def body(eng_name):
            def go(eng):
                lastwait = {}
                sem = sems[eng_name]
                for r in self.eng_ops[eng_name]:
                    if r["dma"] and r["dma_idx"] >= self.DMA_POOL:
                        dsem = dma_sems[eng_name][r["dma_idx"] % self.DMA_POOL]
                        v = (r["dma_idx"] // self.DMA_POOL) * 16
                        key = id(dsem)
                        if lastwait.get(key, -1) < v:
                            eng.wait_ge(dsem, v)
                            lastwait[key] = v
                    for d in r["deps"]:
                        dsem = sem_of(d)
                        v = d["val"]
                        if v is None:
                            raise RuntimeError("dep without value")
                        key = id(dsem)
                        if lastwait.get(key, -1) < v:
                            eng.wait_ge(dsem, v)
                            lastwait[key] = v
                    instr = r["fn"](eng)
                    if r["dma"]:
                        instr.then_inc(sem_of(r), 16)
                    elif r["needed"]:
                        instr.then_inc(sem, 1)
            return go

        for e in ENGINES:
            getattr(block, BLOCK_ATTR[e])(body(e))
        return counts


def splits(lo, hi, align=512):
    out = []
    a = lo
    while a < hi:
        b = min(hi, (a // align + 1) * align)
        out.append((a, b))
        a = b
    return out


def host_consts():
    bf = ml_dtypes.bfloat16
    c = {}
    c["ident_bf"] = np.eye(128, dtype=bf)
    c["ident_f32"] = np.eye(128, dtype=np.float32)
    r = np.arange(128)
    c["slmat"] = (r[:, None] > r[None, :]).astype(bf)
    c["utmask"] = (r[:, None] <= r[None, :]).astype(bf)
    kc = np.zeros((64, T), dtype=np.float32)
    ind = (np.arange(T)[None, :] // 128 == np.arange(16)[:, None]).astype(np.float32)
    kc[0:16] = ind
    kc[32:48] = ind
    c["kconst"] = kc.astype(bf)
    ec = np.zeros((128, 256), dtype=np.float32)
    for J in range(16):
        ec[:, 16 * J + J] = 1.0
    c["ecols"] = ec.astype(bf)
    r16 = np.arange(16)
    c["l16"] = (r16[:, None] > r16[None, :]).astype(bf)
    c["neg1"] = np.full((1, 16), -1.0, dtype=bf)
    c["ones64"] = np.ones((64, 1), dtype=bf)
    op_ = np.zeros((128, 65), dtype=np.float32)
    op_[0:64, 0] = 1.0
    op_[64:128, 64] = 1.0
    c["onespair"] = op_.astype(bf)
    c["ones1x64"] = np.ones((1, 64), dtype=bf)
    c["onesrow"] = np.ones((1, 128), dtype=bf)
    oh = np.zeros((8, 8 * 128), dtype=np.float32)
    for e in range(8):
        oh[e, 128 * e:128 * (e + 1)] = 1.0 / ALPHA
    c["oneh8"] = oh.astype(bf)
    fc = np.zeros((128, 4), dtype=np.float32)
    fc[:, 0] = LN_EPS
    fc[:, 1] = FRMS_EPS
    fc[:, 2] = -math.log(8.0)
    fc[:, 3] = 1.0
    c["fconst"] = fc
    t = np.arange(T, dtype=np.float64)
    c["cosb64"] = np.tile(np.cos(t)[None, :], (64, 1)).astype(bf)
    sb64 = np.zeros((64, T), dtype=np.float64)
    sb64[0:32] = -np.sin(t)[None, :]
    sb64[32:64] = np.sin(t)[None, :]
    c["sinb64"] = sb64.astype(bf)
    return c


def build_nc(taps=(), heads=NH, experts=NE, seqs=(0, 1), attn_lvl=6):
    nc = bass.Bass("TRN2", target_bir_lowering=False)
    tap_set = set(taps)

    x_d = nc.dram_tensor("x", [2, T, C], F32, kind="ExternalInput").ap()
    wT = {}
    for nm in ("wqT", "wkT", "wvT", "wpT", "wprojT"):
        wT[nm] = nc.dram_tensor(nm, [C, C], F32, kind="ExternalInput").ap()
    qrw_d = nc.dram_tensor("qrw", [1, 64], F32, kind="ExternalInput").ap()
    krw_d = nc.dram_tensor("krw", [1, 64], F32, kind="ExternalInput").ap()
    rwl_d = nc.dram_tensor("rw_l", [128, 64], F32, kind="ExternalInput").ap()
    rb_d = nc.dram_tensor("rb", [1, 8], F32, kind="ExternalInput").ap()
    gup_d = nc.dram_tensor("gup", [NE, C, 2 * I], F32, kind="ExternalInput").ap()
    gub_d = nc.dram_tensor("gub_l", [NE, 128, 32], F32, kind="ExternalInput").ap()
    dwn_d = nc.dram_tensor("dwn", [NE, I, C], F32, kind="ExternalInput").ap()
    dbias_d = nc.dram_tensor("dbias", [NE, C], F32, kind="ExternalInput").ap()
    cst_d = {}
    cshapes = {"ident_bf": ([128, 128], BF16), "ident_f32": ([128, 128], F32),
               "slmat": ([128, 128], BF16), "utmask": ([128, 128], BF16),
               "kconst": ([64, T], BF16), "ecols": ([128, 256], BF16),
               "l16": ([16, 16], BF16), "neg1": ([1, 16], BF16),
               "ones64": ([64, 1], BF16), "ones1x64": ([1, 64], BF16),
               "onespair": ([128, 65], BF16),
               "onesrow": ([1, 128], BF16), "oneh8": ([8, 8 * 128], BF16),
               "cosb64": ([64, T], BF16), "sinb64": ([64, T], BF16),
               "fconst": ([128, 4], F32)}
    for nm, (shp, dt) in cshapes.items():
        cst_d[nm] = nc.dram_tensor(nm, shp, dt, kind="ExternalInput").ap()
    out_d = nc.dram_tensor("out", [2, T, C], F32, kind="ExternalOutput").ap()
    xnscr = nc.dram_tensor("xnscr", [128, NKC * 2 * T], BF16, kind="Internal").ap()
    wscr = {nm: nc.dram_tensor("wscr_" + nm, [C, C], BF16, kind="Internal").ap()
            for nm in ("wqT", "wkT", "wvT", "wpT", "wprojT")}
    gupb = nc.dram_tensor("gupb", [NE, C, 2 * I], BF16, kind="Internal").ap()
    dwnb = nc.dram_tensor("dwnb", [NE, I, C], BF16, kind="Internal").ap()
    accd = nc.dram_tensor("accd", [NKC, 128, 2 * T], F32, kind="Internal").ap()
    yscr = nc.dram_tensor("yscr", [2, NH, 64, T], BF16, kind="Internal").ap()

    tap_out = {}
    p = Prog()

    def tap(name, ap_fn, shape, dtype, deps):
        if name not in tap_set:
            return
        td = nc.dram_tensor("tap_" + name, shape, F32, kind="ExternalOutput").ap()
        tap_out[name] = td
        p.op("pool", lambda e, td=td, ap_fn=ap_fn: e.dma_start(out=td, in_=ap_fn()),
             deps=deps, dma=True)

    with ExitStack() as es:
        sems = {}
        for e in ENGINES:
            sems[e] = es.enter_context(nc.semaphore("sem_" + e))
        dma_sems = {q: [es.enter_context(nc.semaphore(f"dsem_{q}_{i}"))
                        for i in range(Prog.DMA_POOL)] for q in ("sp", "pool")}

        pA = es.enter_context(nc.psum_tensor("pA", [128, 1024], F32)).ap()
        pB = es.enter_context(nc.psum_tensor("pB", [128, 1024], F32)).ap()
        pY = es.enter_context(nc.psum_tensor("pY", [128, 1024], F32)).ap()
        pS1 = es.enter_context(nc.psum_tensor("pS1", [128, 512], F32)).ap()
        pS2 = es.enter_context(nc.psum_tensor("pS2", [128, 512], F32)).ap()
        bufA0, bufA1, bufB0, bufB1 = Buf(), Buf(), Buf(), Buf()
        bufY, bufS1, bufS2 = Buf(), Buf(), Buf()

        sb = lambda name, shp, dt: es.enter_context(nc.sbuf_tensor(name, shp, dt)).ap()
        cs = {nm: sb("sb_" + nm, shp, dt) for nm, (shp, dt) in cshapes.items() if nm != "kconst"}
        qrw_b = sb("qrw_b", [65, 64], BF16)
        krw_b = sb("krw_b", [65, 64], BF16)
        onesp64 = sb("onesp64", [65, 64], BF16)
        rw_sb = sb("rw_sb", [128, 128], BF16)
        rw_f = sb("rw_f", [128, 64], F32)
        rb_b = sb("rb_b", [1, 8], BF16)
        db_sb = sb("db_sb", [8, C], BF16)
        gthi = sb("gthi", [8, 2 * T], BF16)
        gtlo = sb("gtlo", [8, 2 * T], BF16)
        b_cs = {nm: Buf() for nm in cshapes}
        b_qrw, b_krw, b_rw, b_rb, b_db, b_rwf = Buf(), Buf(), Buf(), Buf(), Buf(), Buf()
        b_gthi, b_gtlo = Buf(mw=True), Buf(mw=True)
        b_xnscr, b_accd, b_yscr, b_out = Buf(mw=True), Buf(mw=True), Buf(mw=True), Buf(mw=True)

        b_wscr = {nm: Buf(mw=True) for nm in wscr}
        b_gupb = Buf(mw=True)
        b_dwnb = Buf(mw=True)
        for nm in wscr:
            p.op("pool", lambda e, nm=nm: e.dma_start(out=wscr[nm], in_=wT[nm]),
                 writes=[b_wscr[nm]], dma=True)
        for ex in range(NE):
            p.op("pool", lambda e, ex=ex: e.dma_start(out=gupb[ex], in_=gup_d[ex]),
                 writes=[b_gupb], dma=True)
            p.op("pool", lambda e, ex=ex: e.dma_start(out=dwnb[ex], in_=dwn_d[ex]),
                 writes=[b_dwnb], dma=True)
        for nm in cs:
            pass
        for nm in [n for n in cshapes if n != "kconst"]:
            p.op("sp", lambda e, nm=nm: e.dma_start(out=cs[nm][:, :], in_=cst_d[nm]),
                 writes=[b_cs[nm]], dma=True)
        p.op("pool", lambda e: e.dma_start(out=qrw_b[0:1, :], in_=qrw_d), writes=[b_qrw], dma=True)
        p.op("pool", lambda e: e.dma_start(out=qrw_b[64:65, :], in_=qrw_d), writes=[b_qrw], dma=True)
        p.op("pool", lambda e: e.dma_start(out=krw_b[0:1, :], in_=krw_d), writes=[b_krw], dma=True)
        p.op("pool", lambda e: e.dma_start(out=krw_b[64:65, :], in_=krw_d), writes=[b_krw], dma=True)
        p.op("sp", lambda e: e.dma_start(out=onesp64[0:1, :], in_=cst_d["ones1x64"]),
             writes=[b_cs["ones1x64"]], dma=True)
        p.op("sp", lambda e: e.dma_start(out=onesp64[64:65, :], in_=cst_d["ones1x64"]),
             writes=[b_cs["ones1x64"]], dma=True)
        p.op("pool", lambda e: e.dma_start(out=rw_f[:, :], in_=rwl_d), writes=[b_rwf], dma=True)
        p.op("dve", lambda e: e.tensor_copy(rw_sb[:, 0:64], rw_f[:, :]), reads=[b_rwf], writes=[b_rw])
        p.op("dve", lambda e: e.tensor_tensor(out=rw_sb[:, 64:128], in0=rw_f[:, :],
                                              in1=rw_sb[:, 0:64], op=ALU.subtract),
             reads=[b_rwf], writes=[b_rw])
        p.op("pool", lambda e: e.dma_start(out=rb_b[:, :], in_=rb_d), writes=[b_rb], dma=True)
        p.op("pool", lambda e: e.dma_start(out=db_sb[:, :], in_=dbias_d), writes=[b_db], dma=True)

        main_es = ExitStack()
        msb = lambda name, shp, dt: main_es.enter_context(nc.sbuf_tensor(name, shp, dt)).ap()
        lsb = msb

        xstage = [lsb(f"xstage{i}", [128, C], F32) for i in range(2)]
        b_xstage = [Buf(), Buf()]
        xntm = [lsb(f"xntm{i}", [128, C], BF16) for i in range(2)]
        b_xntm = [Buf(), Buf()]
        xnfst = [lsb(f"xnfst{i}", [128, C], BF16) for i in range(2)]
        b_xnfst = [Buf(), Buf()]
        dump = lsb("dump", [128, C], BF16)
        b_dump = Buf()
        nmean = [lsb(f"nmean{i}", [128, 1], F32) for i in range(2)]
        b_nmean = [Buf(), Buf()]
        vsum = [lsb(f"vsum{i}", [128, 1], F32) for i in range(2)]
        b_vsum = [Buf(), Buf()]
        rstd_ln = [lsb(f"rstdln{i}", [128, 1], F32) for i in range(2)]
        b_rstd_ln = [Buf(), Buf()]
        sm8 = lsb("sm8", [128, 8], F32)
        m8a = lsb("m8a", [128, 1], F32)
        m8b = lsb("m8b", [128, 8], F32)
        g8 = lsb("g8", [128, 8], F32)
        b_sm8, b_m8a, b_m8b, b_g8 = Buf(), Buf(), Buf(), Buf()

        for s in range(2):
            for tt in range(NTT):
                par = tt % 2
                gc0 = s * T + tt * 128
                xs, bx = xstage[par], b_xstage[par]
                p.op("sp", lambda e, s=s, tt=tt, xs=xs: e.dma_start(
                    out=xs[:, :], in_=x_d[s, tt * 128:(tt + 1) * 128, :]),
                    writes=[bx], dma=True)
                nm, bnm = nmean[par], b_nmean[par]
                p.op("dve", lambda e, xs=xs, nm=nm: e.reduce_sum(nm[:, :], xs[:, :], axis=AX),
                     reads=[bx], writes=[bnm])
                p.op("dve", lambda e, nm=nm: e.tensor_scalar(
                    out=nm[:, :], in0=nm[:, :], scalar1=-1.0 / C, scalar2=None, op0=ALU.mult),
                    writes=[bnm])
                vs, bvs = vsum[par], b_vsum[par]
                p.op("act", lambda e, xs=xs, nm=nm, vs=vs: e.activation(
                    out=dump[:, :], in_=xs[:, :], func=AF.Square, bias=nm[:, :], accum_out=vs[:, :]),
                    reads=[bx, bnm], writes=[b_dump, bvs])
                rl, brl = rstd_ln[par], b_rstd_ln[par]
                p.op("act", lambda e, vs=vs, rl=rl: e.activation(
                    out=rl[:, :], in_=vs[:, :], func=AF.Ln, scale=1.0 / C,
                    bias=cs["fconst"][:, 0:1]),
                    reads=[bvs, b_cs["fconst"]], writes=[brl])
                p.op("act", lambda e, rl=rl: e.activation(
                    out=rl[:, :], in_=rl[:, :], func=AF.Exp, scale=-0.5),
                    writes=[brl])
                xt, bxt = xntm[par], b_xntm[par]
                p.op("dve", lambda e, xs=xs, nm=nm, rl=rl, xt=xt: e.tensor_scalar(
                    out=xt[:, :], in0=xs[:, :], scalar1=nm[:, :], op0=ALU.add,
                    scalar2=rl[:, :], op1=ALU.mult),
                    reads=[bx, bnm, brl], writes=[bxt])
                if "xn_tm" in tap_set:
                    if "xn_tm" not in tap_out:
                        tap_out["xn_tm"] = nc.dram_tensor("tap_xn_tm", [2 * NTT, 128, C], F32,
                                                          kind="ExternalOutput").ap()
                    p.op("pool", lambda e, s=s, tt=tt, xt=xt: e.dma_start(
                        out=tap_out["xn_tm"][s * NTT + tt], in_=xt[:, :]),
                        reads=[bxt], dma=True)
                pT = pS2.bitcast(BF16)
                p.op("pe", lambda e, xt=xt, pT=pT: [
                    e.transpose(out=pT[:, kc * 128:(kc + 1) * 128],
                                in_=xt[:, kc * 128:(kc + 1) * 128],
                                identity=cs["ident_bf"][:, :])
                    for kc in range(NKC)][-1],
                    reads=[bxt, b_cs["ident_bf"]], writes=[bufS2])
                xf, bxf = xnfst[par], b_xnfst[par]
                p.op("dve", lambda e, pT=pT, xf=xf: e.tensor_copy(xf[:, :], pT[:, :]),
                     reads=[bufS2], writes=[bxf])
                p.op("sp", lambda e, xf=xf, gc0=gc0: e.dma_start(
                    out=xnscr.rearrange("p (kc t) -> p kc t", kc=NKC)[:, :, gc0:gc0 + 128],
                    in_=xf.rearrange("p (kc t) -> p kc t", kc=NKC)[:, :, :]),
                    reads=[bxf], writes=[b_xnscr], dma=True)
                p.op("pe", lambda e, xf=xf: [
                    e.matmul(out=pS1[:, 0:8], lhsT=xf[:, kc * 128:(kc + 1) * 128],
                             rhs=rw_sb[:, hl * 64 + kc * 8:hl * 64 + (kc + 1) * 8],
                             start=(kc == 0 and hl == 0), stop=False)
                    for hl in range(2) for kc in range(NKC)][-1],
                    reads=[bxf, b_rw], writes=[bufS1])
                p.op("pe", lambda e: e.matmul(out=pS1[:, 0:8], lhsT=cs["onesrow"][:, :],
                                              rhs=rb_b[:, :], start=False, stop=True),
                     reads=[b_cs["onesrow"], b_rb], writes=[bufS1])
                p.op("dve", lambda e: e.reduce_max(m8a[:, :], pS1[:, 0:8], axis=AX),
                     reads=[bufS1], writes=[b_m8a])
                p.op("dve", lambda e: e.tensor_scalar(
                    out=m8a[:, :], in0=m8a[:, :], scalar1=-1.0, scalar2=None, op0=ALU.mult),
                    writes=[b_m8a])
                p.op("dve", lambda e: e.tensor_copy(sm8[:, :], pS1[:, 0:8]),
                     reads=[bufS1], writes=[b_sm8])
                p.op("act", lambda e: e.activation(out=sm8[:, :], in_=sm8[:, :],
                                                   func=AF.Exp, bias=m8a[:, :]),
                     reads=[b_m8a], writes=[b_sm8])
                p.op("dve", lambda e: e.tensor_scalar(
                    out=dump[:, 0:8], in0=sm8[:, :], scalar1=1.0, scalar2=None,
                    op0=ALU.mult, op1=ALU.add, accum_out=m8a[:, :]),
                    reads=[b_sm8], writes=[b_dump, b_m8a])
                p.op("dve", lambda e: e.reciprocal(m8a[:, :], m8a[:, :]), writes=[b_m8a])
                p.op("dve", lambda e: e.tensor_scalar(
                    out=sm8[:, :], in0=sm8[:, :], scalar1=m8a[:, :], scalar2=None, op0=ALU.mult),
                    reads=[b_m8a], writes=[b_sm8])
                p.op("dve", lambda e: e.reduce_max(m8a[:, :], sm8[:, :], axis=AX),
                     reads=[b_sm8], writes=[b_m8a])
                p.op("dve", lambda e: e.tensor_scalar(
                    out=m8b[:, :], in0=sm8[:, :], scalar1=m8a[:, :], scalar2=None, op0=ALU.is_ge),
                    reads=[b_sm8, b_m8a], writes=[b_m8b])
                p.op("dve", lambda e: e.scalar_tensor_tensor(
                    out=m8b[:, :], in0=m8b[:, :], scalar=-4.0, in1=sm8[:, :],
                    op0=ALU.mult, op1=ALU.add),
                    reads=[b_sm8], writes=[b_m8b])
                p.op("dve", lambda e: e.reduce_max(m8a[:, :], m8b[:, :], axis=AX),
                     reads=[b_m8b], writes=[b_m8a])
                p.op("dve", lambda e: e.scalar_tensor_tensor(
                    out=g8[:, :], in0=sm8[:, :], scalar=m8a[:, :], in1=sm8[:, :],
                    op0=ALU.is_ge, op1=ALU.mult),
                    reads=[b_sm8, b_m8a], writes=[b_g8])
                p.op("pe", lambda e: e.transpose(out=pS2[0:8, 0:128], in_=g8[:, :],
                                                 identity=cs["ident_f32"][:, :]),
                     reads=[b_g8, b_cs["ident_f32"]], writes=[bufS2])
                p.op("dve", lambda e, gc0=gc0: e.tensor_copy(
                    gthi[:, gc0:gc0 + 128], pS2[0:8, 0:128]),
                    reads=[bufS2], writes=[b_gthi])
                p.op("dve", lambda e, gc0=gc0: e.tensor_tensor(
                    out=gtlo[:, gc0:gc0 + 128], in0=pS2[0:8, 0:128],
                    in1=gthi[:, gc0:gc0 + 128], op=ALU.subtract),
                    reads=[bufS2, b_gthi], writes=[b_gtlo])
        tap("xn_fm", lambda: xnscr, [128, NKC * 2 * T], BF16, deps=list(b_xnscr.wl))
        tap("g8", lambda: gthi, [8, 2 * T], BF16, deps=list(b_gthi.wl))
        p.barrier()

        xng = [msb(f"xng{i}", [128, NKC * 512], BF16) for i in range(2)]
        b_xng = [Buf(), Buf()]
        wstream = msb("wstream", [128, 8192], BF16)
        b_wh = [Buf(), Buf()]
        b_wproj = Buf()
        biasT = msb("biasT", [128, 17408], BF16)
        b_biasT = [Buf() for _ in range(16)]
        offJ = [2048 * J - 64 * J * (J - 1) for J in range(16)]
        ls1_ = [msb(f"ls1_{i}", [128, 1024], F32) for i in range(2)]
        b_ls1_ = [Buf(), Buf()]
        ls1, b_ls1 = ls1_[0], b_ls1_[0]
        bufY0, bufY1 = Buf(), Buf()
        q_ext = [msb(f"q_ext{i}", [128, T], BF16) for i in range(2)]
        k_ext = [msb(f"k_ext{i}", [128, T], BF16) for i in range(2)]
        protc = [msb(f"protc{i}", [128, T], BF16) for i in range(2)]
        p64x2 = [msb(f"p64x2{i}", [128, T], BF16) for i in range(2)]
        v_ext = [msb(f"v_ext{i}", [128, 16 * 68], BF16) for i in range(2)]
        b_qe = [Buf(), Buf()]
        b_ke = [Buf(), Buf()]
        b_pr = [Buf(), Buf()]
        b_p6 = [Buf(), Buf()]
        b_ve = [Buf(), Buf()]
        Tf = [msb(f"Tf{i}", [128, 512], F32) for i in range(3)]
        b_Tf = [Buf(), Buf(), Buf()]
        sqb = [msb(f"sqb{i}", [128, 512], BF16) for i in range(3)]
        b_sq = [Buf(), Buf(), Buf()]
        v_fm = msb("v_fm", [128, T], BF16)
        b_vfm = Buf()
        ET = [msb(f"ET{i}", [128, 1024], BF16) for i in range(2)]
        b_ET = [Buf(), Buf()]
        csth = msb("csth", [16, 512], BF16)
        cstl = msb("cstl", [16, 512], BF16)
        b_csth, b_cstl = Buf(), Buf()
        rows65 = [msb(f"rows65_{i}", [65, 512], F32) for i in range(2)]
        b_rows65 = [Buf(), Buf()]
        rstd65 = [msb(f"rstd65_{i}", [65, 512], BF16) for i in range(3)]
        b_rstd65 = [Buf(), Buf(), Buf()]
        rowkq = [msb(f"rowkq{i}", [1, 512], F32) for i in range(2)]
        b_rowkq = [Buf(), Buf()]
        kmx = msb("kmx", [1, 1], F32)
        kmg = msb("kmg", [1, 1], F32)
        b_kmx, b_kmg = Buf(), Buf()
        m_row = msb("m_row", [1, T], BF16)
        b_mrow = Buf()
        rzrow = msb("rzrow", [1, 1024], F32)
        rzhi = msb("rzhi", [1, 1024], BF16)
        rzlo = msb("rzlo", [1, 1024], BF16)
        b_rz, b_rzhi, b_rzlo = Buf(), Buf(), Buf()
        rzb_sb = msb("rzb_sb", [64, 1024], F32)
        b_rzb = Buf()
        ystage = [rows65[i].bitcast(BF16)[0:64, 0:1024] for i in range(2)]
        b_yst = b_rows65
        ao_st = [p64x2[0].bitcast(F32)[:, 0:512]]
        b_ao = [b_p6[0]]

        for i in range(2):
            p.op("dve", lambda e, i=i: e.memset(q_ext[i][:, :], 0.0), writes=[b_qe[i]])
            p.op("sp", lambda e, i=i: e.dma_start(out=k_ext[i][0:64, :], in_=cst_d["kconst"]),
                 writes=[b_ke[i]], dma=True)
            p.op("dve", lambda e, i=i: e.memset(v_ext[i][:, :], 0.0), writes=[b_ve[i]])
            p.op("dve", lambda e, i=i: e.memset(
                v_ext[i].rearrange("p (b c) -> p b c", c=68)[:, :, 64:65], 1.0),
                writes=[b_ve[i]])

        PSMALL = [(pS1, bufS1), (pS2, bufS2)]

        for s in seqs:
            sc0 = s * T
            def emit_wload(hp_):
                wof_ = (hp_ % 2) * 4096
                bw_ = b_wh[hp_ % 2]
                for ti, nm in enumerate(("wkT", "wqT", "wpT", "wvT")):
                    p.op("sp", lambda e, nm=nm, ti=ti, hp_=hp_, wof_=wof_: e.dma_start(
                        out=wstream[:, wof_ + ti * 1024: wof_ + (ti + 1) * 1024].rearrange("p (kc m) -> p kc m", kc=NKC)[:, :, :],
                        in_=wscr[nm].rearrange("(kc p) m -> p kc m", p=128)[:, :, hp_ * 128:(hp_ + 1) * 128]),
                        writes=[bw_], reads=[b_wscr[nm]],
                        deps=([b_wproj.w] if b_wproj.w is not None else []) + list(b_wproj.r), dma=True)

            def emit_xload(g_):
                xn_g_, bxn_ = xng[g_ % 2], b_xng[g_ % 2]
                c0_ = g_ * 512
                p.op("sp", lambda e, sc0=sc0, c0_=c0_, xn_g_=xn_g_: e.dma_start(
                    out=xn_g_.rearrange("p (kc c) -> p kc c", kc=NKC)[:, :, :],
                    in_=xnscr.rearrange("p (kc t) -> p kc t", kc=NKC)[:, :, sc0 + c0_:sc0 + c0_ + 512]),
                    reads=[b_xnscr], writes=[bxn_], dma=True)

            for hp in range(heads // 2):
                wpar = hp % 2
                wof = wpar * 4096
                bw = b_wh[wpar]
                if hp == 0:
                    emit_wload(0)

                for g in range(4):
                    c0 = g * 512
                    xpar = g % 2
                    xn_g, bxn = xng[xpar], b_xng[xpar]
                    if not (hp > 0 and g == 0):
                        emit_xload(g)
                    pbank = {"k": (pA[:, 0:512], bufA0), "q": (pA[:, 512:1024], bufA1),
                             "p": (pB[:, 0:512], bufB0), "v": (pB[:, 512:1024], bufB1)}
                    sbank = {"k": (pS1, bufS1), "q": (pS2, bufS2), "p": (pS1, bufS1)}
                    tfm = {"k": 0, "q": 1, "p": 2}
                    if attn_lvl < 0.2:
                        continue
                    for ti, tn in enumerate(("k", "q", "p", "v")):
                        pp_, bpp = pbank[tn]
                        wslice = lambda kc, ti=ti, wof=wof: wstream[:, wof + ti * 1024 + kc * 128: wof + ti * 1024 + (kc + 1) * 128]
                        p.op("pe", lambda e, pp_=pp_, wslice=wslice, xn_g=xn_g: [
                            e.matmul(out=pp_, lhsT=wslice(kc), rhs=xn_g[:, kc * 512:(kc + 1) * 512],
                                     start=(kc == 0), stop=(kc == NKC - 1))
                            for kc in range(NKC)][-1],
                            reads=[bw, bxn], writes=[bpp])
                        if tn == "v":
                            p.op("dve", lambda e, pp_=pp_, c0=c0: e.tensor_copy(
                                v_fm[:, c0:c0 + 512], pp_),
                                reads=[bpp], writes=[b_vfm])
                        else:
                            tf, btf = Tf[tfm[tn]], b_Tf[tfm[tn]]
                            p.op("dve", lambda e, pp_=pp_, tf=tf: e.tensor_copy(
                                tf[:, :], pp_),
                                reads=[bpp], writes=[btf])
                            sq_h, bsq_h = sqb[tfm[tn]], b_sq[tfm[tn]]
                            p.op("act", lambda e, tf=tf, sq_h=sq_h: e.activation(
                                out=sq_h[:, :], in_=tf[:, :], func=AF.Square),
                                reads=[btf], writes=[bsq_h])
                    if attn_lvl < 0.25:
                        continue
                    for tn in ("k", "q", "p"):
                        ps_, bps_ = sbank[tn]
                        sq_h, bsq_h = sqb[tfm[tn]], b_sq[tfm[tn]]
                        p.op("pe", lambda e, ps_=ps_, sq_h=sq_h: e.matmul(
                            out=ps_[0:65, :], lhsT=cs["onespair"][:, :], rhs=sq_h[:, :],
                            start=True, stop=True),
                            reads=[bsq_h, b_cs["onespair"]], writes=[bps_])
                        r65, br65 = rows65[tfm[tn] % 2], b_rows65[tfm[tn] % 2]
                        rs65, brs65 = rstd65[tfm[tn]], b_rstd65[tfm[tn]]
                        epsc = 0 if tn in ("k", "q") else 1
                        p.op("dve", lambda e, ps_=ps_, r65=r65: e.tensor_copy(
                            r65[:, :], ps_[0:65, :]),
                            reads=[bps_], writes=[br65])
                        p.op("act", lambda e, epsc=epsc, r65=r65: e.activation(
                            out=r65[:, :], in_=r65[:, :],
                            func=AF.Ln, scale=1.0 / 64, bias=cs["fconst"][0:65, epsc:epsc + 1]),
                            reads=[b_cs["fconst"]], writes=[br65])
                        if tn == "q":
                            p.op("act", lambda e, r65=r65, rs65=rs65: e.activation(
                                out=rs65[:, :], in_=r65[:, :],
                                func=AF.Exp, scale=-0.5, bias=cs["fconst"][0:65, 2:3]),
                                reads=[br65, b_cs["fconst"]], writes=[brs65])
                        else:
                            p.op("act", lambda e, r65=r65, rs65=rs65: e.activation(
                                out=rs65[:, :], in_=r65[:, :], func=AF.Exp, scale=-0.5),
                                reads=[br65], writes=[brs65])
                    for tn in ("k", "q", "p"):
                        ps_, bps_ = sbank[tn]
                        tf, btf = Tf[tfm[tn]], b_Tf[tfm[tn]]
                        rs65, brs65 = rstd65[tfm[tn]], b_rstd65[tfm[tn]]
                        for hh in range(2):
                            p.op("pe", lambda e, hh=hh, ps_=ps_, rs65=rs65,
                                 lrow={"k": krw_b, "q": qrw_b, "p": onesp64}[tn]: e.matmul(
                                out=ps_[64 * hh:64 * hh + 64, :],
                                lhsT=lrow[64 * hh:64 * hh + 1, :],
                                rhs=rs65[64 * hh:64 * hh + 1, :],
                                start=True, stop=True),
                                reads=[brs65,
                                       {"k": b_krw, "q": b_qrw, "p": b_cs["ones1x64"]}[tn]],
                                writes=[bps_])
                            if tn in ("k", "q"):
                                dst, bdst = (k_ext[hh], b_ke[hh]) if tn == "k" else (q_ext[hh], b_qe[hh])
                                p.op("dve", lambda e, tf=tf, hh=hh, c0=c0, dst=dst, ps_=ps_: e.tensor_tensor(
                                    out=dst[64:128, c0:c0 + 512], in0=tf[64 * hh:64 * hh + 64, :],
                                    in1=ps_[64 * hh:64 * hh + 64, :], op=ALU.mult),
                                    reads=[btf, bps_], writes=[bdst])
                            else:
                                p.op("dve", lambda e, tf=tf, hh=hh, c0=c0, ps_=ps_: e.tensor_tensor(
                                    out=ls1[0:64, 0:512], in0=tf[64 * hh:64 * hh + 64, :],
                                    in1=ps_[64 * hh:64 * hh + 64, :], op=ALU.mult),
                                    reads=[btf, bps_], writes=[b_ls1])
                                pr, bpr = protc[hh], b_pr[hh]
                                p.op("dve", lambda e, c0=c0, pr=pr: e.tensor_tensor(
                                    out=pr[0:64, c0:c0 + 512], in0=ls1[0:64, 0:512],
                                    in1=cs["cosb64"][:, c0:c0 + 512], op=ALU.mult),
                                    reads=[b_ls1, b_cs["cosb64"]], writes=[bpr])
                                p.op("pool", lambda e, c0=c0, pr=pr: e.tensor_tensor(
                                    out=pr[64:96, c0:c0 + 512], in0=ls1[32:64, 0:512],
                                    in1=cs["sinb64"][32:64, c0:c0 + 512], op=ALU.mult),
                                    reads=[b_ls1, b_cs["sinb64"]], writes=[bpr])
                                p.op("pool", lambda e, c0=c0, pr=pr: e.tensor_tensor(
                                    out=pr[96:128, c0:c0 + 512], in0=ls1[0:32, 0:512],
                                    in1=cs["sinb64"][0:32, c0:c0 + 512], op=ALU.mult),
                                    reads=[b_ls1, b_cs["sinb64"]], writes=[bpr])
                                p6, bp6 = p64x2[hh], b_p6[hh]
                                p.op("dve", lambda e, tf=tf, hh=hh, c0=c0, p6=p6: e.tensor_scalar(
                                    out=p6[0:64, c0:c0 + 512], in0=tf[64 * hh:64 * hh + 64, :],
                                    scalar1=1.0 / 64, scalar2=None, op0=ALU.mult),
                                    reads=[btf], writes=[bp6])
                                p.op("pool", lambda e, tf=tf, hh=hh, c0=c0, p6=p6: e.tensor_scalar(
                                    out=p6[64:128, c0:c0 + 512], in0=tf[64 * hh:64 * hh + 64, :],
                                    scalar1=1.0 / 64, scalar2=None, op0=ALU.mult),
                                    reads=[btf], writes=[bp6])
                for jt in range(16 if attn_lvl >= 1 else 0):
                    pT = pS2.bitcast(BF16)
                    p.op("pe", lambda e, jt=jt, pT=pT: e.transpose(
                        out=pT[:, 0:128], in_=v_fm[:, jt * 128:(jt + 1) * 128],
                        identity=cs["ident_bf"][:, :]),
                        reads=[b_vfm, b_cs["ident_bf"]], writes=[bufS2])
                    for hh in range(2):
                        p.op("dve", lambda e, jt=jt, hh=hh, pT=pT: e.tensor_copy(
                            v_ext[hh][:, jt * 68:jt * 68 + 64], pT[:, 64 * hh:64 * hh + 64]),
                            reads=[bufS2], writes=[b_ve[hh]])

                if hp + 1 < heads // 2:
                    emit_wload(hp + 1)
                    emit_xload(0)

                for hh in range(2):
                    h = hp * 2 + hh
                    slope = 2.0 ** (-(h + 1) / 2.0)
                    qe, ke, pr, p6, ve = q_ext[hh], k_ext[hh], protc[hh], p64x2[hh], v_ext[hh]
                    bqe, bke, bpr, bp6, bve = b_qe[hh], b_ke[hh], b_pr[hh], b_p6[hh], b_ve[hh]

                    for g in range(4):
                        c0 = g * 512
                        sq_h, bsq_h = sqb[g % 2], b_sq[g % 2]
                        rk, brk = rowkq[g % 2], b_rowkq[g % 2]
                        bY_h = bufY0 if ((c0 % 1024) == 0) else bufY1
                        p.op("act", lambda e, ke=ke, c0=c0, sq_h=sq_h: e.activation(
                            out=sq_h[0:64, :], in_=ke[64:128, c0:c0 + 512], func=AF.Square),
                            reads=[bke], writes=[bsq_h])
                        p.op("pe", lambda e, c0=c0, sq_h=sq_h: e.matmul(
                            out=pY[0:1, (c0 % 1024):(c0 % 1024) + 512],
                            lhsT=cs["ones64"][:, :], rhs=sq_h[0:64, :],
                            start=True, stop=True),
                            reads=[bsq_h, b_cs["ones64"]], writes=[bY_h])
                        p.op("dve", lambda e, c0=c0, rk=rk: e.tensor_copy(
                            rk[:, :], pY[0:1, (c0 % 1024):(c0 % 1024) + 512]),
                            reads=[bY_h], writes=[brk])
                        if g == 0:
                            p.op("dve", lambda e, rk=rk: e.reduce_max(kmx[:, :], rk[:, :], axis=AX),
                                 reads=[brk], writes=[b_kmx])
                        else:
                            p.op("dve", lambda e, rk=rk: e.reduce_max(kmg[:, :], rk[:, :], axis=AX),
                                 reads=[brk], writes=[b_kmg])
                            p.op("dve", lambda e: e.tensor_tensor(
                                out=kmx[:, :], in0=kmg[:, :], in1=kmx[:, :], op=ALU.max),
                                reads=[b_kmg], writes=[b_kmx])
                    p.op("act", lambda e: e.activation(out=kmx[:, :], in_=kmx[:, :], func=AF.Ln),
                         writes=[b_kmx])
                    p.op("dve", lambda e: e.tensor_scalar(
                        out=kmx[:, :], in0=kmx[:, :], scalar1=0.5, scalar2=None, op0=ALU.mult),
                        writes=[b_kmx])
                    for g in range(4):
                        c0 = g * 512
                        sq_h, bsq_h = sqb[g % 2], b_sq[g % 2]
                        rk, brk = rowkq[g % 2], b_rowkq[g % 2]
                        bY_h = bufY0 if ((c0 % 1024) == 0) else bufY1
                        p.op("act", lambda e, qe=qe, c0=c0, sq_h=sq_h: e.activation(
                            out=sq_h[0:64, :], in_=qe[64:128, c0:c0 + 512], func=AF.Square),
                            reads=[bqe], writes=[bsq_h])
                        p.op("pe", lambda e, c0=c0, sq_h=sq_h: e.matmul(
                            out=pY[0:1, (c0 % 1024):(c0 % 1024) + 512],
                            lhsT=cs["ones64"][:, :], rhs=sq_h[0:64, :],
                            start=True, stop=True),
                            reads=[bsq_h, b_cs["ones64"]], writes=[bY_h])
                        p.op("dve", lambda e, c0=c0, rk=rk: e.tensor_copy(
                            rk[:, :], pY[0:1, (c0 % 1024):(c0 % 1024) + 512]),
                            reads=[bY_h], writes=[brk])
                        p.op("act", lambda e, rk=rk: e.activation(
                            out=rk[:, :], in_=rk[:, :], func=AF.Ln),
                            writes=[brk])
                        p.op("act", lambda e, rk=rk, c0=c0: e.activation(
                            out=m_row[:, c0:c0 + 512], in_=rk[:, :],
                            func=AF.Exp, scale=0.5, bias=kmx[:, :]),
                            reads=[brk, b_kmx], writes=[b_mrow])
                        p.op("dve", lambda e, c0=c0: e.tensor_scalar(
                            out=m_row[:, c0:c0 + 512], in0=m_row[:, c0:c0 + 512],
                            scalar1=-MSHIFT, scalar2=None, op0=ALU.add),
                            writes=[b_mrow])
                    pc = 0
                    for J in range(16 if attn_lvl >= 2 else 0):
                        w_J = T - 128 * J
                        for (b0, b1) in splits(0, w_J, 1024):
                            lsx, blsx = ls1_[pc % 2], b_ls1_[pc % 2]
                            pc += 1
                            for (a0, a1) in splits(b0, b1):
                                ps, bps = PSMALL[(a0 // 512) % 2]
                                p.op("pe", lambda e, J=J, a0=a0, a1=a1, ps=ps, pr=pr, p6=p6: e.matmul(
                                    out=ps[:, 0:a1 - a0], lhsT=pr[:, J * 128:(J + 1) * 128],
                                    rhs=p6[:, J * 128 + a0:J * 128 + a1], start=True, stop=True),
                                    reads=[bpr, bp6], writes=[bps])
                                p.op("act", lambda e, a0=a0, a1=a1, b0=b0, ps=ps, lsx=lsx: e.activation(
                                    out=lsx[:, a0 - b0:a1 - b0], in_=ps[:, 0:a1 - a0],
                                    func=AF.Exp, scale=-1.0),
                                    reads=[bps], writes=[blsx])
                            p.op("act", lambda e, b0=b0, b1=b1, lsx=lsx: e.activation(
                                out=lsx[:, 0:b1 - b0], in_=lsx[:, 0:b1 - b0], func=AF.Ln,
                                bias=cs["fconst"][:, 3:4]),
                                reads=[b_cs["fconst"]], writes=[blsx])
                            p.op("pool", lambda e, J=J, b0=b0, b1=b1, slope=slope, lsx=lsx: e.tensor_scalar(
                                out=biasT[:, offJ[J] + b0:offJ[J] + b1], in0=lsx[:, 0:b1 - b0],
                                scalar1=-slope, scalar2=0.0, op0=ALU.mult, op1=ALU.add),
                                reads=[blsx], writes=[b_biasT[J]])
                        p.op("dve", lambda e, J=J: e.tensor_tensor(
                            out=biasT[:, offJ[J]:offJ[J] + 128], in0=biasT[:, offJ[J]:offJ[J] + 128],
                            in1=cs["utmask"][:, :], op=ALU.mult),
                            reads=[b_cs["utmask"]], writes=[b_biasT[J]])

                    for gq in range(4 if attn_lvl >= 3 else 0):
                        i0g = gq * 512
                        Jmax = 4 * gq + 3
                        p.op("pe", lambda e, i0g=i0g, Jmax=Jmax: [
                            e.matmul(out=pS1[0:16, max(0, 128 * J - i0g):512],
                                     lhsT=cs["ecols"][:, 16 * J:16 * J + 16],
                                     rhs=biasT[:, offJ[J] + max(0, i0g - 128 * J):offJ[J] + i0g + 512 - 128 * J],
                                     start=(J == 0), stop=(J == Jmax))
                            for J in range(Jmax + 1)][-1],
                            reads=[b_cs["ecols"]] + [b_biasT[J] for J in range(Jmax + 1)],
                            writes=[bufS1])
                        p.op("dve", lambda e: e.tensor_copy(csth[:, :], pS1[0:16, :]),
                             reads=[bufS1], writes=[b_csth])
                        p.op("dve", lambda e: e.tensor_tensor(
                            out=cstl[:, :], in0=pS1[0:16, :], in1=csth[:, :], op=ALU.subtract),
                            reads=[bufS1, b_csth], writes=[b_cstl])
                        p.op("pe", lambda e, i0g=i0g: [
                            e.matmul(out=pS2[0:16, :], lhsT=cs["l16"][:, :], rhs=csth[:, :],
                                     start=True, stop=False),
                            e.matmul(out=pS2[0:16, :], lhsT=cs["l16"][:, :], rhs=cstl[:, :],
                                     start=False, stop=False),
                            e.matmul(out=pS2[0:16, :], lhsT=cs["neg1"][:, :],
                                     rhs=m_row[:, i0g:i0g + 512], start=False, stop=True)][-1],
                            reads=[b_csth, b_cstl, b_mrow, b_cs["l16"], b_cs["neg1"]],
                            writes=[bufS2])
                        p.op("dve", lambda e, i0g=i0g, qe=qe: e.tensor_copy(
                            qe[0:16, i0g:i0g + 512], pS2[0:16, :]),
                            reads=[bufS2], writes=[bqe])
                        p.op("dve", lambda e, i0g=i0g, qe=qe: e.tensor_tensor(
                            out=qe[32:48, i0g:i0g + 512], in0=pS2[0:16, :],
                            in1=qe[0:16, i0g:i0g + 512], op=ALU.subtract),
                            reads=[bufS2], writes=[bqe])

                    SCF = [pr.bitcast(F32), p6.bitcast(F32)]
                    b_SCF = [bpr, bp6]
                    for H in range(2 if attn_lvl >= 4 else 0):
                        ib = H * 1024
                        Jlist = [J for J in range(16) if 128 * J < ib + 1024]
                        pend_mm3 = []
                        for J in Jlist:
                            i0 = max(128 * J, ib)
                            i1 = ib + 1024
                            o0 = i0 - ib
                            pX = pA if J % 2 == 0 else pB
                            bpXb = [bufA0, bufA1] if J % 2 == 0 else [bufB0, bufB1]
                            et, bet = ET[J % 2], b_ET[J % 2]
                            scf, bscf = SCF[J % 2], b_SCF[J % 2]
                            chain = []
                            for (a0, a1) in splits(o0, 1024):
                                p.op("pe", lambda e, J=J, a0=a0, a1=a1, ib=ib, pX=pX, ke=ke, qe=qe: e.matmul(
                                    out=pX[:, a0:a1], lhsT=ke[:, J * 128:(J + 1) * 128],
                                    rhs=qe[:, ib + a0:ib + a1], start=True, stop=False),
                                    reads=[bke, bqe], writes=[bpXb[a0 // 512]])
                                p.op("pe", lambda e, J=J, a0=a0, a1=a1, ib=ib, pX=pX: e.matmul(
                                    out=pX[:, a0:a1], lhsT=cs["slmat"][:, :],
                                    rhs=biasT[:, offJ[J] + ib + a0 - 128 * J:offJ[J] + ib + a1 - 128 * J],
                                    start=False, stop=True),
                                    reads=[b_biasT[J], b_cs["slmat"]], writes=[bpXb[a0 // 512]])
                                p.op("dve", lambda e, a0=a0, a1=a1, o0=o0, pX=pX, scf=scf: e.tensor_copy(
                                    scf[:, a0 - o0:a1 - o0], pX[:, a0:a1]),
                                    reads=[bpXb[a0 // 512]], writes=[bscf])
                                p.op("act", lambda e, a0=a0, a1=a1, o0=o0, et=et, scf=scf: e.activation(
                                    out=et[:, a0 - o0:a1 - o0], in_=scf[:, a0 - o0:a1 - o0],
                                    func=AF.Exp),
                                    reads=[bscf], writes=[bet])
                                if 128 * J >= ib and a0 == o0:
                                    p.op("dve", lambda e, et=et: e.tensor_tensor(
                                        out=et[:, 0:128], in0=et[:, 0:128],
                                        in1=cs["utmask"][:, :], op=ALU.mult),
                                        reads=[b_cs["utmask"]], writes=[bet])
                                lastJ_for_bank = max(Jx for Jx in Jlist if 128 * Jx < ib + (a0 // 512 + 1) * 512)
                                chain.append((J, a0, a1, i0, et, J == 0, J == lastJ_for_bank))
                            pend_mm3.append((chain, bet, bve))
                            if len(pend_mm3) >= 2:
                                ch, betx, bvex = pend_mm3.pop(0)
                                for (Jx, a0, a1, i0x, etx, st, sp_) in ch:
                                    p.op("pe", lambda e, Jx=Jx, a0=a0, a1=a1, i0x=i0x, etx=etx, ve=ve,
                                         st=st, sp_=sp_, ib=ib: e.matmul(
                                        out=pY[0:65, a0:a1], lhsT=ve[:, Jx * 68:Jx * 68 + 65],
                                        rhs=etx[:, ib + a0 - i0x:ib + a1 - i0x], start=st, stop=sp_),
                                        reads=[betx, bvex], writes=[bufY0 if a0 < 512 else bufY1])
                        for (ch, betx, bvex) in pend_mm3:
                            for (Jx, a0, a1, i0x, etx, st, sp_) in ch:
                                p.op("pe", lambda e, Jx=Jx, a0=a0, a1=a1, i0x=i0x, etx=etx, ve=ve,
                                     st=st, sp_=sp_, ib=ib: e.matmul(
                                    out=pY[0:65, a0:a1], lhsT=ve[:, Jx * 68:Jx * 68 + 65],
                                    rhs=etx[:, ib + a0 - i0x:ib + a1 - i0x], start=st, stop=sp_),
                                    reads=[betx, bvex], writes=[bufY0 if a0 < 512 else bufY1])
                        p.op("dve", lambda e: e.reciprocal(rzrow[:, :], pY[64:65, :]),
                             reads=[bufY0, bufY1], writes=[b_rz])
                        p.op("dve", lambda e: e.tensor_copy(rzhi[:, :], rzrow[:, :]),
                             reads=[b_rz], writes=[b_rzhi])
                        p.op("dve", lambda e: e.tensor_tensor(
                            out=rzlo[:, :], in0=rzrow[:, :], in1=rzhi[:, :], op=ALU.subtract),
                            reads=[b_rz, b_rzhi], writes=[b_rzlo])
                        for (a0, a1) in ((0, 512), (512, 1024)):
                            ps, bps = PSMALL[(a0 // 512) % 2]
                            p.op("pe", lambda e, a0=a0, a1=a1, ps=ps: [
                                e.matmul(out=ps[0:64, :], lhsT=cs["ones1x64"][:, :],
                                         rhs=rzhi[:, a0:a1], start=True, stop=False),
                                e.matmul(out=ps[0:64, :], lhsT=cs["ones1x64"][:, :],
                                         rhs=rzlo[:, a0:a1], start=False, stop=True)][-1],
                                reads=[b_rzhi, b_rzlo, b_cs["ones1x64"]], writes=[bps])
                            p.op("dve", lambda e, a0=a0, a1=a1, ps=ps: e.tensor_copy(
                                rzb_sb[:, a0:a1], ps[0:64, :]),
                                reads=[bps], writes=[b_rzb])
                        if attn_lvl < 5:
                            continue
                        yst, byst = ystage[H], b_yst[H]
                        p.op("dve", lambda e, yst=yst: e.tensor_tensor(
                            out=yst[:, :], in0=pY[0:64, :], in1=rzb_sb[:, :], op=ALU.mult),
                            reads=[bufY0, bufY1, b_rzb], writes=[byst])
                        p.op("sp", lambda e, s=s, h=h, H=H, yst=yst: e.dma_start(
                            out=yscr[s, h, :, H * 1024:(H + 1) * 1024], in_=yst[:, :]),
                            reads=[byst], writes=[b_yscr], dma=True)
                    if ("scoresT" in tap_set) and h == 0 and s == 0:
                        tap("scoresT", lambda: ET[1], [128, 1024], BF16, deps=[b_ET[1].w])

            if attn_lvl < 6:
                continue
            p.op("sp", lambda e: e.dma_start(
                out=wstream.rearrange("p (kc m) -> p kc m", kc=NKC)[:, :, :],
                in_=wscr["wprojT"].rearrange("(kc p) m -> p kc m", p=128)[:, :, :]),
                writes=[b_wproj], reads=[b_wscr["wprojT"]],
                deps=[ev for b in b_wh for ev in ([b.w] + b.r)], dma=True)
            for g in range(4):
                for half2 in range(2):
                    p.op("sp", lambda e, s=s, g=g, half2=half2: e.dma_start(
                        out=protc[half2].rearrange("p (kc t) -> p kc t", kc=4)[:, :, :],
                        in_=yscr[s].rearrange("(k g2) d t -> (g2 d) k t", g2=2)[:, half2 * 4:half2 * 4 + 4, g * 512:(g + 1) * 512]),
                        reads=[b_yscr], writes=[b_pr[half2]], dma=True)

                def ybv(kc):
                    return protc[kc // 4][:, (kc % 4) * 512:(kc % 4 + 1) * 512]

                for m in range(NKC):
                    ps, bps = PSMALL[m % 2]
                    p.op("pe", lambda e, m=m, ps=ps, ybv=ybv: [
                        e.matmul(out=ps[:, :], lhsT=wstream[:, kc * 1024 + m * 128:kc * 1024 + (m + 1) * 128],
                                 rhs=ybv(kc),
                                 start=(kc == 0), stop=(kc == NKC - 1))
                        for kc in range(NKC)][-1],
                        reads=[b_wproj, b_pr[0], b_pr[1]], writes=[bps])
                    ao, bao = ao_st[0], b_ao[0]
                    p.op("dve", lambda e, ps=ps, ao=ao: e.tensor_copy(ao[:, :], ps[:, :]),
                         reads=[bps], writes=[bao])
                    p.op("sp", lambda e, m=m, s=s, g=g, ao=ao: e.dma_start(
                        out=accd[m, :, s * T + g * 512:s * T + (g + 1) * 512], in_=ao[:, :]),
                        reads=[bao], writes=[b_accd], dma=True)
        tap("attn_acc", lambda: accd, [NKC, 128, 2 * T], F32, deps=list(b_accd.wl))

        p.barrier()
        main_es.close()
        ffn_es = ExitStack()
        fsb = lambda name, shp, dt: ffn_es.enter_context(nc.sbuf_tensor(name, shp, dt)).ap()
        xnf = fsb("xnf", [128, NKC * 2 * T], BF16)
        b_xnf = Buf()
        gsl = [fsb(f"gsl{i}", [128, NKC * 512], BF16) for i in range(2)]
        usl = [fsb(f"usl{i}", [128, NKC * 512], BF16) for i in range(2)]
        dsl = [fsb(f"dsl{i}", [128, 4 * 1024], BF16) for i in range(2)]
        b_gsl = [Buf(), Buf()]
        b_usl = [Buf(), Buf()]
        b_dsl = [Buf(), Buf()]
        gub_s = fsb("gub_s", [128, 32], F32)
        gub1 = fsb("gub1", [128, 32], F32)
        b_gub, b_gub1 = Buf(), Buf()
        gwb = fsb("gwb", [128, 4 * 1024], F32)
        b_gwb = Buf()
        act_sb = [fsb(f"act_sb{i}", [128, 4 * 1024], BF16) for i in range(2)]
        b_act = [Buf(), Buf()]
        glu_t = [fsb(f"glu_t{i}", [128, 1024], F32) for i in range(2)]
        upt = [fsb(f"upt{i}", [128, 1024], F32) for i in range(2)]
        b_glu = [Buf(), Buf()]
        b_upt = [Buf(), Buf()]
        dstage = [fsb(f"dstage{i}", [128, 1024], F32) for i in range(2)]
        b_dst = [Buf(), Buf()]

        p.op("sp", lambda e: e.dma_start(out=xnf[:, :], in_=xnscr),
             reads=[b_xnscr], writes=[b_xnf], dma=True)

        def xnfv(kc, c0, c1):
            return xnf[:, kc * 2 * T + c0:kc * 2 * T + c1]

        it_list = [(e_, iq) for e_ in range(experts) for iq in range(4)]

        def emit_ffnw(j):
            exj, iqj = it_list[j]
            parj = j % 2
            p.op("sp", lambda e, ex=exj, iq=iqj, gs=gsl[parj]: e.dma_start(
                out=gs.rearrange("p (kc m) -> p kc m", kc=NKC)[:, :, :],
                in_=gupb[ex].rearrange("(kc p) m -> p kc m", p=128)[:, :, iq * 512:(iq + 1) * 512]),
                writes=[b_gsl[parj]], reads=[b_gupb], dma=True)
            p.op("sp", lambda e, ex=exj, iq=iqj, us=usl[parj]: e.dma_start(
                out=us.rearrange("p (kc m) -> p kc m", kc=NKC)[:, :, :],
                in_=gupb[ex].rearrange("(kc p) m -> p kc m", p=128)[:, :, I + iq * 512:I + (iq + 1) * 512]),
                writes=[b_usl[parj]], reads=[b_gupb], dma=True)
            p.op("sp", lambda e, ex=exj, iq=iqj, ds=dsl[parj]: e.dma_start(
                out=ds.rearrange("p (ic h) -> p ic h", ic=4)[:, :, :],
                in_=dwnb[ex, iq * 512:(iq + 1) * 512, :].rearrange("(ic p) h -> p ic h", p=128)[:, :, :]),
                writes=[b_dsl[parj]], reads=[b_dwnb], dma=True)

        for it_idx, (ex, iq) in enumerate(it_list):
            par = it_idx % 2
            bg, bu, bd = b_gsl[par], b_usl[par], b_dsl[par]
            gs, us, ds = gsl[par], usl[par], dsl[par]
            if it_idx == 0:
                emit_ffnw(0)
            if iq == 0:
                p.op("pool", lambda e, ex=ex: e.dma_start(out=gub_s[:, :], in_=gub_d[ex]),
                     writes=[b_gub], dma=True)
                p.op("dve", lambda e: e.tensor_scalar(
                    out=gub1[:, :], in0=gub_s[:, :], scalar1=1.0, scalar2=None, op0=ALU.add),
                    reads=[b_gub], writes=[b_gub1])
                for gg in range(NKC):
                    ps, bps = PSMALL[gg % 2]
                    p.op("pe", lambda e, ex=ex, gg=gg, ps=ps: [
                        e.matmul(out=ps[:, :], lhsT=cs["oneh8"][:, ex * 128:(ex + 1) * 128],
                                 rhs=gthi[:, gg * 512:(gg + 1) * 512], start=True, stop=False),
                        e.matmul(out=ps[:, :], lhsT=cs["oneh8"][:, ex * 128:(ex + 1) * 128],
                                 rhs=gtlo[:, gg * 512:(gg + 1) * 512], start=False, stop=True)][-1],
                        reads=[b_gthi, b_gtlo, b_cs["oneh8"]], writes=[bps])
                    p.op("dve", lambda e, gg=gg, ps=ps: e.tensor_copy(
                        gwb[:, gg * 512:(gg + 1) * 512], ps[:, :]),
                        reads=[bps], writes=[b_gwb])
            for tgi in range(5):
                if tgi < 4:
                    tg = tgi
                    tc0 = tg * 1024
                    asb, bact = act_sb[tg % 2], b_act[tg % 2]
                    for pr_ in range(4):
                        ppar = pr_ % 2
                        glu, bglu = glu_t[ppar], b_glu[ppar]
                        up_t, bupt = upt[ppar], b_upt[ppar]
                        p.op("pe", lambda e, pr_=pr_, gs=gs, tc0=tc0: [
                            e.matmul(out=pA[:, gg * 512:(gg + 1) * 512],
                                     lhsT=gs[:, kc * 512 + pr_ * 128:kc * 512 + (pr_ + 1) * 128],
                                     rhs=xnfv(kc, tc0 + gg * 512, tc0 + (gg + 1) * 512),
                                     start=(kc == 0), stop=(kc == NKC - 1))
                            for kc in range(NKC) for gg in range(2)][-1],
                            reads=[bg, b_xnf], writes=[bufA0, bufA1])
                        p.op("dve", lambda e, pr_=pr_, iq=iq, glu=glu: e.tensor_scalar(
                            out=glu[:, :], in0=pA[:, :], scalar1=gub_s[:, iq * 4 + pr_:iq * 4 + pr_ + 1],
                            op0=ALU.add, scalar2=LIMIT, op1=ALU.min),
                            reads=[bufA0, bufA1, b_gub], writes=[bglu])
                        p.op("act", lambda e, glu=glu: e.activation(
                            out=glu[:, :], in_=glu[:, :], func=AF.Silu, scale=ALPHA),
                            writes=[bglu])
                        p.op("pe", lambda e, pr_=pr_, us=us, tc0=tc0: [
                            e.matmul(out=pB[:, gg * 512:(gg + 1) * 512],
                                     lhsT=us[:, kc * 512 + pr_ * 128:kc * 512 + (pr_ + 1) * 128],
                                     rhs=xnfv(kc, tc0 + gg * 512, tc0 + (gg + 1) * 512),
                                     start=(kc == 0), stop=(kc == NKC - 1))
                            for kc in range(NKC) for gg in range(2)][-1],
                            reads=[bu, b_xnf], writes=[bufB0, bufB1])
                        p.op("dve", lambda e, pr_=pr_, iq=iq, up_t=up_t: e.tensor_scalar(
                            out=up_t[:, :], in0=pB[:, :], scalar1=gub1[:, 16 + iq * 4 + pr_:16 + iq * 4 + pr_ + 1],
                            op0=ALU.add, scalar2=-(LIMIT - 1.0), op1=ALU.max),
                            reads=[bufB0, bufB1, b_gub1], writes=[bupt])
                        p.op("pool", lambda e, up_t=up_t: e.tensor_scalar(
                            out=up_t[:, :], in0=up_t[:, :], scalar1=LIMIT + 1.0, scalar2=-3.0e38,
                            op0=ALU.min, op1=ALU.max),
                            writes=[bupt])
                        p.op("pool" if pr_ % 2 == 0 else "dve",
                             lambda e, pr_=pr_, asb=asb, up_t=up_t, glu=glu: e.tensor_tensor(
                            out=asb[:, pr_ * 1024:(pr_ + 1) * 1024], in0=up_t[:, :], in1=glu[:, :],
                            op=ALU.mult),
                            reads=[bupt, bglu], writes=[bact])
                    if tgi == 0 and it_idx + 1 < len(it_list):
                        emit_ffnw(it_idx + 1)
                if tgi >= 1:
                    tg = tgi - 1
                    tc0 = tg * 1024
                    asb, bact = act_sb[tg % 2], b_act[tg % 2]
                    for hm in range(NKC):
                        if hm % 2 == 0:
                            outs = [(pY[:, 0:512], bufY0), (pY[:, 512:1024], bufY1)]
                        else:
                            outs = [(pS1[:, :], bufS1), (pS2[:, :], bufS2)]
                        p.op("pe", lambda e, hm=hm, ds=ds, asb=asb, outs=outs: [
                            e.matmul(out=outs[gg][0],
                                     lhsT=ds[:, ic * 1024 + hm * 128:ic * 1024 + (hm + 1) * 128],
                                     rhs=asb[:, ic * 1024 + gg * 512:ic * 1024 + (gg + 1) * 512],
                                     start=(ic == 0), stop=(ic == 3))
                            for ic in range(4) for gg in range(2)][-1],
                            reads=[bd, bact], writes=[outs[0][1], outs[1][1]])
                        dst, bdst2 = dstage[hm % 2], b_dst[hm % 2]
                        p.op("dve", lambda e, tc0=tc0, outs=outs, dst=dst: [
                            e.tensor_tensor(out=dst[:, gg * 512:(gg + 1) * 512], in0=outs[gg][0],
                                            in1=gwb[:, tc0 + gg * 512:tc0 + (gg + 1) * 512],
                                            op=ALU.mult)
                            for gg in range(2)][-1],
                            reads=[outs[0][1], outs[1][1], b_gwb], writes=[bdst2])
                        p.op("pool", lambda e, hm=hm, tc0=tc0, dst=dst: e.dma_start(
                            out=accd[hm, :, tc0:tc0 + 1024], in_=dst[:, :], accum_op=ALU.add),
                            reads=[bdst2], writes=[b_accd], dma=True)
        for hm in range(NKC):
            for g in range(8):
                ps, bps = PSMALL[g % 2]
                p.op("pe", lambda e, hm=hm, g=g, ps=ps: [
                    e.matmul(out=ps[:, :], lhsT=db_sb[:, hm * 128:(hm + 1) * 128],
                             rhs=gthi[:, g * 512:(g + 1) * 512], start=True, stop=False),
                    e.matmul(out=ps[:, :], lhsT=db_sb[:, hm * 128:(hm + 1) * 128],
                             rhs=gtlo[:, g * 512:(g + 1) * 512], start=False, stop=True)][-1],
                    reads=[b_db, b_gthi, b_gtlo], writes=[bps])
                dst, bdst2 = dstage[g % 2], b_dst[g % 2]
                p.op("dve", lambda e, ps=ps, dst=dst: e.tensor_copy(dst[:, 0:512], ps[:, :]),
                     reads=[bps], writes=[bdst2])
                p.op("pool", lambda e, hm=hm, g=g, dst=dst: e.dma_start(
                    out=accd[hm, :, g * 512:(g + 1) * 512], in_=dst[:, 0:512], accum_op=ALU.add),
                    reads=[bdst2], writes=[b_accd], dma=True)

        xr = [gsl[0].bitcast(F32)[:, 0:1024], gsl[0].bitcast(F32)[:, 1024:2048]]
        b_xr = [b_gsl[0], b_gsl[0]]
        astage = [usl[0].bitcast(F32)[:, 0:1024], usl[0].bitcast(F32)[:, 1024:2048]]
        b_ast = [b_usl[0], b_usl[0]]
        ostage = [dsl[0].bitcast(F32)[:, 0:1024], dsl[0].bitcast(F32)[:, 1024:2048]]
        b_ost = [b_dsl[0], b_dsl[0]]
        out_evs = []
        for s in range(2):
            for tt in range(NTT):
                par = tt % 2
                gc0 = s * T + tt * 128
                xrr, bxr = xr[par], b_xr[par]
                p.op("sp", lambda e, s=s, tt=tt, xrr=xrr: e.dma_start(
                    out=xrr[:, :], in_=x_d[s, tt * 128:(tt + 1) * 128, :]),
                    writes=[bxr], dma=True)
                ast, bast = astage[par], b_ast[par]
                p.op("sp", lambda e, gc0=gc0, ast=ast: e.dma_start(
                    out=ast.rearrange("p (kc t) -> p kc t", kc=NKC)[:, :, :],
                    in_=accd.rearrange("kc p t -> p kc t")[:, :, gc0:gc0 + 128]),
                    reads=[b_accd], writes=[bast], dma=True)
                for half in range(2):
                    ps, bps = PSMALL[half]
                    p.op("pe", lambda e, half=half, ast=ast, ps=ps: [
                        e.transpose(out=ps[:, q * 128:(q + 1) * 128],
                                    in_=ast[:, (half * 4 + q) * 128:(half * 4 + q + 1) * 128],
                                    identity=cs["ident_f32"][:, :])
                        for q in range(4)][-1],
                        reads=[bast, b_cs["ident_f32"]], writes=[bps])
                    ost, bost = ostage[par], b_ost[par]
                    p.op("dve", lambda e, half=half, xrr=xrr, ps=ps, ost=ost: e.tensor_tensor(
                        out=ost[:, half * 512:(half + 1) * 512], in0=ps[:, :],
                        in1=xrr[:, half * 512:(half + 1) * 512], op=ALU.add),
                        reads=[bps, bxr], writes=[bost])
                ost, bost = ostage[par], b_ost[par]
                ev = p.op("sp", lambda e, s=s, tt=tt, ost=ost: e.dma_start(
                    out=out_d[s, tt * 128:(tt + 1) * 128, :], in_=ost[:, :]),
                    reads=[bost], writes=[b_out], dma=True)
                out_evs.append(ev)
        p.op("sp", lambda e: e.nop(), deps=out_evs)

        with nc.Block() as block:
            counts = p.emit(nc, block, sems, dma_sems)
        ffn_es.close()
    return nc, counts, tap_out


_BUILD_CACHE = {}


def _get_nc():
    key = "full"
    if key not in _BUILD_CACHE:
        _BUILD_CACHE[key] = build_nc()
    return _BUILD_CACHE[key]


def kernel(L0_attn_x, L0_attn_Wq, L0_attn_Wk, L0_attn_Wv, L0_attn_Wp, L0_attn_Wproj,
           L0_attn_q_rms_w, L0_attn_k_rms_w, L0_ffn_router_weight, L0_ffn_router_bias,
           L0_ffn_gate_up_proj, L0_ffn_gate_up_bias, L0_ffn_down_proj, L0_ffn_down_bias):
    f32 = np.float32
    x = np.ascontiguousarray(np.asarray(L0_attn_x, dtype=f32))
    base = {
        "wqT": np.ascontiguousarray(np.asarray(L0_attn_Wq, f32).T),
        "wkT": np.ascontiguousarray(np.asarray(L0_attn_Wk, f32).T),
        "wvT": np.ascontiguousarray(np.asarray(L0_attn_Wv, f32).T),
        "wpT": np.ascontiguousarray(np.asarray(L0_attn_Wp, f32).T),
        "wprojT": np.ascontiguousarray(np.asarray(L0_attn_Wproj, f32).T),
        "qrw": np.ascontiguousarray(np.asarray(L0_attn_q_rms_w, f32).reshape(1, 64)),
        "krw": np.ascontiguousarray(np.asarray(L0_attn_k_rms_w, f32).reshape(1, 64)),
        "rw_l": np.ascontiguousarray(
            np.asarray(L0_ffn_router_weight, f32).T.reshape(8, 128, 8).transpose(1, 0, 2).reshape(128, 64)),
        "rb": np.ascontiguousarray(np.asarray(L0_ffn_router_bias, f32).reshape(1, 8)),
        "gup": np.ascontiguousarray(np.asarray(L0_ffn_gate_up_proj, f32)),
        "gub_l": np.ascontiguousarray(
            np.asarray(L0_ffn_gate_up_bias, f32).reshape(8, 32, 128).transpose(0, 2, 1)),
        "dwn": np.ascontiguousarray(np.asarray(L0_ffn_down_proj, f32)),
        "dbias": np.ascontiguousarray(np.asarray(L0_ffn_down_bias, f32)),
    }
    base.update(host_consts())
    nc, counts, _ = _get_nc()
    in_maps = []
    for c in range(8):
        m = dict(base)
        m["x"] = np.ascontiguousarray(x[2 * c:2 * c + 2])
        in_maps.append(m)
    res = run_bass_kernel_spmd(nc, in_maps, core_ids=list(range(8)))
    out = np.concatenate([np.asarray(r["out"]).reshape(2, T, C) for r in res.results], axis=0)
    return out.astype(np.float32)
